# version 08
# baseline (speedup 1.0000x reference)
import contextlib
import numpy as np
import concourse.bass as bass
import concourse.mybir as mybir
from concourse.bass_utils import run_bass_kernel_spmd

F32 = mybir.dt.float32
BF16 = mybir.dt.bfloat16
U8 = mybir.dt.uint8
U32 = mybir.dt.uint32
AF = mybir.ActivationFunctionType
ALU = mybir.AluOpType
AX = mybir.AxisListType

NCORES = 8
NB = 2
L = 2048
D = 1024
P = 128
NTOK = NB * L
NT = NTOK // P
TPB = L // P
EPS = 1e-5


class Sem:
    def __init__(self, h):
        self.h = h
        self.c = 0


class Q:
    def __init__(self, name):
        self.name = name
        self.ops = []

    def emit(self, eng):
        import os
        skips = set()
        sv = os.environ.get("GPQ_SKIP_" + self.name, "")
        for part in sv.split(","):
            if not part:
                continue
            if "-" in part:
                a, b = part.split("-")
                skips.update(range(int(a), int(b) + 1))
            else:
                skips.add(int(part))
        dbg = os.environ.get("GPQ_DBG") == "1"
        for idx, fn in enumerate(self.ops):
            if idx in skips:
                continue
            try:
                fn(eng)
            except Exception:
                if dbg:
                    print("QFAIL %s idx %d/%d" % (self.name, idx, len(self.ops)))
                raise


class B:
    ENGINES = ("sync", "gpsimd", "tensor", "vector", "scalar")

    def __init__(self, nc, es):
        self.nc = nc
        self.es = es
        self.sems = {}
        self.bufs = {}
        self.q = {n: Q(n) for n in self.ENGINES}
        self.n_instr = 0
        self.ns = ""

    def sem(self, name):
        if name not in self.sems:
            self.sems[name] = Sem(self.es.enter_context(self.nc.semaphore(name)))
        return self.sems[name]

    def barrier(self, snap):
        for qn in self.ENGINES:
            for name, s in list(self.sems.items()):
                if s.c > 0 and s.c != snap.get(name, 0):
                    self.wait(qn, s, s.c)
        return {name: s.c for name, s in self.sems.items()}

    def sbuf(self, name, shape, dtype):
        if name not in self.bufs:
            self.bufs[name] = self.es.enter_context(self.nc.sbuf_tensor(name, shape, dtype))
        return self.bufs[name]

    def psum(self, name, shape, dtype):
        if name not in self.bufs:
            self.bufs[name] = self.es.enter_context(self.nc.psum_tensor(name, shape, dtype))
        return self.bufs[name]

    def bank(self, k, w=1):
        return self.bufs["PSALL"][:, 512 * k:512 * (k + w)]

    def op(self, qn, fn, sem=None, n=1):
        self.n_instr += 1
        if qn == "gpsimd":
            fn = self.dyn(fn)
        if sem is not None:
            sem.c += n
            v = sem.c
            self.q[qn].ops.append(lambda e, fn=fn, s=sem.h, n=n: fn(e).then_inc(s, n))
            return v
        self.q[qn].ops.append(fn)
        return None

    def opS(self, qn, fn):
        s = self.sem("sSelf_" + qn)
        v = self.op(qn, fn, s, 1)
        self.wait(qn, s, v)
        return v

    def waitall(self, qn, sem):
        self.wait(qn, sem, sem.c)

    def wait(self, qn, sem, val):
        if val is None:
            raise ValueError("wait on None")
        self.q[qn].ops.append(lambda e, s=sem.h, v=val: e.wait_ge(s, v))

    def dma(self, qn, out, in_, sem):
        return self.op(qn, lambda e, o=out, i=in_: e.dma_start(out=o, in_=i), sem, 16)

    def _reg_head(self):
        self._prn = getattr(self, "_prn", 0) + 1
        h = self.nc.alloc_register(mybir.EngineType.Pool, "_pr%d" % self._prn)
        rid = self.nc.lookup_reg(h).reg_id
        self.nc.free_register(h)
        return rid

    def _free_regs(self, a, b):
        for rid in range(a, b):
            self._prn += 1
            h = self.nc.add_register(mybir.EngineType.Pool, "_fr%d" % self._prn, rid)
            self.nc.free_register(h)

    def dyn(self, fn):
        def _w(e, fn=fn):
            a = self._reg_head()
            r = fn(e)
            self._free_regs(a, self._reg_head())
            return r
        return _w

    def dmad(self, qn, out, in_, sem):
        return self.op(qn, self.dyn(lambda e, o=out, i=in_: e.dma_start(out=o, in_=i)),
                       sem, 16)

    def dmad_if(self, qn, out, in_, sem, rcond, dummy):
        self.n_instr += 4
        sem.c += 16
        v = sem.c

        def _go(e, o=out, i=in_, dm=dummy, s=sem.h, rc=rcond):
            a = self._reg_head()
            with e.If_cmp(rc, 0, comp_op="IS_NE"):
                e.dma_start(out=o, in_=i).then_inc(s, 16)
            with e.Else():
                e.dma_start(out=dm[0], in_=dm[1]).then_inc(s, 16)
            self._free_regs(a, self._reg_head())
        self.q[qn].ops.append(_go)
        return v

    def emit_all(self):
        with self.nc.Block() as block:
            @block.sync
            def _(e):
                self.q["sync"].emit(e)

            @block.gpsimd
            def _(e):
                self.q["gpsimd"].emit(e)

            @block.tensor
            def _(e):
                self.q["tensor"].emit(e)

            @block.vector
            def _(e):
                self.q["vector"].emit(e)

            @block.scalar
            def _(e):
                self.q["scalar"].emit(e)


def ap3(buf, offset, dims):
    return bass.AP(buf, offset, dims)


def alloc_shared(c):
    c.sbuf("XT", [P, 4 * D], F32)
    c.sbuf("STAT", [P, 32], F32)
    c.sbuf("MISC", [P, D], F32)
    c.sbuf("LNS", [P, 4 * NT], F32)
    c.sbuf("TBF", [P, 2 * D], F32)
    c.sbuf("TTB", [P, 16 * 8 * P], BF16)
    c.sbuf("H1T", [P, 32 * 1024], BF16)
    c.sbuf("WA", [P, 2 * 4096], BF16)
    c.sbuf("WC", [P, 4 * 512], BF16)
    c.sbuf("ACC", [P, 8 * 1024], F32)
    c.sbuf("MASKS", [P, 4 * NT], F32)
    c.sbuf("ONESB", [1, 512], BF16)
    c.sbuf("ONESF", [1, P], F32)
    c.sbuf("SMALLF", [1, 32], F32)
    c.sbuf("SMALLB", [1, 5120], BF16)
    c.sbuf("WSM", [P, 32], BF16)
    c.sbuf("WSM2", [P, 256], BF16)
    c.sbuf("GSC", [P, 168], F32)
    c.sbuf("CW", [P, 8 * 32], F32)
    c.sbuf("BCAST", [P, 40], F32)
    c.sbuf("OUTT", [P, 2 * D], F32)
    c.sbuf("IDENT", [P, P], F32)
    c.psum("PSALL", [P, 4096], F32)


def emit_init(c, consts):
    sC = c.sem("sC")
    sV = c.sem("sV")
    v = c.dma("sync", c.bufs["IDENT"][:, :], consts["cIDENT"][:, :], sC)
    c.op("vector", lambda e: e.memset(c.bufs["ONESB"][:, :], 1.0))
    vv = c.op("vector", lambda e: e.memset(c.bufs["ONESF"][:, :], 1.0), sV)
    c.wait("tensor", sC, v)
    c.wait("tensor", sV, vv)


def emit_ln(c, xsrc_fn, tile_ids, slot0=0, xt_guards=None, xn_mode=False, gcol=None,
            loader=None):
    XT, STAT, LNS, SQS, TBF, TTB, IDENT = (c.bufs[k] for k in
        ("XT", "STAT", "LNS", "MISC", "TBF", "TTB", "IDENT"))
    sSq, sSt, sRs, sTb, sTr, sEv = (c.sem(k) for k in
        ("sSq", "sSt", "sRs", "sTb", "sTr", "sEv"))
    sXp = [c.sem("sX0"), c.sem("sX1"), c.sem("sX2"), c.sem("sX3")]
    xv, sqv, stv, rsv, tbv, trv, evv = {}, {}, {}, {}, {}, {}, {}

    scl = 1 if xn_mode else 0
    n_tiles = len(tile_ids)
    for t in range(n_tiles):
        tid = tile_ids[t]
        g = 8 * (t % 2)
        sl = (t % 4) * D
        sX = sXp[t % 4]
        if t >= 4:
            c.wait("sync", sTb, tbv[t - 4])
        elif xt_guards and (t % 4) in xt_guards:
            gs, gv = xt_guards[t % 4]
            c.wait("sync", gs, gv)
        if loader is None:
            xv[t] = c.dma("sync", XT[:, sl:sl + D], xsrc_fn(tid), sX)
        else:
            sX = loader(t, XT[:, sl:sl + D])

        c.waitall("vector", sX)
        c.opS("vector", lambda e, sl=sl, g=g: e.tensor_reduce(
            out=STAT[:, g + 1:g + 2], in_=XT[:, sl:sl + D], axis=AX.X, op=ALU.add))
        stv[t] = c.op("vector", lambda e, t=tid, g=g: e.tensor_scalar(
            out=LNS[:, 4 * t:4 * t + 1], in0=STAT[:, g + 1:g + 2],
            scalar1=-1.0 / D, scalar2=None, op0=ALU.mult), sSt)

        c.waitall("scalar", sX)
        c.wait("scalar", sSt, stv[t])
        sqv[t] = c.op("scalar", lambda e, sl=sl, g=g, t=tid: e.activation(
            out=SQS[:, :], in_=XT[:, sl:sl + D], func=AF.Square,
            bias=LNS[:, 4 * t:4 * t + 1], scale=1.0,
            accum_out=STAT[:, g + 0:g + 1]), sSq)

        c.wait("vector", sSq, sqv[t])
        c.opS("vector", lambda e, g=g: e.tensor_scalar(
            out=STAT[:, g + 3:g + 4], in0=STAT[:, g + 0:g + 1],
            scalar1=1.0 / D, scalar2=EPS, op0=ALU.mult, op1=ALU.add))
        rcv = c.op("vector", lambda e, g=g: e.reciprocal(
            out=STAT[:, g + 4:g + 5], in_=STAT[:, g + 3:g + 4]), sSt)

        c.wait("scalar", sSt, rcv)
        rsv[t] = c.op("scalar", lambda e, t=tid, g=g: e.activation(
            out=LNS[:, 4 * t + 1:4 * t + 2], in_=STAT[:, g + 4:g + 5],
            func=AF.Sqrt), sRs)

        c.wait("vector", sRs, rsv[t])
        if xn_mode:
            c.opS("vector", lambda e, t=tid, g=g: e.tensor_tensor(
                out=STAT[:, g + 5:g + 6], in0=LNS[:, 4 * t + 1:4 * t + 2],
                in1=LNS[:, 4 * t + 1:4 * t + 2], op=ALU.mult))
            c.opS("vector", lambda e, g=g: e.reciprocal(
                out=STAT[:, g + 6:g + 7], in_=STAT[:, g + 5:g + 6]))
            s2v = c.op("vector", lambda e, g=g: e.tensor_scalar(
                out=STAT[:, g + 5:g + 6], in0=STAT[:, g + 6:g + 7],
                scalar1=-EPS, scalar2=None, op0=ALU.add), sSt)
            c.wait("scalar", sSt, s2v)
            sqv2 = c.op("scalar", lambda e, g=g: e.activation(
                out=STAT[:, g + 6:g + 7], in_=STAT[:, g + 5:g + 6], func=AF.Sqrt), sRs)
            c.wait("vector", sRs, sqv2)
            c.opS("vector", lambda e, t=tid, g=g: e.tensor_tensor(
                out=STAT[:, g + 5:g + 6], in0=LNS[:, 4 * t + 1:4 * t + 2],
                in1=STAT[:, g + 6:g + 7], op=ALU.mult))
            c.opS("vector", lambda e, g=g: e.tensor_scalar(
                out=STAT[:, g + 5:g + 6], in0=STAT[:, g + 5:g + 6],
                scalar1=1e-5, scalar2=None, op0=ALU.max))
            c.opS("vector", lambda e, g=g: e.reciprocal(
                out=STAT[:, g + 6:g + 7], in_=STAT[:, g + 5:g + 6]))
            c.opS("vector", lambda e, g=g, gcol=gcol: e.tensor_tensor(
                out=STAT[:, g + 5:g + 6], in0=STAT[:, g + 6:g + 7],
                in1=gcol, op=ALU.mult))
            c.opS("vector", lambda e, t=tid, g=g: e.tensor_tensor(
                out=LNS[:, 4 * t + 2:4 * t + 3], in0=STAT[:, g + 5:g + 6],
                in1=LNS[:, 4 * t + 1:4 * t + 2], op=ALU.mult))
        if t >= 2:
            c.wait("vector", sTr, trv[t - 2])
        tbv[t] = c.op("vector", lambda e, t=t, tid=tid, sl=sl: e.tensor_scalar(
            out=TBF[:, (t % 2) * D:(t % 2 + 1) * D], in0=XT[:, sl:sl + D],
            scalar1=LNS[:, 4 * tid:4 * tid + 1], scalar2=LNS[:, 4 * tid + 1 + scl:4 * tid + 2 + scl],
            op0=ALU.add, op1=ALU.mult), sTb)

        c.wait("tensor", sTb, tbv[t])
        if t >= 2:
            c.wait("tensor", sEv, evv[t - 2])
        for dt in range(8):
            fn = lambda e, t=t, dt=dt: e.transpose(
                out=c.bank(2 * (t % 2), 2)[:, dt * P:(dt + 1) * P],
                in_=TBF[:, (t % 2) * D + dt * P:(t % 2) * D + (dt + 1) * P],
                identity=IDENT[:, :])
            if dt == 7:
                trv[t] = c.op("tensor", fn, sTr)
            else:
                c.op("tensor", fn)

        c.wait("scalar", sTr, trv[t])
        slot = slot0 + t
        evv[t] = c.op("scalar", lambda e, t=t, slot=slot: e.copy(
            out=TTB[:, slot * 1024:(slot + 1) * 1024], in_=c.bank(2 * (t % 2), 2)), sEv)
    return {"evv": evv, "tbv": tbv, "ln_last_ev": evv[n_tiles - 1],
            "tb_last": tbv[n_tiles - 1]}


def emit_l0ffn(c, W, xsrc_fn, xdst_fn):
    TTB, H1T, WA, WC, ACC, MASKS, XT, STAT = (c.bufs[k] for k in
        ("TTB", "H1T", "WA", "WC", "ACC", "MASKS", "XT", "STAT"))
    WSM, SMALLF, SMALLB, ONESB, ONESF, BCAST = (c.bufs[k] for k in
        ("WSM", "SMALLF", "SMALLB", "ONESB", "ONESF", "BCAST"))
    sMM, sEp, sAc, sV, sEv, sSq, sTb = (c.sem(k) for k in
        ("sMM", "sEp", "sAc", "sV", "sEv", "sSq", "sTb"))
    sCst = c.sem("sCst")
    sBE = c.sem("sBE")
    sWA = [c.sem("sWA0"), c.sem("sWA1")]
    sWC = [c.sem("sWC0"), c.sem("sWC1"), c.sem("sWC2"), c.sem("sWC3")]
    sXp = [c.sem("sX0"), c.sem("sX1"), c.sem("sX2"), c.sem("sX3")]
    sOutp = [c.sem("sOutA"), c.sem("sOutB")]

    c.dma("gpsimd", WSM[:, 0:32].rearrange("p (dt e) -> p dt e", e=4),
          W["Wr"].rearrange("(dt p) e -> p dt e", p=P), sCst)
    c.dma("gpsimd", SMALLF[:, 0:4], ap3(W["br"], 0, [[4, 1], [1, 4]]), sCst)
    c.waitall("tensor", sCst)
    v = c.op("tensor", lambda e: e.matmul(out=c.bank(6)[:, 0:4], lhsT=ONESF[:, :],
                                          rhs=SMALLF[:, 0:4], start=True, stop=True), sMM)
    c.wait("scalar", sMM, v)
    brB_v = c.op("scalar", lambda e: e.copy(out=BCAST[:, 0:4], in_=c.bank(6)[:, 0:4]), sAc)
    c.op("vector", lambda e: e.memset(STAT[:, 31:32], 0.0), sV)

    NGRP, GT = 4, 8
    TTB_PITCH = 16 * 8 * P
    chunk_seq = 0
    wa_done = {}
    prod_seq = 0
    prod_evac = {}
    pc_last_mm = None
    prev_ep_all = None
    out_v = {}

    for g in range(NGRP):
        tiles = list(range(g * GT, (g + 1) * GT))
        xt_guards = None
        if g > 0:
            xt_guards = {s: (sV, out_v[("vadd", g - 1, 4 + s)]) for s in range(4)}
        ln = emit_ln(c, xsrc_fn, tiles, slot0=0, xt_guards=xt_guards)

        ep_tt = {}
        for i in range(GT):
            tid = tiles[i]
            bk = 6 + (i % 2)
            c.wait("tensor", sEv, ln["evv"][i])
            if i >= 2:
                c.wait("tensor", sEp, ep_tt[i - 2])
            elif prev_ep_all is not None:
                c.wait("tensor", sEp, prev_ep_all)
            if g == 0 and i == 0:
                c.wait("tensor", sAc, brB_v)
            for dt in range(8):
                fn = lambda e, i=i, dt=dt, bk=bk: e.matmul(
                    out=c.bank(bk)[:, 0:4],
                    lhsT=TTB[:, i * 1024 + dt * P: i * 1024 + (dt + 1) * P],
                    rhs=WSM[:, 4 * dt:4 * dt + 4], start=(dt == 0), stop=(dt == 7))
                if dt == 7:
                    mmv = c.op("tensor", fn, sMM)
                else:
                    c.op("tensor", fn)
            gg = 16 + 8 * (i % 2)
            c.wait("vector", sMM, mmv)
            if g == 0 and i == 0:
                c.wait("vector", sAc, brB_v)
            ep_tt[i] = c.op("vector", lambda e, gg=gg, bk=bk: e.tensor_tensor(
                out=STAT[:, gg:gg + 4], in0=c.bank(bk)[:, 0:4], in1=BCAST[:, 0:4],
                op=ALU.add), sEp)
            c.wait("vector", sEp, ep_tt[i])
            c.opS("vector", lambda e, gg=gg: e.tensor_reduce(
                out=STAT[:, gg + 4:gg + 5], in_=STAT[:, gg:gg + 4], axis=AX.X, op=ALU.max))
            c.opS("vector", lambda e, gg=gg: e.tensor_scalar(
                out=STAT[:, gg:gg + 4], in0=STAT[:, gg:gg + 4],
                scalar1=STAT[:, gg + 4:gg + 5], scalar2=None, op0=ALU.is_ge))
            c.opS("vector", lambda e, t=tid, gg=gg: e.tensor_tensor_scan(
                out=MASKS[:, 4 * t:4 * t + 4], data0=STAT[:, gg:gg + 4],
                data1=STAT[:, gg:gg + 4], initial=STAT[:, 31:32], op0=ALU.add, op1=ALU.bypass))
            c.opS("vector", lambda e, t=tid, gg=gg: e.tensor_tensor(
                out=MASKS[:, 4 * t:4 * t + 4], in0=MASKS[:, 4 * t:4 * t + 4],
                in1=STAT[:, gg:gg + 4], op=ALU.is_equal))
            mask_v = c.op("vector", lambda e, t=tid, gg=gg: e.tensor_tensor(
                out=MASKS[:, 4 * t:4 * t + 4], in0=MASKS[:, 4 * t:4 * t + 4],
                in1=STAT[:, gg:gg + 4], op=ALU.mult), sEp)
        mask_done = mask_v

        accz = c.op("vector", lambda e: e.memset(ACC[:, :], 0.0), sEp)

        for ei in range(4):
            if not (g == 0 and ei == 0):
                c.wait("gpsimd", sMM, last_opener_mm)
            c.dma("gpsimd", SMALLB[:, 0:4096],
                  ap3(W["be1"], 4096 * ei, [[4096, 1], [1, 4096]]), sBE)
            c.dma("gpsimd", SMALLB[:, 4096:4096 + 1024],
                  ap3(W["be2"], 1024 * ei, [[1024, 1], [1, 1024]]), sBE)
            first_of_ge = True
            for ch in range(8):
                slot = chunk_seq % 2
                if chunk_seq >= 2:
                    c.wait("gpsimd", sMM, wa_done[chunk_seq - 2])
                c.dma("gpsimd",
                      WA[:, slot * 4096:(slot + 1) * 4096]
                      .rearrange("p (dt f) -> p dt f", f=512),
                      W["We1"][ei][:, 512 * ch:512 * (ch + 1)]
                      .rearrange("(dt p) f -> p dt f", p=P), sWA[slot])
                c.waitall("tensor", sWA[slot])
                if ch == 0:
                    c.waitall("tensor", sBE)
                    if prev_ep_all is not None and ei == 0:
                        c.wait("tensor", sEp, prev_ep_all)
                    c.wait("tensor", sEp, mask_done)
                    c.wait("tensor", sEp, accz)
                    if ei > 0:
                        c.wait("tensor", sEp, cur_ep_all)
                for ft in range(4):
                    for th in range(2):
                        bk = 4 + (prod_seq % 2)
                        if prod_seq >= 2:
                            c.wait("tensor", sAc, prod_evac[prod_seq - 2])
                        c.op("tensor", lambda e, ch=ch, ft=ft, bk=bk: e.matmul(
                            out=c.bank(bk),
                            lhsT=SMALLB[0:1, 512 * ch + 128 * ft:512 * ch + 128 * (ft + 1)],
                            rhs=ONESB[0:1, 0:512], start=True, stop=False))
                        for dt in range(8):
                            off = (th * 4) * 1024 + dt * P
                            rhs = ap3(TTB, off, [[TTB_PITCH, P], [1024, 4], [1, P]])
                            fn = lambda e, slot=slot, dt=dt, ft=ft, rhs=rhs, bk=bk: e.matmul(
                                out=c.bank(bk),
                                lhsT=WA[:, slot * 4096 + dt * 512 + 128 * ft:
                                        slot * 4096 + dt * 512 + 128 * (ft + 1)],
                                rhs=rhs, start=False, stop=(dt == 7))
                            if dt == 7:
                                mmv = c.op("tensor", fn, sMM)
                            else:
                                c.op("tensor", fn)
                        fi = ch * 4 + ft
                        c.wait("scalar", sMM, mmv)
                        if first_of_ge and pc_last_mm is not None:
                            c.wait("scalar", sMM, pc_last_mm)
                        first_of_ge = False
                        prod_evac[prod_seq] = c.op("scalar",
                            lambda e, fi=fi, th=th, bk=bk: e.activation(
                                out=H1T[:, fi * 1024 + th * 512: fi * 1024 + th * 512 + 512],
                                in_=c.bank(bk), func=AF.Relu), sAc)
                        prod_seq += 1
                wa_done[chunk_seq] = mmv
                chunk_seq += 1
            h1_done = prod_evac[prod_seq - 1]

            wc_seq = 0
            wc_done = {}
            ep_bank = {}
            for oh in range(2):
                for ft in range(32):
                    slot = wc_seq % 4
                    if wc_seq >= 4:
                        c.wait("gpsimd", sMM, wc_done[wc_seq - 4])
                    c.dma("gpsimd", WC[:, slot * 512:(slot + 1) * 512],
                          W["We2"][ei][128 * ft:128 * (ft + 1), 512 * oh:512 * (oh + 1)],
                          sWC[slot])
                    c.waitall("tensor", sWC[slot])
                    if ft == 0:
                        c.wait("tensor", sAc, h1_done)
                        if oh == 1:
                            for tau in range(8):
                                c.wait("tensor", sEp, ep_bank[(tau, 0)])
                        for tau in range(8):
                            last_opener_mm = c.op("tensor", lambda e, tau=tau, oh=oh: e.matmul(
                                out=c.bank(tau), lhsT=ONESB[0:1, 0:P],
                                rhs=SMALLB[0:1, 4096 + 512 * oh:4096 + 512 * (oh + 1)],
                                start=True, stop=False), sMM)
                    for tau in range(8):
                        fn = lambda e, ft=ft, tau=tau, slot=slot: e.matmul(
                            out=c.bank(tau),
                            lhsT=H1T[:, ft * 1024 + tau * P: ft * 1024 + (tau + 1) * P],
                            rhs=WC[:, slot * 512:(slot + 1) * 512],
                            start=False, stop=(ft == 31))
                        if tau == 7:
                            mmv = c.op("tensor", fn, sMM)
                        else:
                            c.op("tensor", fn)
                    wc_done[wc_seq] = mmv
                    wc_seq += 1
                pc_last_mm = mmv
                c.wait("vector", sMM, mmv)
                for tau in range(8):
                    tt = g * GT + tau
                    ep_bank[(tau, oh)] = c.op("vector",
                        lambda e, tau=tau, oh=oh, tt=tt, ei=ei: e.scalar_tensor_tensor(
                            out=ACC[:, tau * 1024 + oh * 512: tau * 1024 + oh * 512 + 512],
                            in0=c.bank(tau), scalar=MASKS[:, 4 * tt + ei:4 * tt + ei + 1],
                            in1=ACC[:, tau * 1024 + oh * 512: tau * 1024 + oh * 512 + 512],
                            op0=ALU.mult, op1=ALU.add), sEp)
            cur_ep_all = ep_bank[(7, 1)]
        prev_ep_all = cur_ep_all

        OUTT = c.bufs["OUTT"]
        for tau in range(8):
            tt = g * GT + tau
            sl = (tau % 4) * D
            c.wait("sync", sTb, ln["tbv"][4 + (tau % 4)])
            if tau >= 4:
                c.wait("sync", sV, out_v[("vadd", g, tau - 4)])
            sX = sXp[tau % 4]
            xv = c.dma("sync", XT[:, sl:sl + D], xsrc_fn(tt), sX)
            c.waitall("vector", sX)
            c.waitall("vector", sOutp[tau % 2])
            vv = c.op("vector", lambda e, sl=sl, tau=tau: e.tensor_tensor(
                out=OUTT[:, (tau % 2) * D:(tau % 2 + 1) * D], in0=XT[:, sl:sl + D],
                in1=ACC[:, tau * 1024:(tau + 1) * 1024], op=ALU.add), sV)
            out_v[("vadd", g, tau)] = vv
            c.wait("sync", sV, vv)
            out_v[("st", g, tau)] = c.dma("sync", xdst_fn(tt),
                                          OUTT[:, (tau % 2) * D:(tau % 2 + 1) * D],
                                          sOutp[tau % 2])
    c.waitall("sync", sOutp[0])
    c.waitall("sync", sOutp[1])


def emit_l1ffn(c, W, xsrc_fn, xdst_fn):
    TTB, H1T, WA, WC, ACC, CW, XT, STAT, GSC = (c.bufs[k] for k in
        ("TTB", "H1T", "WA", "WC", "ACC", "CW", "XT", "STAT", "GSC"))
    WSM2, SMALLF, SMALLB, ONESB, ONESF, BCAST, OUTT = (c.bufs[k] for k in
        ("WSM2", "SMALLF", "SMALLB", "ONESB", "ONESF", "BCAST", "OUTT"))
    sMM, sEp, sAc, sV, sEv, sSq, sSt, sTb = (c.sem(k) for k in
        ("sMM", "sEp", "sAc", "sV", "sEv", "sSq", "sSt", "sTb"))
    sCst = c.sem("sCst")
    sBE = c.sem("sBE")
    sWA = [c.sem("sWA0"), c.sem("sWA1")]
    sWC = [c.sem("sWC0"), c.sem("sWC1"), c.sem("sWC2"), c.sem("sWC3")]
    sXp = [c.sem("sX0"), c.sem("sX1"), c.sem("sX2"), c.sem("sX3")]
    sOutp = [c.sem("sOutA"), c.sem("sOutB")]
    BIG = 60000.0

    c.dma("gpsimd", WSM2[:, :].rearrange("p (dt e) -> p dt e", e=32),
          W["gw"].rearrange("(dt p) e -> p dt e", p=P), sCst)
    c.dma("gpsimd", SMALLF[:, 0:32], ap3(W["gb"], 0, [[32, 1], [1, 32]]), sCst)
    c.waitall("tensor", sCst)
    v = c.op("tensor", lambda e: e.matmul(out=c.bank(5)[:, 0:32], lhsT=ONESF[:, :],
                                          rhs=SMALLF[:, 0:32], start=True, stop=True), sMM)
    c.wait("scalar", sMM, v)
    gbB_v = c.op("scalar", lambda e: e.copy(out=BCAST[:, 0:32], in_=c.bank(5)[:, 0:32]), sAc)

    NGRP, GT = 4, 8
    TTB_PITCH = 16 * 8 * P
    chunk_seq = 0
    wa_done = {}
    prod_seq = 0
    prod_evac = {}
    pc_last_mm = None
    prev_ep_all = None
    out_v = {}

    def phase_prod(g, src_h, src_off, n_ft, slot0, bias_off, act, extra_mm_wait=None):
        nonlocal chunk_seq, prod_seq
        nch = (n_ft + 3) // 4
        last_ev = None
        for ch in range(nch):
            slot = chunk_seq % 2
            if chunk_seq >= 2:
                c.wait("gpsimd", sMM, wa_done[chunk_seq - 2])
            cols0 = src_off + 512 * ch
            ncol = min(512, src_off + 128 * n_ft - cols0)
            c.dma("gpsimd",
                  WA[:, slot * 4096:slot * 4096 + 8 * ncol]
                  .rearrange("p (dt f) -> p dt f", f=ncol),
                  src_h[:, cols0:cols0 + ncol].rearrange("(dt p) f -> p dt f", p=P),
                  sWA[slot])
            c.waitall("tensor", sWA[slot])
            if ch == 0:
                c.waitall("tensor", sBE)
                if extra_mm_wait is not None:
                    for w in extra_mm_wait:
                        c.wait("tensor", *w)
            for ft in range(ncol // 128):
                for th in range(2):
                    bk = 4 + (prod_seq % 2)
                    if prod_seq >= 2:
                        c.wait("tensor", sAc, prod_evac[prod_seq - 2])
                    c.op("tensor", lambda e, ch=ch, ft=ft, bk=bk, bias_off=bias_off: e.matmul(
                        out=c.bank(bk),
                        lhsT=SMALLB[0:1, bias_off + 512 * ch + 128 * ft:
                                    bias_off + 512 * ch + 128 * (ft + 1)],
                        rhs=ONESB[0:1, 0:512], start=True, stop=False))
                    for dt in range(8):
                        off = (th * 4) * 1024 + dt * P
                        rhs = ap3(TTB, off, [[TTB_PITCH, P], [1024, 4], [1, P]])
                        fn = lambda e, slot=slot, dt=dt, ft=ft, rhs=rhs, bk=bk, ncol=ncol: e.matmul(
                            out=c.bank(bk),
                            lhsT=WA[:, slot * 4096 + dt * ncol + 128 * ft:
                                    slot * 4096 + dt * ncol + 128 * (ft + 1)],
                            rhs=rhs, start=False, stop=(dt == 7))
                        if dt == 7:
                            mmv = c.op("tensor", fn, sMM)
                        else:
                            c.op("tensor", fn)
                    fi = slot0 + ch * 4 + ft
                    c.wait("scalar", sMM, mmv)
                    prod_evac[prod_seq] = c.op("scalar",
                        lambda e, fi=fi, th=th, bk=bk, act=act: e.activation(
                            out=H1T[:, fi * 1024 + th * 512: fi * 1024 + th * 512 + 512],
                            in_=c.bank(bk), func=act), sAc)
                    last_ev = prod_evac[prod_seq]
                    prod_seq += 1
            wa_done[chunk_seq] = mmv
            chunk_seq += 1
        return last_ev

    def phase_mult(h1_slot0, h3_slot0, hs_slot0, n_ft, prod_last):
        c.wait("vector", sAc, prod_last)
        vv = None
        for i in range(n_ft):
            vv = c.op("vector", lambda e, i=i, a=h1_slot0, b=h3_slot0, o=hs_slot0: e.tensor_tensor(
                out=H1T[:, (o + i) * 1024:(o + i + 1) * 1024],
                in0=H1T[:, (a + i) * 1024:(a + i + 1) * 1024],
                in1=H1T[:, (b + i) * 1024:(b + i + 1) * 1024], op=ALU.mult), sV)
        return vv

    def phase_out(g, ei, hs_slot0, n_ft, w2_h, b2_off, cw_scalar_fn, hs_ready):
        nonlocal pc_last_mm
        wc_seq = 0
        wc_done = {}
        ep_bank = {}
        for oh in range(2):
            for ft in range(n_ft):
                slot = wc_seq % 4
                if wc_seq >= 4:
                    c.wait("gpsimd", sMM, wc_done[wc_seq - 4])
                c.dma("gpsimd", WC[:, slot * 512:(slot + 1) * 512],
                      w2_h[128 * ft:128 * (ft + 1), 512 * oh:512 * (oh + 1)], sWC[slot])
                c.waitall("tensor", sWC[slot])
                if ft == 0:
                    c.wait("vector", sV, hs_ready) if False else None
                    c.wait("tensor", sV, hs_ready)
                    if oh == 1:
                        for tau in range(8):
                            c.wait("tensor", sEp, ep_bank[(tau, 0)])
                    for tau in range(8):
                        pc_last_mm = c.op("tensor", lambda e, tau=tau, oh=oh, b2_off=b2_off: e.matmul(
                            out=c.bank(tau), lhsT=ONESB[0:1, 0:P],
                            rhs=SMALLB[0:1, b2_off + 512 * oh:b2_off + 512 * (oh + 1)],
                            start=True, stop=False), sMM)
                for tau in range(8):
                    fn = lambda e, ft=ft, tau=tau, slot=slot, hs_slot0=hs_slot0: e.matmul(
                        out=c.bank(tau),
                        lhsT=H1T[:, (hs_slot0 + ft) * 1024 + tau * P:
                                 (hs_slot0 + ft) * 1024 + (tau + 1) * P],
                        rhs=WC[:, slot * 512:(slot + 1) * 512],
                        start=False, stop=(ft == n_ft - 1))
                    if tau == 7:
                        mmv = c.op("tensor", fn, sMM)
                    else:
                        c.op("tensor", fn)
                wc_done[wc_seq] = mmv
                wc_seq += 1
            pc_last_mm = mmv
            c.wait("vector", sMM, mmv)
            for tau in range(8):
                sc = cw_scalar_fn(tau)
                ep_bank[(tau, oh)] = c.op("vector",
                    lambda e, tau=tau, oh=oh, sc=sc: e.scalar_tensor_tensor(
                        out=ACC[:, tau * 1024 + oh * 512: tau * 1024 + oh * 512 + 512],
                        in0=c.bank(tau), scalar=sc,
                        in1=ACC[:, tau * 1024 + oh * 512: tau * 1024 + oh * 512 + 512],
                        op0=ALU.mult, op1=ALU.add), sEp)
        return ep_bank[(7, 1)]

    for g in range(NGRP):
        tiles = list(range(g * GT, (g + 1) * GT))
        xt_guards = None
        if g > 0:
            xt_guards = {s: (sV, out_v[("vadd", g - 1, 4 + s)]) for s in range(4)}
        ln = emit_ln(c, xsrc_fn, tiles, slot0=0, xt_guards=xt_guards)

        for i in range(GT):
            bk = 6 + (i % 2)
            c.wait("tensor", sEv, ln["evv"][i])
            if i >= 2:
                c.wait("tensor", sEp, ep_cw[i - 2])
            elif prev_ep_all is not None:
                c.wait("tensor", sEp, prev_ep_all)
            if g == 0 and i == 0:
                c.wait("tensor", sAc, gbB_v)
            for dt in range(8):
                fn = lambda e, i=i, dt=dt, bk=bk: e.matmul(
                    out=c.bank(bk)[:, 0:32],
                    lhsT=TTB[:, i * 1024 + dt * P: i * 1024 + (dt + 1) * P],
                    rhs=WSM2[:, 32 * dt:32 * dt + 32], start=(dt == 0), stop=(dt == 7))
                if dt == 7:
                    mmv = c.op("tensor", fn, sMM)
                else:
                    c.op("tensor", fn)
            if i == 0:
                ep_cw = {}
            c.wait("vector", sMM, mmv)
            if g == 0 and i == 0:
                c.wait("vector", sAc, gbB_v)
            c.opS("vector", lambda e, bk=bk: e.tensor_tensor(
                out=GSC[:, 0:32], in0=c.bank(bk)[:, 0:32], in1=BCAST[:, 0:32], op=ALU.add))
            ep_cw[i] = c.op("vector", lambda e: e.nop(), sEp)
            c.opS("vector", lambda e: e.tensor_reduce(
                out=GSC[:, 32:33], in_=GSC[:, 0:32], axis=AX.X, op=ALU.max))
            nm = c.op("vector", lambda e: e.tensor_scalar(
                out=GSC[:, 33:34], in0=GSC[:, 32:33], scalar1=-1.0, scalar2=None,
                op0=ALU.mult), sSt)
            c.wait("scalar", sSt, nm)
            ex = c.op("scalar", lambda e: e.activation(
                out=GSC[:, 64:96], in_=GSC[:, 0:32], func=AF.Exp,
                bias=GSC[:, 33:34], scale=1.0, accum_out=GSC[:, 34:35]), sSq)
            c.wait("vector", sSq, ex)
            c.opS("vector", lambda e: e.reciprocal(out=GSC[:, 35:36], in_=GSC[:, 34:35]))
            c.opS("vector", lambda e: e.tensor_scalar(
                out=GSC[:, 0:32], in0=GSC[:, 64:96], scalar1=GSC[:, 35:36],
                scalar2=None, op0=ALU.mult))
            pairs = [(0, 1), (0, 2), (0, 3), (1, 2), (1, 3), (2, 3)]
            for pi, (a, b) in enumerate(pairs):
                c.op("vector", lambda e, pi=pi, a=a, b=b: e.tensor_tensor(
                    out=GSC[:, 96 + 8 * pi:96 + 8 * (pi + 1)],
                    in0=ap3(GSC, a, [[168, P], [4, 8]]),
                    in1=ap3(GSC, b, [[168, P], [4, 8]]), op=ALU.add))
            c.opS("vector", lambda e: e.nop())
            c.opS("vector", lambda e: e.tensor_reduce(
                out=GSC[:, 144:152], in_=ap3(GSC, 96, [[168, P], [1, 8], [8, 6]]),
                axis=AX.X, op=ALU.max))
            c.op("vector", lambda e: e.memset(GSC[:, 152:160], 0.0))
            for gi in range(8):
                c.opS("vector", lambda e, gi=gi: e.tensor_scalar(
                    out=GSC[:, 160:168], in0=GSC[:, 144:152],
                    scalar1=GSC[:, 144 + gi:145 + gi], scalar2=None, op0=ALU.is_lt))
                c.opS("vector", lambda e: e.tensor_tensor(
                    out=GSC[:, 152:160], in0=GSC[:, 152:160], in1=GSC[:, 160:168],
                    op=ALU.add))
            c.opS("vector", lambda e: e.tensor_scalar(
                out=GSC[:, 152:160], in0=GSC[:, 152:160], scalar1=3.5, scalar2=None,
                op0=ALU.is_le))
            c.opS("vector", lambda e: e.tensor_scalar(
                out=GSC[:, 64:96], in0=GSC[:, 0:32], scalar1=BIG, scalar2=None, op0=ALU.add))
            c.opS("vector", lambda e: e.tensor_tensor(
                out=GSC[:, 64:96], in0=GSC[:, 64:96],
                in1=ap3(GSC, 152, [[168, P], [1, 8], [0, 4]]), op=ALU.mult))
            c.opS("vector", lambda e: e.tensor_scalar(
                out=GSC[:, 64:96], in0=GSC[:, 64:96], scalar1=-BIG, scalar2=None, op0=ALU.add))
            c.op("vector", lambda e: e.memset(GSC[:, 96:128], 0.0))
            for r in range(4):
                c.opS("vector", lambda e: e.tensor_reduce(
                    out=GSC[:, 36:37], in_=GSC[:, 64:96], axis=AX.X, op=ALU.max))
                c.opS("vector", lambda e: e.tensor_scalar(
                    out=GSC[:, 128:160], in0=GSC[:, 64:96], scalar1=GSC[:, 36:37],
                    scalar2=None, op0=ALU.is_ge))
                c.opS("vector", lambda e: e.tensor_tensor(
                    out=GSC[:, 96:128], in0=GSC[:, 96:128], in1=GSC[:, 128:160], op=ALU.max))
                c.opS("vector", lambda e: e.scalar_tensor_tensor(
                    out=GSC[:, 64:96], in0=GSC[:, 128:160], scalar=-2 * BIG,
                    in1=GSC[:, 64:96], op0=ALU.mult, op1=ALU.add))
            c.op("vector", lambda e, i=i: e.tensor_tensor(
                out=CW[:, 32 * i:32 * (i + 1)], in0=GSC[:, 0:32], in1=GSC[:, 96:128],
                op=ALU.mult), sEp)

        accz = c.op("vector", lambda e: e.memset(ACC[:, :], 0.0), sEp)
        first_wait = [(sEp, accz)]
        if prev_ep_all is not None:
            first_wait.append((sEp, prev_ep_all))
        if pc_last_mm is not None:
            pass

        if not (g == 0):
            c.wait("gpsimd", sMM, pc_last_mm)
        c.dma("gpsimd", SMALLB[:, 0:1024], ap3(W["sb1"], 0, [[1024, 1], [1, 1024]]), sBE)
        c.dma("gpsimd", SMALLB[:, 1024:2048], ap3(W["sb3"], 0, [[1024, 1], [1, 1024]]), sBE)
        c.dma("gpsimd", SMALLB[:, 4096:5120], ap3(W["sb2"], 0, [[1024, 1], [1, 1024]]), sBE)
        phase_prod(g, W["sw1"], 0, 8, 0, 0, AF.Silu, extra_mm_wait=first_wait)
        ev3 = phase_prod(g, W["sw3"], 0, 8, 8, 1024, AF.Identity)
        hsv = phase_mult(0, 8, 16, 8, ev3)
        cur_ep_all = phase_out(g, "S", 16, 8, W["sw2"], 4096, lambda tau: 1.0, hsv)

        for ei in range(32):
            c.wait("gpsimd", sMM, pc_last_mm)
            c.dma("gpsimd", SMALLB[:, 0:512],
                  ap3(W["b1"], 512 * ei, [[512, 1], [1, 512]]), sBE)
            c.dma("gpsimd", SMALLB[:, 512:1024],
                  ap3(W["b3"], 512 * ei, [[512, 1], [1, 512]]), sBE)
            c.dma("gpsimd", SMALLB[:, 4096:5120],
                  ap3(W["b2"], 1024 * ei, [[1024, 1], [1, 1024]]), sBE)
            phase_prod(g, W["w1"][ei], 0, 4, 0, 0, AF.Silu,
                       extra_mm_wait=[(sEp, cur_ep_all)])
            ev3 = phase_prod(g, W["w3"][ei], 0, 4, 4, 512, AF.Identity)
            hsv = phase_mult(0, 4, 8, 4, ev3)
            cur_ep_all = phase_out(g, ei, 8, 4, W["w2"][ei], 4096,
                                   lambda tau, ei=ei: CW[:, 32 * tau + ei:32 * tau + ei + 1],
                                   hsv)
        prev_ep_all = cur_ep_all

        for tau in range(8):
            tt = g * GT + tau
            sl = (tau % 4) * D
            c.wait("sync", sTb, ln["tbv"][4 + (tau % 4)])
            if tau >= 4:
                c.wait("sync", sV, out_v[("vadd", g, tau - 4)])
            sX = sXp[tau % 4]
            xv = c.dma("sync", XT[:, sl:sl + D], xsrc_fn(tt), sX)
            c.waitall("vector", sX)
            c.waitall("vector", sOutp[tau % 2])
            vv = c.op("vector", lambda e, sl=sl, tau=tau: e.tensor_tensor(
                out=OUTT[:, (tau % 2) * D:(tau % 2 + 1) * D], in0=XT[:, sl:sl + D],
                in1=ACC[:, tau * 1024:(tau + 1) * 1024], op=ALU.add), sV)
            out_v[("vadd", g, tau)] = vv
            c.wait("sync", sV, vv)
            out_v[("st", g, tau)] = c.dma("sync", xdst_fn(tt),
                                          OUTT[:, (tau % 2) * D:(tau % 2 + 1) * D],
                                          sOutp[tau % 2])
    c.waitall("sync", sOutp[0])
    c.waitall("sync", sOutp[1])


def emit_l0aa(c, W, xsrc_fn, xdst_fn):
    nc = c.nc
    TTB, H1T, WA, XT, STAT, LNS, TBF, OUTT, MISC, ACC, BCAST = (c.bufs[k] for k in
        ("TTB", "H1T", "WA", "XT", "STAT", "LNS", "TBF", "OUTT", "MISC", "ACC", "BCAST"))
    SMALLF, ONESF, IDENT = (c.bufs[k] for k in ("SMALLF", "ONESF", "IDENT"))
    sMM, sEp, sAc, sV, sEv, sSq, sSt, sTb = (c.sem(k) for k in
        ("sMM", "sEp", "sAc", "sV", "sEv", "sSq", "sSt", "sTb"))
    sCst = c.sem("sCst")
    sWA = [c.sem("sWA0"), c.sem("sWA1")]
    sXp = [c.sem("sX0"), c.sem("sX1"), c.sem("sX2"), c.sem("sX3")]
    sOutp = [c.sem("sOutA"), c.sem("sOutB")]
    sBB = c.sem("sBB")
    sKW = c.sem("sKW")
    sKB = [c.sem("sKB0"), c.sem("sKB1")]
    sZW = c.sem("sZW")
    sZQ = [c.sem("sZQ0"), c.sem("sZQ1")]
    sOW = [c.sem("sOW0"), c.sem("sOW1")]
    AB = ACC.bitcast(BF16)
    QR = lambda: AB[:, 0:2048]
    KRb = lambda: AB[:, 2048:4096]
    UB = [(4096, 4608), (4608, 5120)]
    ZTS = [(5120, 5632), (5632, 6144)]
    KBUF = [(6144, 8192), (8192, 10240)]
    ZQs = [(10240, 10752), (10752, 11264)]
    OWS = [(11264, 12288), (12288, 13312)]
    S = 128
    invS = 1.0 / np.sqrt(S)
    TTB_PITCH = 16 * 8 * P

    kernbuf = nc.dram_tensor(c.ns + "l0aa_kern", [L, L], BF16)
    zbuf = nc.dram_tensor(c.ns + "l0aa_z", [L, L], BF16)
    CCS = c.sbuf("CCS", [P, L], BF16)
    SSC = c.sbuf("SSC", [P, L], BF16)

    sC = c.sem("sC")
    c.dma("gpsimd", CCS[:, :], W["cCCS"][:, :], sCst)
    c.dma("gpsimd", SSC[:, :], W["cSSC"][:, :], sCst)
    c.dma("sync", BCAST[:, 32:36], W["cGB"][:, :], sC)
    c.dma("sync", SMALLF[0:1, 0:1], ap3(W["g"], 0, [[1, 1], [1, 1]]), sC)
    c.waitall("tensor", sCst)
    c.waitall("vector", sCst)
    c.waitall("tensor", sC)
    c.waitall("vector", sC)
    c.opS("vector", lambda e: e.tensor_scalar(
        out=BCAST[:, 36:38], in0=BCAST[:, 32:34], scalar1=invS, scalar2=None, op0=ALU.mult))
    v = c.op("tensor", lambda e: e.matmul(out=c.bank(6)[:, 0:1], lhsT=ONESF[:, :],
                                          rhs=SMALLF[0:1, 0:1], start=True, stop=True), sMM)
    c.wait("vector", sMM, v)
    gc_v = c.op("vector", lambda e: e.tensor_copy(out=BCAST[:, 38:39], in_=c.bank(6)[:, 0:1]), sV)
    gcol = BCAST[:, 38:39]

    out_v = {}
    prev_b_done = None
    for b in range(NB):
        tiles = list(range(b * TPB, (b + 1) * TPB))
        xt_guards = None
        if b > 0:
            xt_guards = {s: (sV, out_v[(b - 1, 12 + s)]) for s in range(4)}
        c.wait("vector", sV, gc_v)
        ln = emit_ln(c, xsrc_fn, tiles, slot0=0, xt_guards=xt_guards,
                     xn_mode=True, gcol=gcol)

        seq = 0
        ev = {}
        chv = {}
        for ch in range(4):
            slot = ch % 2
            if ch >= 2:
                c.wait("gpsimd", sMM, chv[ch - 2])
            elif prev_b_done is not None and ch == 0:
                c.wait("gpsimd", sV, prev_b_done)
            c.dma("gpsimd",
                  WA[:, slot * 4096:(slot + 1) * 4096].rearrange("p (dt f) -> p dt f", f=512),
                  W["cTuv"][:, 2048 + 512 * ch:2048 + 512 * (ch + 1)]
                  .rearrange("(dt p) f -> p dt f", p=P), sWA[slot])
            c.waitall("tensor", sWA[slot])
            for j in range(16):
                c.wait("tensor", sEv, ln["evv"][j])
                if seq == 0 and prev_b_done is not None:
                    c.wait("tensor", sV, prev_b_done)
                bk = 4 + (seq % 2)
                if seq >= 2:
                    c.wait("tensor", sAc, ev[seq - 2])
                for dt in range(8):
                    fn_ = lambda e, j=j, dt=dt, slot=slot, bk=bk: e.matmul(
                        out=c.bank(bk),
                        lhsT=TTB[:, j * 1024 + dt * P:j * 1024 + (dt + 1) * P],
                        rhs=WA[:, slot * 4096 + dt * 512:slot * 4096 + (dt + 1) * 512],
                        start=(dt == 0), stop=(dt == 7))
                    if dt == 7:
                        mmv = c.op("tensor", fn_, sMM)
                    else:
                        c.op("tensor", fn_)
                c.wait("scalar", sMM, mmv)
                ev[seq] = c.op("scalar", lambda e, j=j, ch=ch, bk=bk: e.activation(
                    out=H1T[:, j * 2048 + ch * 512:j * 2048 + (ch + 1) * 512],
                    in_=c.bank(bk), func=AF.Silu), sAc)
                seq += 1
            chv[ch] = mmv

        c.wait("gpsimd", sMM, chv[2])
        c.dma("gpsimd",
              WA[:, 0:1024].rearrange("p (dt f) -> p dt f", f=P),
              W["cTuv"][:, 4096:4224].rearrange("(dt p) f -> p dt f", p=P), sWA[0])
        c.waitall("tensor", sWA[0])
        bev = {}
        c.waitall("scalar", sOutp[0])
        c.waitall("scalar", sOutp[1])
        for th in range(4):
            bk = 6 + (th % 2)
            if th == 0:
                c.wait("tensor", sV, gc_v)
            if th >= 2:
                c.wait("tensor", sAc, bev[th - 2])
            for dt in range(8):
                rhs = ap3(TTB, th * 4 * 1024 + dt * P, [[TTB_PITCH, P], [1024, 4], [1, P]])
                fn_ = lambda e, dt=dt, rhs=rhs, bk=bk: e.matmul(
                    out=c.bank(bk), lhsT=WA[:, dt * P:(dt + 1) * P], rhs=rhs,
                    start=(dt == 0), stop=(dt == 7))
                if dt == 7:
                    mmv = c.op("tensor", fn_, sMM)
                else:
                    c.op("tensor", fn_)
            c.wait("scalar", sMM, mmv)
            bev[th] = c.op("scalar", lambda e, th=th, bk=bk: e.activation(
                out=OUTT[:, th * 512:(th + 1) * 512], in_=c.bank(bk), func=AF.Silu), sAc)
        c.wait("vector", sAc, bev[3])
        MISCb = MISC.bitcast(BF16)
        sRP = c.sem("sRP")
        for qk in range(2):
            gci = 36 if qk == 0 else 34
            dst0 = (0 if qk == 0 else 2048)
            tsv = c.op("vector", lambda e, gci=gci: e.tensor_scalar(
                out=TBF[:, 0:2048], in0=OUTT[:, 0:2048],
                scalar1=BCAST[:, gci:gci + 1], scalar2=BCAST[:, gci + 1:gci + 2],
                op0=ALU.mult, op1=ALU.add), sV)
            c.wait("vector", sV, tsv)
            t1v = c.op("vector", lambda e, dst0=dst0: e.tensor_tensor(
                out=AB[:, dst0:dst0 + 2048], in0=TBF[:, 0:2048], in1=CCS[:, :],
                op=ALU.mult), sV)
            t2v = c.op("vector", lambda e: e.tensor_tensor(
                out=MISCb[:, 0:2048], in0=TBF[:, 0:2048], in1=SSC[:, :],
                op=ALU.mult), sV)
            c.wait("sync", sV, t2v)
            c.dma("sync", bass.AP(ACC.bitcast(BF16), 4096, [[16384, 64], [1, 2048]]),
                  bass.AP(ACC.bitcast(BF16), dst0 + 64 * 16384, [[16384, 64], [1, 2048]]), sRP)
            c.dma("sync", bass.AP(ACC.bitcast(BF16), 6144, [[16384, 64], [1, 2048]]),
                  bass.AP(MISC.bitcast(BF16), 64 * 2048, [[2048, 64], [1, 2048]]), sRP)
            c.waitall("vector", sRP)
            c.op("vector", lambda e, dst0=dst0: e.tensor_tensor(
                out=bass.AP(ACC.bitcast(BF16), dst0, [[16384, 64], [1, 2048]]),
                in0=bass.AP(ACC.bitcast(BF16), dst0, [[16384, 64], [1, 2048]]),
                in1=bass.AP(ACC.bitcast(BF16), 4096, [[16384, 64], [1, 2048]]),
                op=ALU.subtract))
            lastv = c.op("vector", lambda e, dst0=dst0: e.tensor_tensor(
                out=bass.AP(ACC.bitcast(BF16), dst0 + 64 * 16384, [[16384, 64], [1, 2048]]),
                in0=bass.AP(MISC.bitcast(BF16), 0, [[2048, 64], [1, 2048]]),
                in1=bass.AP(ACC.bitcast(BF16), 6144, [[16384, 64], [1, 2048]]),
                op=ALU.add), sV)
            c.wait("vector", sV, lastv)
        rope_done = lastv

        kv_ep = {}
        for i in range(16):
            if i == 0:
                c.wait("sync", sV, rope_done)
            else:
                c.wait("sync", sEp, kv_ep[i - 1])
            bbv = c.dma("sync", OUTT[:, :].rearrange("p (jt f) -> p jt f", f=P),
                        ap3(W["cBIAST"], 128 * i, [[L, P], [P * L, 16], [1, P]]), sBB)
            c.wait("tensor", sV, rope_done)
            if i == 0:
                c.wait("tensor", sEv, ln["evv"][15])
            if i >= 1:
                c.wait("tensor", sEp, kv_ep[i - 1])
            for jt in range(16):
                bk = jt // 4
                fn_ = lambda e, jt=jt, i=i, bk=bk: e.matmul(
                    out=c.bank(bk)[:, (jt % 4) * P:(jt % 4 + 1) * P],
                    lhsT=KRb()[:, jt * P:(jt + 1) * P],
                    rhs=QR()[:, i * P:(i + 1) * P], start=True, stop=True)
                if jt == 15:
                    mmv = c.op("tensor", fn_, sMM)
                else:
                    c.op("tensor", fn_)
            c.wait("vector", sMM, mmv)
            c.waitall("vector", sBB)
            if i >= 2:
                c.waitall("vector", sKW)
            for q4 in range(4):
                c.opS("vector", lambda e, q4=q4: e.scalar_tensor_tensor(
                    out=TBF[:, q4 * 512:(q4 + 1) * 512], in0=OUTT[:, q4 * 512:(q4 + 1) * 512],
                    scalar=invS, in1=c.bank(q4), op0=ALU.mult, op1=ALU.add))
                kv_ep[i] = c.op("vector", lambda e, q4=q4, i=i: e.scalar_tensor_tensor(
                    out=AB[:, KBUF[i % 2][0] + q4 * 512:KBUF[i % 2][0] + (q4 + 1) * 512],
                    in0=TBF[:, q4 * 512:(q4 + 1) * 512], scalar=0.0,
                    in1=TBF[:, q4 * 512:(q4 + 1) * 512], op0=ALU.max, op1=ALU.mult), sEp)
            c.wait("sync", sEp, kv_ep[i])
            c.dma("sync", ap3(kernbuf, 128 * i, [[L, P], [P * L, 16], [1, P]]),
                  AB[:, KBUF[i % 2][0]:KBUF[i % 2][0] + 2048]
                  .rearrange("p (jt f) -> p jt f", f=P), sKW)
        a2_last_mm = mmv

        zseq = 0
        zts_ev = {}
        useq = 0
        u_ev = {}
        for ch in range(4):
            slot = ch % 2
            if ch == 0:
                c.wait("gpsimd", sMM, a2_last_mm)
            if ch >= 2:
                c.wait("gpsimd", sMM, a3_ch_done[ch - 2])
            c.dma("gpsimd",
                  WA[:, slot * 4096:(slot + 1) * 4096].rearrange("p (dt f) -> p dt f", f=512),
                  W["cTuv"][:, 512 * ch:512 * (ch + 1)]
                  .rearrange("(dt p) f -> p dt f", p=P), sWA[slot])
            c.waitall("tensor", sWA[slot])
            for i in range(16):
                ks = i % 2
                if zseq >= 2:
                    c.wait("sync", sMM, kb_used[zseq - 2])
                kbv = c.dma("sync", AB[:, KBUF[ks][0]:KBUF[ks][0] + 2048]
                            .rearrange("p (jt f) -> p jt f", f=P),
                            ap3(kernbuf, 128 * i, [[L, P], [P * L, 16], [1, P]]), sKB[ks])
                bku = 0 + (useq % 2)
                if useq == 0:
                    c.wait("tensor", sEp, kv_ep[15])
                if useq >= 2:
                    c.wait("tensor", sAc, u_ev[useq - 2])
                for dt in range(8):
                    fn_ = lambda e, i=i, dt=dt, slot=slot, bku=bku: e.matmul(
                        out=c.bank(bku),
                        lhsT=TTB[:, i * 1024 + dt * P:i * 1024 + (dt + 1) * P],
                        rhs=WA[:, slot * 4096 + dt * 512:slot * 4096 + (dt + 1) * 512],
                        start=(dt == 0), stop=(dt == 7))
                    if dt == 7:
                        ummv = c.op("tensor", fn_, sMM)
                    else:
                        c.op("tensor", fn_)
                c.wait("scalar", sMM, ummv)
                u_ev[useq] = c.op("scalar", lambda e, ks=ks, bku=bku: e.activation(
                    out=AB[:, UB[ks][0]:UB[ks][1]], in_=c.bank(bku), func=AF.Silu), sAc)
                useq += 1
                bkv = 2 + (zseq % 2)
                c.waitall("tensor", sKB[ks])
                if zseq >= 2:
                    c.wait("tensor", sV, z_ev[zseq - 2])
                for jt in range(16):
                    fn_ = lambda e, jt=jt, ks=ks, ch=ch, bkv=bkv: e.matmul(
                        out=c.bank(bkv),
                        lhsT=AB[:, KBUF[ks][0] + jt * P:KBUF[ks][0] + (jt + 1) * P],
                        rhs=H1T[:, jt * 2048 + ch * 512:jt * 2048 + (ch + 1) * 512],
                        start=(jt == 0), stop=(jt == 15))
                    if jt == 15:
                        kvv = c.op("tensor", fn_, sMM)
                    else:
                        c.op("tensor", fn_)
                if zseq == 0:
                    kb_used, z_ev, a3_ch_done = {}, {}, {}
                kb_used[zseq] = kvv
                c.wait("vector", sMM, kvv)
                c.wait("vector", sAc, u_ev[useq - 1])
                zv = c.op("vector", lambda e, ks=ks, bkv=bkv: e.tensor_tensor(
                    out=TBF[:, 0:512], in0=AB[:, UB[ks][0]:UB[ks][1]],
                    in1=c.bank(bkv), op=ALU.mult), sV)
                z_ev[zseq] = zv
                bkt = 6 + (zseq % 2)
                c.wait("tensor", sV, zv)
                if zseq >= 2:
                    c.wait("tensor", sAc, zts_ev[zseq - 2])
                for q4 in range(4):
                    fn_ = lambda e, q4=q4, bkt=bkt: e.transpose(
                        out=c.bank(bkt)[:, q4 * P:(q4 + 1) * P],
                        in_=TBF[:, q4 * P + 0:512][:, 0:P] if False else bass.AP(TBF, q4 * P, [[2 * D, P], [1, P]]),
                        identity=IDENT[:, :])
                    if q4 == 3:
                        trv_ = c.op("tensor", fn_, sMM)
                    else:
                        c.op("tensor", fn_)
                c.wait("scalar", sMM, trv_)
                c.waitall("scalar", sZW)
                zts_ev[zseq] = c.op("scalar", lambda e, ks=ks, bkt=bkt: e.copy(
                    out=AB[:, ZTS[ks][0]:ZTS[ks][1]], in_=c.bank(bkt)), sAc)
                c.wait("sync", sAc, zts_ev[zseq])
                c.dma("sync", ap3(zbuf, (512 * ch) * L + 128 * i, [[L, P], [P * L, 4], [1, P]]),
                      AB[:, ZTS[ks][0]:ZTS[ks][1]].rearrange("p (q f) -> p q f", f=P), sZW)
                zseq += 1
            a3_ch_done[ch] = kvv

        c.wait("sync", sEp, kv_ep[15])
        rsl = c.dma("sync", MISC[0:1, 0:1024],
                    ap3(W["res_scale"], 0, [[1024, 1], [1, 1024]]), sC)
        c.waitall("tensor", sC)
        for hh in range(2):
            if hh == 0:
                c.wait("tensor", sAc, zts_ev[zseq - 1])
            v2 = c.op("tensor", lambda e, hh=hh: e.matmul(
                out=c.bank(6 + hh), lhsT=ONESF[:, :],
                rhs=MISC[0:1, hh * 512:(hh + 1) * 512], start=True, stop=True), sMM)
        c.wait("scalar", sMM, v2)
        rs_v = c.op("scalar", lambda e: e.copy(out=TBF[:, 0:1024], in_=c.bank(6, 2)), sAc)
        for iq in range(4):
            owv = {}
            zqv = {}
            for et in range(16):
                ws = et % 2
                if et >= 2:
                    c.wait("gpsimd", sMM, ow_used[et - 2])
                elif iq > 0:
                    c.wait("gpsimd", sMM, pc_done)
                c.dma("gpsimd", AB[:, OWS[ws][0]:OWS[ws][1]],
                      W["cTow"][128 * et:128 * (et + 1), :], sOW[ws])
                c.dma("gpsimd", AB[:, ZQs[ws][0]:ZQs[ws][1]]
                      .rearrange("p (q f) -> p q f", f=P),
                      ap3(zbuf, (128 * et) * L + 512 * iq, [[L, P], [P, 4], [1, P]]),
                      sZQ[ws])
                c.waitall("tensor", sOW[ws])
                c.waitall("tensor", sZQ[ws])
                if et == 0:
                    if iq > 0:
                        for bb in (0, 2, 4, 6):
                            c.wait("tensor", sV, out_ep[bb])
                    else:
                        c.wait("tensor", sAc, rs_v)
                        c.wait("tensor", sV, z_ev[zseq - 1])
                        c.wait("tensor", sAc, u_ev[useq - 1])
                for il in range(4):
                    for oh in range(2):
                        fn_ = lambda e, il=il, oh=oh, ws=ws, et=et: e.matmul(
                            out=c.bank(2 * il + oh),
                            lhsT=AB[:, ZQs[ws][0] + il * P:ZQs[ws][0] + (il + 1) * P],
                            rhs=AB[:, OWS[ws][0] + oh * 512:OWS[ws][0] + (oh + 1) * 512],
                            start=(et == 0), stop=(et == 15))
                        if il == 3 and oh == 1:
                            mmv = c.op("tensor", fn_, sMM)
                        else:
                            c.op("tensor", fn_)
                if et == 0:
                    ow_used = {}
                ow_used[et] = mmv
            pc_done = mmv
            c.wait("vector", sMM, pc_done)
            if iq == 0:
                c.wait("vector", sAc, rs_v)
            out_ep = {}
            for il in range(4):
                i = iq * 4 + il
                tt = b * TPB + i
                sl = (il % 4) * D
                sX = sXp[il]
                if iq > 0:
                    c.wait("sync", sV, out_v[(b, (iq - 1) * 4 + il)])
                else:
                    c.wait("sync", sTb, ln["tbv"][12 + il])
                xv = c.dma("sync", XT[:, sl:sl + D], xsrc_fn(tt), sX)
                c.waitall("vector", sX)
                c.opS("vector", lambda e, sl=sl, tt=tt: e.tensor_scalar(
                    out=OUTT[:, (il % 2) * D:(il % 2 + 1) * D] if False else MISC[:, 0:1024],
                    in0=XT[:, sl:sl + D],
                    scalar1=LNS[:, 4 * tt:4 * tt + 1], scalar2=LNS[:, 4 * tt + 1:4 * tt + 2],
                    op0=ALU.add, op1=ALU.mult))
                c.opS("vector", lambda e: e.tensor_tensor(
                    out=MISC[:, 0:1024], in0=MISC[:, 0:1024], in1=TBF[:, 0:1024], op=ALU.mult))
                c.opS("vector", lambda e, sl=sl: e.tensor_tensor(
                    out=MISC[:, 0:1024], in0=MISC[:, 0:1024], in1=XT[:, sl:sl + D], op=ALU.add))
                c.waitall("vector", sOutp[il % 2])
                vv = c.op("vector", lambda e, il=il: e.tensor_tensor(
                    out=OUTT[:, (il % 2) * D:(il % 2 + 1) * D], in0=MISC[:, 0:1024],
                    in1=c.bank(2 * il, 2), op=ALU.add), sV)
                out_ep[2 * il] = vv
                out_ep[2 * il + 1] = vv
                out_v[(b, i)] = vv
                c.wait("sync", sV, vv)
                c.dma("sync", xdst_fn(tt), OUTT[:, (il % 2) * D:(il % 2 + 1) * D], sOutp[il % 2])
        prev_b_done = out_v[(b, 15)]
    c.waitall("sync", sOutp[0])
    c.waitall("sync", sOutp[1])
    c.waitall("sync", sKW)
    c.waitall("sync", sZW)


def emit_l0ab(c, W, xsrc_fn, xdst_fn):
    nc = c.nc
    TTB, H1T, WA, XT, STAT, LNS, TBF, OUTT, MISC, ACC, GSC = (c.bufs[k] for k in
        ("TTB", "H1T", "WA", "XT", "STAT", "LNS", "TBF", "OUTT", "MISC", "ACC", "GSC"))
    SMALLF, SMALLB, ONESB, ONESF, BCAST, IDENT = (c.bufs[k] for k in
        ("SMALLF", "SMALLB", "ONESB", "ONESF", "BCAST", "IDENT"))
    sMM, sEp, sAc, sV, sEv, sSq, sSt, sTb = (c.sem(k) for k in
        ("sMM", "sEp", "sAc", "sV", "sEv", "sSq", "sSt", "sTb"))
    sRs2 = c.sem("sRs2")
    sCst = c.sem("sCst")
    sC = c.sem("sC")
    sWA = [c.sem("sWA0"), c.sem("sWA1")]
    sXp = [c.sem("sX0"), c.sem("sX1"), c.sem("sX2"), c.sem("sX3")]
    sOutp = [c.sem("sOutA"), c.sem("sOutB")]
    sDX = c.sem("sDX")
    WS = [4.0, 16.0, 64.0, 256.0]
    NBIG = -30000.0
    TTB_PITCH = 16 * 8 * P
    PS = c.bufs["PSALL"]

    ONESC = c.sbuf("ONESC", [P, 1], BF16)
    dxbounce = nc.dram_tensor("l0ab_dx", [16, P], F32)
    sCM = c.sem("sCM")

    c.dma("gpsimd", SMALLB[:, 0:3072], ap3(W["qkv_b"], 0, [[3072, 1], [1, 3072]]), sCst)
    c.dma("sync", SMALLF[0:1, 0:1], ap3(W["lin2_b"], 0, [[1, 1], [1, 1]]), sC)
    c.dma("sync", MISC[0:1, 0:256], ap3(W["lin2_w"], 0, [[256, 1], [1, 256]]), sC)
    c.op("vector", lambda e: e.memset(ONESC[:, :], 1.0))
    for gg_ in range(4):
        c.op("vector", lambda e, gg_=gg_: e.memset(BCAST[:, 32 + gg_:33 + gg_], WS[gg_]))
    nbig_v = c.op("vector", lambda e: e.memset(
        ap3(H1T.bitcast(mybir.dt.float16), 31808, [[32768, P], [1, 1]]), NBIG), sV)
    c.waitall("tensor", sC)
    c.waitall("tensor", sCst)
    v = c.op("tensor", lambda e: e.matmul(out=c.bank(6)[:, 0:256], lhsT=ONESF[:, :],
                                          rhs=MISC[0:1, 0:256], start=True, stop=True), sMM)
    v2 = c.op("tensor", lambda e: e.matmul(out=c.bank(7)[:, 0:1], lhsT=ONESF[:, :],
                                           rhs=SMALLF[0:1, 0:1], start=True, stop=True), sMM)
    c.wait("scalar", sMM, v2)
    l2b_v = c.op("scalar", lambda e: e.copy(out=MISC[:, 256:512], in_=c.bank(6)[:, 0:256]), sAc)
    l2c_v = c.op("scalar", lambda e: e.copy(out=BCAST[:, 39:40], in_=c.bank(7)[:, 0:1]), sAc)

    VOFF, QOFF, KOFF, POFF, SOFF, UOFF, ROFF = 0, 16448, 20544, 24640, 26688, 28736, 29760
    H1T16 = H1T.bitcast(mybir.dt.float16)
    H1TU8 = H1T.bitcast(U8)
    AB = ACC.bitcast(BF16)

    out_v = {}
    prev_b_done = None
    prod_ev = {4: None, 5: None}
    sel_last = exp_last = av_last = norm_last = dxb_last = None
    for b in range(NB):
        tiles = list(range(b * TPB, (b + 1) * TPB))
        xt_guards = None
        if b > 0:
            xt_guards = {s: (sV, out_v[(b - 1, 12 + s)]) for s in range(4)}
        ln = emit_ln(c, xsrc_fn, tiles, slot0=0, xt_guards=xt_guards)

        seq = 0
        ev = {}
        wa_last = {0: None, 1: None}
        for ch in range(2):
            slot = ch % 2
            if prev_b_done is not None and ch == 0:
                c.wait("gpsimd", sV, prev_b_done)
            if wa_last[slot] is not None:
                c.wait("gpsimd", sMM, wa_last[slot])
            c.dma("gpsimd",
                  WA[:, slot * 4096:(slot + 1) * 4096]
                  .rearrange("p (dt f) -> p dt f", f=512),
                  W["cTqkv"][:, 2048 + 512 * ch:2048 + 512 * (ch + 1)]
                  .rearrange("(dt p) f -> p dt f", p=P), sWA[slot])
            c.waitall("tensor", sWA[slot])
            for j in range(16):
                c.wait("tensor", sEv, ln["evv"][j])
                if seq == 0 and prev_b_done is not None:
                    c.wait("tensor", sV, prev_b_done)
                bk = 4 + (seq % 2)
                if seq >= 2:
                    c.wait("tensor", sAc, ev[seq - 2])
                c.op("tensor", lambda e, ch=ch, bk=bk: e.matmul(
                    out=c.bank(bk), lhsT=ONESB[0:1, 0:P],
                    rhs=SMALLB[0:1, 2048 + 512 * ch:2048 + 512 * (ch + 1)],
                    start=True, stop=False))
                for dt in range(8):
                    fn_ = lambda e, j=j, dt=dt, slot=slot, bk=bk: e.matmul(
                        out=c.bank(bk),
                        lhsT=TTB[:, j * 1024 + dt * P:j * 1024 + (dt + 1) * P],
                        rhs=WA[:, slot * 4096 + dt * 512:slot * 4096 + (dt + 1) * 512],
                        start=False, stop=(dt == 7))
                    if dt == 7:
                        mmv = c.op("tensor", fn_, sMM)
                    else:
                        c.op("tensor", fn_)
                c.wait("scalar", sMM, mmv)
                dst = ap3(H1T, VOFF + j * 1028 + (2 * ch) * 257, [[32768, P], [257, 2], [1, 256]])
                srcp = ap3(PS, bk * 512, [[4096, P], [256, 2], [1, 256]])
                ev[seq] = c.op("scalar", lambda e, dst=dst, srcp=srcp: e.copy(
                    out=dst, in_=srcp), sAc)
                seq += 1
            wa_last[slot] = mmv
        ones_done = None
        for j in range(16):
            c.wait("vector", sAc, ev[16 + j])
            ones_done = c.op("vector", lambda e, j=j: e.memset(
                ap3(H1T, VOFF + j * 1028 + 256, [[32768, P], [257, 4], [1, 1]]), 1.0), sV)

        c.wait("gpsimd", sMM, wa_last[0])
        c.dma("gpsimd", WA[:, 0:2048].rearrange("p (dt f) -> p dt f", f=256),
              W["cToff"][:, :].rearrange("(dt p) f -> p dt f", p=P), sWA[0])
        c.waitall("tensor", sWA[0])
        dxv = {}
        rts = {}
        for t in range(16):
            if t >= 1:
                c.wait("scalar", sMM, rts[t - 1])
            rv = c.op("scalar", lambda e, t=t: e.activation(
                out=H1T[:, ROFF:ROFF + 1024], in_=TTB[:, t * 1024:(t + 1) * 1024],
                func=AF.Relu), sAc)
            c.wait("tensor", sAc, rv)
            bk = 6 + (t % 2)
            if t >= 2:
                c.wait("tensor", sSt, dxv[t - 2])
            for dt in range(8):
                fn_ = lambda e, dt=dt, bk=bk: e.matmul(
                    out=c.bank(bk)[:, 0:256],
                    lhsT=H1T[:, ROFF + dt * P:ROFF + (dt + 1) * P],
                    rhs=WA[:, dt * 256:(dt + 1) * 256],
                    start=(dt == 0), stop=(dt == 7))
                if dt == 7:
                    mmv = c.op("tensor", fn_, sMM)
                else:
                    c.op("tensor", fn_)
            rts[t] = mmv
            c.wait("vector", sMM, mmv)
            if t == 0:
                c.wait("vector", sAc, l2b_v)
            c.opS("vector", lambda e, bk=bk: e.tensor_tensor(
                out=MISC[:, 512:768], in0=c.bank(bk)[:, 0:256], in1=MISC[:, 256:512],
                op=ALU.mult))
            c.opS("vector", lambda e: e.tensor_reduce(
                out=STAT[:, 20:21], in_=MISC[:, 512:768], axis=AX.X, op=ALU.add))
            sv = c.op("vector", lambda e: e.tensor_tensor(
                out=STAT[:, 21:22], in0=STAT[:, 20:21], in1=BCAST[:, 39:40],
                op=ALU.add), sSt)
            c.wait("scalar", sSt, sv)
            sg = c.op("scalar", lambda e: e.activation(
                out=STAT[:, 22:23], in_=STAT[:, 21:22], func=AF.Sigmoid), sRs2)
            c.wait("vector", sRs2, sg)
            dxv[t] = c.op("vector", lambda e, t=t: e.tensor_scalar(
                out=GSC[:, t:t + 1], in0=STAT[:, 22:23],
                scalar1=float(L), scalar2=None, op0=ALU.mult), sSt)
        wa_last[0] = rts[15]
        c.wait("tensor", sSt, dxv[15])
        trv_ = c.op("tensor", lambda e: e.transpose(
            out=ap3(PS, 0, [[4096, 16], [1, P]]),
            in_=GSC[:, 0:16], identity=IDENT[:, :]), sMM)
        c.wait("scalar", sMM, trv_)
        dxe = c.op("scalar", lambda e: e.copy(
            out=MISC[0:16, 768:896], in_=ap3(PS, 0, [[4096, 16], [1, P]])), sAc)
        c.wait("sync", sAc, dxe)
        c.dma("sync", dxbounce[:, :], MISC[0:16, 768:896], sDX)
        c.waitall("sync", sDX)
        c.dma("sync", ap3(OUTT, 0, [[2048, 1], [1, 2048]]),
              ap3(dxbounce, 0, [[2048, 1], [1, 2048]]), sDX)

        for g in range(4):
            for qk in range(2):
                base = (0 if qk == 0 else 1024) + 256 * g
                hoff = QOFF if qk == 0 else KOFF
                slot = qk
                if wa_last[slot] is not None:
                    c.wait("gpsimd", sMM, wa_last[slot])
                c.dma("gpsimd", WA[:, slot * 4096:slot * 4096 + 2048]
                      .rearrange("p (dt f) -> p dt f", f=256),
                      W["cTqkv"][:, base:base + 256]
                      .rearrange("(dt p) f -> p dt f", p=P), sWA[slot])
                c.waitall("tensor", sWA[slot])
                if g == 0 and qk == 0:
                    c.wait("tensor", sV, ones_done)
                if qk == 0 and norm_last is not None:
                    c.wait("tensor", sEp, norm_last)
                    c.wait("tensor", sAc, dxb_last)
                for ft in range(2):
                    for th in range(4):
                        bk = 4 + (th % 2)
                        if prod_ev[bk] is not None:
                            c.wait("tensor", sAc, prod_ev[bk])
                        c.op("tensor", lambda e, base=base, ft=ft, bk=bk: e.matmul(
                            out=c.bank(bk), lhsT=SMALLB[0:1, base + 128 * ft:base + 128 * (ft + 1)],
                            rhs=ONESB[0:1, 0:512], start=True, stop=False))
                        for dt in range(8):
                            off = (th * 4) * 1024 + dt * P
                            rhs = ap3(TTB, off, [[TTB_PITCH, P], [1024, 4], [1, P]])
                            fn_ = lambda e, slot=slot, dt=dt, ft=ft, rhs=rhs, bk=bk: e.matmul(
                                out=c.bank(bk),
                                lhsT=WA[:, slot * 4096 + dt * 256 + 128 * ft:
                                        slot * 4096 + dt * 256 + 128 * (ft + 1)],
                                rhs=rhs, start=False, stop=(dt == 7))
                            if dt == 7:
                                mmv = c.op("tensor", fn_, sMM)
                            else:
                                c.op("tensor", fn_)
                        c.wait("scalar", sMM, mmv)
                        prod_ev[bk] = c.op("scalar",
                            lambda e, hoff=hoff, ft=ft, th=th, bk=bk: e.copy(
                                out=ap3(H1T, hoff + ft * 2048 + th * 512, [[32768, P], [1, 512]]),
                                in_=c.bank(bk)), sAc)
                wa_last[slot] = mmv
            qkprod_done = prod_ev[bk]

            c.wait("scalar", sAc, qkprod_done)
            if av_last is not None:
                c.wait("scalar", sMM, av_last)
            nmv = {}
            for qk in range(2):
                hoff = QOFF if qk == 0 else KOFF
                sqv_ = c.op("scalar", lambda e, hoff=hoff: e.activation(
                    out=H1T[:, POFF:POFF + 4096], in_=H1T[:, hoff:hoff + 4096],
                    func=AF.Square), sAc)
                c.wait("tensor", sAc, sqv_)
                for half in range(2):
                    if not (g == 0 and qk == 0 and half == 0):
                        c.wait("tensor", sSt, nm_red)
                    for nh in range(2):
                        for ft in range(2):
                            fn_ = lambda e, ft=ft, nh=nh, half=half: e.matmul(
                                out=ap3(PS, 3072 + nh * 512, [[4096, 1], [1, 512]]),
                                lhsT=ONESC[:, 0:1],
                                rhs=H1T[:, POFF + ft * 2048 + half * 1024 + nh * 512:
                                        POFF + ft * 2048 + half * 1024 + (nh + 1) * 512],
                                start=(ft == 0), stop=(ft == 1))
                            if nh == 1 and ft == 1:
                                mmv = c.op("tensor", fn_, sMM)
                            else:
                                c.op("tensor", fn_)
                    c.wait("vector", sMM, mmv)
                    nm_red = c.opS("vector", lambda e: e.tensor_reduce(
                        out=STAT[0:1, 29:30], in_=ap3(PS, 3072, [[4096, 1], [1, 1024]]),
                        axis=AX.X, op=ALU.max))
                    if half == 0:
                        nm_red = c.op("vector", lambda e, qk=qk: e.tensor_copy(
                            out=STAT[0:1, 24 + qk:25 + qk], in_=STAT[0:1, 29:30]), sSt)
                    else:
                        nm_red = c.op("vector", lambda e, qk=qk: e.tensor_tensor(
                            out=STAT[0:1, 24 + qk:25 + qk], in0=STAT[0:1, 24 + qk:25 + qk],
                            in1=STAT[0:1, 29:30], op=ALU.max), sSt)
                nmv[qk] = nm_red
            c.wait("vector", sSt, nmv[1])
            pv = c.op("vector", lambda e: e.tensor_tensor(
                out=STAT[0:1, 26:27], in0=STAT[0:1, 24:25], in1=STAT[0:1, 25:26],
                op=ALU.mult), sSt)
            c.wait("scalar", sSt, pv)
            sqb = c.op("scalar", lambda e: e.activation(
                out=STAT[0:1, 27:28], in_=STAT[0:1, 26:27], func=AF.Sqrt), sRs2)
            c.wait("vector", sRs2, sqb)
            ngv = c.op("vector", lambda e: e.tensor_scalar(
                out=STAT[0:1, 26:27], in0=STAT[0:1, 27:28],
                scalar1=-2.0 / float(np.sqrt(L)), scalar2=None, op0=ALU.mult), sSt)
            c.wait("tensor", sSt, ngv)
            bmv = c.op("tensor", lambda e: e.matmul(
                out=c.bank(5)[:, 0:1], lhsT=ONESF[:, :], rhs=STAT[0:1, 26:27],
                start=True, stop=True), sMM)
            c.wait("scalar", sMM, bmv)
            ngb = c.op("scalar", lambda e: e.copy(out=BCAST[:, 37:38], in_=c.bank(5)[:, 0:1]), sAc)

            for i in range(16):
                c.wait("tensor", sAc, ngb if (i == 0) else dxb_last)
                dmv = c.op("tensor", lambda e, i=i: e.matmul(
                    out=c.bank(5)[:, 0:P], lhsT=ONESF[:, :],
                    rhs=ap3(OUTT, i * P, [[2048, 1], [1, P]]), start=True, stop=True), sMM)
                c.wait("scalar", sMM, dmv)
                if g == 0 and i == 0 and b == 0:
                    c.wait("scalar", sV, nbig_v)
                dxb_last = c.op("scalar", lambda e, g=g: e.activation(
                    out=MISC[:, 0:128], in_=c.bank(5)[:, 0:P], func=AF.Identity,
                    bias=BCAST[:, 32 + g:33 + g], scale=1.0), sAc)
                if sel_last is not None:
                    c.wait("tensor", sEp, sel_last)
                for jt in range(16):
                    for ft in range(2):
                        fn_ = lambda e, jt=jt, ft=ft, i=i: e.matmul(
                            out=ap3(PS, (jt // 4) * 512 + (jt % 4) * P, [[4096, P], [1, P]]),
                            lhsT=H1T[:, KOFF + ft * 2048 + jt * P:KOFF + ft * 2048 + (jt + 1) * P],
                            rhs=H1T[:, QOFF + ft * 2048 + i * P:QOFF + ft * 2048 + (i + 1) * P],
                            start=(ft == 0), stop=(ft == 1))
                        if jt == 15 and ft == 1:
                            smv = c.op("tensor", fn_, sMM)
                        else:
                            c.op("tensor", fn_)
                if sel_last is not None:
                    c.wait("sync", sEp, sel_last)
                c.dma("sync", ap3(H1T16, 29760, [[32768, P], [1, 2048]]),
                      W["cCST"][i], sCM)
                c.wait("vector", sAc, dxb_last)
                c.waitall("vector", sCM)
                cview = ap3(H1T16, 29760, [[32768, P], [1, 2048]])
                dxb = ap3(MISC, 0, [[1024, P], [0, 16], [1, P]])
                c.opS("vector", lambda e, cview=cview, dxb=dxb: e.tensor_tensor(
                    out=ap3(H1TU8, 2 * UOFF, [[65536, P], [1, 2048]]),
                    in0=cview, in1=dxb, op=ALU.is_le))
                c.wait("vector", sMM, smv)
                if exp_last is not None:
                    c.wait("vector", sSq, exp_last)
                if g == 0 and i == 0 and b == 0:
                    c.wait("vector", sV, nbig_v)
                sel_last = c.op("vector", lambda e: e.select(
                    out=ap3(H1T16, SOFF, [[32768, P], [1, 2048]]),
                    mask=ap3(H1TU8, 2 * UOFF, [[65536, P], [1, 2048]]),
                    on_true=ap3(PS, 0, [[4096, P], [1, 2048]]),
                    on_false=ap3(H1T16, 31808, [[32768, P], [0, 2048]])), sEp)
                c.wait("scalar", sEp, sel_last)
                if av_last is not None:
                    c.wait("scalar", sMM, av_last)
                if i == 0:
                    c.wait("scalar", sMM, bmv)
                exp_last = c.op("scalar", lambda e: e.activation(
                    out=H1T[:, POFF:POFF + 2048], in_=ap3(H1T16, SOFF, [[32768, P], [1, 2048]]),
                    func=AF.Exp, bias=BCAST[:, 37:38], scale=2.0 / float(np.sqrt(L))), sSq)
                c.wait("tensor", sSq, exp_last)
                if norm_last is not None:
                    c.wait("tensor", sEp, norm_last)
                for jt in range(16):
                    fn_ = lambda e, jt=jt, g=g: e.matmul(
                        out=c.bank(4)[:, 0:257],
                        lhsT=H1T[:, POFF + jt * P:POFF + (jt + 1) * P],
                        rhs=H1T[:, VOFF + jt * 1028 + 257 * g:VOFF + jt * 1028 + 257 * (g + 1)],
                        start=(jt == 0), stop=(jt == 15))
                    if jt == 15:
                        avv = c.op("tensor", fn_, sMM)
                    else:
                        c.op("tensor", fn_)
                av_last = avv
                c.wait("vector", sMM, avv)
                c.opS("vector", lambda e: e.reciprocal(
                    out=STAT[:, 28:29], in_=c.bank(4)[:, 256:257]))
                norm_last = c.op("vector", lambda e, i=i, g=g: e.tensor_scalar(
                    out=AB[:, i * 1024 + 256 * g:i * 1024 + 256 * (g + 1)],
                    in0=c.bank(4)[:, 0:256], scalar1=STAT[:, 28:29], scalar2=None,
                    op0=ALU.mult), sEp)

        for tau in range(16):
            tt = b * TPB + tau
            sl = (tau % 4) * D
            c.wait("sync", sTb, ln["tbv"][12 + (tau % 4)])
            if tau >= 4:
                c.wait("sync", sV, out_v[(b, tau - 4)])
            sX = sXp[tau % 4]
            xv = c.dma("sync", XT[:, sl:sl + D], xsrc_fn(tt), sX)
            c.waitall("vector", sX)
            if tau == 0:
                c.wait("vector", sEp, norm_last)
            c.waitall("vector", sOutp[tau % 2])
            vv = c.op("vector", lambda e, sl=sl, tau=tau: e.tensor_tensor(
                out=OUTT[:, (tau % 2) * D:(tau % 2 + 1) * D], in0=XT[:, sl:sl + D],
                in1=AB[:, tau * 1024:(tau + 1) * 1024], op=ALU.add), sV)
            out_v[(b, tau)] = vv
            c.wait("sync", sV, vv)
            c.dma("sync", xdst_fn(tt), OUTT[:, (tau % 2) * D:(tau % 2 + 1) * D], sOutp[tau % 2])
        prev_b_done = out_v[(b, 15)]
    c.waitall("sync", sOutp[0])
    c.waitall("sync", sOutp[1])
    c.waitall("sync", sDX)


def emit_l1ab(c, W, xsrc_fn, xdst_fn):
    nc = c.nc
    TTB, H1T, WA, XT, STAT, LNS, TBF, OUTT, MISC, ACC = (c.bufs[k] for k in
        ("TTB", "H1T", "WA", "XT", "STAT", "LNS", "TBF", "OUTT", "MISC", "ACC"))
    SMALLF, SMALLB, ONESB, ONESF, BCAST, IDENT = (c.bufs[k] for k in
        ("SMALLF", "SMALLB", "ONESB", "ONESF", "BCAST", "IDENT"))
    ONESC = c.sbuf("ONESC", [P, 1], BF16)
    sMM, sEp, sAc, sV, sEv, sSq, sSt, sTb = (c.sem(k) for k in
        ("sMM", "sEp", "sAc", "sV", "sEv", "sSq", "sSt", "sTb"))
    sRs2 = c.sem("sRs2")
    sCst = c.sem("sCst")
    sC = c.sem("sC")
    sWA = [c.sem("sWA0"), c.sem("sWA1")]
    sXp = [c.sem("sX0"), c.sem("sX1"), c.sem("sX2"), c.sem("sX3")]
    sOutp = [c.sem("sOutA"), c.sem("sOutB")]
    sQW = c.sem("sKW")
    sQR = [c.sem("sKB0"), c.sem("sKB1")]
    TTB_PITCH = 16 * 8 * P
    PS = c.bufs["PSALL"]
    AB = ACC.bitcast(BF16)
    invD = 1.0 / float(np.sqrt(D))
    qbuf = nc.dram_tensor(c.ns + "l1ab_q", [D, L], BF16)

    c.op("vector", lambda e: e.memset(ONESC[:, :], 1.0))
    c.dma("gpsimd", SMALLB[:, 0:1024], ap3(W["bff"], 0, [[1024, 1], [1, 1024]]), sCst)

    VOFF, QSOFF, PTOFF = 0, 16640, 20736

    out_v = {}
    prev_b_done = None
    prod_ev = {4: None, 5: None}
    exp_last = av_last = norm_last = None
    bce_prev = None
    for b in range(NB):
        tiles = list(range(b * TPB, (b + 1) * TPB))
        xt_guards = None
        if b > 0:
            xt_guards = {s: (sV, out_v[(b - 1, 12 + s)]) for s in range(4)}
        ln = emit_ln(c, xsrc_fn, tiles, slot0=0, xt_guards=xt_guards)

        for qk in range(2):
            wsrc = W["cTk"] if qk == 0 else W["cTq"]
            slot = qk
            if prev_b_done is not None or qk == 1:
                c.wait("gpsimd", sMM, wa_qk if qk == 1 else prev_wa)
            c.dma("gpsimd", WA[:, slot * 4096:(slot + 1) * 4096]
                  .rearrange("p (dt f) -> p dt f", f=512),
                  wsrc[:, 0:512].rearrange("(dt p) f -> p dt f", p=P), sWA[slot])
            for ch in range(2):
                if ch == 1:
                    c.wait("gpsimd", sMM, mmv)
                    c.dma("gpsimd", WA[:, slot * 4096:(slot + 1) * 4096]
                          .rearrange("p (dt f) -> p dt f", f=512),
                          wsrc[:, 512:1024].rearrange("(dt p) f -> p dt f", p=P), sWA[slot])
                c.waitall("tensor", sWA[slot])
                for ft in range(4):
                    fglob = ch * 4 + ft
                    for th in range(4):
                        bk = 4 + (th % 2)
                        if prod_ev[bk] is not None:
                            c.wait("tensor", sAc, prod_ev[bk])
                        if qk == 0 and ch == 0 and ft == 0 and th == 0 and prev_b_done is not None:
                            c.wait("tensor", sV, prev_b_done)
                        for dt in range(8):
                            off = (th * 4) * 1024 + dt * P
                            rhs = ap3(TTB, off, [[TTB_PITCH, P], [1024, 4], [1, P]])
                            fn_ = lambda e, slot=slot, dt=dt, ft=ft, rhs=rhs, bk=bk: e.matmul(
                                out=c.bank(bk),
                                lhsT=WA[:, slot * 4096 + dt * 512 + 128 * ft:
                                        slot * 4096 + dt * 512 + 128 * (ft + 1)],
                                rhs=rhs, start=(dt == 0), stop=(dt == 7))
                            if dt == 7:
                                mmv = c.op("tensor", fn_, sMM)
                            else:
                                c.op("tensor", fn_)
                        c.wait("scalar", sMM, mmv)
                        if qk == 0:
                            prod_ev[bk] = c.op("scalar", lambda e, fglob=fglob, th=th, bk=bk: e.copy(
                                out=AB[:, fglob * 2048 + th * 512:fglob * 2048 + (th + 1) * 512],
                                in_=c.bank(bk)), sAc)
                        else:
                            pe = c.op("scalar", lambda e, th=th, bk=bk: e.copy(
                                out=ap3(H1T, QSOFF + th * 512, [[32768, P], [1, 512]]),
                                in_=c.bank(bk)), sAc)
                            prod_ev[bk] = pe
                            c.wait("sync", sAc, pe)
                            c.dma("sync", ap3(qbuf, fglob * P * L + th * 512,
                                              [[L, P], [1, 512]]),
                                  ap3(H1T, QSOFF + th * 512, [[32768, P], [1, 512]]), sQW)
            if qk == 0:
                wa_qk = mmv
            else:
                prev_wa = mmv
        qk_done = mmv
        seq = 0
        ev = {}
        c.wait("gpsimd", sMM, wa_qk)
        c.dma("gpsimd", WA[:, 0:4096].rearrange("p (dt f) -> p dt f", f=512),
              W["cTv"][:, 0:512].rearrange("(dt p) f -> p dt f", p=P), sWA[0])
        for ch in range(2):
            if ch == 1:
                c.wait("gpsimd", sMM, mmv)
                c.dma("gpsimd", WA[:, 0:4096].rearrange("p (dt f) -> p dt f", f=512),
                      W["cTv"][:, 512:1024].rearrange("(dt p) f -> p dt f", p=P), sWA[0])
            c.waitall("tensor", sWA[0])
            for j in range(16):
                bk = 4 + (seq % 2)
                if prod_ev[bk] is not None:
                    c.wait("tensor", sAc, prod_ev[bk])
                for dt in range(8):
                    fn_ = lambda e, j=j, dt=dt, bk=bk: e.matmul(
                        out=c.bank(bk),
                        lhsT=TTB[:, j * 1024 + dt * P:j * 1024 + (dt + 1) * P],
                        rhs=WA[:, dt * 512:(dt + 1) * 512],
                        start=(dt == 0), stop=(dt == 7))
                    if dt == 7:
                        mmv = c.op("tensor", fn_, sMM)
                    else:
                        c.op("tensor", fn_)
                c.wait("scalar", sMM, mmv)
                dst = ap3(H1T, VOFF + j * 1040 + (8 * ch) * 65, [[32768, P], [65, 8], [1, 64]])
                srcp = ap3(PS, bk * 512, [[4096, P], [64, 8], [1, 64]])
                pe = c.op("scalar", lambda e, dst=dst, srcp=srcp: e.copy(
                    out=dst, in_=srcp), sAc)
                prod_ev[bk] = pe
                ev[seq] = pe
                seq += 1
        ones_done = None
        for j in range(16):
            c.wait("vector", sAc, ev[16 + j])
            ones_done = c.op("vector", lambda e, j=j: e.memset(
                ap3(H1T, VOFF + j * 1040 + 64, [[32768, P], [65, 16], [1, 1]]), 1.0), sV)

        c.wait("scalar", sAc, ev[31])
        nm = {}
        for qk in range(2):
            srcbuf = (lambda ft: AB[:, ft * 2048:(ft + 1) * 2048]) if qk == 0 else None
            sq_ev = {}
            for ft in range(8):
                if qk == 0:
                    sq_ev[ft] = c.op("scalar", lambda e, ft=ft: e.activation(
                        out=ap3(H1T, PTOFF + (ft % 4) * 2048, [[32768, P], [1, 2048]]),
                        in_=AB[:, ft * 2048:(ft + 1) * 2048], func=AF.Square), sAc)
                else:
                    c.waitall("sync", sQW)
                    qv = c.dma("sync", ap3(H1T, QSOFF, [[32768, P], [1, 2048]]),
                               ap3(qbuf, ft * P * L, [[L, P], [1, 2048]]), sQR[0])
                    c.waitall("scalar", sQR[0])
                    sq_ev[ft] = c.op("scalar", lambda e, ft=ft: e.activation(
                        out=ap3(H1T, PTOFF + (ft % 4) * 2048, [[32768, P], [1, 2048]]),
                        in_=ap3(H1T, QSOFF, [[32768, P], [1, 2048]]), func=AF.Square), sAc)
                if ft % 4 == 3:
                    c.wait("tensor", sAc, sq_ev[ft])
                    for half in range(2):
                        if not (qk == 0 and ft == 3 and half == 0):
                            c.wait("tensor", sSt, nred)
                        for nh in range(2):
                            for sub in range(4):
                                fn_ = lambda e, nh=nh, sub=sub, half=half: e.matmul(
                                    out=ap3(PS, 3072 + nh * 512, [[4096, 1], [1, 512]]),
                                    lhsT=ONESC[:, 0:1],
                                    rhs=ap3(H1T, PTOFF + sub * 2048 + half * 1024 + nh * 512,
                                            [[32768, P], [1, 512]]),
                                    start=(sub == 0), stop=(sub == 3))
                                if nh == 1 and sub == 3:
                                    mv2 = c.op("tensor", fn_, sMM)
                                else:
                                    c.op("tensor", fn_)
                        c.wait("vector", sMM, mv2)
                        nred0 = c.opS("vector", lambda e: e.tensor_reduce(
                            out=STAT[0:1, 29:30], in_=ap3(PS, 3072, [[4096, 1], [1, 1024]]),
                            axis=AX.X, op=ALU.max))
                        first = (qk == 0 and ft == 3 and half == 0)
                        if first:
                            nred = c.op("vector", lambda e, qk=qk: e.tensor_copy(
                                out=STAT[0:1, 24 + qk:25 + qk], in_=STAT[0:1, 29:30]), sSt)
                        else:
                            if half == 0 and ft == 3:
                                nred = c.op("vector", lambda e, qk=qk: e.tensor_copy(
                                    out=STAT[0:1, 24 + qk:25 + qk], in_=STAT[0:1, 29:30]), sSt)
                            else:
                                nred = c.op("vector", lambda e, qk=qk: e.tensor_tensor(
                                    out=STAT[0:1, 24 + qk:25 + qk], in0=STAT[0:1, 24 + qk:25 + qk],
                                    in1=STAT[0:1, 29:30], op=ALU.max), sSt)
            nm[qk] = nred
        c.wait("vector", sSt, nm[1])
        pv = c.op("vector", lambda e: e.tensor_tensor(
            out=STAT[0:1, 26:27], in0=STAT[0:1, 24:25], in1=STAT[0:1, 25:26],
            op=ALU.mult), sSt)
        c.wait("scalar", sSt, pv)
        sqb = c.op("scalar", lambda e: e.activation(
            out=STAT[0:1, 27:28], in_=STAT[0:1, 26:27], func=AF.Sqrt), sRs2)
        c.wait("vector", sRs2, sqb)
        ngv = c.op("vector", lambda e: e.tensor_scalar(
            out=STAT[0:1, 26:27], in0=STAT[0:1, 27:28],
            scalar1=-invD, scalar2=None, op0=ALU.mult), sSt)
        c.wait("tensor", sSt, ngv)
        bmv = c.op("tensor", lambda e: e.matmul(
            out=c.bank(5)[:, 0:1], lhsT=ONESF[:, :], rhs=STAT[0:1, 26:27],
            start=True, stop=True), sMM)
        c.wait("scalar", sMM, bmv)
        ngb = c.op("scalar", lambda e: e.copy(out=BCAST[:, 37:38], in_=c.bank(5)[:, 0:1]), sAc)

        ttb_free = None
        qs_v = {}
        for i5 in range(4):
            if i5 >= 1:
                c.wait("sync", sMM, q_used[i5 - 1])
            else:
                c.waitall("sync", sQW)
            qs_v[i5] = c.dma("sync", ap3(H1T, QSOFF, [[32768, P], [256, 8], [1, 256]])
                             if False else
                             ap3(H1T, QSOFF, [[32768, P], [1, 4096]])
                             .rearrange("p (dt f) -> p dt f", f=512),
                             ap3(qbuf, 512 * i5, [[L, P], [P * L, 8], [1, 512]]), sQR[0])
            c.waitall("tensor", sQR[0])
            for h in range(16):
                ft = h // 2
                rb = 64 * (h % 2)
                for jt in range(16):
                    bk = jt % 2
                    if not (i5 == 0 and h == 0 and jt < 2):
                        c.wait("tensor", sSq, exp_prev[bk])
                    fn_ = lambda e, jt=jt, ft=ft, rb=rb, bk=bk: e.matmul(
                        out=c.bank(bk),
                        lhsT=ap3(AB, ft * 2048 + jt * P + rb * 16384, [[16384, 64], [1, P]]),
                        rhs=ap3(H1T, QSOFF + ft * 512 + rb * 32768, [[32768, 64], [1, 512]]),
                        start=True, stop=True)
                    smv = c.op("tensor", fn_, sMM)
                    c.wait("scalar", sMM, smv)
                    if jt == 0 and av_last is not None:
                        c.wait("scalar", sMM, av_last)
                    if i5 == 0 and h == 0 and jt < 2 and b == 0:
                        c.wait("scalar", sAc, ngb)
                    ee = c.op("scalar", lambda e, jt=jt, bk=bk: e.activation(
                        out=ap3(H1T, PTOFF + jt * 512, [[32768, P], [1, 512]]),
                        in_=c.bank(bk), func=AF.Exp, bias=BCAST[:, 37:38],
                        scale=invD), sSq)
                    if i5 == 0 and h == 0 and jt < 2:
                        if jt == 0:
                            exp_prev = {}
                    exp_prev[bk] = ee
                c.wait("tensor", sSq, exp_prev[0])
                c.wait("tensor", sSq, exp_prev[1])
                if norm_last is not None:
                    c.wait("tensor", sEp, norm_last)
                for jt in range(16):
                    fn_ = lambda e, jt=jt, h=h: e.matmul(
                        out=c.bank(2)[0:65, :],
                        lhsT=H1T[:, VOFF + jt * 1040 + 65 * h:VOFF + jt * 1040 + 65 * (h + 1)],
                        rhs=ap3(H1T, PTOFF + jt * 512, [[32768, P], [1, 512]]),
                        start=(jt == 0), stop=(jt == 15))
                    if jt == 15:
                        avv = c.op("tensor", fn_, sMM)
                    else:
                        c.op("tensor", fn_)
                av_last = avv
                c.wait("vector", sMM, avv)
                rcv = c.op("vector", lambda e: e.reciprocal(
                    out=STAT[0:1, 30:31] if False else MISC[0:1, 0:512],
                    in_=ap3(PS, 2 * 512 + 64 * 4096, [[4096, 1], [1, 512]])), sSt)
                c.wait("tensor", sSt, rcv)
                if bce_prev is not None:
                    c.wait("tensor", sAc, bce_prev)
                bcv = c.op("tensor", lambda e: e.matmul(
                    out=c.bank(3)[0:64, :], lhsT=ONESF[0:1, 0:64],
                    rhs=MISC[0:1, 0:512], start=True, stop=True), sMM)
                c.wait("scalar", sMM, bcv)
                bce = c.op("scalar", lambda e: e.copy(
                    out=MISC[0:64, 0:512], in_=c.bank(3)[0:64, :]), sAc)
                bce_prev = bce
                c.wait("vector", sAc, bce)
                norm_last = c.op("vector", lambda e, ft=ft, rb=rb, i5=i5: e.tensor_tensor(
                    out=ap3(TTB, ft * 2048 + rb * 16384 + i5 * 512, [[16384, 64], [1, 512]]),
                    in0=ap3(PS, 2 * 512, [[4096, 64], [1, 512]]),
                    in1=MISC[0:64, 0:512], op=ALU.mult), sEp)
            if i5 == 0:
                q_used = {}
            q_used[i5] = smv

        c.wait("gpsimd", sMM, q_used[3] if False else smv)
        c.dma("gpsimd", WA[:, 0:4096].rearrange("p (dt f) -> p dt f", f=512),
              W["cTff"][:, 0:512].rearrange("(dt p) f -> p dt f", p=P), sWA[0])
        c.dma("gpsimd", WA[:, 4096:8192].rearrange("p (dt f) -> p dt f", f=512),
              W["cTff"][:, 512:1024].rearrange("(dt p) f -> p dt f", p=P), sWA[1])
        c.waitall("tensor", sWA[0])
        c.waitall("tensor", sWA[1])
        c.waitall("tensor", sCst)
        pr_ev = {}
        for tau in range(16):
            bks = (tau % 2) * 2
            if tau == 0:
                c.wait("tensor", sEp, norm_last)
            if tau >= 2:
                c.wait("tensor", sV, pr_ev[tau - 2])
            for oh in range(2):
                c.op("tensor", lambda e, oh=oh, bks=bks: e.matmul(
                    out=c.bank(bks + oh), lhsT=ONESB[0:1, 0:P],
                    rhs=SMALLB[0:1, 512 * oh:512 * (oh + 1)], start=True, stop=False))
            for dt in range(8):
                for oh in range(2):
                    fn_ = lambda e, dt=dt, oh=oh, tau=tau, bks=bks: e.matmul(
                        out=c.bank(bks + oh),
                        lhsT=TTB[:, dt * 2048 + tau * P:dt * 2048 + tau * P + P],
                        rhs=WA[:, oh * 4096 + dt * 512:oh * 4096 + (dt + 1) * 512],
                        start=False, stop=(dt == 7))
                    if dt == 7 and oh == 1:
                        pmv = c.op("tensor", fn_, sMM)
                    else:
                        c.op("tensor", fn_)
            tt = b * TPB + tau
            sl = (tau % 4) * D
            c.wait("sync", sTb, ln["tbv"][12 + (tau % 4)])
            if tau >= 4:
                c.wait("sync", sV, out_v[(b, tau - 4)])
            sX = sXp[tau % 4]
            xv = c.dma("sync", XT[:, sl:sl + D], xsrc_fn(tt), sX)
            c.waitall("vector", sX)
            c.wait("vector", sMM, pmv)
            c.waitall("vector", sOutp[tau % 2])
            vv = c.op("vector", lambda e, sl=sl, tau=tau, bks=bks: e.tensor_tensor(
                out=OUTT[:, (tau % 2) * D:(tau % 2 + 1) * D], in0=XT[:, sl:sl + D],
                in1=ap3(c.bufs["PSALL"], bks * 512, [[4096, P], [1, 1024]]), op=ALU.add), sV)
            pr_ev[tau] = vv
            out_v[(b, tau)] = vv
            c.wait("sync", sV, vv)
            c.dma("sync", xdst_fn(tt), OUTT[:, (tau % 2) * D:(tau % 2 + 1) * D], sOutp[tau % 2])
        prev_b_done = out_v[(b, 15)]
        prev_wa = pmv
    c.waitall("sync", sOutp[0])
    c.waitall("sync", sOutp[1])
    c.waitall("sync", sQW)


def emit_l1aa(c, W, xsrc_fn, xdst_fn):
    nc = c.nc
    TTB, H1T, WA, XT, STAT, LNS, TBF, OUTT, MISC, ACC, GSC = (c.bufs[k] for k in
        ("TTB", "H1T", "WA", "XT", "STAT", "LNS", "TBF", "OUTT", "MISC", "ACC", "GSC"))
    SMALLF, SMALLB, ONESB, ONESF, BCAST, IDENT = (c.bufs[k] for k in
        ("SMALLF", "SMALLB", "ONESB", "ONESF", "BCAST", "IDENT"))
    sMM, sEp, sAc, sV, sEv, sSq, sSt, sTb = (c.sem(k) for k in
        ("sMM", "sEp", "sAc", "sV", "sEv", "sSq", "sSt", "sTb"))
    sRs2 = c.sem("sRs2")
    sCst = c.sem("sCst")
    sC = c.sem("sC")
    sWA = [c.sem("sWA0"), c.sem("sWA1")]
    sXp = [c.sem("sX0"), c.sem("sX1"), c.sem("sX2"), c.sem("sX3")]
    sOutp = [c.sem("sOutA"), c.sem("sOutB")]
    sSG = c.sem("sKW")
    sSR = [c.sem("sKB0"), c.sem("sKB1")]
    PS = c.bufs["PSALL"]
    TTB_PITCH = 16 * 8 * P
    H1T16 = H1T
    CLT = c.sbuf("CCS", [P, L], BF16)[:, 0:P]

    qpt_d = nc.dram_tensor(c.ns + "l1aa_qpt", [2048, L], BF16)
    kpt_d = nc.dram_tensor(c.ns + "l1aa_kpt", [2048, L], BF16)
    kpb_d = nc.dram_tensor(c.ns + "l1aa_kpb", [L, 2048], BF16)
    vb_d = nc.dram_tensor(c.ns + "l1aa_vb", [L, 1088], BF16)

    QPTO, KPTO, KPBO, VBO, AMTO, OPTO = 0, 2048, 4096, 6144, 7232, 8256

    c.dma("gpsimd", CLT[:, :], W["cLT"][:, :], sCst)
    c.dma("gpsimd", SMALLB[:, 0:1024], ap3(W["bq"], 0, [[1024, 1], [1, 1024]]), sCst)
    c.dma("gpsimd", SMALLB[:, 1024:2048], ap3(W["bk"], 0, [[1024, 1], [1, 1024]]), sCst)
    c.dma("gpsimd", SMALLB[:, 2048:3072], ap3(W["bv"], 0, [[1024, 1], [1, 1024]]), sCst)
    c.dma("gpsimd", SMALLB[:, 3072:4096], ap3(W["bo"], 0, [[1024, 1], [1, 1024]]), sCst)
    c.dma("sync", SMALLF[0:1, 0:1], ap3(W["beta"], 0, [[1, 1], [1, 1]]), sC)
    c.waitall("scalar", sC)
    bev = c.op("scalar", lambda e: e.activation(
        out=SMALLF[0:1, 1:2], in_=SMALLF[0:1, 0:1], func=AF.Exp), sRs2)
    c.wait("tensor", sRs2, bev)
    c.waitall("tensor", sC)
    c.waitall("tensor", sCst)
    v0 = c.op("tensor", lambda e: e.matmul(out=c.bank(6)[:, 0:1], lhsT=ONESF[:, :],
                                           rhs=SMALLF[0:1, 1:2], start=True, stop=True), sMM)
    c.wait("scalar", sMM, v0)
    be_v = c.op("scalar", lambda e: e.copy(out=BCAST[:, 36:37], in_=c.bank(6)[:, 0:1]), sAc)

    out_v = {}
    prev_b_done = None
    wa_last = None
    amt_done = {}
    amt_cons = {}
    sm_cons = {0: None, 1: None}
    sm_last = None
    fin_prev = None
    pr_prev = None
    st_add_prev = None
    for b in range(NB):
        tiles = list(range(b * TPB, (b + 1) * TPB))
        xt_guards = None
        if b > 0:
            xt_guards = {s: (sV, out_v[(b - 1, 12 + s)]) for s in range(4)}
        ln = emit_ln(c, xsrc_fn, tiles, slot0=0, xt_guards=xt_guards)

        zp_v = c.op("vector", lambda e: e.memset(ACC[:, 0:4096], 0.0), sV)

        tr_ev = {}
        tr_seq = 0
        for wsel in range(3):
            wsrc = [W["cTq"], W["cTk"], W["cTv"]][wsel]
            boff = [0, 1024, 2048][wsel]
            if wa_last is not None:
                c.wait("gpsimd", sMM, wa_last)
            elif prev_b_done is not None:
                c.wait("gpsimd", sV, prev_b_done)
            c.dma("gpsimd", WA[:, 0:8192].rearrange("p (dt f) -> p dt f", f=1024),
                  wsrc[:, :].rearrange("(dt p) f -> p dt f", p=P), sWA[0])
            c.waitall("tensor", sWA[0])
            for tau in range(16):
                bks = (tau % 2) * 2
                c.wait("tensor", sEv, ln["evv"][tau])
                if prev_b_done is not None and wsel == 0 and tau == 0:
                    c.wait("tensor", sV, prev_b_done)
                if sm_cons[tau % 2] is not None:
                    gs_, gv_ = sm_cons[tau % 2]
                    c.wait("tensor", gs_, gv_)
                for nh in range(2):
                    c.op("tensor", lambda e, nh=nh, bks=bks, boff=boff: e.matmul(
                        out=c.bank(bks + nh), lhsT=ONESB[0:1, 0:P],
                        rhs=SMALLB[0:1, boff + nh * 512:boff + (nh + 1) * 512],
                        start=True, stop=False))
                    for dt in range(8):
                        fn_ = lambda e, tau=tau, dt=dt, nh=nh, bks=bks: e.matmul(
                            out=c.bank(bks + nh),
                            lhsT=TTB[:, tau * 1024 + dt * P:tau * 1024 + (dt + 1) * P],
                            rhs=WA[:, dt * 1024 + nh * 512:dt * 1024 + nh * 512 + 512],
                            start=False, stop=(dt == 7))
                        if dt == 7 and nh == 1:
                            mmv = c.op("tensor", fn_, sMM)
                        else:
                            c.op("tensor", fn_)
                if wsel < 2:
                    qp = ap3(PS, bks * 512, [[4096, P], [1, 1024]])
                    qp3 = ap3(PS, bks * 512, [[4096, P], [16, 64], [1, 16]])
                    c.wait("vector", sMM, mmv)
                    if b == 0 and wsel == 0 and tau == 0:
                        c.wait("vector", sAc, be_v)
                        c.wait("vector", sV, zp_v)
                    c.opS("vector", lambda e, qp3=qp3: e.tensor_reduce(
                        out=GSC[:, 0:64], in_=qp3, axis=AX.X, op=ALU.max))
                    c.opS("vector", lambda e: e.tensor_scalar(
                        out=GSC[:, 64:128], in0=GSC[:, 0:64], scalar1=BCAST[:, 36:37],
                        scalar2=None, op0=ALU.mult))
                    sv = c.op("vector", lambda e, qp3=qp3: e.scalar_tensor_tensor(
                        out=MISC[:, 0:1024].rearrange("p (g f) -> p g f", f=16),
                        in0=qp3, scalar=BCAST[:, 36:37],
                        in1=ap3(GSC, 64, [[168, P], [1, 64], [0, 16]]),
                        op0=ALU.mult, op1=ALU.subtract), sSt)
                    c.wait("scalar", sSt, sv)
                    ev_ = c.op("scalar", lambda e: e.activation(
                        out=TBF[:, 0:1024], in_=MISC[:, 0:1024], func=AF.Exp), sSq)
                    c.wait("vector", sSq, ev_)
                    c.opS("vector", lambda e: e.tensor_reduce(
                        out=GSC[:, 0:64], in_=TBF[:, 0:1024].rearrange("p (g f) -> p g f", f=16),
                        axis=AX.X, op=ALU.add))
                    c.opS("vector", lambda e: e.reciprocal(
                        out=GSC[:, 64:128], in_=GSC[:, 0:64]))
                    poff = 0 if wsel == 0 else 2048
                    pdst = ap3(ACC, poff, [[8192, P], [32, 64], [1, 16]])
                    sm_l = c.op("vector", lambda e, pdst=pdst: e.tensor_tensor(
                        out=pdst, in0=TBF[:, 0:1024].rearrange("p (g f) -> p g f", f=16),
                        in1=ap3(GSC, 64, [[168, P], [1, 64], [0, 16]]), op=ALU.mult), sSt)
                    sm_cons[tau % 2] = (sSt, sm_l)
                    sm_last = sm_l
                    if wsel == 1:
                        c.wait("scalar", sSt, sm_l)
                        kb_ = c.op("scalar", lambda e: e.copy(
                            out=H1T[:, KPBO:KPBO + 2048], in_=ACC[:, 2048:4096]), sAc)
                        c.wait("sync", sAc, kb_)
                        c.dma("sync", ap3(kpb_d, tau * P * 2048, [[2048, P], [1, 2048]]),
                              H1T[:, KPBO:KPBO + 2048], sSG)
                    dstd = qpt_d if wsel == 0 else kpt_d
                    c.wait("tensor", sSt, sm_l)
                    for wv in range(4):
                        tb = 4 + (tr_seq % 2) * 2
                        if tr_seq >= 2:
                            c.wait("tensor", sAc, tr_ev[tr_seq - 2])
                        poff2 = (0 if wsel == 0 else 2048) + wv * 512
                        for q4 in range(4):
                            fn_ = lambda e, q4=q4, tb=tb, poff2=poff2: e.transpose(
                                out=ap3(PS, tb * 512 + q4 * P, [[4096, P], [1, P]]),
                                in_=ap3(ACC, poff2 + q4 * P, [[8192, P], [1, P]]),
                                identity=IDENT[:, :])
                            if q4 == 3:
                                tv_ = c.op("tensor", fn_, sMM)
                            else:
                                c.op("tensor", fn_)
                        c.wait("scalar", sMM, tv_)
                        c.waitall("scalar", sSG)
                        tr_ev[tr_seq] = c.op("scalar", lambda e, tb=tb: e.copy(
                            out=H1T[:, QPTO:QPTO + 512] if False else
                            ap3(H1T, QPTO, [[32768, P], [1, 512]]),
                            in_=ap3(PS, tb * 512, [[4096, P], [1, 512]])), sAc)
                        c.wait("sync", sAc, tr_ev[tr_seq])
                        c.dma("sync", ap3(dstd, (wv * 512) * L + tau * P,
                                          [[L, P], [P * L, 4], [1, P]]),
                              ap3(H1T, QPTO, [[32768, P], [128, 4], [1, P]]), sSG)
                        tr_seq += 1
                else:
                    c.wait("scalar", sMM, mmv)
                    c.waitall("scalar", sSG)
                    vb1 = c.op("scalar", lambda e, bks=bks: e.copy(
                        out=ap3(H1T, VBO, [[32768, P], [17, 64], [1, 16]]),
                        in_=ap3(PS, bks * 512, [[4096, P], [16, 64], [1, 16]])), sAc)
                    sm_cons[tau % 2] = (sAc, vb1)
                    c.wait("vector", sAc, vb1)
                    on_ = c.op("vector", lambda e: e.memset(
                        ap3(H1T, VBO + 16, [[32768, P], [17, 64], [1, 1]]), 1.0), sV)
                    c.wait("sync", sV, on_)
                    c.dma("sync", ap3(vb_d, tau * P * 1088, [[1088, P], [1, 1088]]),
                          ap3(H1T, VBO, [[32768, P], [1, 1088]]), sSG)
            wa_last = mmv

        MISCb = MISC.bitcast(BF16)
        kvz = c.op("vector", lambda e: e.memset(MISC[:, 0:272], 0.0), sV)
        kvz = c.op("vector", lambda e: e.memset(
            ap3(MISCb, 1024, [[2048, P], [1, 272]]), 0.0), sV)
        kvsb_v = None
        amt_prev = {0: None, 1: None}
        st_add_prev = kvz
        fin_prev = None
        pr_prev = None
        ld_prev = {}
        for tau in range(16):
            sl2 = tau % 2
            if tau >= 2:
                c.wait("sync", sMM, ld_prev[tau - 2])
            c.waitall("sync", sSG)
            c.dma("sync", ap3(H1T, QPTO, [[32768, P], [128, 16], [1, P]]),
                  ap3(qpt_d, tau * P, [[L, P], [P * L, 16], [1, P]]), sSR[0])
            c.dma("sync", ap3(H1T, KPTO, [[32768, P], [128, 16], [1, P]]),
                  ap3(kpt_d, tau * P, [[L, P], [P * L, 16], [1, P]]), sSR[0])
            c.dma("sync", ap3(H1T, KPBO, [[32768, P], [1, 2048]]),
                  ap3(kpb_d, tau * P * 2048, [[2048, P], [1, 2048]]), sSR[0])
            c.dma("sync", ap3(H1T, VBO, [[32768, P], [1, 1088]]),
                  ap3(vb_d, tau * P * 1088, [[1088, P], [1, 1088]]), sSR[0])
            c.waitall("tensor", sSR[0])
            for g in range(16):
                for r in range(4):
                    h = 4 * g + r
                    if amt_done.get((g % 2, r)) is not None:
                        c.wait("tensor", sEp, amt_done[(g % 2, r)])
                    if g <= 1 and r == 0 and st_add_prev is not None:
                        c.wait("tensor", sV, st_add_prev)
                    fn_ = lambda e, g=g, r=r: e.matmul(
                        out=ap3(PS, r * 512 + (g % 2) * P, [[4096, P], [1, P]]),
                        lhsT=bass.AP(H1T, KPTO + g * P + 32 * r * 32768, [[32768, 32], [1, P]]),
                        rhs=bass.AP(H1T, QPTO + g * P + 32 * r * 32768, [[32768, 32], [1, P]]),
                        start=True, stop=True, tile_position=(32 * r, 0))
                    amv = c.op("tensor", fn_, sMM)
                    c.wait("vector", sMM, amv)
                    if amt_cons.get((g % 2, r)) is not None:
                        c.wait("vector", sMM, amt_cons[(g % 2, r)])
                    amt_done[(g % 2, r)] = c.op("vector",
                        lambda e, g=g, r=r: e.tensor_tensor(
                            out=ap3(H1T, AMTO + (g % 2) * 512 + r * P, [[32768, P], [1, P]]),
                            in0=ap3(PS, r * 512 + (g % 2) * P, [[4096, P], [1, P]]),
                            in1=CLT[:, :], op=ALU.mult), sEp)
                for r in range(4):
                    h = 4 * g + r
                    c.wait("tensor", sEp, amt_done[(g % 2, r)])
                    if fin_prev is not None and g == 0 and r == 0:
                        c.wait("tensor", sSt, fin_prev)
                        c.wait("tensor", sV, pr_prev)
                    c.op("tensor", lambda e, g=g, r=r, h=h: e.matmul(
                        out=ap3(PS, (4 + r) * 512 + g * 17, [[4096, P], [1, 17]]),
                        lhsT=ap3(H1T, AMTO + (g % 2) * 512 + r * P, [[32768, P], [1, P]]),
                        rhs=ap3(H1T, VBO + 17 * h, [[32768, P], [1, 17]]),
                        start=True, stop=False))
                    if kvsb_v is not None and g == 0 and r == 0:
                        c.wait("tensor", sAc, kvsb_v)
                    elif kvsb_v is None and g == 0 and r == 0:
                        c.wait("tensor", sV, kvz)
                    iv = c.op("tensor", lambda e, g=g, r=r, h=h: e.matmul(
                        out=ap3(PS, (4 + r) * 512 + g * 17, [[4096, P], [1, 17]]),
                        lhsT=bass.AP(H1T, QPTO + g * P + 32 * r * 32768, [[32768, 32], [1, P]]),
                        rhs=bass.AP(MISCb, 1024 + 17 * g + 32 * r * 2048, [[2048, 32], [1, 17]]),
                        start=False, stop=True, tile_position=(32 * r, 0)), sMM)
                    amt_cons[(g % 2, r)] = iv
                av_done = iv
            for g in range(16):
                for r in range(4):
                    h = 4 * g + r
                    if g == 0 and r == 0:
                        for par in range(2):
                            for rr in range(4):
                                if amt_done.get((par, rr)) is not None:
                                    c.wait("tensor", sEp, amt_done[(par, rr)])
                        if st_add_prev is not None:
                            c.wait("tensor", sV, st_add_prev)
                    smv2 = c.op("tensor", lambda e, g=g, r=r, h=h: e.matmul(
                        out=bass.AP(PS, 17 * g + 32 * r * 4096, [[4096, 32], [1, 17]]),
                        lhsT=ap3(H1T, KPBO + 32 * h, [[32768, P], [1, 32]]),
                        rhs=ap3(H1T, VBO + 17 * h, [[32768, P], [1, 17]]),
                        start=True, stop=True, tile_position=(0, 32 * r)), sMM)
            st_last_mm = smv2
            c.wait("vector", sMM, av_done)
            c.opS("vector", lambda e: e.reciprocal(
                out=GSC[:, 0:64], in_=ap3(PS, 4 * 512 + 16, [[4096, P], [512, 4], [17, 16]])))
            fin = c.op("vector", lambda e: e.tensor_tensor(
                out=bass.AP(ACC, 4096, [[8192, P], [16, 4], [64, 16], [1, 16]]),
                in0=bass.AP(PS, 4 * 512, [[4096, P], [512, 4], [17, 16], [1, 16]]),
                in1=bass.AP(GSC, 0, [[168, P], [16, 4], [1, 16], [0, 16]]),
                op=ALU.mult), sSt)
            fin_prev = fin
            c.wait("vector", sMM, st_last_mm)
            st_add_prev = c.op("vector", lambda e: e.tensor_tensor(
                out=MISC[:, 0:272], in0=MISC[:, 0:272],
                in1=ap3(PS, 0, [[4096, P], [1, 272]]), op=ALU.add), sV)
            c.wait("scalar", sV, st_add_prev)
            kvsb_v = c.op("scalar", lambda e: e.copy(
                out=ap3(MISCb, 1024, [[2048, P], [1, 272]]), in_=MISC[:, 0:272]), sAc)
            c.wait("tensor", sV, st_add_prev)
            c.wait("tensor", sSt, fin)
            for wv in range(2):
                for q4 in range(4):
                    fn_ = lambda e, wv=wv, q4=q4: e.transpose(
                        out=ap3(PS, (2 + wv) * 512 + q4 * P, [[4096, P], [1, P]]),
                        in_=ap3(ACC, 4096 + wv * 512 + q4 * P, [[8192, P], [1, P]]),
                        identity=IDENT[:, :])
                    if q4 == 3:
                        tv_ = c.op("tensor", fn_, sMM)
                    else:
                        c.op("tensor", fn_)
                c.wait("scalar", sMM, tv_)
                ope = c.op("scalar", lambda e, wv=wv: e.copy(
                    out=ap3(H1T, OPTO + wv * 512, [[32768, P], [1, 512]]),
                    in_=ap3(PS, (2 + wv) * 512, [[4096, P], [1, 512]])), sAc)
            if tau == 0:
                c.wait("gpsimd", sMM, wa_last)
                c.dma("gpsimd", WA[:, 0:8192].rearrange("p (dt f) -> p dt f", f=1024),
                      W["cTo"][:, :].rearrange("(dt p) f -> p dt f", p=P), sWA[0])
                wa_last = None
            c.waitall("tensor", sWA[0])
            c.wait("tensor", sAc, ope)
            if pr_prev is not None:
                c.wait("tensor", sV, pr_prev)
            for nh in range(2):
                c.op("tensor", lambda e, nh=nh: e.matmul(
                    out=c.bank(6 + nh), lhsT=ONESB[0:1, 0:P],
                    rhs=SMALLB[0:1, 3072 + nh * 512:3072 + (nh + 1) * 512],
                    start=True, stop=False))
                for dt in range(8):
                    fn_ = lambda e, dt=dt, nh=nh: e.matmul(
                        out=c.bank(6 + nh),
                        lhsT=ap3(H1T, OPTO + dt * P, [[32768, P], [1, P]]),
                        rhs=WA[:, dt * 1024 + nh * 512:dt * 1024 + nh * 512 + 512],
                        start=False, stop=(dt == 7))
                    if dt == 7 and nh == 1:
                        pmv = c.op("tensor", fn_, sMM)
                    else:
                        c.op("tensor", fn_)
            ld_prev[tau] = pmv
            tt = b * TPB + tau
            sl = (tau % 4) * D
            c.wait("sync", sTb, ln["tbv"][12 + (tau % 4)])
            if tau >= 4:
                c.wait("sync", sV, out_v[(b, tau - 4)])
            sX = sXp[tau % 4]
            xv = c.dma("sync", XT[:, sl:sl + D], xsrc_fn(tt), sX)
            c.waitall("vector", sX)
            c.wait("vector", sMM, pmv)
            c.waitall("vector", sOutp[tau % 2])
            vv = c.op("vector", lambda e, sl=sl, tau=tau: e.tensor_tensor(
                out=OUTT[:, (tau % 2) * D:(tau % 2 + 1) * D], in0=XT[:, sl:sl + D],
                in1=ap3(PS, 6 * 512, [[4096, P], [1, 1024]]), op=ALU.add), sV)
            pr_prev = vv
            out_v[(b, tau)] = vv
            c.wait("sync", sV, vv)
            c.dma("sync", xdst_fn(tt), OUTT[:, (tau % 2) * D:(tau % 2 + 1) * D], sOutp[tau % 2])
        prev_b_done = out_v[(b, 15)]
        wa_last = pmv
    c.waitall("sync", sOutp[0])
    c.waitall("sync", sOutp[1])
    c.waitall("sync", sSG)


def emit_l0ffn_sp(c, W, xsrc_fn, xdst_fn, xfull):
    nc = c.nc
    TTB, H1T, WA, WC, XT, STAT, LNS, TBF, OUTT, MISC, ACC = (c.bufs[k] for k in
        ("TTB", "H1T", "WA", "WC", "XT", "STAT", "LNS", "TBF", "OUTT", "MISC", "ACC"))
    MASKS, SMALLF, SMALLB, ONESB, ONESF, BCAST, IDENT, WSM, GSC = (c.bufs[k] for k in
        ("MASKS", "SMALLF", "SMALLB", "ONESB", "ONESF", "BCAST", "IDENT", "WSM", "GSC"))
    _SSC = c.sbuf("SSC", [P, L], BF16)
    PM4 = _SSC.bitcast(F32)[:, 0:128]
    IVC = _SSC.bitcast(F32)[:, 128:140]
    IVU = _SSC.bitcast(U32)[:, 140:144]
    IVI = _SSC.bitcast(mybir.dt.int32)[:, 144:146]
    sMM, sEp, sAc, sV, sEv, sSq, sSt, sTb = (c.sem(k) for k in
        ("sMM", "sEp", "sAc", "sV", "sEv", "sSq", "sSt", "sTb"))
    sCst = c.sem("sCst")
    sWA = [c.sem("sWA0"), c.sem("sWA1")]
    sWC = [c.sem("sWC0"), c.sem("sWC1"), c.sem("sWC2"), c.sem("sWC3")]
    sXp = [c.sem("sX0"), c.sem("sX1"), c.sem("sX2"), c.sem("sX3")]
    sOutp = [c.sem("sOutA"), c.sem("sOutB")]
    sBE = c.sem("sBE")
    sIV = [c.sem("sKB0"), c.sem("sKB1")]
    sGX = [c.sem("sZQ0"), c.sem("sZQ1")]
    sSC = c.sem("sSCg")
    sYG = c.sem("sYGg")
    sPF = c.sem("sPF")
    sYW = c.sem("sYW")
    PS = c.bufs["PSALL"]
    TTB_PITCH = 16 * 8 * P
    SL = 512
    NSLOT = 12
    NPAD = NSLOT * SL
    H1F = H1T.bitcast(F32)

    invcw = nc.dram_tensor(c.ns + "l0f_ivc", [NPAD, 4], F32)
    ybuf = nc.dram_tensor(c.ns + "l0f_y", [NPAD, D], BF16)
    we1f = W["We1"]
    we2f = W["We2"]
    be1f = W["be1"]
    be2f = W["be2"]

    c.dma("gpsimd", WSM[:, 0:32].rearrange("p (dt e) -> p dt e", e=4),
          W["Wr"].rearrange("(dt p) e -> p dt e", p=P), sCst)
    c.dma("gpsimd", SMALLF[0:1, 0:4], ap3(W["br"], 0, [[4, 1], [1, 4]]), sCst)
    c.dma("sync", WSM[:, 32:36] if False else IVC[:, 8:9], W["cIOTAC"][:, :], c.sem("sC"))
    c.waitall("tensor", sCst)
    c.waitall("tensor", c.sem("sC"))
    v = c.op("tensor", lambda e: e.matmul(out=c.bank(6)[:, 0:4], lhsT=ONESF[:, :],
                                          rhs=SMALLF[0:1, 0:4], start=True, stop=True), sMM)
    c.wait("scalar", sMM, v)
    brB_v = c.op("scalar", lambda e: e.copy(out=BCAST[:, 0:4], in_=c.bank(6)[:, 0:4]), sAc)
    c.op("vector", lambda e: e.memset(STAT[:, 31:32], 0.0), sV)
    for e4 in range(4):
        c.op("vector", lambda e, e4=e4: e.memset(BCAST[:, 8 + e4:9 + e4], float(e4)), sV)
    c.op("vector", lambda e: e.memset(MISC[:, 0:128], 0.0), sV)
    pf_v = c.op("vector", lambda e: e.memset(
        ap3(MISC, 0, [[1024, P], [4, 32]]), 1.0e9), sV)
    c.waitall("sync", sV)
    c.dma("sync", ap3(invcw, 0, [[128, P], [1, 128]]), MISC[:, 0:128], sPF)
    c.dma("sync", ap3(invcw, P * 32 * 4, [[64, P], [1, 64]]), MISC[:, 0:64], sPF)
    c.waitall("scalar", sPF)

    ep_tt = {}
    mask_v = None
    tr_ev = {}
    ln_prev = None
    for half in range(2):
        tiles = list(range(half * 16, (half + 1) * 16))
        guards = None
        if ln_prev is not None:
            guards = {sl_: (sTb, ln_prev["tbv"][12 + sl_]) for sl_ in range(4)}
        ln = emit_ln(c, xsrc_fn, tiles, slot0=0, xt_guards=guards)
        ln_prev = ln
        for i in range(16):
            tid = tiles[i]
            bk = 6 + (i % 2)
            c.wait("tensor", sEv, ln["evv"][i])
            if ep_tt.get(tid - 2) is not None:
                c.wait("tensor", sEp, ep_tt[tid - 2])
            if half == 0 and i == 0:
                c.wait("tensor", sAc, brB_v)
            for dt in range(8):
                fn_ = lambda e, i=i, dt=dt, bk=bk: e.matmul(
                    out=c.bank(bk)[:, 0:4],
                    lhsT=TTB[:, i * 1024 + dt * P: i * 1024 + (dt + 1) * P],
                    rhs=WSM[:, 4 * dt:4 * dt + 4], start=(dt == 0), stop=(dt == 7))
                if dt == 7:
                    mmv = c.op("tensor", fn_, sMM)
                else:
                    c.op("tensor", fn_)
            gg = 16 + 8 * (i % 2)
            c.wait("vector", sMM, mmv)
            if half == 0 and i == 0:
                c.wait("vector", sAc, brB_v)
            ep_tt[tid] = c.op("vector", lambda e, gg=gg, bk=bk: e.tensor_tensor(
                out=STAT[:, gg:gg + 4], in0=c.bank(bk)[:, 0:4], in1=BCAST[:, 0:4],
                op=ALU.add), sEp)
            c.wait("vector", sEp, ep_tt[tid])
            c.opS("vector", lambda e, gg=gg: e.tensor_reduce(
                out=STAT[:, gg + 4:gg + 5], in_=STAT[:, gg:gg + 4], axis=AX.X, op=ALU.max))
            c.opS("vector", lambda e, gg=gg: e.tensor_scalar(
                out=STAT[:, gg:gg + 4], in0=STAT[:, gg:gg + 4],
                scalar1=STAT[:, gg + 4:gg + 5], scalar2=None, op0=ALU.is_ge))
            c.opS("vector", lambda e, t=tid, gg=gg: e.tensor_tensor_scan(
                out=MASKS[:, 4 * t:4 * t + 4], data0=STAT[:, gg:gg + 4],
                data1=STAT[:, gg:gg + 4], initial=STAT[:, 31:32], op0=ALU.add, op1=ALU.bypass))
            c.opS("vector", lambda e, t=tid, gg=gg: e.tensor_tensor(
                out=MASKS[:, 4 * t:4 * t + 4], in0=MASKS[:, 4 * t:4 * t + 4],
                in1=STAT[:, gg:gg + 4], op=ALU.is_equal))
            mask_v = c.op("vector", lambda e, t=tid, gg=gg: e.tensor_tensor(
                out=MASKS[:, 4 * t:4 * t + 4], in0=MASKS[:, 4 * t:4 * t + 4],
                in1=STAT[:, gg:gg + 4], op=ALU.mult), sEp)

    for t in range(NT):
        c.wait("tensor", sEp, mask_v if t == 0 else mask_v)
        if t >= 2:
            c.wait("tensor", sAc, tr_ev[t - 2])
        trv = c.op("tensor", lambda e, t=t: e.transpose(
            out=ap3(PS, (t % 2) * 512, [[4096, 4], [1, P]]),
            in_=MASKS[:, 4 * t:4 * t + 4], identity=IDENT[:, :]), sMM)
        c.wait("scalar", sMM, trv)
        tr_ev[t] = c.op("scalar", lambda e, t=t: e.copy(
            out=ap3(H1F, t * P, [[16384, 4], [1, P]]),
            in_=ap3(PS, (t % 2) * 512, [[4096, 4], [1, P]])), sAc)
    c.wait("vector", sAc, tr_ev[NT - 1])
    c.op("vector", lambda e: e.memset(GSC[:, 0:1], 0.0))
    sc_v = c.opS("vector", lambda e: e.tensor_tensor_scan(
        out=ap3(H1F, 0, [[16384, 4], [1, NTOK]]),
        data0=ap3(H1F, 0, [[16384, 4], [1, NTOK]]),
        data1=ap3(H1F, 0, [[16384, 4], [1, NTOK]]),
        initial=GSC[0:4, 0:1], op0=ALU.add, op1=ALU.bypass))
    cc_v = c.op("vector", lambda e: e.tensor_copy(
        out=GSC[0:4, 1:2], in_=ap3(H1F, NTOK - 1, [[16384, 4], [1, 1]])), sSt)
    c.wait("tensor", sSt, cc_v)
    trc = c.op("tensor", lambda e: e.transpose(
        out=ap3(PS, 0, [[4096, 1], [1, 4]]), in_=GSC[0:4, 1:2],
        identity=IDENT[0:4, 0:4]), sMM)
    c.wait("vector", sMM, trc)
    c.opS("vector", lambda e: e.tensor_scalar(
        out=GSC[0:1, 20:24], in0=ap3(PS, 0, [[4096, 1], [1, 4]]),
        scalar1=float(SL - 1), scalar2=1.0 / SL, op0=ALU.add, op1=ALU.mult))
    c.opS("vector", lambda e: e.tensor_copy(
        out=IVU[0:1, 0:4], in_=GSC[0:1, 20:24]))
    c.opS("vector", lambda e: e.tensor_copy(
        out=GSC[0:1, 8:12], in_=IVU[0:1, 0:4]))
    c.opS("vector", lambda e: e.tensor_scalar(
        out=GSC[0:1, 8:12], in0=GSC[0:1, 8:12], scalar1=float(SL), scalar2=None,
        op0=ALU.mult))
    c.op("vector", lambda e: e.memset(GSC[0:1, 12:13], 0.0))
    c.opS("vector", lambda e: e.tensor_tensor_scan(
        out=GSC[0:1, 16:20], data0=GSC[0:1, 8:12], data1=GSC[0:1, 8:12],
        initial=GSC[0:1, 12:13], op0=ALU.add, op1=ALU.bypass))
    of_v = c.op("vector", lambda e: e.tensor_tensor(
        out=GSC[0:1, 16:20], in0=GSC[0:1, 16:20], in1=GSC[0:1, 8:12], op=ALU.subtract), sSt)
    c.wait("vector", sSt, of_v)
    c.wait("tensor", sSt, of_v)
    obm = c.op("tensor", lambda e: e.matmul(
        out=c.bank(6)[:, 0:4], lhsT=ONESF[:, :], rhs=GSC[0:1, 16:20],
        start=True, stop=True), sMM)
    c.wait("scalar", sMM, obm)
    ofe = c.op("scalar", lambda e: e.copy(out=BCAST[:, 4:8], in_=c.bank(6)[:, 0:4]), sAc)

    sc_done = {}
    ps_rd = {}
    for t in range(NT):
        bkt = (t % 2)
        if t >= 2:
            c.wait("tensor", c.sem("sSelf_vector"), ps_rd[t - 2])
        if t == 0:
            c.wait("tensor", sAc, ofe)
        trv = c.op("tensor", lambda e, t=t, bkt=bkt: e.transpose(
            out=ap3(PS, bkt * 512, [[4096, P], [1, 4]]),
            in_=ap3(H1F, t * P, [[16384, 4], [1, P]]), identity=IDENT[0:4, 0:4]), sMM)
        c.wait("vector", sMM, trv)
        ps_rd[t] = c.opS("vector", lambda e, bkt=bkt: e.scalar_tensor_tensor(
            out=GSC[:, 32:36], in0=ap3(PS, bkt * 512, [[4096, P], [1, 4]]),
            scalar=-1.0, in1=BCAST[:, 4:8], op0=ALU.add, op1=ALU.add))
        c.opS("vector", lambda e, t=t: e.tensor_tensor(
            out=GSC[:, 32:36], in0=GSC[:, 32:36], in1=MASKS[:, 4 * t:4 * t + 4],
            op=ALU.mult))
        pv = c.op("vector", lambda e, t=t: e.tensor_reduce(
            out=PM4[:, t:t + 1], in_=GSC[:, 32:36], axis=AX.X, op=ALU.add), sEp)
        c.wait("vector", sEp, pv)
        c.opS("vector", lambda e, t=t: e.tensor_tensor(
            out=GSC[:, 36:40], in0=MASKS[:, 4 * t:4 * t + 4], in1=BCAST[:, 8:12],
            op=ALU.mult))
        c.opS("vector", lambda e: e.tensor_reduce(
            out=GSC[:, 40:41], in_=GSC[:, 36:40], axis=AX.X, op=ALU.add))
        if t >= 1:
            c.waitall("vector", sSC)
        c.opS("vector", lambda e, t=t: e.tensor_scalar(
            out=GSC[:, 44:45], in0=IVC[:, 8:9], scalar1=float(P * t), scalar2=None,
            op0=ALU.add))
        c.op("vector", lambda e: e.memset(GSC[:, 45:46], 1.0))
        c.opS("vector", lambda e: e.tensor_copy(out=GSC[:, 46:47], in_=GSC[:, 40:41]))
        c.op("vector", lambda e: e.memset(GSC[:, 47:48], 0.0))
        uv = c.op("vector", lambda e, t=t: e.tensor_copy(
            out=IVU[:, 2:3], in_=PM4[:, t:t + 1]), sV)
        sc_done[t] = uv
        c.wait("gpsimd", sV, uv)
        if t == 0:
            c.waitall("gpsimd", sPF)
        gsc_pay = bass.AP(GSC, 44, [[168, P], [1, 4]])
        c.op("gpsimd", c.dyn(lambda e, gsc_pay=gsc_pay: e.indirect_dma_start(
            out=invcw[:, :], out_offset=bass.IndirectOffsetOnAxis(ap=IVU[:, 2:3], axis=0),
            in_=gsc_pay, in_offset=None,
            bounds_check=NPAD - 1, oob_is_err=False)), sSC, 16)
    route_done = sSC.c
    route_ep_val = sEp.c

    prod_evac = {}
    prod_seq = 0
    wa_done = {}
    chunk_seq = 0
    wc_done = {}
    wc_seq = 0
    prev_ep_all = None
    prev_tbv = {sl_: ln_prev["tbv"][12 + sl_] for sl_ in range(4)}
    ivc_prev = {}
    if True:
        rE = c.es.enter_context(c.nc.gpsimd.register(c.ns + "rE"))
        rO1 = c.es.enter_context(c.nc.gpsimd.register(c.ns + "rO1"))
        rO2 = c.es.enter_context(c.nc.gpsimd.register(c.ns + "rO2"))
        rT = c.es.enter_context(c.nc.gpsimd.register(c.ns + "rT"))
        rV = c.es.enter_context(c.nc.gpsimd.register(c.ns + "rV"))
        for s in range(NSLOT):
            ivv = {}

            def loader(i, dst, s=s, ivv=ivv):
                st = i
                sem = sGX[st % 2]
                if s == 0 and st == 0:
                    c.waitall("sync", sPF)
                    c.waitall("sync", sSC)
                if s > 0 or st >= 2:
                    pass
                if ivc_prev.get(st % 2) is not None:
                    c.wait("sync", sV, ivc_prev[st % 2])
                c.dma("sync", IVC[:, 4 * (st % 2):4 * (st % 2) + 4],
                      ap3(invcw, (s * SL + st * P) * 4, [[4, P], [1, 4]]), sIV[st % 2])
                c.waitall("vector", sIV[st % 2])
                uv = c.op("vector", lambda e, st=st: e.tensor_copy(
                    out=IVU[:, st % 2:st % 2 + 1],
                    in_=IVC[:, 4 * (st % 2):4 * (st % 2) + 1]), sV)
                cwv = c.op("vector", lambda e, st=st: e.tensor_copy(
                    out=GSC[:, 48 + st:49 + st],
                    in_=IVC[:, 4 * (st % 2) + 1:4 * (st % 2) + 2]), sV)
                ivv[st] = cwv
                ivc_prev[st % 2] = cwv
                if st == 0:
                    eidv = c.op("vector", lambda e: e.tensor_copy(
                        out=IVC[:, 10:11], in_=IVC[:, 2:3]), sV)
                    tkv = c.op("vector", lambda e: e.tensor_copy(
                        out=IVC[:, 11:12], in_=IVC[:, 0:1]), sV)
                    ivv["eid"] = eidv
                    ivc_prev[0] = tkv
                c.wait("gpsimd", sV, uv)
                if prev_tbv is not None:
                    c.wait("gpsimd", sTb, prev_tbv[st])
                c.op("gpsimd", c.dyn(lambda e, st=st, dst=dst: e.indirect_dma_start(
                    out=dst, out_offset=None, in_=xfull,
                    in_offset=bass.IndirectOffsetOnAxis(ap=IVU[:, st % 2:st % 2 + 1], axis=0),
                    bounds_check=NTOK - 1, oob_is_err=False)), sem, 16)
                return sem

            gd = None
            if prev_tbv is not None:
                gd = {sl_: (sTb, prev_tbv[sl_]) for sl_ in range(4)}
            ln = emit_ln(c, None, [0, 1, 2, 3], slot0=0, xt_guards=gd, loader=loader)
            prev_tbv = ln["tbv"]
            c.wait("vector", sV, ivv["eid"])
            c.opS("vector", lambda e: e.tensor_scalar(
                out=IVC[:, 9:10], in0=IVC[:, 10:11], scalar1=0.0, scalar2=3.0,
                op0=ALU.max, op1=ALU.min))
            eci = c.op("vector", lambda e: e.tensor_copy(
                out=IVI[:, 0:1], in_=IVC[:, 9:10]), sV)
            c.opS("vector", lambda e: e.tensor_scalar(
                out=IVC[:, 11:12], in0=IVC[:, 11:12], scalar1=float(NTOK),
                scalar2=None, op0=ALU.is_lt))
            fci = c.op("vector", lambda e: e.tensor_copy(
                out=IVI[:, 1:2], in_=IVC[:, 11:12]), sV)
            c.wait("gpsimd", sV, eci)
            c.wait("gpsimd", sV, fci)
            c.op("gpsimd", lambda e: e.reg_load(rE, IVI[0:1, 0:1]))
            c.op("gpsimd", lambda e: e.reg_load(rV, IVI[0:1, 1:2]))
            c.op("gpsimd", lambda e: e.reg_mul(rO1, rE, 1024 * 4096))
            c.op("gpsimd", lambda e: e.reg_mul(rO2, rE, 4096 * 1024))
            DMY = (GSC[0:1, 58:59], GSC[0:1, 44:45])
            if prev_ep_all is not None:
                c.wait("gpsimd", sEp, prev_ep_all)
            c.op("gpsimd", lambda e: e.reg_mul(rT, rE, 4096))
            c.dmad_if("gpsimd", SMALLB[:, 0:4096],
                      bass.AP(be1f, rT, [[4096, 1], [1, 4096]]), sBE, rV, DMY)
            c.op("gpsimd", lambda e: e.reg_mul(rT, rE, 1024))
            c.dmad_if("gpsimd", SMALLB[:, 4096:5120],
                      bass.AP(be2f, rT, [[1024, 1], [1, 1024]]), sBE, rV, DMY)
            first_of_s = True
            for ch in range(8):
                slot = chunk_seq % 2
                if chunk_seq >= 2:
                    c.wait("gpsimd", sMM, wa_done[chunk_seq - 2])
                c.op("gpsimd", lambda e, ch=ch: e.reg_add(rT, rO1, 512 * ch))
                c.dmad_if("gpsimd",
                          WA[:, slot * 4096:(slot + 1) * 4096]
                          .rearrange("p (dt f) -> p dt f", f=512),
                          bass.AP(we1f, rT, [[4096, P], [P * 4096, 8], [1, 512]]),
                          sWA[slot], rV, DMY)
                c.waitall("tensor", sWA[slot])
                if ch == 0:
                    c.waitall("tensor", sBE)
                    if prev_ep_all is not None:
                        c.wait("tensor", sEp, prev_ep_all)
                for ft in range(4):
                    bk = 4 + (prod_seq % 2)
                    if prod_seq >= 2:
                        c.wait("tensor", sAc, prod_evac[prod_seq - 2])
                    c.op("tensor", lambda e, ch=ch, ft=ft, bk=bk: e.matmul(
                        out=c.bank(bk),
                        lhsT=SMALLB[0:1, 512 * ch + 128 * ft:512 * ch + 128 * (ft + 1)],
                        rhs=ONESB[0:1, 0:512], start=True, stop=False))
                    for dt in range(8):
                        rhs = ap3(TTB, dt * P, [[TTB_PITCH, P], [1024, 4], [1, P]])
                        if first_of_s and dt == 0:
                            c.wait("tensor", sEv, ln["evv"][3])
                            first_of_s = False
                        fn_ = lambda e, slot=slot, dt=dt, ft=ft, rhs=rhs, bk=bk: e.matmul(
                            out=c.bank(bk),
                            lhsT=WA[:, slot * 4096 + dt * 512 + 128 * ft:
                                    slot * 4096 + dt * 512 + 128 * (ft + 1)],
                            rhs=rhs, start=False, stop=(dt == 7))
                        if dt == 7:
                            mmv = c.op("tensor", fn_, sMM)
                        else:
                            c.op("tensor", fn_)
                    fi = ch * 4 + ft
                    c.wait("scalar", sMM, mmv)
                    prod_evac[prod_seq] = c.op("scalar",
                        lambda e, fi=fi, bk=bk: e.activation(
                            out=H1T[:, fi * 512:(fi + 1) * 512],
                            in_=c.bank(bk), func=AF.Relu), sAc)
                    prod_seq += 1
                wa_done[chunk_seq] = mmv
                chunk_seq += 1
            h1_done = prod_evac[prod_seq - 1]

            ep_bank = {}
            WCV = [H1T[:, 16384:18432], H1T[:, 18432:20480]]
            for oh in range(2):
                for fg in range(8):
                    wslot = wc_seq % 2
                    if wc_seq >= 2:
                        c.wait("gpsimd", sMM, wc_done[wc_seq - 2])
                    c.op("gpsimd", lambda e, fg=fg, oh=oh: e.reg_add(
                        rT, rO2, 512 * fg * 1024 + 512 * oh))
                    c.dmad_if("gpsimd", WCV[wslot].rearrange("p (ft f) -> p ft f", f=512),
                              bass.AP(we2f, rT, [[1024, P], [P * 1024, 4], [1, 512]]),
                              sWC[wslot], rV, DMY)
                    c.waitall("tensor", sWC[wslot])
                    if fg == 0:
                        c.wait("tensor", sAc, h1_done)
                        if s == 0 and oh == 0:
                            c.wait("tensor", sEp, route_ep_val)
                        if oh == 1:
                            for st in range(4):
                                c.wait("tensor", sEp, ep_bank[(st, 0)])
                        for st in range(4):
                            c.op("tensor", lambda e, st=st, oh=oh: e.matmul(
                                out=c.bank(2 * st + oh), lhsT=ONESB[0:1, 0:P],
                                rhs=SMALLB[0:1, 4096 + 512 * oh:4096 + 512 * (oh + 1)],
                                start=True, stop=False))
                    for ftl in range(4):
                        ft = 4 * fg + ftl
                        for st in range(4):
                            fn_ = lambda e, ft=ft, ftl=ftl, st=st, wslot=wslot, oh=oh: \
                                e.matmul(
                                    out=c.bank(2 * st + oh),
                                    lhsT=H1T[:, ft * 512 + st * P:ft * 512 + (st + 1) * P],
                                    rhs=WCV[wslot][:, ftl * 512:(ftl + 1) * 512],
                                    start=False, stop=(ft == 31))
                            if st == 3 and ftl == 3:
                                mmv = c.op("tensor", fn_, sMM)
                            else:
                                c.op("tensor", fn_)
                    wc_done[wc_seq] = mmv
                    wc_seq += 1
                c.wait("vector", sMM, mmv)
                OUTTb = OUTT.bitcast(BF16)
                for st in range(4):
                    if ivv.get(st) is not None:
                        c.wait("vector", sV, ivv[st])
                    if oh == 0 and st == 0:
                        c.waitall("vector", sYW)
                    ep_bank[(st, oh)] = c.op("vector",
                        lambda e, st=st, oh=oh, OUTTb=OUTTb: e.tensor_scalar(
                            out=ap3(OUTTb, 1024 * st + 512 * oh, [[4096, P], [1, 512]]),
                            in0=c.bank(2 * st + oh), scalar1=GSC[:, 48 + st:49 + st],
                            scalar2=None, op0=ALU.mult), sEp)
                if oh == 1:
                    for st in range(4):
                        c.wait("sync", sEp, ep_bank[(st, 1)])
                        c.wait("sync", sEp, ep_bank[(st, 0)])
                        c.dma("sync", ap3(ybuf, (s * SL + st * P) * D, [[D, P], [1, D]]),
                              ap3(OUTT.bitcast(BF16), 1024 * st, [[4096, P], [1, 1024]]), sYW)
            prev_ep_all = ep_bank[(3, 1)]

    c.waitall("gpsimd", sYW)
    out_v = {}
    for t in range(NT):
        sem = sYG
        uv = c.op("vector", lambda e, t=t: e.tensor_copy(
            out=IVU[:, 3:4], in_=PM4[:, t:t + 1]), sV)
        c.wait("gpsimd", sV, uv)
        if t >= 1:
            c.wait("gpsimd", sV, out_v[t - 1])
        c.op("gpsimd", c.dyn(lambda e, t=t: e.indirect_dma_start(
            out=ap3(OUTT.bitcast(BF16), 2048, [[4096, P], [1, 1024]]),
            out_offset=None, in_=ybuf[:, :],
            in_offset=bass.IndirectOffsetOnAxis(ap=IVU[:, 3:4], axis=0),
            bounds_check=NPAD - 1, oob_is_err=False)), sYG, 16)
        sl = (t % 4) * D
        c.wait("sync", sTb, sTb.c)
        sX = sXp[t % 4]
        xv = c.dma("sync", XT[:, sl:sl + D], xsrc_fn(t), sX)
        c.waitall("vector", sX)
        c.waitall("vector", sYG)
        c.waitall("vector", sOutp[0])
        vv = c.op("vector", lambda e, sl=sl: e.tensor_tensor(
            out=OUTT[:, 0:D], in0=XT[:, sl:sl + D],
            in1=ap3(OUTT.bitcast(BF16), 2048, [[4096, P], [1, 1024]]), op=ALU.add), sV)
        out_v[t] = vv
        c.wait("sync", sV, vv)
        c.dma("sync", xdst_fn(t), OUTT[:, 0:D], sOutp[0])
    c.waitall("sync", sOutp[0])
    c.waitall("sync", sSC)
    c.waitall("sync", sYW)
    c.waitall("sync", sYG)


def emit_l1ffn_sp(c, W, xsrc_fn, xdst_fn, xfull):
    nc = c.nc
    TTB, H1T, WA, WC, XT, STAT, LNS, TBF, OUTT, MISC, ACC = (c.bufs[k] for k in
        ("TTB", "H1T", "WA", "WC", "XT", "STAT", "LNS", "TBF", "OUTT", "MISC", "ACC"))
    MASKS, SMALLF, SMALLB, ONESB, ONESF, BCAST, IDENT, WSM2, GSC, CW = (c.bufs[k] for k in
        ("MASKS", "SMALLF", "SMALLB", "ONESB", "ONESF", "BCAST", "IDENT", "WSM2", "GSC", "CW"))
    _SSC = c.sbuf("SSC", [P, L], BF16)
    PM4 = _SSC.bitcast(F32)[:, 0:128]
    IVC = _SSC.bitcast(F32)[:, 128:140]
    IVU = _SSC.bitcast(U32)[:, 140:144]
    IVI = _SSC.bitcast(mybir.dt.int32)[:, 144:146]
    sMM, sEp, sAc, sV, sEv, sSq, sSt, sTb = (c.sem(k) for k in
        ("sMM", "sEp", "sAc", "sV", "sEv", "sSq", "sSt", "sTb"))
    sCst = c.sem("sCst")
    sWA = [c.sem("sWA0"), c.sem("sWA1")]
    sWC = [c.sem("sWC0"), c.sem("sWC1"), c.sem("sWC2"), c.sem("sWC3")]
    sXp = [c.sem("sX0"), c.sem("sX1"), c.sem("sX2"), c.sem("sX3")]
    sOutp = [c.sem("sOutA"), c.sem("sOutB")]
    sBE = c.sem("sBE")
    sIV = [c.sem("sKB0"), c.sem("sKB1")]
    sGX = [c.sem("sZQ0"), c.sem("sZQ1")]
    sSC = c.sem("sSCg")
    sYG = c.sem("sYGg")
    sPF = c.sem("sPF")
    sYW = c.sem("sYW")
    PS = c.bufs["PSALL"]
    TTB_PITCH = 16 * 8 * P
    SL = 512
    NE = 32
    NSLOT = 64
    NPAD = NSLOT * SL
    BIG = 60000.0
    H1F = H1T.bitcast(F32)
    AB = ACC.bitcast(BF16)

    invcw = nc.dram_tensor(c.ns + "l1f_ivc", [NPAD, 4], F32)
    ybuf = nc.dram_tensor(c.ns + "l1f_y", [NPAD, D], BF16)
    sybuf = nc.dram_tensor(c.ns + "l1f_sy", [NTOK, D], BF16)
    w1f = W["w1"]
    w3f = W["w3"]
    w2f = W["w2"]
    b1f = W["b1"]
    b3f = W["b3"]
    b2f = W["b2"]

    c.dma("gpsimd", WSM2[:, :].rearrange("p (dt e) -> p dt e", e=32),
          W["gw"].rearrange("(dt p) e -> p dt e", p=P), sCst)
    c.dma("gpsimd", SMALLF[0:1, 0:32], ap3(W["gb"], 0, [[32, 1], [1, 32]]), sCst)
    c.dma("sync", IVC[:, 8:9], W["cIOTAC"][:, :], c.sem("sC"))
    c.waitall("tensor", sCst)
    c.waitall("tensor", c.sem("sC"))
    v = c.op("tensor", lambda e: e.matmul(out=c.bank(5)[:, 0:32], lhsT=ONESF[:, :],
                                          rhs=SMALLF[0:1, 0:32], start=True, stop=True), sMM)
    c.wait("scalar", sMM, v)
    gbB_v = c.op("scalar", lambda e: e.copy(out=BCAST[:, 0:32], in_=c.bank(5)[:, 0:32]), sAc)
    c.op("vector", lambda e: e.memset(STAT[:, 31:32], 0.0), sV)
    for e32 in range(NE):
        c.op("vector", lambda e, e32=e32: e.memset(
            MASKS[:, 96 + e32:97 + e32], float(e32)), sV)
    c.op("vector", lambda e: e.memset(MISC[:, 0:128], 0.0), sV)
    pf_v = c.op("vector", lambda e: e.memset(
        ap3(MISC, 0, [[1024, P], [4, 32]]), 1.0e9), sV)
    c.wait("sync", sV, pf_v)
    for blk in range(NPAD // (P * 32)):
        c.dma("sync", ap3(invcw, blk * P * 32 * 4, [[128, P], [1, 128]]),
              MISC[:, 0:128], sPF)
    c.waitall("scalar", sPF)

    chunk_seq = 0
    wa_done = {}
    prod_seq = 0
    prod_evac = {}

    def sh_prod(half, gl, wsrc, bias_off, slot0, act, first_tw=None):
        nonlocal chunk_seq, prod_seq
        last_ev = None
        for ch in range(2):
            slot = chunk_seq % 2
            if chunk_seq >= 2:
                c.wait("gpsimd", sMM, wa_done[chunk_seq - 2])
            c.dma("gpsimd",
                  WA[:, slot * 4096:(slot + 1) * 4096]
                  .rearrange("p (dt f) -> p dt f", f=512),
                  wsrc[:, ch * 512:(ch + 1) * 512]
                  .rearrange("(dt p) f -> p dt f", p=P), sWA[slot])
            c.waitall("tensor", sWA[slot])
            if ch == 0:
                c.waitall("tensor", sBE)
                if first_tw is not None:
                    for w_ in first_tw:
                        c.wait("tensor", *w_)
            for ft in range(4):
                for th in range(2):
                    bk = 4 + (prod_seq % 2)
                    if prod_seq >= 2:
                        c.wait("tensor", sAc, prod_evac[prod_seq - 2])
                    c.op("tensor", lambda e, ch=ch, ft=ft, bk=bk, bias_off=bias_off: e.matmul(
                        out=c.bank(bk),
                        lhsT=SMALLB[0:1, bias_off + 512 * ch + 128 * ft:
                                    bias_off + 512 * ch + 128 * (ft + 1)],
                        rhs=ONESB[0:1, 0:512], start=True, stop=False))
                    for dt in range(8):
                        off = (gl * 8 + th * 4) * 1024 + dt * P
                        rhs = ap3(TTB, off, [[TTB_PITCH, P], [1024, 4], [1, P]])
                        fn_ = lambda e, slot=slot, dt=dt, ft=ft, rhs=rhs, bk=bk: e.matmul(
                            out=c.bank(bk),
                            lhsT=WA[:, slot * 4096 + dt * 512 + 128 * ft:
                                    slot * 4096 + dt * 512 + 128 * (ft + 1)],
                            rhs=rhs, start=False, stop=(dt == 7))
                        if dt == 7:
                            mmv = c.op("tensor", fn_, sMM)
                        else:
                            c.op("tensor", fn_)
                    fi = slot0 + ch * 4 + ft
                    c.wait("scalar", sMM, mmv)
                    prod_evac[prod_seq] = c.op("scalar",
                        lambda e, fi=fi, th=th, bk=bk, act=act: e.activation(
                            out=H1T[:, fi * 1024 + th * 512: fi * 1024 + th * 512 + 512],
                            in_=c.bank(bk), func=act), sAc)
                    last_ev = prod_evac[prod_seq]
                    prod_seq += 1
            wa_done[chunk_seq] = mmv
            chunk_seq += 1
        return last_ev

    sh_wc_seq = 0
    sh_wc_done = {}

    def sh_out(half, gl, hs_ready, ab_guard=None):
        nonlocal sh_wc_seq
        ep_bank = {}
        syv = {}
        for oh in range(2):
            for ft in range(8):
                slot = sh_wc_seq % 4
                if sh_wc_seq >= 4:
                    c.wait("gpsimd", sMM, sh_wc_done[sh_wc_seq - 4])
                c.dma("gpsimd", WC[:, slot * 512:(slot + 1) * 512],
                      W["sw2"][128 * ft:128 * (ft + 1), 512 * oh:512 * (oh + 1)],
                      sWC[slot])
                c.waitall("tensor", sWC[slot])
                if ft == 0:
                    c.wait("tensor", sAc, hs_ready)
                    if oh == 1:
                        for tau in range(8):
                            c.wait("tensor", sAc, ep_bank[(tau, 0)])
                    if ab_guard is not None and oh == 0:
                        for w_ in ab_guard:
                            c.wait("tensor", *w_)
                    for tau in range(8):
                        c.op("tensor", lambda e, tau=tau, oh=oh: e.matmul(
                            out=c.bank(tau), lhsT=ONESB[0:1, 0:P],
                            rhs=SMALLB[0:1, 4096 + 512 * oh:4096 + 512 * (oh + 1)],
                            start=True, stop=False))
                for tau in range(8):
                    fn_ = lambda e, ft=ft, tau=tau, slot=slot: e.matmul(
                        out=c.bank(tau),
                        lhsT=H1T[:, (24 + ft) * 1024 + tau * P:
                                 (24 + ft) * 1024 + (tau + 1) * P],
                        rhs=WC[:, slot * 512:(slot + 1) * 512],
                        start=False, stop=(ft == 7))
                    if tau == 7:
                        mmv = c.op("tensor", fn_, sMM)
                    else:
                        c.op("tensor", fn_)
                sh_wc_done[sh_wc_seq] = mmv
                sh_wc_seq += 1
            c.wait("scalar", sMM, mmv)
            for tau in range(8):
                tl = gl * 8 + tau
                ep_bank[(tau, oh)] = c.op("scalar",
                    lambda e, tau=tau, oh=oh, tl=tl: e.activation(
                        out=ap3(AB, tl * 1024 + oh * 512, [[16384, P], [1, 512]]),
                        in_=c.bank(tau), func=AF.Identity), sAc)
                if oh == 1:
                    tid = half * 16 + tl
                    c.wait("sync", sAc, ep_bank[(tau, 0)])
                    c.wait("sync", sAc, ep_bank[(tau, 1)])
                    syv[tau] = c.dma("sync", ap3(sybuf, tid * P * D, [[D, P], [1, D]]),
                                     ap3(AB, tl * 1024, [[16384, P], [1, 1024]]), sYW)
        return ep_bank[(7, 1)], syv

    ln_prev = None
    tr_ev = {}
    sh_ep_last = None
    for half in range(2):
        tiles = list(range(half * 16, (half + 1) * 16))
        guards = None
        if ln_prev is not None:
            guards = {sl_: (sTb, ln_prev["tbv"][12 + sl_]) for sl_ in range(4)}
        if sh_ep_last is not None:
            c.wait("tensor", sAc, sh_ep_last)
        ln = emit_ln(c, xsrc_fn, tiles, slot0=0, xt_guards=guards)
        ln_prev = ln
        um_v = {}
        for i in range(16):
            tid = tiles[i]
            bk = 6 + (i % 2)
            c.wait("tensor", sEv, ln["evv"][i])
            if i >= 2:
                c.wait("tensor", sEp, um_v[i - 2])
            elif sh_ep_last is not None:
                c.wait("tensor", sAc, sh_ep_last)
            if half == 0 and i == 0:
                c.wait("tensor", sAc, gbB_v)
            for dt in range(8):
                fn_ = lambda e, i=i, dt=dt, bk=bk: e.matmul(
                    out=c.bank(bk)[:, 0:32],
                    lhsT=TTB[:, i * 1024 + dt * P: i * 1024 + (dt + 1) * P],
                    rhs=WSM2[:, 32 * dt:32 * dt + 32], start=(dt == 0), stop=(dt == 7))
                if dt == 7:
                    mmv = c.op("tensor", fn_, sMM)
                else:
                    c.op("tensor", fn_)
            c.wait("vector", sMM, mmv)
            if half == 0 and i == 0:
                c.wait("vector", sAc, gbB_v)
            c.opS("vector", lambda e, bk=bk: e.tensor_tensor(
                out=GSC[:, 0:32], in0=c.bank(bk)[:, 0:32], in1=BCAST[:, 0:32], op=ALU.add))
            c.opS("vector", lambda e: e.tensor_reduce(
                out=GSC[:, 32:33], in_=GSC[:, 0:32], axis=AX.X, op=ALU.max))
            nm = c.op("vector", lambda e: e.tensor_scalar(
                out=GSC[:, 33:34], in0=GSC[:, 32:33], scalar1=-1.0, scalar2=None,
                op0=ALU.mult), sSt)
            c.wait("scalar", sSt, nm)
            ex = c.op("scalar", lambda e: e.activation(
                out=GSC[:, 64:96], in_=GSC[:, 0:32], func=AF.Exp,
                bias=GSC[:, 33:34], scale=1.0, accum_out=GSC[:, 34:35]), sSq)
            c.wait("vector", sSq, ex)
            c.opS("vector", lambda e: e.reciprocal(out=GSC[:, 35:36], in_=GSC[:, 34:35]))
            c.opS("vector", lambda e: e.tensor_scalar(
                out=GSC[:, 0:32], in0=GSC[:, 64:96], scalar1=GSC[:, 35:36],
                scalar2=None, op0=ALU.mult))
            pairs = [(0, 1), (0, 2), (0, 3), (1, 2), (1, 3), (2, 3)]
            for pi, (a, b) in enumerate(pairs):
                c.op("vector", lambda e, pi=pi, a=a, b=b: e.tensor_tensor(
                    out=GSC[:, 96 + 8 * pi:96 + 8 * (pi + 1)],
                    in0=ap3(GSC, a, [[168, P], [4, 8]]),
                    in1=ap3(GSC, b, [[168, P], [4, 8]]), op=ALU.add))
            c.opS("vector", lambda e: e.nop())
            c.opS("vector", lambda e: e.tensor_reduce(
                out=GSC[:, 144:152], in_=ap3(GSC, 96, [[168, P], [1, 8], [8, 6]]),
                axis=AX.X, op=ALU.max))
            c.op("vector", lambda e: e.memset(GSC[:, 152:160], 0.0))
            for gi in range(8):
                c.opS("vector", lambda e, gi=gi: e.tensor_scalar(
                    out=GSC[:, 160:168], in0=GSC[:, 144:152],
                    scalar1=GSC[:, 144 + gi:145 + gi], scalar2=None, op0=ALU.is_lt))
                c.opS("vector", lambda e: e.tensor_tensor(
                    out=GSC[:, 152:160], in0=GSC[:, 152:160], in1=GSC[:, 160:168],
                    op=ALU.add))
            c.opS("vector", lambda e: e.tensor_scalar(
                out=GSC[:, 152:160], in0=GSC[:, 152:160], scalar1=3.5, scalar2=None,
                op0=ALU.is_le))
            c.opS("vector", lambda e: e.tensor_scalar(
                out=GSC[:, 64:96], in0=GSC[:, 0:32], scalar1=BIG, scalar2=None, op0=ALU.add))
            c.opS("vector", lambda e: e.tensor_tensor(
                out=GSC[:, 64:96], in0=GSC[:, 64:96],
                in1=ap3(GSC, 152, [[168, P], [1, 8], [0, 4]]), op=ALU.mult))
            c.opS("vector", lambda e: e.tensor_scalar(
                out=GSC[:, 64:96], in0=GSC[:, 64:96], scalar1=-BIG, scalar2=None, op0=ALU.add))
            c.op("vector", lambda e: e.memset(GSC[:, 96:128], 0.0))
            for r in range(4):
                c.opS("vector", lambda e: e.tensor_reduce(
                    out=GSC[:, 36:37], in_=GSC[:, 64:96], axis=AX.X, op=ALU.max))
                c.opS("vector", lambda e: e.tensor_scalar(
                    out=MASKS[:, 0:32], in0=GSC[:, 64:96], scalar1=GSC[:, 36:37],
                    scalar2=None, op0=ALU.is_ge))
                c.opS("vector", lambda e: e.tensor_tensor_scan(
                    out=MASKS[:, 32:64], data0=MASKS[:, 0:32], data1=MASKS[:, 0:32],
                    initial=STAT[:, 31:32], op0=ALU.add, op1=ALU.bypass))
                c.opS("vector", lambda e: e.tensor_tensor(
                    out=MASKS[:, 32:64], in0=MASKS[:, 32:64], in1=MASKS[:, 0:32],
                    op=ALU.is_equal))
                c.opS("vector", lambda e: e.tensor_tensor(
                    out=MASKS[:, 32:64], in0=MASKS[:, 32:64], in1=MASKS[:, 0:32],
                    op=ALU.mult))
                c.opS("vector", lambda e: e.tensor_tensor(
                    out=GSC[:, 96:128], in0=GSC[:, 96:128], in1=MASKS[:, 32:64],
                    op=ALU.max))
                c.opS("vector", lambda e: e.tensor_tensor(
                    out=GSC[:, 128:160], in0=MASKS[:, 32:64], in1=MASKS[:, 96:128],
                    op=ALU.mult))
                c.op("vector", lambda e, tid=tid, r=r: e.tensor_reduce(
                    out=CW[:, 8 * tid + r:8 * tid + r + 1],
                    in_=GSC[:, 128:160], axis=AX.X, op=ALU.add), sEp)
                c.opS("vector", lambda e: e.tensor_tensor(
                    out=GSC[:, 128:160], in0=MASKS[:, 32:64], in1=GSC[:, 0:32],
                    op=ALU.mult))
                c.op("vector", lambda e, tid=tid, r=r: e.tensor_reduce(
                    out=CW[:, 8 * tid + 4 + r:8 * tid + 5 + r],
                    in_=GSC[:, 128:160], axis=AX.X, op=ALU.add), sEp)
                c.opS("vector", lambda e: e.scalar_tensor_tensor(
                    out=GSC[:, 64:96], in0=MASKS[:, 32:64], scalar=-2 * BIG,
                    in1=GSC[:, 64:96], op0=ALU.mult, op1=ALU.add))
            um_v[i] = c.op("vector", lambda e: e.nop(), sEp)
            bkt = 4 + (i % 2)
            c.wait("tensor", sEp, um_v[i])
            if tr_ev.get(tid - 2) is not None:
                c.wait("tensor", sAc, tr_ev[tid - 2])
            trv = c.op("tensor", lambda e, bkt=bkt: e.transpose(
                out=ap3(PS, bkt * 512, [[4096, 32], [1, P]]),
                in_=GSC[:, 96:128], identity=IDENT[:, :]), sMM)
            c.wait("scalar", sMM, trv)
            tr_ev[tid] = c.op("scalar", lambda e, tid=tid, bkt=bkt: e.copy(
                out=ap3(H1F, tid * P, [[16384, 32], [1, P]]),
                in_=ap3(PS, bkt * 512, [[4096, 32], [1, P]])), sAc)

        c.wait("gpsimd", sMM, mmv)
        c.dma("gpsimd", SMALLB[:, 0:1024], ap3(W["sb1"], 0, [[1024, 1], [1, 1024]]), sBE)
        c.dma("gpsimd", SMALLB[:, 1024:2048], ap3(W["sb3"], 0, [[1024, 1], [1, 1024]]), sBE)
        c.dma("gpsimd", SMALLB[:, 4096:5120], ap3(W["sb2"], 0, [[1024, 1], [1, 1024]]), sBE)
        for gl in range(2):
            ftw = [(sAc, tr_ev[half * 16 + 15])]
            if sh_ep_last is not None:
                ftw.append((sAc, sh_ep_last))
            sh_prod(half, gl, W["sw1"], 0, 8, AF.Silu, first_tw=ftw)
            ev3 = sh_prod(half, gl, W["sw3"], 1024, 16, AF.Identity)
            c.wait("vector", sAc, ev3)
            hsv = None
            for fi in range(8):
                hsv = c.op("vector", lambda e, fi=fi: e.tensor_tensor(
                    out=H1T[:, (24 + fi) * 1024:(25 + fi) * 1024],
                    in0=H1T[:, (8 + fi) * 1024:(9 + fi) * 1024],
                    in1=H1T[:, (16 + fi) * 1024:(17 + fi) * 1024], op=ALU.mult), sV)
            c.wait("scalar", sV, hsv)
            hs_ready = c.op("scalar", lambda e: e.copy(
                out=MISC[:, 128:129], in_=STAT[:, 31:32]), sAc)
            if half == 1 and gl == 0:
                c.waitall("scalar", sYW)
            sh_ep_last, _syv = sh_out(half, gl, hs_ready)

    c.wait("vector", sAc, tr_ev[NT - 1])
    c.op("vector", lambda e: e.memset(GSC[0:NE, 0:1], 0.0))
    sc_v = c.opS("vector", lambda e: e.tensor_tensor_scan(
        out=ap3(H1F, 0, [[16384, NE], [1, NTOK]]),
        data0=ap3(H1F, 0, [[16384, NE], [1, NTOK]]),
        data1=ap3(H1F, 0, [[16384, NE], [1, NTOK]]),
        initial=GSC[0:NE, 0:1], op0=ALU.add, op1=ALU.bypass))
    cc_v = c.op("vector", lambda e: e.tensor_copy(
        out=GSC[0:NE, 1:2], in_=ap3(H1F, NTOK - 1, [[16384, NE], [1, 1]])), sSt)
    c.wait("tensor", sSt, cc_v)
    c.wait("tensor", sAc, sh_ep_last)
    trc = c.op("tensor", lambda e: e.transpose(
        out=ap3(PS, 0, [[4096, 1], [1, NE]]), in_=GSC[0:NE, 1:2],
        identity=IDENT[0:NE, 0:NE]), sMM)
    c.wait("vector", sMM, trc)
    c.opS("vector", lambda e: e.tensor_scalar(
        out=GSC[0:1, 64:96], in0=ap3(PS, 0, [[4096, 1], [1, NE]]),
        scalar1=float(SL - 1), scalar2=1.0 / SL, op0=ALU.add, op1=ALU.mult))
    c.opS("vector", lambda e: e.tensor_copy(
        out=MASKS.bitcast(U32)[0:1, 0:32], in_=GSC[0:1, 64:96]))
    c.opS("vector", lambda e: e.tensor_copy(
        out=GSC[0:1, 64:96], in_=MASKS.bitcast(U32)[0:1, 0:32]))
    c.opS("vector", lambda e: e.tensor_scalar(
        out=GSC[0:1, 64:96], in0=GSC[0:1, 64:96], scalar1=float(SL), scalar2=None,
        op0=ALU.mult))
    c.op("vector", lambda e: e.memset(GSC[0:1, 63:64], 0.0))
    c.opS("vector", lambda e: e.tensor_tensor_scan(
        out=GSC[0:1, 96:128], data0=GSC[0:1, 64:96], data1=GSC[0:1, 64:96],
        initial=GSC[0:1, 63:64], op0=ALU.add, op1=ALU.bypass))
    of_v = c.op("vector", lambda e: e.tensor_tensor(
        out=GSC[0:1, 96:128], in0=GSC[0:1, 96:128], in1=GSC[0:1, 64:96],
        op=ALU.subtract), sSt)
    c.wait("tensor", sSt, of_v)
    obm = c.op("tensor", lambda e: e.matmul(
        out=c.bank(6)[:, 0:NE], lhsT=ONESF[:, :], rhs=GSC[0:1, 96:128],
        start=True, stop=True), sMM)
    c.wait("scalar", sMM, obm)
    ofe = c.op("scalar", lambda e: e.copy(out=BCAST[:, 0:NE], in_=c.bank(6)[:, 0:NE]), sAc)

    ps_rd = {}
    sc_vals = []
    for t in range(NT):
        bkt = (t % 2)
        if t >= 2:
            c.wait("tensor", c.sem("sSelf_vector"), ps_rd[t - 2])
        if t == 0:
            c.wait("tensor", sAc, ofe)
        trv = c.op("tensor", lambda e, t=t, bkt=bkt: e.transpose(
            out=ap3(PS, bkt * 512, [[4096, P], [1, NE]]),
            in_=ap3(H1F, t * P, [[16384, NE], [1, P]]),
            identity=IDENT[0:NE, 0:NE]), sMM)
        c.wait("vector", sMM, trv)
        ps_rd[t] = c.opS("vector", lambda e, bkt=bkt: e.scalar_tensor_tensor(
            out=GSC[:, 96:128], in0=ap3(PS, bkt * 512, [[4096, P], [1, NE]]),
            scalar=-1.0, in1=BCAST[:, 0:NE], op0=ALU.add, op1=ALU.add))
        c.op("vector", lambda e: e.memset(GSC[:, 47:48], 0.0))
        c.op("vector", lambda e: e.memset(GSC[:, 51:52], 0.0))
        for r in range(4):
            pcol = 44 if r % 2 == 0 else 48
            icol = 2 if r % 2 == 0 else 3
            c.opS("vector", lambda e, t=t, r=r: e.tensor_scalar(
                out=GSC[:, 64:96], in0=MASKS[:, 96:128],
                scalar1=CW[:, 8 * t + r:8 * t + r + 1],
                scalar2=None, op0=ALU.is_equal))
            c.opS("vector", lambda e: e.tensor_tensor(
                out=GSC[:, 64:96], in0=GSC[:, 64:96], in1=GSC[:, 96:128],
                op=ALU.mult))
            pv = c.op("vector", lambda e, t=t, r=r: e.tensor_reduce(
                out=PM4[:, 4 * t + r:4 * t + r + 1], in_=GSC[:, 64:96],
                axis=AX.X, op=ALU.add), sEp)
            c.wait("vector", sEp, pv)
            if len(sc_vals) >= 2:
                c.wait("vector", sSC, sc_vals[-2])
            c.opS("vector", lambda e, t=t, pcol=pcol: e.tensor_scalar(
                out=GSC[:, pcol:pcol + 1], in0=IVC[:, 8:9], scalar1=float(P * t),
                scalar2=None, op0=ALU.add))
            c.opS("vector", lambda e, t=t, r=r, pcol=pcol: e.tensor_copy(
                out=GSC[:, pcol + 1:pcol + 2],
                in_=CW[:, 8 * t + 4 + r:8 * t + 5 + r]))
            c.opS("vector", lambda e, t=t, r=r, pcol=pcol: e.tensor_copy(
                out=GSC[:, pcol + 2:pcol + 3],
                in_=CW[:, 8 * t + r:8 * t + r + 1]))
            uv = c.op("vector", lambda e, t=t, r=r, icol=icol: e.tensor_copy(
                out=IVU[:, icol:icol + 1], in_=PM4[:, 4 * t + r:4 * t + r + 1]), sV)
            c.wait("gpsimd", sV, uv)
            if t == 0 and r == 0:
                c.waitall("gpsimd", sPF)
            gsc_pay = bass.AP(GSC, pcol, [[168, P], [1, 4]])
            c.op("gpsimd", c.dyn(lambda e, gsc_pay=gsc_pay, icol=icol: e.indirect_dma_start(
                out=invcw[:, :],
                out_offset=bass.IndirectOffsetOnAxis(ap=IVU[:, icol:icol + 1], axis=0),
                in_=gsc_pay, in_offset=None,
                bounds_check=NPAD - 1, oob_is_err=False)), sSC, 16)
            sc_vals.append(sSC.c)
    route_ep_val = sEp.c
    route_mm_val = sMM.c

    prod_evac = {}
    prod_seq = 0
    wa_done = {}
    chunk_seq = 0
    wc_done = {}
    wc_seq = 0
    prev_ep_all = None
    prev_tbv = {sl_: ln_prev["tbv"][12 + sl_] for sl_ in range(4)}
    ivc_prev = {}
    if True:
        rE = c.es.enter_context(c.nc.gpsimd.register(c.ns + "rE1"))
        rO1 = c.es.enter_context(c.nc.gpsimd.register(c.ns + "rO1x"))
        rT = c.es.enter_context(c.nc.gpsimd.register(c.ns + "rT1"))
        rV = c.es.enter_context(c.nc.gpsimd.register(c.ns + "rV1"))
        for s in range(NSLOT):
            ivv = {}

            def loader(i, dst, s=s, ivv=ivv):
                st = i
                sem = sGX[st % 2]
                if s == 0 and st == 0:
                    c.waitall("sync", sPF)
                    c.waitall("sync", sSC)
                    c.waitall("vector", sSC)
                if ivc_prev.get(st % 2) is not None:
                    c.wait("sync", sV, ivc_prev[st % 2])
                c.dma("sync", IVC[:, 4 * (st % 2):4 * (st % 2) + 4],
                      ap3(invcw, (s * SL + st * P) * 4, [[4, P], [1, 4]]), sIV[st % 2])
                c.waitall("vector", sIV[st % 2])
                uv = c.op("vector", lambda e, st=st: e.tensor_copy(
                    out=IVU[:, st % 2:st % 2 + 1],
                    in_=IVC[:, 4 * (st % 2):4 * (st % 2) + 1]), sV)
                cwv = c.op("vector", lambda e, st=st: e.tensor_copy(
                    out=GSC[:, 48 + st:49 + st],
                    in_=IVC[:, 4 * (st % 2) + 1:4 * (st % 2) + 2]), sV)
                ivv[st] = cwv
                ivc_prev[st % 2] = cwv
                if st == 0:
                    eidv = c.op("vector", lambda e: e.tensor_copy(
                        out=IVC[:, 10:11], in_=IVC[:, 2:3]), sV)
                    tkv = c.op("vector", lambda e: e.tensor_copy(
                        out=IVC[:, 11:12], in_=IVC[:, 0:1]), sV)
                    ivv["eid"] = eidv
                    ivc_prev[0] = tkv
                c.wait("gpsimd", sV, uv)
                if prev_tbv is not None:
                    c.wait("gpsimd", sTb, prev_tbv[st])
                c.op("gpsimd", c.dyn(lambda e, st=st, dst=dst: e.indirect_dma_start(
                    out=dst, out_offset=None, in_=xfull,
                    in_offset=bass.IndirectOffsetOnAxis(ap=IVU[:, st % 2:st % 2 + 1], axis=0),
                    bounds_check=NTOK - 1, oob_is_err=False)), sem, 16)
                return sem

            gd = {sl_: (sTb, prev_tbv[sl_]) for sl_ in range(4)}
            ln = emit_ln(c, None, [0, 1, 2, 3], slot0=0, xt_guards=gd, loader=loader)
            prev_tbv = ln["tbv"]
            c.wait("vector", sV, ivv["eid"])
            c.opS("vector", lambda e: e.tensor_scalar(
                out=IVC[:, 9:10], in0=IVC[:, 10:11], scalar1=0.0, scalar2=float(NE - 1),
                op0=ALU.max, op1=ALU.min))
            eci = c.op("vector", lambda e: e.tensor_copy(
                out=IVI[:, 0:1], in_=IVC[:, 9:10]), sV)
            c.opS("vector", lambda e: e.tensor_scalar(
                out=IVC[:, 11:12], in0=IVC[:, 11:12], scalar1=float(NTOK),
                scalar2=None, op0=ALU.is_lt))
            if s > 0:
                c.opS("vector", lambda e: e.tensor_scalar(
                    out=GSC[:, 62:63], in0=IVC[:, 10:11], scalar1=GSC[:, 61:62],
                    scalar2=None, op0=ALU.not_equal))
                c.opS("vector", lambda e: e.tensor_tensor(
                    out=IVC[:, 11:12], in0=IVC[:, 11:12], in1=GSC[:, 62:63],
                    op=ALU.mult))
            fci = c.op("vector", lambda e: e.tensor_copy(
                out=IVI[:, 1:2], in_=IVC[:, 11:12]), sV)
            c.op("vector", lambda e: e.tensor_copy(
                out=GSC[:, 61:62], in_=IVC[:, 10:11]), sV)
            c.wait("gpsimd", sV, eci)
            c.wait("gpsimd", sV, fci)
            c.op("gpsimd", lambda e: e.reg_load(rE, IVI[0:1, 0:1]))
            c.op("gpsimd", lambda e: e.reg_load(rV, IVI[0:1, 1:2]))
            c.op("gpsimd", lambda e: e.reg_mul(rO1, rE, 1024 * 512))
            DMY = (GSC[0:1, 58:59], GSC[0:1, 44:45])
            if prev_ep_all is not None:
                c.wait("gpsimd", sEp, prev_ep_all)
            elif s == 0:
                c.wait("gpsimd", sEp, route_ep_val)
            c.op("gpsimd", lambda e: e.reg_mul(rT, rE, 512))
            c.dmad_if("gpsimd", SMALLB[:, 0:512],
                      bass.AP(b1f, rT, [[512, 1], [1, 512]]), sBE, rV, DMY)
            c.dmad_if("gpsimd", SMALLB[:, 512:1024],
                      bass.AP(b3f, rT, [[512, 1], [1, 512]]), sBE, rV, DMY)
            c.op("gpsimd", lambda e: e.reg_mul(rT, rE, 1024))
            c.dmad_if("gpsimd", SMALLB[:, 4096:5120],
                      bass.AP(b2f, rT, [[1024, 1], [1, 1024]]), sBE, rV, DMY)
            first_of_s = True
            hv = {}
            for wsel in range(2):
                wf = w1f if wsel == 0 else w3f
                slot = chunk_seq % 2
                if chunk_seq >= 2:
                    c.wait("gpsimd", sMM, wa_done[chunk_seq - 2])
                c.op("gpsimd", lambda e: e.reg_add(rT, rO1, 0))
                c.dmad_if("gpsimd",
                          WA[:, slot * 4096:(slot + 1) * 4096]
                          .rearrange("p (dt f) -> p dt f", f=512),
                          bass.AP(wf, rT, [[512, P], [P * 512, 8], [1, 512]]),
                          sWA[slot], rV, DMY)
                c.waitall("tensor", sWA[slot])
                if wsel == 0:
                    c.waitall("tensor", sBE)
                    if prev_ep_all is not None:
                        c.wait("tensor", sEp, prev_ep_all)
                for ft in range(4):
                    bk = 4 + (prod_seq % 2)
                    if prod_seq >= 2:
                        c.wait("tensor", sAc, prod_evac[prod_seq - 2])
                    c.op("tensor", lambda e, ft=ft, bk=bk, wsel=wsel: e.matmul(
                        out=c.bank(bk),
                        lhsT=SMALLB[0:1, wsel * 512 + 128 * ft:wsel * 512 + 128 * (ft + 1)],
                        rhs=ONESB[0:1, 0:512], start=True, stop=False))
                    for dt in range(8):
                        rhs = ap3(TTB, dt * P, [[TTB_PITCH, P], [1024, 4], [1, P]])
                        if first_of_s and dt == 0:
                            c.wait("tensor", sEv, ln["evv"][3])
                            first_of_s = False
                        fn_ = lambda e, slot=slot, dt=dt, ft=ft, rhs=rhs, bk=bk: e.matmul(
                            out=c.bank(bk),
                            lhsT=WA[:, slot * 4096 + dt * 512 + 128 * ft:
                                    slot * 4096 + dt * 512 + 128 * (ft + 1)],
                            rhs=rhs, start=False, stop=(dt == 7))
                        if dt == 7:
                            mmv = c.op("tensor", fn_, sMM)
                        else:
                            c.op("tensor", fn_)
                    fi = wsel * 4 + ft
                    c.wait("scalar", sMM, mmv)
                    act = AF.Silu if wsel == 0 else AF.Identity
                    prod_evac[prod_seq] = c.op("scalar",
                        lambda e, fi=fi, bk=bk, act=act: e.activation(
                            out=H1T[:, fi * 512:(fi + 1) * 512],
                            in_=c.bank(bk), func=act), sAc)
                    prod_seq += 1
                wa_done[chunk_seq] = mmv
                chunk_seq += 1
            c.wait("vector", sAc, prod_evac[prod_seq - 1])
            hsv = None
            for fi in range(4):
                hsv = c.op("vector", lambda e, fi=fi: e.tensor_tensor(
                    out=H1T[:, (8 + fi) * 512:(9 + fi) * 512],
                    in0=H1T[:, fi * 512:(fi + 1) * 512],
                    in1=H1T[:, (4 + fi) * 512:(5 + fi) * 512], op=ALU.mult), sV)
            ep_bank = {}
            WCV = [WC[:, 0:2048], H1T[:, 6144:8192]]
            for oh in range(2):
                if wc_seq >= 2:
                    c.wait("gpsimd", sMM, wc_done[wc_seq - 2])
                if s == 0:
                    c.wait("gpsimd", sEp, route_ep_val)
                c.op("gpsimd", lambda e, oh=oh: e.reg_add(rT, rO1, 512 * oh))
                c.dmad_if("gpsimd",
                          WCV[oh].rearrange("p (ft f) -> p ft f", f=512),
                          bass.AP(w2f, rT, [[1024, P], [P * 1024, 4], [1, 512]]),
                          sWC[oh], rV, DMY)
                c.waitall("tensor", sWC[oh])
                c.wait("tensor", sV, hsv)
                if s == 0 and oh == 0:
                    c.wait("tensor", sEp, route_ep_val)
                    c.wait("tensor", sMM, route_mm_val)
                if oh == 1:
                    for st in range(4):
                        c.wait("tensor", sEp, ep_bank[(st, 0)])
                for st in range(4):
                    c.op("tensor", lambda e, st=st, oh=oh: e.matmul(
                        out=c.bank(2 * st + oh), lhsT=ONESB[0:1, 0:P],
                        rhs=SMALLB[0:1, 4096 + 512 * oh:4096 + 512 * (oh + 1)],
                        start=True, stop=False))
                for ft in range(4):
                    for st in range(4):
                        fn_ = lambda e, ft=ft, st=st, oh=oh: e.matmul(
                            out=c.bank(2 * st + oh),
                            lhsT=H1T[:, (8 + ft) * 512 + st * P:(8 + ft) * 512 + (st + 1) * P],
                            rhs=WCV[oh][:, ft * 512:(ft + 1) * 512],
                            start=False, stop=(ft == 3))
                        if ft == 3 and st == 3:
                            mmv = c.op("tensor", fn_, sMM)
                        else:
                            c.op("tensor", fn_)
                wc_done[wc_seq] = mmv
                wc_seq += 1
                c.wait("vector", sMM, mmv)
                OUTTb = OUTT.bitcast(BF16)
                for st in range(4):
                    if ivv.get(st) is not None:
                        c.wait("vector", sV, ivv[st])
                    if oh == 0 and st == 0:
                        c.waitall("vector", sYW)
                    ep_bank[(st, oh)] = c.op("vector",
                        lambda e, st=st, oh=oh, OUTTb=OUTTb: e.tensor_scalar(
                            out=ap3(OUTTb, 1024 * st + 512 * oh, [[4096, P], [1, 512]]),
                            in0=c.bank(2 * st + oh), scalar1=GSC[:, 48 + st:49 + st],
                            scalar2=None, op0=ALU.mult), sEp)
                if oh == 1:
                    for st in range(4):
                        c.wait("sync", sEp, ep_bank[(st, 1)])
                        c.wait("sync", sEp, ep_bank[(st, 0)])
                        c.dma("sync", ap3(ybuf, (s * SL + st * P) * D, [[D, P], [1, D]]),
                              ap3(OUTT.bitcast(BF16), 1024 * st, [[4096, P], [1, 1024]]), sYW)
            prev_ep_all = ep_bank[(3, 1)]

    c.waitall("gpsimd", sYW)
    c.waitall("gpsimd", sMM)
    out_v = {}
    H1Tb_off = 28 * 1024
    for t in range(NT):
        uvl = {}
        for r in range(4):
            uvl[r] = c.op("vector", lambda e, t=t, r=r: e.tensor_copy(
                out=IVU[:, r:r + 1], in_=PM4[:, 4 * t + r:4 * t + r + 1]), sV)
        if t >= 1:
            c.wait("gpsimd", sV, out_v[t - 1])
        for r in range(4):
            c.wait("gpsimd", sV, uvl[r])
            c.op("gpsimd", c.dyn(lambda e, t=t, r=r: e.indirect_dma_start(
                out=ap3(H1T, H1Tb_off + r * 1024, [[32768, P], [1, 1024]]),
                out_offset=None, in_=ybuf[:, :],
                in_offset=bass.IndirectOffsetOnAxis(ap=IVU[:, r:r + 1], axis=0),
                bounds_check=NPAD - 1, oob_is_err=False)), sYG, 16)
        sl = (t % 4) * D
        c.wait("sync", sTb, sTb.c)
        sX = sXp[t % 4]
        xv = c.dma("sync", XT[:, sl:sl + D], xsrc_fn(t), sX)
        if t >= 1:
            c.wait("sync", sV, out_v[t - 1])
        c.dma("sync", ap3(OUTT.bitcast(BF16), 2048, [[4096, P], [1, 1024]]),
              ap3(sybuf, t * P * D, [[D, P], [1, D]]), sOutp[1])
        c.waitall("vector", sX)
        c.waitall("vector", sYG)
        c.waitall("vector", sOutp[1])
        c.waitall("vector", sOutp[0])
        c.opS("vector", lambda e, sl=sl: e.tensor_tensor(
            out=OUTT[:, 0:D], in0=XT[:, sl:sl + D],
            in1=ap3(OUTT.bitcast(BF16), 2048, [[4096, P], [1, 1024]]), op=ALU.add))
        for r in range(4):
            fn_ = lambda e, r=r: e.tensor_tensor(
                out=OUTT[:, 0:D], in0=OUTT[:, 0:D],
                in1=ap3(H1T, H1Tb_off + r * 1024, [[32768, P], [1, 1024]]), op=ALU.add)
            if r == 3:
                vv = c.op("vector", fn_, sV)
            else:
                c.opS("vector", fn_)
        out_v[t] = vv
        c.wait("sync", sV, vv)
        c.dma("sync", xdst_fn(t), OUTT[:, 0:D], sOutp[0])
    c.waitall("sync", sOutp[0])
    c.waitall("sync", sSC)
    c.waitall("sync", sYW)
    c.waitall("sync", sYG)
    c.waitall("sync", sPF)


def host_consts():
    return {"cIDENT": np.eye(P, dtype=np.float32)}


def rope_tables():
    import jax
    cpu = jax.devices("cpu")[0]
    with jax.default_device(cpu):
        import jax.numpy as jnp
        half = 64
        pos = jnp.arange(L, dtype=jnp.float32)[:, None]
        inv_freq = 10000.0 ** (jnp.arange(half, dtype=jnp.float32) / half)
        sinusoid = pos[..., None] * inv_freq[None, None, :]
        sin = np.asarray(jnp.sin(sinusoid))[:, 0, :]
        cos = np.asarray(jnp.cos(sinusoid))[:, 0, :]
    return np.ascontiguousarray(cos.T), np.ascontiguousarray(sin.T)


def derived_l0aa(inp):
    uv_w = inp["L0_attn_a_uv_w"]
    o_w = inp["L0_attn_a_o_w"]
    w_rel = inp["L0_attn_a_w_rel"]
    gamma = inp["L0_attn_a_gamma"]
    beta = inp["L0_attn_a_beta"]
    idx = np.clip(np.arange(L)[:, None] - np.arange(L)[None, :] + (L - 1), 0, 2 * L - 2)
    idxT = np.clip(np.arange(L)[:, None] - np.arange(L)[None, :] + (L - 1), 0, w_rel.shape[0] - 1)
    biasT = np.ascontiguousarray(w_rel[idxT]).astype(np.float32)
    cos, sin = rope_tables()
    cGB = np.ascontiguousarray(np.stack([gamma[0], beta[0], gamma[1], beta[1]], axis=1))
    return {
        "cTuv": np.ascontiguousarray(uv_w.T),
        "cTow": np.ascontiguousarray(o_w.T),
        "cBIAST": biasT,
        "cGB": cGB.astype(np.float32),
        "cCCS": np.ascontiguousarray(np.vstack([cos, sin])),
        "cSSC": np.ascontiguousarray(np.vstack([sin, cos])),
    }


def derived_l1aa(inp):
    lt = (np.arange(P)[:, None] <= np.arange(P)[None, :]).astype(np.float32)
    return {
        "cTq": np.ascontiguousarray(inp["L1_attn_a_Wq"].T),
        "cTk": np.ascontiguousarray(inp["L1_attn_a_Wk"].T),
        "cTv": np.ascontiguousarray(inp["L1_attn_a_Wv"].T),
        "cTo": np.ascontiguousarray(inp["L1_attn_a_Wo"].T),
        "cLT": lt,
    }


def derived_l1ab(inp):
    return {
        "cTq": np.ascontiguousarray(inp["L1_attn_b_Wq"].T),
        "cTk": np.ascontiguousarray(inp["L1_attn_b_Wk"].T),
        "cTv": np.ascontiguousarray(inp["L1_attn_b_Wv"].T),
        "cTff": np.ascontiguousarray(inp["L1_attn_b_Wff"].T),
    }


def derived_l0ab(inp):
    qkv_w = inp["L0_attn_b_qkv_w"]
    off_w = inp["L0_attn_b_off_w"]
    j = (np.arange(16)[:, None] * P + np.arange(P)[None, :]).reshape(-1)
    cst = np.zeros((16, P, 2048), np.float32)
    for i in range(16):
        itok = i * P + np.arange(P)
        jj = j.reshape(16, P)
        cm = jj.T[:, :, None] - itok[None, None, :]
        cm = np.transpose(jj.T[:, :, None] - itok[None, None, :], (0, 1, 2))
        cm = cm.reshape(P, 2048, order="C") if False else cm.reshape(P, 16 * P)
        cst[i] = np.where(cm < 0, 30000.0, cm)
    return {
        "cTqkv": np.ascontiguousarray(qkv_w.T),
        "cToff": np.ascontiguousarray(off_w.T),
        "cCST": cst.astype(np.float16),
    }


def derived_iotac(inp):
    return {"cIOTAC": np.arange(P, dtype=np.float32).reshape(P, 1)}


SUB_DERIVED = {"l0aa": derived_l0aa, "l0ab": derived_l0ab, "l1ab": derived_l1ab,
               "l1aa": derived_l1aa, "l0ffn_sp": derived_iotac,
               "l1ffn_sp": derived_iotac}


SUB_WEIGHTS = {
    "l0ffn": [("Wr", "L0_ffn_Wr"), ("br", "L0_ffn_br"), ("We1", "L0_ffn_We1"),
              ("be1", "L0_ffn_be1"), ("We2", "L0_ffn_We2"), ("be2", "L0_ffn_be2")],
    "l0ffn_sp": [("Wr", "L0_ffn_Wr"), ("br", "L0_ffn_br"), ("We1", "L0_ffn_We1"),
                 ("be1", "L0_ffn_be1"), ("We2", "L0_ffn_We2"), ("be2", "L0_ffn_be2")],
    "l0aa": [("g", "L0_attn_a_g"), ("res_scale", "L0_attn_a_res_scale")],
    "l0ab": [("qkv_b", "L0_attn_b_qkv_b"), ("lin2_w", "L0_attn_b_lin2_w"),
             ("lin2_b", "L0_attn_b_lin2_b")],
    "l1ab": [("bff", "L1_attn_b_bff")],
    "l1aa": [("bq", "L1_attn_a_bq"), ("bk", "L1_attn_a_bk"), ("bv", "L1_attn_a_bv"),
             ("bo", "L1_attn_a_bo"), ("beta", "L1_attn_a_beta")],
    "l1ffn": [("gw", "L1_ffn_gw"), ("gb", "L1_ffn_gb"), ("w1", "L1_ffn_w1"),
              ("b1", "L1_ffn_b1"), ("w3", "L1_ffn_w3"), ("b3", "L1_ffn_b3"),
              ("w2", "L1_ffn_w2"), ("b2", "L1_ffn_b2"), ("sw1", "L1_ffn_sw1"),
              ("sb1", "L1_ffn_sb1"), ("sw3", "L1_ffn_sw3"), ("sb3", "L1_ffn_sb3"),
              ("sw2", "L1_ffn_sw2"), ("sb2", "L1_ffn_sb2")],
}
SUB_WEIGHTS["l1ffn_sp"] = SUB_WEIGHTS["l1ffn"]

SUB_EMIT = {}


def _emit_sub_l0ffn(c, W, xsrc_fn, xdst_fn):
    emit_l0ffn(c, W, xsrc_fn, xdst_fn)


def _emit_sub_l0ffn_sp(c, W, xsrc_fn, xdst_fn):
    emit_l0ffn_sp(c, W, xsrc_fn, xdst_fn, W["_xfull"])


def _emit_sub_l1ffn_sp(c, W, xsrc_fn, xdst_fn):
    emit_l1ffn_sp(c, W, xsrc_fn, xdst_fn, W["_xfull"])


SUB_EMIT["l0ffn"] = _emit_sub_l0ffn
SUB_EMIT["l0ffn_sp"] = _emit_sub_l0ffn_sp
SUB_EMIT["l1ffn_sp"] = _emit_sub_l1ffn_sp
SUB_EMIT["l1ffn"] = emit_l1ffn
SUB_EMIT["l0aa"] = emit_l0aa
SUB_EMIT["l0ab"] = emit_l0ab
SUB_EMIT["l1ab"] = emit_l1ab
SUB_EMIT["l1aa"] = emit_l1aa


_WEIGHT_SHAPES = {
    "L0_ffn_Wr": (D, 4), "L0_ffn_br": (4,), "L0_ffn_We1": (4, D, 4096),
    "L0_ffn_be1": (4, 4096), "L0_ffn_We2": (4, 4096, D), "L0_ffn_be2": (4, D),
    "L1_ffn_gw": (D, 32), "L1_ffn_gb": (32,), "L1_ffn_w1": (32, D, 512),
    "L1_ffn_b1": (32, 512), "L1_ffn_w3": (32, D, 512), "L1_ffn_b3": (32, 512),
    "L1_ffn_w2": (32, 512, D), "L1_ffn_b2": (32, D), "L1_ffn_sw1": (D, 2 * 512),
    "L1_ffn_sb1": (2 * 512,), "L1_ffn_sw3": (D, 2 * 512), "L1_ffn_sb3": (2 * 512,),
    "L1_ffn_sw2": (2 * 512, D), "L1_ffn_sb2": (D,),
    "L0_attn_a_g": (1,), "L0_attn_a_res_scale": (D,),
    "L0_attn_b_qkv_b": (3 * D,), "L0_attn_b_lin2_w": (1, 256),
    "L0_attn_b_lin2_b": (1,), "L1_attn_b_bff": (D,),
    "L1_attn_a_bq": (D,), "L1_attn_a_bk": (D,), "L1_attn_a_bv": (D,),
    "L1_attn_a_bo": (D,), "L1_attn_a_beta": (1,),
}
NPDT = {np.dtype(np.float32): F32, np.dtype(np.float16): mybir.dt.float16,
        np.dtype(np.uint8): U8}


def build_single(sub, inp=None):
    nc = bass.Bass("TRN2", target_bir_lowering=False, detect_race_conditions=False)
    nc.gpsimd.get_value_cache().nocache = True
    es = contextlib.ExitStack()
    c = B(nc, es)
    xin = nc.dram_tensor("xin", [NB, L, D], F32, kind="ExternalInput")
    xout = nc.dram_tensor("xout", [NB, L, D], F32, kind="ExternalOutput")
    consts_np = host_consts()
    consts = {k: nc.dram_tensor(k, list(v.shape), F32, kind="ExternalInput")
              for k, v in consts_np.items()}
    W = {}
    for wname, iname in SUB_WEIGHTS[sub]:
        h = nc.dram_tensor(iname, list(_WEIGHT_SHAPES[iname]), F32, kind="ExternalInput")
        W[wname] = h
    extra_in = {}
    if sub in SUB_DERIVED:
        der = SUB_DERIVED[sub](inp)
        for k, v in der.items():
            W[k] = nc.dram_tensor(k, list(v.shape), NPDT[v.dtype], kind="ExternalInput")
            extra_in[k] = v

    W["_xfull"] = xin.rearrange("b l d -> (b l) d")

    def xsrc_fn(t):
        b, l0 = t // TPB, (t % TPB) * P
        return xin[b, l0:l0 + P, :]

    def xdst_fn(t):
        b, l0 = t // TPB, (t % TPB) * P
        return xout[b, l0:l0 + P, :]

    with es:
        alloc_shared(c)
        emit_init(c, consts)
        SUB_EMIT[sub](c, W, xsrc_fn, xdst_fn)
        c.emit_all()
    consts_np.update(extra_in)
    return nc, consts_np


_SUB_ORDER = ["l0aa", "l0ffn_sp", "l0ab", "l0ffn_sp", "l1aa", "l1ffn_sp", "l1ab",
              "l1ffn_sp"]

FUSED = True


def build_fused(inp):
    nc = bass.Bass("TRN2", target_bir_lowering=False, detect_race_conditions=False)
    nc.gpsimd.get_value_cache().nocache = True
    es = contextlib.ExitStack()
    c = B(nc, es)
    xin = nc.dram_tensor("xin", [NB, L, D], F32, kind="ExternalInput")
    xout = nc.dram_tensor("xout", [NB, L, D], F32, kind="ExternalOutput")
    nsub = len(_SUB_ORDER)
    mids = [nc.dram_tensor("xmid%d" % i, [NB, L, D], F32) for i in range(nsub - 1)]
    consts_np = host_consts()
    consts = {k: nc.dram_tensor(k, list(v.shape), F32, kind="ExternalInput")
              for k, v in consts_np.items()}
    whandles = {}
    extra_in = {}
    der_handles = {}
    for sub in set(_SUB_ORDER):
        for wname, iname in SUB_WEIGHTS[sub]:
            if iname not in whandles:
                whandles[iname] = nc.dram_tensor(
                    iname, list(_WEIGHT_SHAPES[iname]), F32, kind="ExternalInput")
        if sub in SUB_DERIVED and sub not in der_handles:
            der = SUB_DERIVED[sub](inp)
            dh = {}
            for k, v in der.items():
                name = sub + "_" + k
                dh[k] = nc.dram_tensor(name, list(v.shape), NPDT[v.dtype],
                                       kind="ExternalInput")
                extra_in[name] = v
            der_handles[sub] = dh

    with es:
        alloc_shared(c)
        emit_init(c, consts)
        snap = {}
        for i, sub in enumerate(_SUB_ORDER):
            c.ns = "i%d_" % i
            xs = xin if i == 0 else mids[i - 1]
            xd = xout if i == nsub - 1 else mids[i]
            W = {}
            for wname, iname in SUB_WEIGHTS[sub]:
                W[wname] = whandles[iname]
            if sub in der_handles:
                W.update(der_handles[sub])
            W["_xfull"] = xs.rearrange("b l d -> (b l) d")

            def xsrc_fn(t, xs=xs):
                b, l0 = t // TPB, (t % TPB) * P
                return xs[b, l0:l0 + P, :]

            def xdst_fn(t, xd=xd):
                b, l0 = t // TPB, (t % TPB) * P
                return xd[b, l0:l0 + P, :]

            SUB_EMIT[sub](c, W, xsrc_fn, xdst_fn)
            if i < nsub - 1:
                snap = c.barrier(snap)
        c.emit_all()
    consts_np.update(extra_in)
    return nc, consts_np


_FUSED_CACHE = {}


def kernel(**inputs):
    x = np.ascontiguousarray(inputs["L0_attn_a_x"], dtype=np.float32)
    if FUSED:
        if "prog" not in _FUSED_CACHE:
            _FUSED_CACHE["prog"] = build_fused(inputs)
        nc, consts_np = _FUSED_CACHE["prog"]
        wnames = set()
        for sub in set(_SUB_ORDER):
            for wname, iname in SUB_WEIGHTS[sub]:
                wnames.add(iname)
        in_maps = []
        for core in range(NCORES):
            m = {"xin": np.ascontiguousarray(x[NB * core:NB * (core + 1)])}
            m.update(consts_np)
            for iname in wnames:
                m[iname] = inputs[iname]
            in_maps.append(m)
        res = run_bass_kernel_spmd(nc, in_maps, core_ids=list(range(NCORES)))
        return np.concatenate([np.asarray(r["xout"]) for r in res.results], axis=0)
    progs = {}
    for sub in _SUB_ORDER:
        if sub not in progs:
            progs[sub] = build_single(sub, inputs)
    for sub in _SUB_ORDER:
        nc, consts_np = progs[sub]
        in_maps = []
        for core in range(NCORES):
            m = {"xin": np.ascontiguousarray(x[NB * core:NB * (core + 1)])}
            m.update(consts_np)
            for wname, iname in SUB_WEIGHTS[sub]:
                m[iname] = inputs[iname]
            in_maps.append(m)
        res = run_bass_kernel_spmd(nc, in_maps, core_ids=list(range(NCORES)))
        x = np.concatenate([np.asarray(r["xout"]) for r in res.results], axis=0)
    return x


# version 09 (final)
# speedup vs baseline: 1.1601x; 1.1601x over previous
import contextlib
import numpy as np
import concourse.bass as bass
import concourse.mybir as mybir
from concourse.bass_utils import run_bass_kernel_spmd

F32 = mybir.dt.float32
BF16 = mybir.dt.bfloat16
U8 = mybir.dt.uint8
U32 = mybir.dt.uint32
AF = mybir.ActivationFunctionType
ALU = mybir.AluOpType
AX = mybir.AxisListType

NCORES = 8
NB = 2
L = 2048
D = 1024
P = 128
NTOK = NB * L
NT = NTOK // P
TPB = L // P
EPS = 1e-5


class Sem:
    def __init__(self, h):
        self.h = h
        self.c = 0


class Q:
    def __init__(self, name):
        self.name = name
        self.ops = []

    def emit(self, eng):
        import os
        skips = set()
        sv = os.environ.get("GPQ_SKIP_" + self.name, "")
        for part in sv.split(","):
            if not part:
                continue
            if "-" in part:
                a, b = part.split("-")
                skips.update(range(int(a), int(b) + 1))
            else:
                skips.add(int(part))
        dbg = os.environ.get("GPQ_DBG") == "1"
        for idx, fn in enumerate(self.ops):
            if idx in skips:
                continue
            try:
                fn(eng)
            except Exception:
                if dbg:
                    print("QFAIL %s idx %d/%d" % (self.name, idx, len(self.ops)))
                raise


class B:
    ENGINES = ("sync", "gpsimd", "tensor", "vector", "scalar")

    def __init__(self, nc, es):
        self.nc = nc
        self.es = es
        self.sems = {}
        self.bufs = {}
        self.q = {n: Q(n) for n in self.ENGINES}
        self.n_instr = 0
        self.ns = ""

    def sem(self, name):
        if name not in self.sems:
            self.sems[name] = Sem(self.es.enter_context(self.nc.semaphore(name)))
        return self.sems[name]

    def barrier(self, snap):
        for qn in self.ENGINES:
            for name, s in list(self.sems.items()):
                if s.c > 0 and s.c != snap.get(name, 0):
                    self.wait(qn, s, s.c)
        return {name: s.c for name, s in self.sems.items()}

    def sbuf(self, name, shape, dtype):
        if name not in self.bufs:
            self.bufs[name] = self.es.enter_context(self.nc.sbuf_tensor(name, shape, dtype))
        return self.bufs[name]

    def psum(self, name, shape, dtype):
        if name not in self.bufs:
            self.bufs[name] = self.es.enter_context(self.nc.psum_tensor(name, shape, dtype))
        return self.bufs[name]

    def bank(self, k, w=1):
        return self.bufs["PSALL"][:, 512 * k:512 * (k + w)]

    def op(self, qn, fn, sem=None, n=1):
        self.n_instr += 1
        if qn == "gpsimd":
            fn = self.dyn(fn)
        if sem is not None:
            sem.c += n
            v = sem.c
            self.q[qn].ops.append(lambda e, fn=fn, s=sem.h, n=n: fn(e).then_inc(s, n))
            return v
        self.q[qn].ops.append(fn)
        return None

    def opS(self, qn, fn):
        s = self.sem("sSelf_" + qn)
        v = self.op(qn, fn, s, 1)
        self.wait(qn, s, v)
        return v

    def waitall(self, qn, sem):
        self.wait(qn, sem, sem.c)

    def wait(self, qn, sem, val):
        if val is None:
            raise ValueError("wait on None")
        self.q[qn].ops.append(lambda e, s=sem.h, v=val: e.wait_ge(s, v))

    def dma(self, qn, out, in_, sem):
        return self.op(qn, lambda e, o=out, i=in_: e.dma_start(out=o, in_=i), sem, 16)

    def _reg_head(self):
        self._prn = getattr(self, "_prn", 0) + 1
        h = self.nc.alloc_register(mybir.EngineType.Pool, "_pr%d" % self._prn)
        rid = self.nc.lookup_reg(h).reg_id
        self.nc.free_register(h)
        return rid

    def _free_regs(self, a, b):
        for rid in range(a, b):
            self._prn += 1
            h = self.nc.add_register(mybir.EngineType.Pool, "_fr%d" % self._prn, rid)
            self.nc.free_register(h)

    def dyn(self, fn):
        def _w(e, fn=fn):
            a = self._reg_head()
            r = fn(e)
            self._free_regs(a, self._reg_head())
            return r
        return _w

    def dmad(self, qn, out, in_, sem):
        return self.op(qn, self.dyn(lambda e, o=out, i=in_: e.dma_start(out=o, in_=i)),
                       sem, 16)

    def dmad_if(self, qn, out, in_, sem, rcond, dummy):
        self.n_instr += 4
        sem.c += 16
        v = sem.c

        def _go(e, o=out, i=in_, dm=dummy, s=sem.h, rc=rcond):
            a = self._reg_head()
            with e.If_cmp(rc, 0, comp_op="IS_NE"):
                e.dma_start(out=o, in_=i).then_inc(s, 16)
            with e.Else():
                e.dma_start(out=dm[0], in_=dm[1]).then_inc(s, 16)
            self._free_regs(a, self._reg_head())
        self.q[qn].ops.append(_go)
        return v

    def emit_all(self):
        with self.nc.Block() as block:
            @block.sync
            def _(e):
                self.q["sync"].emit(e)

            @block.gpsimd
            def _(e):
                self.q["gpsimd"].emit(e)

            @block.tensor
            def _(e):
                self.q["tensor"].emit(e)

            @block.vector
            def _(e):
                self.q["vector"].emit(e)

            @block.scalar
            def _(e):
                self.q["scalar"].emit(e)


def ap3(buf, offset, dims):
    return bass.AP(buf, offset, dims)


def alloc_shared(c):
    c.sbuf("XT", [P, 4 * D], F32)
    c.sbuf("STAT", [P, 32], F32)
    c.sbuf("MISC", [P, D], F32)
    c.sbuf("LNS", [P, 4 * NT], F32)
    c.sbuf("TBF", [P, 2 * D], F32)
    c.sbuf("TTB", [P, 16 * 8 * P], BF16)
    c.sbuf("H1T", [P, 32 * 1024], BF16)
    c.sbuf("WA", [P, 2 * 4096], BF16)
    c.sbuf("WC", [P, 4 * 512], BF16)
    c.sbuf("ACC", [P, 8 * 1024], F32)
    c.sbuf("MASKS", [P, 4 * NT], F32)
    c.sbuf("ONESB", [1, 512], BF16)
    c.sbuf("ONESF", [1, P], F32)
    c.sbuf("SMALLF", [1, 32], F32)
    c.sbuf("SMALLB", [1, 5120], BF16)
    c.sbuf("WSM", [P, 32], BF16)
    c.sbuf("WSM2", [P, 256], BF16)
    c.sbuf("GSC", [P, 168], F32)
    c.sbuf("CW", [P, 8 * 32], F32)
    c.sbuf("BCAST", [P, 40], F32)
    c.sbuf("OUTT", [P, 2 * D], F32)
    c.sbuf("IDENT", [P, P], F32)
    c.psum("PSALL", [P, 4096], F32)


def emit_init(c, consts):
    sC = c.sem("sC")
    sV = c.sem("sV")
    v = c.dma("sync", c.bufs["IDENT"][:, :], consts["cIDENT"][:, :], sC)
    c.op("vector", lambda e: e.memset(c.bufs["ONESB"][:, :], 1.0))
    vv = c.op("vector", lambda e: e.memset(c.bufs["ONESF"][:, :], 1.0), sV)
    c.wait("tensor", sC, v)
    c.wait("tensor", sV, vv)


def emit_ln(c, xsrc_fn, tile_ids, slot0=0, xt_guards=None, xn_mode=False, gcol=None,
            loader=None):
    XT, STAT, LNS, SQS, TBF, TTB, IDENT = (c.bufs[k] for k in
        ("XT", "STAT", "LNS", "MISC", "TBF", "TTB", "IDENT"))
    sSq, sSt, sRs, sTb, sTr, sEv = (c.sem(k) for k in
        ("sSq", "sSt", "sRs", "sTb", "sTr", "sEv"))
    sXp = [c.sem("sX0"), c.sem("sX1"), c.sem("sX2"), c.sem("sX3")]
    xv, sqv, stv, rsv, tbv, trv, evv = {}, {}, {}, {}, {}, {}, {}

    scl = 1 if xn_mode else 0
    n_tiles = len(tile_ids)
    for t in range(n_tiles):
        tid = tile_ids[t]
        g = 8 * (t % 2)
        sl = (t % 4) * D
        sX = sXp[t % 4]
        if t >= 4:
            c.wait("sync", sTb, tbv[t - 4])
        elif xt_guards and (t % 4) in xt_guards:
            gs, gv = xt_guards[t % 4]
            c.wait("sync", gs, gv)
        if loader is None:
            xv[t] = c.dma("sync", XT[:, sl:sl + D], xsrc_fn(tid), sX)
        else:
            sX = loader(t, XT[:, sl:sl + D])

        c.waitall("vector", sX)
        c.opS("vector", lambda e, sl=sl, g=g: e.tensor_reduce(
            out=STAT[:, g + 1:g + 2], in_=XT[:, sl:sl + D], axis=AX.X, op=ALU.add))
        stv[t] = c.op("vector", lambda e, t=tid, g=g: e.tensor_scalar(
            out=LNS[:, 4 * t:4 * t + 1], in0=STAT[:, g + 1:g + 2],
            scalar1=-1.0 / D, scalar2=None, op0=ALU.mult), sSt)

        c.waitall("scalar", sX)
        c.wait("scalar", sSt, stv[t])
        sqv[t] = c.op("scalar", lambda e, sl=sl, g=g, t=tid: e.activation(
            out=SQS[:, :], in_=XT[:, sl:sl + D], func=AF.Square,
            bias=LNS[:, 4 * t:4 * t + 1], scale=1.0,
            accum_out=STAT[:, g + 0:g + 1]), sSq)

        c.wait("vector", sSq, sqv[t])
        c.opS("vector", lambda e, g=g: e.tensor_scalar(
            out=STAT[:, g + 3:g + 4], in0=STAT[:, g + 0:g + 1],
            scalar1=1.0 / D, scalar2=EPS, op0=ALU.mult, op1=ALU.add))
        rcv = c.op("vector", lambda e, g=g: e.reciprocal(
            out=STAT[:, g + 4:g + 5], in_=STAT[:, g + 3:g + 4]), sSt)

        c.wait("scalar", sSt, rcv)
        rsv[t] = c.op("scalar", lambda e, t=tid, g=g: e.activation(
            out=LNS[:, 4 * t + 1:4 * t + 2], in_=STAT[:, g + 4:g + 5],
            func=AF.Sqrt), sRs)

        c.wait("vector", sRs, rsv[t])
        if xn_mode:
            c.opS("vector", lambda e, t=tid, g=g: e.tensor_tensor(
                out=STAT[:, g + 5:g + 6], in0=LNS[:, 4 * t + 1:4 * t + 2],
                in1=LNS[:, 4 * t + 1:4 * t + 2], op=ALU.mult))
            c.opS("vector", lambda e, g=g: e.reciprocal(
                out=STAT[:, g + 6:g + 7], in_=STAT[:, g + 5:g + 6]))
            s2v = c.op("vector", lambda e, g=g: e.tensor_scalar(
                out=STAT[:, g + 5:g + 6], in0=STAT[:, g + 6:g + 7],
                scalar1=-EPS, scalar2=None, op0=ALU.add), sSt)
            c.wait("scalar", sSt, s2v)
            sqv2 = c.op("scalar", lambda e, g=g: e.activation(
                out=STAT[:, g + 6:g + 7], in_=STAT[:, g + 5:g + 6], func=AF.Sqrt), sRs)
            c.wait("vector", sRs, sqv2)
            c.opS("vector", lambda e, t=tid, g=g: e.tensor_tensor(
                out=STAT[:, g + 5:g + 6], in0=LNS[:, 4 * t + 1:4 * t + 2],
                in1=STAT[:, g + 6:g + 7], op=ALU.mult))
            c.opS("vector", lambda e, g=g: e.tensor_scalar(
                out=STAT[:, g + 5:g + 6], in0=STAT[:, g + 5:g + 6],
                scalar1=1e-5, scalar2=None, op0=ALU.max))
            c.opS("vector", lambda e, g=g: e.reciprocal(
                out=STAT[:, g + 6:g + 7], in_=STAT[:, g + 5:g + 6]))
            c.opS("vector", lambda e, g=g, gcol=gcol: e.tensor_tensor(
                out=STAT[:, g + 5:g + 6], in0=STAT[:, g + 6:g + 7],
                in1=gcol, op=ALU.mult))
            c.opS("vector", lambda e, t=tid, g=g: e.tensor_tensor(
                out=LNS[:, 4 * t + 2:4 * t + 3], in0=STAT[:, g + 5:g + 6],
                in1=LNS[:, 4 * t + 1:4 * t + 2], op=ALU.mult))
        if t >= 2:
            c.wait("vector", sTr, trv[t - 2])
        tbv[t] = c.op("vector", lambda e, t=t, tid=tid, sl=sl: e.tensor_scalar(
            out=TBF[:, (t % 2) * D:(t % 2 + 1) * D], in0=XT[:, sl:sl + D],
            scalar1=LNS[:, 4 * tid:4 * tid + 1], scalar2=LNS[:, 4 * tid + 1 + scl:4 * tid + 2 + scl],
            op0=ALU.add, op1=ALU.mult), sTb)

        c.wait("tensor", sTb, tbv[t])
        if t >= 2:
            c.wait("tensor", sEv, evv[t - 2])
        for dt in range(8):
            fn = lambda e, t=t, dt=dt: e.transpose(
                out=c.bank(2 * (t % 2), 2)[:, dt * P:(dt + 1) * P],
                in_=TBF[:, (t % 2) * D + dt * P:(t % 2) * D + (dt + 1) * P],
                identity=IDENT[:, :])
            if dt == 7:
                trv[t] = c.op("tensor", fn, sTr)
            else:
                c.op("tensor", fn)

        c.wait("scalar", sTr, trv[t])
        slot = slot0 + t
        evv[t] = c.op("scalar", lambda e, t=t, slot=slot: e.copy(
            out=TTB[:, slot * 1024:(slot + 1) * 1024], in_=c.bank(2 * (t % 2), 2)), sEv)
    return {"evv": evv, "tbv": tbv, "ln_last_ev": evv[n_tiles - 1],
            "tb_last": tbv[n_tiles - 1]}


def emit_l0ffn(c, W, xsrc_fn, xdst_fn):
    TTB, H1T, WA, WC, ACC, MASKS, XT, STAT = (c.bufs[k] for k in
        ("TTB", "H1T", "WA", "WC", "ACC", "MASKS", "XT", "STAT"))
    WSM, SMALLF, SMALLB, ONESB, ONESF, BCAST = (c.bufs[k] for k in
        ("WSM", "SMALLF", "SMALLB", "ONESB", "ONESF", "BCAST"))
    sMM, sEp, sAc, sV, sEv, sSq, sTb = (c.sem(k) for k in
        ("sMM", "sEp", "sAc", "sV", "sEv", "sSq", "sTb"))
    sCst = c.sem("sCst")
    sBE = c.sem("sBE")
    sWA = [c.sem("sWA0"), c.sem("sWA1")]
    sWC = [c.sem("sWC0"), c.sem("sWC1"), c.sem("sWC2"), c.sem("sWC3")]
    sXp = [c.sem("sX0"), c.sem("sX1"), c.sem("sX2"), c.sem("sX3")]
    sOutp = [c.sem("sOutA"), c.sem("sOutB")]

    c.dma("gpsimd", WSM[:, 0:32].rearrange("p (dt e) -> p dt e", e=4),
          W["Wr"].rearrange("(dt p) e -> p dt e", p=P), sCst)
    c.dma("gpsimd", SMALLF[:, 0:4], ap3(W["br"], 0, [[4, 1], [1, 4]]), sCst)
    c.waitall("tensor", sCst)
    v = c.op("tensor", lambda e: e.matmul(out=c.bank(6)[:, 0:4], lhsT=ONESF[:, :],
                                          rhs=SMALLF[:, 0:4], start=True, stop=True), sMM)
    c.wait("scalar", sMM, v)
    brB_v = c.op("scalar", lambda e: e.copy(out=BCAST[:, 0:4], in_=c.bank(6)[:, 0:4]), sAc)
    c.op("vector", lambda e: e.memset(STAT[:, 31:32], 0.0), sV)

    NGRP, GT = 4, 8
    TTB_PITCH = 16 * 8 * P
    chunk_seq = 0
    wa_done = {}
    prod_seq = 0
    prod_evac = {}
    pc_last_mm = None
    prev_ep_all = None
    out_v = {}

    for g in range(NGRP):
        tiles = list(range(g * GT, (g + 1) * GT))
        xt_guards = None
        if g > 0:
            xt_guards = {s: (sV, out_v[("vadd", g - 1, 4 + s)]) for s in range(4)}
        ln = emit_ln(c, xsrc_fn, tiles, slot0=0, xt_guards=xt_guards)

        ep_tt = {}
        for i in range(GT):
            tid = tiles[i]
            bk = 6 + (i % 2)
            c.wait("tensor", sEv, ln["evv"][i])
            if i >= 2:
                c.wait("tensor", sEp, ep_tt[i - 2])
            elif prev_ep_all is not None:
                c.wait("tensor", sEp, prev_ep_all)
            if g == 0 and i == 0:
                c.wait("tensor", sAc, brB_v)
            for dt in range(8):
                fn = lambda e, i=i, dt=dt, bk=bk: e.matmul(
                    out=c.bank(bk)[:, 0:4],
                    lhsT=TTB[:, i * 1024 + dt * P: i * 1024 + (dt + 1) * P],
                    rhs=WSM[:, 4 * dt:4 * dt + 4], start=(dt == 0), stop=(dt == 7))
                if dt == 7:
                    mmv = c.op("tensor", fn, sMM)
                else:
                    c.op("tensor", fn)
            gg = 16 + 8 * (i % 2)
            c.wait("vector", sMM, mmv)
            if g == 0 and i == 0:
                c.wait("vector", sAc, brB_v)
            ep_tt[i] = c.op("vector", lambda e, gg=gg, bk=bk: e.tensor_tensor(
                out=STAT[:, gg:gg + 4], in0=c.bank(bk)[:, 0:4], in1=BCAST[:, 0:4],
                op=ALU.add), sEp)
            c.wait("vector", sEp, ep_tt[i])
            c.opS("vector", lambda e, gg=gg: e.tensor_reduce(
                out=STAT[:, gg + 4:gg + 5], in_=STAT[:, gg:gg + 4], axis=AX.X, op=ALU.max))
            c.opS("vector", lambda e, gg=gg: e.tensor_scalar(
                out=STAT[:, gg:gg + 4], in0=STAT[:, gg:gg + 4],
                scalar1=STAT[:, gg + 4:gg + 5], scalar2=None, op0=ALU.is_ge))
            c.opS("vector", lambda e, t=tid, gg=gg: e.tensor_tensor_scan(
                out=MASKS[:, 4 * t:4 * t + 4], data0=STAT[:, gg:gg + 4],
                data1=STAT[:, gg:gg + 4], initial=STAT[:, 31:32], op0=ALU.add, op1=ALU.bypass))
            c.opS("vector", lambda e, t=tid, gg=gg: e.tensor_tensor(
                out=MASKS[:, 4 * t:4 * t + 4], in0=MASKS[:, 4 * t:4 * t + 4],
                in1=STAT[:, gg:gg + 4], op=ALU.is_equal))
            mask_v = c.op("vector", lambda e, t=tid, gg=gg: e.tensor_tensor(
                out=MASKS[:, 4 * t:4 * t + 4], in0=MASKS[:, 4 * t:4 * t + 4],
                in1=STAT[:, gg:gg + 4], op=ALU.mult), sEp)
        mask_done = mask_v

        accz = c.op("vector", lambda e: e.memset(ACC[:, :], 0.0), sEp)

        for ei in range(4):
            if not (g == 0 and ei == 0):
                c.wait("gpsimd", sMM, last_opener_mm)
            c.dma("gpsimd", SMALLB[:, 0:4096],
                  ap3(W["be1"], 4096 * ei, [[4096, 1], [1, 4096]]), sBE)
            c.dma("gpsimd", SMALLB[:, 4096:4096 + 1024],
                  ap3(W["be2"], 1024 * ei, [[1024, 1], [1, 1024]]), sBE)
            first_of_ge = True
            for ch in range(8):
                slot = chunk_seq % 2
                if chunk_seq >= 2:
                    c.wait("gpsimd", sMM, wa_done[chunk_seq - 2])
                c.dma("gpsimd",
                      WA[:, slot * 4096:(slot + 1) * 4096]
                      .rearrange("p (dt f) -> p dt f", f=512),
                      W["We1"][ei][:, 512 * ch:512 * (ch + 1)]
                      .rearrange("(dt p) f -> p dt f", p=P), sWA[slot])
                c.waitall("tensor", sWA[slot])
                if ch == 0:
                    c.waitall("tensor", sBE)
                    if prev_ep_all is not None and ei == 0:
                        c.wait("tensor", sEp, prev_ep_all)
                    c.wait("tensor", sEp, mask_done)
                    c.wait("tensor", sEp, accz)
                    if ei > 0:
                        c.wait("tensor", sEp, cur_ep_all)
                for ft in range(4):
                    for th in range(2):
                        bk = 4 + (prod_seq % 2)
                        if prod_seq >= 2:
                            c.wait("tensor", sAc, prod_evac[prod_seq - 2])
                        c.op("tensor", lambda e, ch=ch, ft=ft, bk=bk: e.matmul(
                            out=c.bank(bk),
                            lhsT=SMALLB[0:1, 512 * ch + 128 * ft:512 * ch + 128 * (ft + 1)],
                            rhs=ONESB[0:1, 0:512], start=True, stop=False))
                        for dt in range(8):
                            off = (th * 4) * 1024 + dt * P
                            rhs = ap3(TTB, off, [[TTB_PITCH, P], [1024, 4], [1, P]])
                            fn = lambda e, slot=slot, dt=dt, ft=ft, rhs=rhs, bk=bk: e.matmul(
                                out=c.bank(bk),
                                lhsT=WA[:, slot * 4096 + dt * 512 + 128 * ft:
                                        slot * 4096 + dt * 512 + 128 * (ft + 1)],
                                rhs=rhs, start=False, stop=(dt == 7))
                            if dt == 7:
                                mmv = c.op("tensor", fn, sMM)
                            else:
                                c.op("tensor", fn)
                        fi = ch * 4 + ft
                        c.wait("scalar", sMM, mmv)
                        if first_of_ge and pc_last_mm is not None:
                            c.wait("scalar", sMM, pc_last_mm)
                        first_of_ge = False
                        prod_evac[prod_seq] = c.op("scalar",
                            lambda e, fi=fi, th=th, bk=bk: e.activation(
                                out=H1T[:, fi * 1024 + th * 512: fi * 1024 + th * 512 + 512],
                                in_=c.bank(bk), func=AF.Relu), sAc)
                        prod_seq += 1
                wa_done[chunk_seq] = mmv
                chunk_seq += 1
            h1_done = prod_evac[prod_seq - 1]

            wc_seq = 0
            wc_done = {}
            ep_bank = {}
            for oh in range(2):
                for ft in range(32):
                    slot = wc_seq % 4
                    if wc_seq >= 4:
                        c.wait("gpsimd", sMM, wc_done[wc_seq - 4])
                    c.dma("gpsimd", WC[:, slot * 512:(slot + 1) * 512],
                          W["We2"][ei][128 * ft:128 * (ft + 1), 512 * oh:512 * (oh + 1)],
                          sWC[slot])
                    c.waitall("tensor", sWC[slot])
                    if ft == 0:
                        c.wait("tensor", sAc, h1_done)
                        if oh == 1:
                            for tau in range(8):
                                c.wait("tensor", sEp, ep_bank[(tau, 0)])
                        for tau in range(8):
                            last_opener_mm = c.op("tensor", lambda e, tau=tau, oh=oh: e.matmul(
                                out=c.bank(tau), lhsT=ONESB[0:1, 0:P],
                                rhs=SMALLB[0:1, 4096 + 512 * oh:4096 + 512 * (oh + 1)],
                                start=True, stop=False), sMM)
                    for tau in range(8):
                        fn = lambda e, ft=ft, tau=tau, slot=slot: e.matmul(
                            out=c.bank(tau),
                            lhsT=H1T[:, ft * 1024 + tau * P: ft * 1024 + (tau + 1) * P],
                            rhs=WC[:, slot * 512:(slot + 1) * 512],
                            start=False, stop=(ft == 31))
                        if tau == 7:
                            mmv = c.op("tensor", fn, sMM)
                        else:
                            c.op("tensor", fn)
                    wc_done[wc_seq] = mmv
                    wc_seq += 1
                pc_last_mm = mmv
                c.wait("vector", sMM, mmv)
                for tau in range(8):
                    tt = g * GT + tau
                    ep_bank[(tau, oh)] = c.op("vector",
                        lambda e, tau=tau, oh=oh, tt=tt, ei=ei: e.scalar_tensor_tensor(
                            out=ACC[:, tau * 1024 + oh * 512: tau * 1024 + oh * 512 + 512],
                            in0=c.bank(tau), scalar=MASKS[:, 4 * tt + ei:4 * tt + ei + 1],
                            in1=ACC[:, tau * 1024 + oh * 512: tau * 1024 + oh * 512 + 512],
                            op0=ALU.mult, op1=ALU.add), sEp)
            cur_ep_all = ep_bank[(7, 1)]
        prev_ep_all = cur_ep_all

        OUTT = c.bufs["OUTT"]
        for tau in range(8):
            tt = g * GT + tau
            sl = (tau % 4) * D
            c.wait("sync", sTb, ln["tbv"][4 + (tau % 4)])
            if tau >= 4:
                c.wait("sync", sV, out_v[("vadd", g, tau - 4)])
            sX = sXp[tau % 4]
            xv = c.dma("sync", XT[:, sl:sl + D], xsrc_fn(tt), sX)
            c.waitall("vector", sX)
            c.waitall("vector", sOutp[tau % 2])
            vv = c.op("vector", lambda e, sl=sl, tau=tau: e.tensor_tensor(
                out=OUTT[:, (tau % 2) * D:(tau % 2 + 1) * D], in0=XT[:, sl:sl + D],
                in1=ACC[:, tau * 1024:(tau + 1) * 1024], op=ALU.add), sV)
            out_v[("vadd", g, tau)] = vv
            c.wait("sync", sV, vv)
            out_v[("st", g, tau)] = c.dma("sync", xdst_fn(tt),
                                          OUTT[:, (tau % 2) * D:(tau % 2 + 1) * D],
                                          sOutp[tau % 2])
    c.waitall("sync", sOutp[0])
    c.waitall("sync", sOutp[1])


def emit_l1ffn(c, W, xsrc_fn, xdst_fn):
    TTB, H1T, WA, WC, ACC, CW, XT, STAT, GSC = (c.bufs[k] for k in
        ("TTB", "H1T", "WA", "WC", "ACC", "CW", "XT", "STAT", "GSC"))
    WSM2, SMALLF, SMALLB, ONESB, ONESF, BCAST, OUTT = (c.bufs[k] for k in
        ("WSM2", "SMALLF", "SMALLB", "ONESB", "ONESF", "BCAST", "OUTT"))
    sMM, sEp, sAc, sV, sEv, sSq, sSt, sTb = (c.sem(k) for k in
        ("sMM", "sEp", "sAc", "sV", "sEv", "sSq", "sSt", "sTb"))
    sCst = c.sem("sCst")
    sBE = c.sem("sBE")
    sWA = [c.sem("sWA0"), c.sem("sWA1")]
    sWC = [c.sem("sWC0"), c.sem("sWC1"), c.sem("sWC2"), c.sem("sWC3")]
    sXp = [c.sem("sX0"), c.sem("sX1"), c.sem("sX2"), c.sem("sX3")]
    sOutp = [c.sem("sOutA"), c.sem("sOutB")]
    BIG = 60000.0

    c.dma("gpsimd", WSM2[:, :].rearrange("p (dt e) -> p dt e", e=32),
          W["gw"].rearrange("(dt p) e -> p dt e", p=P), sCst)
    c.dma("gpsimd", SMALLF[:, 0:32], ap3(W["gb"], 0, [[32, 1], [1, 32]]), sCst)
    c.waitall("tensor", sCst)
    v = c.op("tensor", lambda e: e.matmul(out=c.bank(5)[:, 0:32], lhsT=ONESF[:, :],
                                          rhs=SMALLF[:, 0:32], start=True, stop=True), sMM)
    c.wait("scalar", sMM, v)
    gbB_v = c.op("scalar", lambda e: e.copy(out=BCAST[:, 0:32], in_=c.bank(5)[:, 0:32]), sAc)

    NGRP, GT = 4, 8
    TTB_PITCH = 16 * 8 * P
    chunk_seq = 0
    wa_done = {}
    prod_seq = 0
    prod_evac = {}
    pc_last_mm = None
    prev_ep_all = None
    out_v = {}

    def phase_prod(g, src_h, src_off, n_ft, slot0, bias_off, act, extra_mm_wait=None):
        nonlocal chunk_seq, prod_seq
        nch = (n_ft + 3) // 4
        last_ev = None
        for ch in range(nch):
            slot = chunk_seq % 2
            if chunk_seq >= 2:
                c.wait("gpsimd", sMM, wa_done[chunk_seq - 2])
            cols0 = src_off + 512 * ch
            ncol = min(512, src_off + 128 * n_ft - cols0)
            c.dma("gpsimd",
                  WA[:, slot * 4096:slot * 4096 + 8 * ncol]
                  .rearrange("p (dt f) -> p dt f", f=ncol),
                  src_h[:, cols0:cols0 + ncol].rearrange("(dt p) f -> p dt f", p=P),
                  sWA[slot])
            c.waitall("tensor", sWA[slot])
            if ch == 0:
                c.waitall("tensor", sBE)
                if extra_mm_wait is not None:
                    for w in extra_mm_wait:
                        c.wait("tensor", *w)
            for ft in range(ncol // 128):
                for th in range(2):
                    bk = 4 + (prod_seq % 2)
                    if prod_seq >= 2:
                        c.wait("tensor", sAc, prod_evac[prod_seq - 2])
                    c.op("tensor", lambda e, ch=ch, ft=ft, bk=bk, bias_off=bias_off: e.matmul(
                        out=c.bank(bk),
                        lhsT=SMALLB[0:1, bias_off + 512 * ch + 128 * ft:
                                    bias_off + 512 * ch + 128 * (ft + 1)],
                        rhs=ONESB[0:1, 0:512], start=True, stop=False))
                    for dt in range(8):
                        off = (th * 4) * 1024 + dt * P
                        rhs = ap3(TTB, off, [[TTB_PITCH, P], [1024, 4], [1, P]])
                        fn = lambda e, slot=slot, dt=dt, ft=ft, rhs=rhs, bk=bk, ncol=ncol: e.matmul(
                            out=c.bank(bk),
                            lhsT=WA[:, slot * 4096 + dt * ncol + 128 * ft:
                                    slot * 4096 + dt * ncol + 128 * (ft + 1)],
                            rhs=rhs, start=False, stop=(dt == 7))
                        if dt == 7:
                            mmv = c.op("tensor", fn, sMM)
                        else:
                            c.op("tensor", fn)
                    fi = slot0 + ch * 4 + ft
                    c.wait("scalar", sMM, mmv)
                    prod_evac[prod_seq] = c.op("scalar",
                        lambda e, fi=fi, th=th, bk=bk, act=act: e.activation(
                            out=H1T[:, fi * 1024 + th * 512: fi * 1024 + th * 512 + 512],
                            in_=c.bank(bk), func=act), sAc)
                    last_ev = prod_evac[prod_seq]
                    prod_seq += 1
            wa_done[chunk_seq] = mmv
            chunk_seq += 1
        return last_ev

    def phase_mult(h1_slot0, h3_slot0, hs_slot0, n_ft, prod_last):
        c.wait("vector", sAc, prod_last)
        vv = None
        for i in range(n_ft):
            vv = c.op("vector", lambda e, i=i, a=h1_slot0, b=h3_slot0, o=hs_slot0: e.tensor_tensor(
                out=H1T[:, (o + i) * 1024:(o + i + 1) * 1024],
                in0=H1T[:, (a + i) * 1024:(a + i + 1) * 1024],
                in1=H1T[:, (b + i) * 1024:(b + i + 1) * 1024], op=ALU.mult), sV)
        return vv

    def phase_out(g, ei, hs_slot0, n_ft, w2_h, b2_off, cw_scalar_fn, hs_ready):
        nonlocal pc_last_mm
        wc_seq = 0
        wc_done = {}
        ep_bank = {}
        for oh in range(2):
            for ft in range(n_ft):
                slot = wc_seq % 4
                if wc_seq >= 4:
                    c.wait("gpsimd", sMM, wc_done[wc_seq - 4])
                c.dma("gpsimd", WC[:, slot * 512:(slot + 1) * 512],
                      w2_h[128 * ft:128 * (ft + 1), 512 * oh:512 * (oh + 1)], sWC[slot])
                c.waitall("tensor", sWC[slot])
                if ft == 0:
                    c.wait("vector", sV, hs_ready) if False else None
                    c.wait("tensor", sV, hs_ready)
                    if oh == 1:
                        for tau in range(8):
                            c.wait("tensor", sEp, ep_bank[(tau, 0)])
                    for tau in range(8):
                        pc_last_mm = c.op("tensor", lambda e, tau=tau, oh=oh, b2_off=b2_off: e.matmul(
                            out=c.bank(tau), lhsT=ONESB[0:1, 0:P],
                            rhs=SMALLB[0:1, b2_off + 512 * oh:b2_off + 512 * (oh + 1)],
                            start=True, stop=False), sMM)
                for tau in range(8):
                    fn = lambda e, ft=ft, tau=tau, slot=slot, hs_slot0=hs_slot0: e.matmul(
                        out=c.bank(tau),
                        lhsT=H1T[:, (hs_slot0 + ft) * 1024 + tau * P:
                                 (hs_slot0 + ft) * 1024 + (tau + 1) * P],
                        rhs=WC[:, slot * 512:(slot + 1) * 512],
                        start=False, stop=(ft == n_ft - 1))
                    if tau == 7:
                        mmv = c.op("tensor", fn, sMM)
                    else:
                        c.op("tensor", fn)
                wc_done[wc_seq] = mmv
                wc_seq += 1
            pc_last_mm = mmv
            c.wait("vector", sMM, mmv)
            for tau in range(8):
                sc = cw_scalar_fn(tau)
                ep_bank[(tau, oh)] = c.op("vector",
                    lambda e, tau=tau, oh=oh, sc=sc: e.scalar_tensor_tensor(
                        out=ACC[:, tau * 1024 + oh * 512: tau * 1024 + oh * 512 + 512],
                        in0=c.bank(tau), scalar=sc,
                        in1=ACC[:, tau * 1024 + oh * 512: tau * 1024 + oh * 512 + 512],
                        op0=ALU.mult, op1=ALU.add), sEp)
        return ep_bank[(7, 1)]

    for g in range(NGRP):
        tiles = list(range(g * GT, (g + 1) * GT))
        xt_guards = None
        if g > 0:
            xt_guards = {s: (sV, out_v[("vadd", g - 1, 4 + s)]) for s in range(4)}
        ln = emit_ln(c, xsrc_fn, tiles, slot0=0, xt_guards=xt_guards)

        for i in range(GT):
            bk = 6 + (i % 2)
            c.wait("tensor", sEv, ln["evv"][i])
            if i >= 2:
                c.wait("tensor", sEp, ep_cw[i - 2])
            elif prev_ep_all is not None:
                c.wait("tensor", sEp, prev_ep_all)
            if g == 0 and i == 0:
                c.wait("tensor", sAc, gbB_v)
            for dt in range(8):
                fn = lambda e, i=i, dt=dt, bk=bk: e.matmul(
                    out=c.bank(bk)[:, 0:32],
                    lhsT=TTB[:, i * 1024 + dt * P: i * 1024 + (dt + 1) * P],
                    rhs=WSM2[:, 32 * dt:32 * dt + 32], start=(dt == 0), stop=(dt == 7))
                if dt == 7:
                    mmv = c.op("tensor", fn, sMM)
                else:
                    c.op("tensor", fn)
            if i == 0:
                ep_cw = {}
            c.wait("vector", sMM, mmv)
            if g == 0 and i == 0:
                c.wait("vector", sAc, gbB_v)
            c.opS("vector", lambda e, bk=bk: e.tensor_tensor(
                out=GSC[:, 0:32], in0=c.bank(bk)[:, 0:32], in1=BCAST[:, 0:32], op=ALU.add))
            ep_cw[i] = c.op("vector", lambda e: e.nop(), sEp)
            c.opS("vector", lambda e: e.tensor_reduce(
                out=GSC[:, 32:33], in_=GSC[:, 0:32], axis=AX.X, op=ALU.max))
            nm = c.op("vector", lambda e: e.tensor_scalar(
                out=GSC[:, 33:34], in0=GSC[:, 32:33], scalar1=-1.0, scalar2=None,
                op0=ALU.mult), sSt)
            c.wait("scalar", sSt, nm)
            ex = c.op("scalar", lambda e: e.activation(
                out=GSC[:, 64:96], in_=GSC[:, 0:32], func=AF.Exp,
                bias=GSC[:, 33:34], scale=1.0, accum_out=GSC[:, 34:35]), sSq)
            c.wait("vector", sSq, ex)
            c.opS("vector", lambda e: e.reciprocal(out=GSC[:, 35:36], in_=GSC[:, 34:35]))
            c.opS("vector", lambda e: e.tensor_scalar(
                out=GSC[:, 0:32], in0=GSC[:, 64:96], scalar1=GSC[:, 35:36],
                scalar2=None, op0=ALU.mult))
            pairs = [(0, 1), (0, 2), (0, 3), (1, 2), (1, 3), (2, 3)]
            for pi, (a, b) in enumerate(pairs):
                c.op("vector", lambda e, pi=pi, a=a, b=b: e.tensor_tensor(
                    out=GSC[:, 96 + 8 * pi:96 + 8 * (pi + 1)],
                    in0=ap3(GSC, a, [[168, P], [4, 8]]),
                    in1=ap3(GSC, b, [[168, P], [4, 8]]), op=ALU.add))
            c.opS("vector", lambda e: e.nop())
            c.opS("vector", lambda e: e.tensor_reduce(
                out=GSC[:, 144:152], in_=ap3(GSC, 96, [[168, P], [1, 8], [8, 6]]),
                axis=AX.X, op=ALU.max))
            c.op("vector", lambda e: e.memset(GSC[:, 152:160], 0.0))
            for gi in range(8):
                c.opS("vector", lambda e, gi=gi: e.tensor_scalar(
                    out=GSC[:, 160:168], in0=GSC[:, 144:152],
                    scalar1=GSC[:, 144 + gi:145 + gi], scalar2=None, op0=ALU.is_lt))
                c.opS("vector", lambda e: e.tensor_tensor(
                    out=GSC[:, 152:160], in0=GSC[:, 152:160], in1=GSC[:, 160:168],
                    op=ALU.add))
            c.opS("vector", lambda e: e.tensor_scalar(
                out=GSC[:, 152:160], in0=GSC[:, 152:160], scalar1=3.5, scalar2=None,
                op0=ALU.is_le))
            c.opS("vector", lambda e: e.tensor_scalar(
                out=GSC[:, 64:96], in0=GSC[:, 0:32], scalar1=BIG, scalar2=None, op0=ALU.add))
            c.opS("vector", lambda e: e.tensor_tensor(
                out=GSC[:, 64:96], in0=GSC[:, 64:96],
                in1=ap3(GSC, 152, [[168, P], [1, 8], [0, 4]]), op=ALU.mult))
            c.opS("vector", lambda e: e.tensor_scalar(
                out=GSC[:, 64:96], in0=GSC[:, 64:96], scalar1=-BIG, scalar2=None, op0=ALU.add))
            c.op("vector", lambda e: e.memset(GSC[:, 96:128], 0.0))
            for r in range(4):
                c.opS("vector", lambda e: e.tensor_reduce(
                    out=GSC[:, 36:37], in_=GSC[:, 64:96], axis=AX.X, op=ALU.max))
                c.opS("vector", lambda e: e.tensor_scalar(
                    out=GSC[:, 128:160], in0=GSC[:, 64:96], scalar1=GSC[:, 36:37],
                    scalar2=None, op0=ALU.is_ge))
                c.opS("vector", lambda e: e.tensor_tensor(
                    out=GSC[:, 96:128], in0=GSC[:, 96:128], in1=GSC[:, 128:160], op=ALU.max))
                c.opS("vector", lambda e: e.scalar_tensor_tensor(
                    out=GSC[:, 64:96], in0=GSC[:, 128:160], scalar=-2 * BIG,
                    in1=GSC[:, 64:96], op0=ALU.mult, op1=ALU.add))
            c.op("vector", lambda e, i=i: e.tensor_tensor(
                out=CW[:, 32 * i:32 * (i + 1)], in0=GSC[:, 0:32], in1=GSC[:, 96:128],
                op=ALU.mult), sEp)

        accz = c.op("vector", lambda e: e.memset(ACC[:, :], 0.0), sEp)
        first_wait = [(sEp, accz)]
        if prev_ep_all is not None:
            first_wait.append((sEp, prev_ep_all))
        if pc_last_mm is not None:
            pass

        if not (g == 0):
            c.wait("gpsimd", sMM, pc_last_mm)
        c.dma("gpsimd", SMALLB[:, 0:1024], ap3(W["sb1"], 0, [[1024, 1], [1, 1024]]), sBE)
        c.dma("gpsimd", SMALLB[:, 1024:2048], ap3(W["sb3"], 0, [[1024, 1], [1, 1024]]), sBE)
        c.dma("gpsimd", SMALLB[:, 4096:5120], ap3(W["sb2"], 0, [[1024, 1], [1, 1024]]), sBE)
        phase_prod(g, W["sw1"], 0, 8, 0, 0, AF.Silu, extra_mm_wait=first_wait)
        ev3 = phase_prod(g, W["sw3"], 0, 8, 8, 1024, AF.Identity)
        hsv = phase_mult(0, 8, 16, 8, ev3)
        cur_ep_all = phase_out(g, "S", 16, 8, W["sw2"], 4096, lambda tau: 1.0, hsv)

        for ei in range(32):
            c.wait("gpsimd", sMM, pc_last_mm)
            c.dma("gpsimd", SMALLB[:, 0:512],
                  ap3(W["b1"], 512 * ei, [[512, 1], [1, 512]]), sBE)
            c.dma("gpsimd", SMALLB[:, 512:1024],
                  ap3(W["b3"], 512 * ei, [[512, 1], [1, 512]]), sBE)
            c.dma("gpsimd", SMALLB[:, 4096:5120],
                  ap3(W["b2"], 1024 * ei, [[1024, 1], [1, 1024]]), sBE)
            phase_prod(g, W["w1"][ei], 0, 4, 0, 0, AF.Silu,
                       extra_mm_wait=[(sEp, cur_ep_all)])
            ev3 = phase_prod(g, W["w3"][ei], 0, 4, 4, 512, AF.Identity)
            hsv = phase_mult(0, 4, 8, 4, ev3)
            cur_ep_all = phase_out(g, ei, 8, 4, W["w2"][ei], 4096,
                                   lambda tau, ei=ei: CW[:, 32 * tau + ei:32 * tau + ei + 1],
                                   hsv)
        prev_ep_all = cur_ep_all

        for tau in range(8):
            tt = g * GT + tau
            sl = (tau % 4) * D
            c.wait("sync", sTb, ln["tbv"][4 + (tau % 4)])
            if tau >= 4:
                c.wait("sync", sV, out_v[("vadd", g, tau - 4)])
            sX = sXp[tau % 4]
            xv = c.dma("sync", XT[:, sl:sl + D], xsrc_fn(tt), sX)
            c.waitall("vector", sX)
            c.waitall("vector", sOutp[tau % 2])
            vv = c.op("vector", lambda e, sl=sl, tau=tau: e.tensor_tensor(
                out=OUTT[:, (tau % 2) * D:(tau % 2 + 1) * D], in0=XT[:, sl:sl + D],
                in1=ACC[:, tau * 1024:(tau + 1) * 1024], op=ALU.add), sV)
            out_v[("vadd", g, tau)] = vv
            c.wait("sync", sV, vv)
            out_v[("st", g, tau)] = c.dma("sync", xdst_fn(tt),
                                          OUTT[:, (tau % 2) * D:(tau % 2 + 1) * D],
                                          sOutp[tau % 2])
    c.waitall("sync", sOutp[0])
    c.waitall("sync", sOutp[1])


def emit_l0aa(c, W, xsrc_fn, xdst_fn):
    nc = c.nc
    TTB, H1T, WA, XT, STAT, LNS, TBF, OUTT, MISC, ACC, BCAST = (c.bufs[k] for k in
        ("TTB", "H1T", "WA", "XT", "STAT", "LNS", "TBF", "OUTT", "MISC", "ACC", "BCAST"))
    SMALLF, ONESF, IDENT = (c.bufs[k] for k in ("SMALLF", "ONESF", "IDENT"))
    sMM, sEp, sAc, sV, sEv, sSq, sSt, sTb = (c.sem(k) for k in
        ("sMM", "sEp", "sAc", "sV", "sEv", "sSq", "sSt", "sTb"))
    sCst = c.sem("sCst")
    sWA = [c.sem("sWA0"), c.sem("sWA1")]
    sXp = [c.sem("sX0"), c.sem("sX1"), c.sem("sX2"), c.sem("sX3")]
    sOutp = [c.sem("sOutA"), c.sem("sOutB")]
    sBB = c.sem("sBB")
    sKW = c.sem("sKW")
    sKB = [c.sem("sKB0"), c.sem("sKB1")]
    sZW = c.sem("sZW")
    sZQ = [c.sem("sZQ0"), c.sem("sZQ1")]
    sOW = [c.sem("sOW0"), c.sem("sOW1")]
    AB = ACC.bitcast(BF16)
    QR = lambda: AB[:, 0:2048]
    KRb = lambda: AB[:, 2048:4096]
    UB = [(4096, 4608), (4608, 5120)]
    ZTS = [(5120, 5632), (5632, 6144)]
    KBUF = [(6144, 8192), (8192, 10240)]
    ZQs = [(10240, 10752), (10752, 11264)]
    OWS = [(11264, 12288), (12288, 13312)]
    S = 128
    invS = 1.0 / np.sqrt(S)
    TTB_PITCH = 16 * 8 * P

    kernbuf = nc.dram_tensor(c.ns + "l0aa_kern", [L, L], BF16)
    zbuf = nc.dram_tensor(c.ns + "l0aa_z", [L, L], BF16)
    CCS = c.sbuf("CCS", [P, L], BF16)
    SSC = c.sbuf("SSC", [P, L], BF16)

    sC = c.sem("sC")
    c.dma("gpsimd", CCS[:, :], W["cCCS"][:, :], sCst)
    c.dma("gpsimd", SSC[:, :], W["cSSC"][:, :], sCst)
    c.dma("sync", BCAST[:, 32:36], W["cGB"][:, :], sC)
    c.dma("sync", SMALLF[0:1, 0:1], ap3(W["g"], 0, [[1, 1], [1, 1]]), sC)
    c.waitall("tensor", sCst)
    c.waitall("vector", sCst)
    c.waitall("tensor", sC)
    c.waitall("vector", sC)
    c.opS("vector", lambda e: e.tensor_scalar(
        out=BCAST[:, 36:38], in0=BCAST[:, 32:34], scalar1=invS, scalar2=None, op0=ALU.mult))
    v = c.op("tensor", lambda e: e.matmul(out=c.bank(6)[:, 0:1], lhsT=ONESF[:, :],
                                          rhs=SMALLF[0:1, 0:1], start=True, stop=True), sMM)
    c.wait("vector", sMM, v)
    gc_v = c.op("vector", lambda e: e.tensor_copy(out=BCAST[:, 38:39], in_=c.bank(6)[:, 0:1]), sV)
    gcol = BCAST[:, 38:39]

    out_v = {}
    prev_b_done = None
    for b in range(NB):
        tiles = list(range(b * TPB, (b + 1) * TPB))
        xt_guards = None
        if b > 0:
            xt_guards = {s: (sV, out_v[(b - 1, 12 + s)]) for s in range(4)}
        c.wait("vector", sV, gc_v)
        ln = emit_ln(c, xsrc_fn, tiles, slot0=0, xt_guards=xt_guards,
                     xn_mode=True, gcol=gcol)

        seq = 0
        ev = {}
        chv = {}
        for ch in range(4):
            slot = ch % 2
            if ch >= 2:
                c.wait("gpsimd", sMM, chv[ch - 2])
            elif prev_b_done is not None and ch == 0:
                c.wait("gpsimd", sV, prev_b_done)
            c.dma("gpsimd",
                  WA[:, slot * 4096:(slot + 1) * 4096].rearrange("p (dt f) -> p dt f", f=512),
                  W["cTuv"][:, 2048 + 512 * ch:2048 + 512 * (ch + 1)]
                  .rearrange("(dt p) f -> p dt f", p=P), sWA[slot])
            c.waitall("tensor", sWA[slot])
            for j in range(16):
                c.wait("tensor", sEv, ln["evv"][j])
                if seq == 0 and prev_b_done is not None:
                    c.wait("tensor", sV, prev_b_done)
                bk = 4 + (seq % 2)
                if seq >= 2:
                    c.wait("tensor", sAc, ev[seq - 2])
                for dt in range(8):
                    fn_ = lambda e, j=j, dt=dt, slot=slot, bk=bk: e.matmul(
                        out=c.bank(bk),
                        lhsT=TTB[:, j * 1024 + dt * P:j * 1024 + (dt + 1) * P],
                        rhs=WA[:, slot * 4096 + dt * 512:slot * 4096 + (dt + 1) * 512],
                        start=(dt == 0), stop=(dt == 7))
                    if dt == 7:
                        mmv = c.op("tensor", fn_, sMM)
                    else:
                        c.op("tensor", fn_)
                c.wait("scalar", sMM, mmv)
                ev[seq] = c.op("scalar", lambda e, j=j, ch=ch, bk=bk: e.activation(
                    out=H1T[:, j * 2048 + ch * 512:j * 2048 + (ch + 1) * 512],
                    in_=c.bank(bk), func=AF.Silu), sAc)
                seq += 1
            chv[ch] = mmv

        c.wait("gpsimd", sMM, chv[2])
        c.dma("gpsimd",
              WA[:, 0:1024].rearrange("p (dt f) -> p dt f", f=P),
              W["cTuv"][:, 4096:4224].rearrange("(dt p) f -> p dt f", p=P), sWA[0])
        c.waitall("tensor", sWA[0])
        bev = {}
        c.waitall("scalar", sOutp[0])
        c.waitall("scalar", sOutp[1])
        for th in range(4):
            bk = 6 + (th % 2)
            if th == 0:
                c.wait("tensor", sV, gc_v)
            if th >= 2:
                c.wait("tensor", sAc, bev[th - 2])
            for dt in range(8):
                rhs = ap3(TTB, th * 4 * 1024 + dt * P, [[TTB_PITCH, P], [1024, 4], [1, P]])
                fn_ = lambda e, dt=dt, rhs=rhs, bk=bk: e.matmul(
                    out=c.bank(bk), lhsT=WA[:, dt * P:(dt + 1) * P], rhs=rhs,
                    start=(dt == 0), stop=(dt == 7))
                if dt == 7:
                    mmv = c.op("tensor", fn_, sMM)
                else:
                    c.op("tensor", fn_)
            c.wait("scalar", sMM, mmv)
            bev[th] = c.op("scalar", lambda e, th=th, bk=bk: e.activation(
                out=OUTT[:, th * 512:(th + 1) * 512], in_=c.bank(bk), func=AF.Silu), sAc)
        c.wait("vector", sAc, bev[3])
        MISCb = MISC.bitcast(BF16)
        sRP = c.sem("sRP")
        for qk in range(2):
            gci = 36 if qk == 0 else 34
            dst0 = (0 if qk == 0 else 2048)
            tsv = c.op("vector", lambda e, gci=gci: e.tensor_scalar(
                out=TBF[:, 0:2048], in0=OUTT[:, 0:2048],
                scalar1=BCAST[:, gci:gci + 1], scalar2=BCAST[:, gci + 1:gci + 2],
                op0=ALU.mult, op1=ALU.add), sV)
            c.wait("vector", sV, tsv)
            t1v = c.op("vector", lambda e, dst0=dst0: e.tensor_tensor(
                out=AB[:, dst0:dst0 + 2048], in0=TBF[:, 0:2048], in1=CCS[:, :],
                op=ALU.mult), sV)
            t2v = c.op("vector", lambda e: e.tensor_tensor(
                out=MISCb[:, 0:2048], in0=TBF[:, 0:2048], in1=SSC[:, :],
                op=ALU.mult), sV)
            c.wait("sync", sV, t2v)
            c.dma("sync", bass.AP(ACC.bitcast(BF16), 4096, [[16384, 64], [1, 2048]]),
                  bass.AP(ACC.bitcast(BF16), dst0 + 64 * 16384, [[16384, 64], [1, 2048]]), sRP)
            c.dma("sync", bass.AP(ACC.bitcast(BF16), 6144, [[16384, 64], [1, 2048]]),
                  bass.AP(MISC.bitcast(BF16), 64 * 2048, [[2048, 64], [1, 2048]]), sRP)
            c.waitall("vector", sRP)
            c.op("vector", lambda e, dst0=dst0: e.tensor_tensor(
                out=bass.AP(ACC.bitcast(BF16), dst0, [[16384, 64], [1, 2048]]),
                in0=bass.AP(ACC.bitcast(BF16), dst0, [[16384, 64], [1, 2048]]),
                in1=bass.AP(ACC.bitcast(BF16), 4096, [[16384, 64], [1, 2048]]),
                op=ALU.subtract))
            lastv = c.op("vector", lambda e, dst0=dst0: e.tensor_tensor(
                out=bass.AP(ACC.bitcast(BF16), dst0 + 64 * 16384, [[16384, 64], [1, 2048]]),
                in0=bass.AP(MISC.bitcast(BF16), 0, [[2048, 64], [1, 2048]]),
                in1=bass.AP(ACC.bitcast(BF16), 6144, [[16384, 64], [1, 2048]]),
                op=ALU.add), sV)
            c.wait("vector", sV, lastv)
        rope_done = lastv

        kv_ep = {}
        for i in range(16):
            if i == 0:
                c.wait("sync", sV, rope_done)
            else:
                c.wait("sync", sEp, kv_ep[i - 1])
            bbv = c.dma("sync", OUTT[:, :].rearrange("p (jt f) -> p jt f", f=P),
                        ap3(W["cBIAST"], 128 * i, [[L, P], [P * L, 16], [1, P]]), sBB)
            c.wait("tensor", sV, rope_done)
            if i == 0:
                c.wait("tensor", sEv, ln["evv"][15])
            if i >= 1:
                c.wait("tensor", sEp, kv_ep[i - 1])
            for jt in range(16):
                bk = jt // 4
                fn_ = lambda e, jt=jt, i=i, bk=bk: e.matmul(
                    out=c.bank(bk)[:, (jt % 4) * P:(jt % 4 + 1) * P],
                    lhsT=KRb()[:, jt * P:(jt + 1) * P],
                    rhs=QR()[:, i * P:(i + 1) * P], start=True, stop=True)
                if jt == 15:
                    mmv = c.op("tensor", fn_, sMM)
                else:
                    c.op("tensor", fn_)
            c.wait("vector", sMM, mmv)
            c.waitall("vector", sBB)
            if i >= 2:
                c.waitall("vector", sKW)
            for q4 in range(4):
                c.opS("vector", lambda e, q4=q4: e.scalar_tensor_tensor(
                    out=TBF[:, q4 * 512:(q4 + 1) * 512], in0=OUTT[:, q4 * 512:(q4 + 1) * 512],
                    scalar=invS, in1=c.bank(q4), op0=ALU.mult, op1=ALU.add))
                kv_ep[i] = c.op("vector", lambda e, q4=q4, i=i: e.scalar_tensor_tensor(
                    out=AB[:, KBUF[i % 2][0] + q4 * 512:KBUF[i % 2][0] + (q4 + 1) * 512],
                    in0=TBF[:, q4 * 512:(q4 + 1) * 512], scalar=0.0,
                    in1=TBF[:, q4 * 512:(q4 + 1) * 512], op0=ALU.max, op1=ALU.mult), sEp)
            c.wait("sync", sEp, kv_ep[i])
            c.dma("sync", ap3(kernbuf, 128 * i, [[L, P], [P * L, 16], [1, P]]),
                  AB[:, KBUF[i % 2][0]:KBUF[i % 2][0] + 2048]
                  .rearrange("p (jt f) -> p jt f", f=P), sKW)
        a2_last_mm = mmv

        zseq = 0
        zts_ev = {}
        useq = 0
        u_ev = {}
        for ch in range(4):
            slot = ch % 2
            if ch == 0:
                c.wait("gpsimd", sMM, a2_last_mm)
            if ch >= 2:
                c.wait("gpsimd", sMM, a3_ch_done[ch - 2])
            c.dma("gpsimd",
                  WA[:, slot * 4096:(slot + 1) * 4096].rearrange("p (dt f) -> p dt f", f=512),
                  W["cTuv"][:, 512 * ch:512 * (ch + 1)]
                  .rearrange("(dt p) f -> p dt f", p=P), sWA[slot])
            c.waitall("tensor", sWA[slot])
            for i in range(16):
                ks = i % 2
                if zseq >= 2:
                    c.wait("sync", sMM, kb_used[zseq - 2])
                kbv = c.dma("sync", AB[:, KBUF[ks][0]:KBUF[ks][0] + 2048]
                            .rearrange("p (jt f) -> p jt f", f=P),
                            ap3(kernbuf, 128 * i, [[L, P], [P * L, 16], [1, P]]), sKB[ks])
                bku = 0 + (useq % 2)
                if useq == 0:
                    c.wait("tensor", sEp, kv_ep[15])
                if useq >= 2:
                    c.wait("tensor", sAc, u_ev[useq - 2])
                for dt in range(8):
                    fn_ = lambda e, i=i, dt=dt, slot=slot, bku=bku: e.matmul(
                        out=c.bank(bku),
                        lhsT=TTB[:, i * 1024 + dt * P:i * 1024 + (dt + 1) * P],
                        rhs=WA[:, slot * 4096 + dt * 512:slot * 4096 + (dt + 1) * 512],
                        start=(dt == 0), stop=(dt == 7))
                    if dt == 7:
                        ummv = c.op("tensor", fn_, sMM)
                    else:
                        c.op("tensor", fn_)
                c.wait("scalar", sMM, ummv)
                u_ev[useq] = c.op("scalar", lambda e, ks=ks, bku=bku: e.activation(
                    out=AB[:, UB[ks][0]:UB[ks][1]], in_=c.bank(bku), func=AF.Silu), sAc)
                useq += 1
                bkv = 2 + (zseq % 2)
                c.waitall("tensor", sKB[ks])
                if zseq >= 2:
                    c.wait("tensor", sV, z_ev[zseq - 2])
                for jt in range(16):
                    fn_ = lambda e, jt=jt, ks=ks, ch=ch, bkv=bkv: e.matmul(
                        out=c.bank(bkv),
                        lhsT=AB[:, KBUF[ks][0] + jt * P:KBUF[ks][0] + (jt + 1) * P],
                        rhs=H1T[:, jt * 2048 + ch * 512:jt * 2048 + (ch + 1) * 512],
                        start=(jt == 0), stop=(jt == 15))
                    if jt == 15:
                        kvv = c.op("tensor", fn_, sMM)
                    else:
                        c.op("tensor", fn_)
                if zseq == 0:
                    kb_used, z_ev, a3_ch_done = {}, {}, {}
                kb_used[zseq] = kvv
                c.wait("vector", sMM, kvv)
                c.wait("vector", sAc, u_ev[useq - 1])
                zv = c.op("vector", lambda e, ks=ks, bkv=bkv: e.tensor_tensor(
                    out=TBF[:, 0:512], in0=AB[:, UB[ks][0]:UB[ks][1]],
                    in1=c.bank(bkv), op=ALU.mult), sV)
                z_ev[zseq] = zv
                bkt = 6 + (zseq % 2)
                c.wait("tensor", sV, zv)
                if zseq >= 2:
                    c.wait("tensor", sAc, zts_ev[zseq - 2])
                for q4 in range(4):
                    fn_ = lambda e, q4=q4, bkt=bkt: e.transpose(
                        out=c.bank(bkt)[:, q4 * P:(q4 + 1) * P],
                        in_=TBF[:, q4 * P + 0:512][:, 0:P] if False else bass.AP(TBF, q4 * P, [[2 * D, P], [1, P]]),
                        identity=IDENT[:, :])
                    if q4 == 3:
                        trv_ = c.op("tensor", fn_, sMM)
                    else:
                        c.op("tensor", fn_)
                c.wait("scalar", sMM, trv_)
                c.waitall("scalar", sZW)
                zts_ev[zseq] = c.op("scalar", lambda e, ks=ks, bkt=bkt: e.copy(
                    out=AB[:, ZTS[ks][0]:ZTS[ks][1]], in_=c.bank(bkt)), sAc)
                c.wait("sync", sAc, zts_ev[zseq])
                c.dma("sync", ap3(zbuf, (512 * ch) * L + 128 * i, [[L, P], [P * L, 4], [1, P]]),
                      AB[:, ZTS[ks][0]:ZTS[ks][1]].rearrange("p (q f) -> p q f", f=P), sZW)
                zseq += 1
            a3_ch_done[ch] = kvv

        c.wait("sync", sEp, kv_ep[15])
        rsl = c.dma("sync", MISC[0:1, 0:1024],
                    ap3(W["res_scale"], 0, [[1024, 1], [1, 1024]]), sC)
        c.waitall("tensor", sC)
        for hh in range(2):
            if hh == 0:
                c.wait("tensor", sAc, zts_ev[zseq - 1])
            v2 = c.op("tensor", lambda e, hh=hh: e.matmul(
                out=c.bank(6 + hh), lhsT=ONESF[:, :],
                rhs=MISC[0:1, hh * 512:(hh + 1) * 512], start=True, stop=True), sMM)
        c.wait("scalar", sMM, v2)
        rs_v = c.op("scalar", lambda e: e.copy(out=TBF[:, 0:1024], in_=c.bank(6, 2)), sAc)
        for iq in range(4):
            owv = {}
            zqv = {}
            for et in range(16):
                ws = et % 2
                if et >= 2:
                    c.wait("gpsimd", sMM, ow_used[et - 2])
                elif iq > 0:
                    c.wait("gpsimd", sMM, pc_done)
                c.dma("gpsimd", AB[:, OWS[ws][0]:OWS[ws][1]],
                      W["cTow"][128 * et:128 * (et + 1), :], sOW[ws])
                c.dma("gpsimd", AB[:, ZQs[ws][0]:ZQs[ws][1]]
                      .rearrange("p (q f) -> p q f", f=P),
                      ap3(zbuf, (128 * et) * L + 512 * iq, [[L, P], [P, 4], [1, P]]),
                      sZQ[ws])
                c.waitall("tensor", sOW[ws])
                c.waitall("tensor", sZQ[ws])
                if et == 0:
                    if iq > 0:
                        for bb in (0, 2, 4, 6):
                            c.wait("tensor", sV, out_ep[bb])
                    else:
                        c.wait("tensor", sAc, rs_v)
                        c.wait("tensor", sV, z_ev[zseq - 1])
                        c.wait("tensor", sAc, u_ev[useq - 1])
                for il in range(4):
                    for oh in range(2):
                        fn_ = lambda e, il=il, oh=oh, ws=ws, et=et: e.matmul(
                            out=c.bank(2 * il + oh),
                            lhsT=AB[:, ZQs[ws][0] + il * P:ZQs[ws][0] + (il + 1) * P],
                            rhs=AB[:, OWS[ws][0] + oh * 512:OWS[ws][0] + (oh + 1) * 512],
                            start=(et == 0), stop=(et == 15))
                        if il == 3 and oh == 1:
                            mmv = c.op("tensor", fn_, sMM)
                        else:
                            c.op("tensor", fn_)
                if et == 0:
                    ow_used = {}
                ow_used[et] = mmv
            pc_done = mmv
            c.wait("vector", sMM, pc_done)
            if iq == 0:
                c.wait("vector", sAc, rs_v)
            out_ep = {}
            for il in range(4):
                i = iq * 4 + il
                tt = b * TPB + i
                sl = (il % 4) * D
                sX = sXp[il]
                if iq > 0:
                    c.wait("sync", sV, out_v[(b, (iq - 1) * 4 + il)])
                else:
                    c.wait("sync", sTb, ln["tbv"][12 + il])
                xv = c.dma("sync", XT[:, sl:sl + D], xsrc_fn(tt), sX)
                c.waitall("vector", sX)
                c.opS("vector", lambda e, sl=sl, tt=tt: e.tensor_scalar(
                    out=OUTT[:, (il % 2) * D:(il % 2 + 1) * D] if False else MISC[:, 0:1024],
                    in0=XT[:, sl:sl + D],
                    scalar1=LNS[:, 4 * tt:4 * tt + 1], scalar2=LNS[:, 4 * tt + 1:4 * tt + 2],
                    op0=ALU.add, op1=ALU.mult))
                c.opS("vector", lambda e: e.tensor_tensor(
                    out=MISC[:, 0:1024], in0=MISC[:, 0:1024], in1=TBF[:, 0:1024], op=ALU.mult))
                c.opS("vector", lambda e, sl=sl: e.tensor_tensor(
                    out=MISC[:, 0:1024], in0=MISC[:, 0:1024], in1=XT[:, sl:sl + D], op=ALU.add))
                c.waitall("vector", sOutp[il % 2])
                vv = c.op("vector", lambda e, il=il: e.tensor_tensor(
                    out=OUTT[:, (il % 2) * D:(il % 2 + 1) * D], in0=MISC[:, 0:1024],
                    in1=c.bank(2 * il, 2), op=ALU.add), sV)
                out_ep[2 * il] = vv
                out_ep[2 * il + 1] = vv
                out_v[(b, i)] = vv
                c.wait("sync", sV, vv)
                c.dma("sync", xdst_fn(tt), OUTT[:, (il % 2) * D:(il % 2 + 1) * D], sOutp[il % 2])
        prev_b_done = out_v[(b, 15)]
    c.waitall("sync", sOutp[0])
    c.waitall("sync", sOutp[1])
    c.waitall("sync", sKW)
    c.waitall("sync", sZW)


def emit_l0ab(c, W, xsrc_fn, xdst_fn):
    nc = c.nc
    TTB, H1T, WA, XT, STAT, LNS, TBF, OUTT, MISC, ACC, GSC = (c.bufs[k] for k in
        ("TTB", "H1T", "WA", "XT", "STAT", "LNS", "TBF", "OUTT", "MISC", "ACC", "GSC"))
    SMALLF, SMALLB, ONESB, ONESF, BCAST, IDENT = (c.bufs[k] for k in
        ("SMALLF", "SMALLB", "ONESB", "ONESF", "BCAST", "IDENT"))
    sMM, sEp, sAc, sV, sEv, sSq, sSt, sTb = (c.sem(k) for k in
        ("sMM", "sEp", "sAc", "sV", "sEv", "sSq", "sSt", "sTb"))
    sRs2 = c.sem("sRs2")
    sCst = c.sem("sCst")
    sC = c.sem("sC")
    sWA = [c.sem("sWA0"), c.sem("sWA1")]
    sXp = [c.sem("sX0"), c.sem("sX1"), c.sem("sX2"), c.sem("sX3")]
    sOutp = [c.sem("sOutA"), c.sem("sOutB")]
    sDX = c.sem("sDX")
    WS = [4.0, 16.0, 64.0, 256.0]
    NBIG = -30000.0
    TTB_PITCH = 16 * 8 * P
    PS = c.bufs["PSALL"]

    ONESC = c.sbuf("ONESC", [P, 1], BF16)
    dxbounce = nc.dram_tensor("l0ab_dx", [16, P], F32)
    sCM = c.sem("sCM")

    c.dma("gpsimd", SMALLB[:, 0:3072], ap3(W["qkv_b"], 0, [[3072, 1], [1, 3072]]), sCst)
    c.dma("sync", SMALLF[0:1, 0:1], ap3(W["lin2_b"], 0, [[1, 1], [1, 1]]), sC)
    c.dma("sync", MISC[0:1, 0:256], ap3(W["lin2_w"], 0, [[256, 1], [1, 256]]), sC)
    c.op("vector", lambda e: e.memset(ONESC[:, :], 1.0))
    for gg_ in range(4):
        c.op("vector", lambda e, gg_=gg_: e.memset(BCAST[:, 32 + gg_:33 + gg_], WS[gg_]))
    nbig_v = c.op("vector", lambda e: e.memset(
        ap3(H1T.bitcast(mybir.dt.float16), 31808, [[32768, P], [1, 1]]), NBIG), sV)
    c.waitall("tensor", sC)
    c.waitall("tensor", sCst)
    v = c.op("tensor", lambda e: e.matmul(out=c.bank(6)[:, 0:256], lhsT=ONESF[:, :],
                                          rhs=MISC[0:1, 0:256], start=True, stop=True), sMM)
    v2 = c.op("tensor", lambda e: e.matmul(out=c.bank(7)[:, 0:1], lhsT=ONESF[:, :],
                                           rhs=SMALLF[0:1, 0:1], start=True, stop=True), sMM)
    c.wait("scalar", sMM, v2)
    l2b_v = c.op("scalar", lambda e: e.copy(out=MISC[:, 256:512], in_=c.bank(6)[:, 0:256]), sAc)
    l2c_v = c.op("scalar", lambda e: e.copy(out=BCAST[:, 39:40], in_=c.bank(7)[:, 0:1]), sAc)

    VOFF, QOFF, KOFF, POFF, SOFF, UOFF, ROFF = 0, 16448, 20544, 24640, 26688, 28736, 29760
    H1T16 = H1T.bitcast(mybir.dt.float16)
    H1TU8 = H1T.bitcast(U8)
    AB = ACC.bitcast(BF16)

    out_v = {}
    prev_b_done = None
    prod_ev = {4: None, 5: None}
    sel_last = exp_last = av_last = norm_last = dxb_last = None
    for b in range(NB):
        tiles = list(range(b * TPB, (b + 1) * TPB))
        xt_guards = None
        if b > 0:
            xt_guards = {s: (sV, out_v[(b - 1, 12 + s)]) for s in range(4)}
        ln = emit_ln(c, xsrc_fn, tiles, slot0=0, xt_guards=xt_guards)

        seq = 0
        ev = {}
        wa_last = {0: None, 1: None}
        for ch in range(2):
            slot = ch % 2
            if prev_b_done is not None and ch == 0:
                c.wait("gpsimd", sV, prev_b_done)
            if wa_last[slot] is not None:
                c.wait("gpsimd", sMM, wa_last[slot])
            c.dma("gpsimd",
                  WA[:, slot * 4096:(slot + 1) * 4096]
                  .rearrange("p (dt f) -> p dt f", f=512),
                  W["cTqkv"][:, 2048 + 512 * ch:2048 + 512 * (ch + 1)]
                  .rearrange("(dt p) f -> p dt f", p=P), sWA[slot])
            c.waitall("tensor", sWA[slot])
            for j in range(16):
                c.wait("tensor", sEv, ln["evv"][j])
                if seq == 0 and prev_b_done is not None:
                    c.wait("tensor", sV, prev_b_done)
                bk = 4 + (seq % 2)
                if seq >= 2:
                    c.wait("tensor", sAc, ev[seq - 2])
                c.op("tensor", lambda e, ch=ch, bk=bk: e.matmul(
                    out=c.bank(bk), lhsT=ONESB[0:1, 0:P],
                    rhs=SMALLB[0:1, 2048 + 512 * ch:2048 + 512 * (ch + 1)],
                    start=True, stop=False))
                for dt in range(8):
                    fn_ = lambda e, j=j, dt=dt, slot=slot, bk=bk: e.matmul(
                        out=c.bank(bk),
                        lhsT=TTB[:, j * 1024 + dt * P:j * 1024 + (dt + 1) * P],
                        rhs=WA[:, slot * 4096 + dt * 512:slot * 4096 + (dt + 1) * 512],
                        start=False, stop=(dt == 7))
                    if dt == 7:
                        mmv = c.op("tensor", fn_, sMM)
                    else:
                        c.op("tensor", fn_)
                c.wait("scalar", sMM, mmv)
                dst = ap3(H1T, VOFF + j * 1028 + (2 * ch) * 257, [[32768, P], [257, 2], [1, 256]])
                srcp = ap3(PS, bk * 512, [[4096, P], [256, 2], [1, 256]])
                ev[seq] = c.op("scalar", lambda e, dst=dst, srcp=srcp: e.copy(
                    out=dst, in_=srcp), sAc)
                seq += 1
            wa_last[slot] = mmv
        ones_done = None
        for j in range(16):
            c.wait("vector", sAc, ev[16 + j])
            ones_done = c.op("vector", lambda e, j=j: e.memset(
                ap3(H1T, VOFF + j * 1028 + 256, [[32768, P], [257, 4], [1, 1]]), 1.0), sV)

        c.wait("gpsimd", sMM, wa_last[0])
        c.dma("gpsimd", WA[:, 0:2048].rearrange("p (dt f) -> p dt f", f=256),
              W["cToff"][:, :].rearrange("(dt p) f -> p dt f", p=P), sWA[0])
        c.waitall("tensor", sWA[0])
        dxv = {}
        rts = {}
        for t in range(16):
            if t >= 1:
                c.wait("scalar", sMM, rts[t - 1])
            rv = c.op("scalar", lambda e, t=t: e.activation(
                out=H1T[:, ROFF:ROFF + 1024], in_=TTB[:, t * 1024:(t + 1) * 1024],
                func=AF.Relu), sAc)
            c.wait("tensor", sAc, rv)
            bk = 6 + (t % 2)
            if t >= 2:
                c.wait("tensor", sSt, dxv[t - 2])
            for dt in range(8):
                fn_ = lambda e, dt=dt, bk=bk: e.matmul(
                    out=c.bank(bk)[:, 0:256],
                    lhsT=H1T[:, ROFF + dt * P:ROFF + (dt + 1) * P],
                    rhs=WA[:, dt * 256:(dt + 1) * 256],
                    start=(dt == 0), stop=(dt == 7))
                if dt == 7:
                    mmv = c.op("tensor", fn_, sMM)
                else:
                    c.op("tensor", fn_)
            rts[t] = mmv
            c.wait("vector", sMM, mmv)
            if t == 0:
                c.wait("vector", sAc, l2b_v)
            c.opS("vector", lambda e, bk=bk: e.tensor_tensor(
                out=MISC[:, 512:768], in0=c.bank(bk)[:, 0:256], in1=MISC[:, 256:512],
                op=ALU.mult))
            c.opS("vector", lambda e: e.tensor_reduce(
                out=STAT[:, 20:21], in_=MISC[:, 512:768], axis=AX.X, op=ALU.add))
            sv = c.op("vector", lambda e: e.tensor_tensor(
                out=STAT[:, 21:22], in0=STAT[:, 20:21], in1=BCAST[:, 39:40],
                op=ALU.add), sSt)
            c.wait("scalar", sSt, sv)
            sg = c.op("scalar", lambda e: e.activation(
                out=STAT[:, 22:23], in_=STAT[:, 21:22], func=AF.Sigmoid), sRs2)
            c.wait("vector", sRs2, sg)
            dxv[t] = c.op("vector", lambda e, t=t: e.tensor_scalar(
                out=GSC[:, t:t + 1], in0=STAT[:, 22:23],
                scalar1=float(L), scalar2=None, op0=ALU.mult), sSt)
        wa_last[0] = rts[15]
        c.wait("tensor", sSt, dxv[15])
        trv_ = c.op("tensor", lambda e: e.transpose(
            out=ap3(PS, 0, [[4096, 16], [1, P]]),
            in_=GSC[:, 0:16], identity=IDENT[:, :]), sMM)
        c.wait("scalar", sMM, trv_)
        dxe = c.op("scalar", lambda e: e.copy(
            out=MISC[0:16, 768:896], in_=ap3(PS, 0, [[4096, 16], [1, P]])), sAc)
        c.wait("sync", sAc, dxe)
        c.dma("sync", dxbounce[:, :], MISC[0:16, 768:896], sDX)
        c.waitall("sync", sDX)
        c.dma("sync", ap3(OUTT, 0, [[2048, 1], [1, 2048]]),
              ap3(dxbounce, 0, [[2048, 1], [1, 2048]]), sDX)

        for g in range(4):
            for qk in range(2):
                base = (0 if qk == 0 else 1024) + 256 * g
                hoff = QOFF if qk == 0 else KOFF
                slot = qk
                if wa_last[slot] is not None:
                    c.wait("gpsimd", sMM, wa_last[slot])
                c.dma("gpsimd", WA[:, slot * 4096:slot * 4096 + 2048]
                      .rearrange("p (dt f) -> p dt f", f=256),
                      W["cTqkv"][:, base:base + 256]
                      .rearrange("(dt p) f -> p dt f", p=P), sWA[slot])
                c.waitall("tensor", sWA[slot])
                if g == 0 and qk == 0:
                    c.wait("tensor", sV, ones_done)
                if qk == 0 and norm_last is not None:
                    c.wait("tensor", sEp, norm_last)
                    c.wait("tensor", sAc, dxb_last)
                for ft in range(2):
                    for th in range(4):
                        bk = 4 + (th % 2)
                        if prod_ev[bk] is not None:
                            c.wait("tensor", sAc, prod_ev[bk])
                        c.op("tensor", lambda e, base=base, ft=ft, bk=bk: e.matmul(
                            out=c.bank(bk), lhsT=SMALLB[0:1, base + 128 * ft:base + 128 * (ft + 1)],
                            rhs=ONESB[0:1, 0:512], start=True, stop=False))
                        for dt in range(8):
                            off = (th * 4) * 1024 + dt * P
                            rhs = ap3(TTB, off, [[TTB_PITCH, P], [1024, 4], [1, P]])
                            fn_ = lambda e, slot=slot, dt=dt, ft=ft, rhs=rhs, bk=bk: e.matmul(
                                out=c.bank(bk),
                                lhsT=WA[:, slot * 4096 + dt * 256 + 128 * ft:
                                        slot * 4096 + dt * 256 + 128 * (ft + 1)],
                                rhs=rhs, start=False, stop=(dt == 7))
                            if dt == 7:
                                mmv = c.op("tensor", fn_, sMM)
                            else:
                                c.op("tensor", fn_)
                        c.wait("scalar", sMM, mmv)
                        prod_ev[bk] = c.op("scalar",
                            lambda e, hoff=hoff, ft=ft, th=th, bk=bk: e.copy(
                                out=ap3(H1T, hoff + ft * 2048 + th * 512, [[32768, P], [1, 512]]),
                                in_=c.bank(bk)), sAc)
                wa_last[slot] = mmv
            qkprod_done = prod_ev[bk]

            c.wait("scalar", sAc, qkprod_done)
            if av_last is not None:
                c.wait("scalar", sMM, av_last)
            nmv = {}
            for qk in range(2):
                hoff = QOFF if qk == 0 else KOFF
                sqv_ = c.op("scalar", lambda e, hoff=hoff: e.activation(
                    out=H1T[:, POFF:POFF + 4096], in_=H1T[:, hoff:hoff + 4096],
                    func=AF.Square), sAc)
                c.wait("tensor", sAc, sqv_)
                for half in range(2):
                    if not (g == 0 and qk == 0 and half == 0):
                        c.wait("tensor", sSt, nm_red)
                    for nh in range(2):
                        for ft in range(2):
                            fn_ = lambda e, ft=ft, nh=nh, half=half: e.matmul(
                                out=ap3(PS, 3072 + nh * 512, [[4096, 1], [1, 512]]),
                                lhsT=ONESC[:, 0:1],
                                rhs=H1T[:, POFF + ft * 2048 + half * 1024 + nh * 512:
                                        POFF + ft * 2048 + half * 1024 + (nh + 1) * 512],
                                start=(ft == 0), stop=(ft == 1))
                            if nh == 1 and ft == 1:
                                mmv = c.op("tensor", fn_, sMM)
                            else:
                                c.op("tensor", fn_)
                    c.wait("vector", sMM, mmv)
                    nm_red = c.opS("vector", lambda e: e.tensor_reduce(
                        out=STAT[0:1, 29:30], in_=ap3(PS, 3072, [[4096, 1], [1, 1024]]),
                        axis=AX.X, op=ALU.max))
                    if half == 0:
                        nm_red = c.op("vector", lambda e, qk=qk: e.tensor_copy(
                            out=STAT[0:1, 24 + qk:25 + qk], in_=STAT[0:1, 29:30]), sSt)
                    else:
                        nm_red = c.op("vector", lambda e, qk=qk: e.tensor_tensor(
                            out=STAT[0:1, 24 + qk:25 + qk], in0=STAT[0:1, 24 + qk:25 + qk],
                            in1=STAT[0:1, 29:30], op=ALU.max), sSt)
                nmv[qk] = nm_red
            c.wait("vector", sSt, nmv[1])
            pv = c.op("vector", lambda e: e.tensor_tensor(
                out=STAT[0:1, 26:27], in0=STAT[0:1, 24:25], in1=STAT[0:1, 25:26],
                op=ALU.mult), sSt)
            c.wait("scalar", sSt, pv)
            sqb = c.op("scalar", lambda e: e.activation(
                out=STAT[0:1, 27:28], in_=STAT[0:1, 26:27], func=AF.Sqrt), sRs2)
            c.wait("vector", sRs2, sqb)
            ngv = c.op("vector", lambda e: e.tensor_scalar(
                out=STAT[0:1, 26:27], in0=STAT[0:1, 27:28],
                scalar1=-2.0 / float(np.sqrt(L)), scalar2=None, op0=ALU.mult), sSt)
            c.wait("tensor", sSt, ngv)
            bmv = c.op("tensor", lambda e: e.matmul(
                out=c.bank(5)[:, 0:1], lhsT=ONESF[:, :], rhs=STAT[0:1, 26:27],
                start=True, stop=True), sMM)
            c.wait("scalar", sMM, bmv)
            ngb = c.op("scalar", lambda e: e.copy(out=BCAST[:, 37:38], in_=c.bank(5)[:, 0:1]), sAc)

            for i in range(16):
                c.wait("tensor", sAc, ngb if (i == 0) else dxb_last)
                dmv = c.op("tensor", lambda e, i=i: e.matmul(
                    out=c.bank(5)[:, 0:P], lhsT=ONESF[:, :],
                    rhs=ap3(OUTT, i * P, [[2048, 1], [1, P]]), start=True, stop=True), sMM)
                c.wait("scalar", sMM, dmv)
                if g == 0 and i == 0 and b == 0:
                    c.wait("scalar", sV, nbig_v)
                dxb_last = c.op("scalar", lambda e, g=g: e.activation(
                    out=MISC[:, 0:128], in_=c.bank(5)[:, 0:P], func=AF.Identity,
                    bias=BCAST[:, 32 + g:33 + g], scale=1.0), sAc)
                if sel_last is not None:
                    c.wait("tensor", sEp, sel_last)
                for jt in range(16):
                    for ft in range(2):
                        fn_ = lambda e, jt=jt, ft=ft, i=i: e.matmul(
                            out=ap3(PS, (jt // 4) * 512 + (jt % 4) * P, [[4096, P], [1, P]]),
                            lhsT=H1T[:, KOFF + ft * 2048 + jt * P:KOFF + ft * 2048 + (jt + 1) * P],
                            rhs=H1T[:, QOFF + ft * 2048 + i * P:QOFF + ft * 2048 + (i + 1) * P],
                            start=(ft == 0), stop=(ft == 1))
                        if jt == 15 and ft == 1:
                            smv = c.op("tensor", fn_, sMM)
                        else:
                            c.op("tensor", fn_)
                if sel_last is not None:
                    c.wait("sync", sEp, sel_last)
                c.dma("sync", ap3(H1T16, 29760, [[32768, P], [1, 2048]]),
                      W["cCST"][i], sCM)
                c.wait("vector", sAc, dxb_last)
                c.waitall("vector", sCM)
                cview = ap3(H1T16, 29760, [[32768, P], [1, 2048]])
                dxb = ap3(MISC, 0, [[1024, P], [0, 16], [1, P]])
                c.opS("vector", lambda e, cview=cview, dxb=dxb: e.tensor_tensor(
                    out=ap3(H1TU8, 2 * UOFF, [[65536, P], [1, 2048]]),
                    in0=cview, in1=dxb, op=ALU.is_le))
                c.wait("vector", sMM, smv)
                if exp_last is not None:
                    c.wait("vector", sSq, exp_last)
                if g == 0 and i == 0 and b == 0:
                    c.wait("vector", sV, nbig_v)
                sel_last = c.op("vector", lambda e: e.select(
                    out=ap3(H1T16, SOFF, [[32768, P], [1, 2048]]),
                    mask=ap3(H1TU8, 2 * UOFF, [[65536, P], [1, 2048]]),
                    on_true=ap3(PS, 0, [[4096, P], [1, 2048]]),
                    on_false=ap3(H1T16, 31808, [[32768, P], [0, 2048]])), sEp)
                c.wait("scalar", sEp, sel_last)
                if av_last is not None:
                    c.wait("scalar", sMM, av_last)
                if i == 0:
                    c.wait("scalar", sMM, bmv)
                exp_last = c.op("scalar", lambda e: e.activation(
                    out=H1T[:, POFF:POFF + 2048], in_=ap3(H1T16, SOFF, [[32768, P], [1, 2048]]),
                    func=AF.Exp, bias=BCAST[:, 37:38], scale=2.0 / float(np.sqrt(L))), sSq)
                c.wait("tensor", sSq, exp_last)
                if norm_last is not None:
                    c.wait("tensor", sEp, norm_last)
                for jt in range(16):
                    fn_ = lambda e, jt=jt, g=g: e.matmul(
                        out=c.bank(4)[:, 0:257],
                        lhsT=H1T[:, POFF + jt * P:POFF + (jt + 1) * P],
                        rhs=H1T[:, VOFF + jt * 1028 + 257 * g:VOFF + jt * 1028 + 257 * (g + 1)],
                        start=(jt == 0), stop=(jt == 15))
                    if jt == 15:
                        avv = c.op("tensor", fn_, sMM)
                    else:
                        c.op("tensor", fn_)
                av_last = avv
                c.wait("vector", sMM, avv)
                c.opS("vector", lambda e: e.reciprocal(
                    out=STAT[:, 28:29], in_=c.bank(4)[:, 256:257]))
                norm_last = c.op("vector", lambda e, i=i, g=g: e.tensor_scalar(
                    out=AB[:, i * 1024 + 256 * g:i * 1024 + 256 * (g + 1)],
                    in0=c.bank(4)[:, 0:256], scalar1=STAT[:, 28:29], scalar2=None,
                    op0=ALU.mult), sEp)

        for tau in range(16):
            tt = b * TPB + tau
            sl = (tau % 4) * D
            c.wait("sync", sTb, ln["tbv"][12 + (tau % 4)])
            if tau >= 4:
                c.wait("sync", sV, out_v[(b, tau - 4)])
            sX = sXp[tau % 4]
            xv = c.dma("sync", XT[:, sl:sl + D], xsrc_fn(tt), sX)
            c.waitall("vector", sX)
            if tau == 0:
                c.wait("vector", sEp, norm_last)
            c.waitall("vector", sOutp[tau % 2])
            vv = c.op("vector", lambda e, sl=sl, tau=tau: e.tensor_tensor(
                out=OUTT[:, (tau % 2) * D:(tau % 2 + 1) * D], in0=XT[:, sl:sl + D],
                in1=AB[:, tau * 1024:(tau + 1) * 1024], op=ALU.add), sV)
            out_v[(b, tau)] = vv
            c.wait("sync", sV, vv)
            c.dma("sync", xdst_fn(tt), OUTT[:, (tau % 2) * D:(tau % 2 + 1) * D], sOutp[tau % 2])
        prev_b_done = out_v[(b, 15)]
    c.waitall("sync", sOutp[0])
    c.waitall("sync", sOutp[1])
    c.waitall("sync", sDX)


def emit_l1ab(c, W, xsrc_fn, xdst_fn):
    nc = c.nc
    TTB, H1T, WA, XT, STAT, LNS, TBF, OUTT, MISC, ACC = (c.bufs[k] for k in
        ("TTB", "H1T", "WA", "XT", "STAT", "LNS", "TBF", "OUTT", "MISC", "ACC"))
    SMALLF, SMALLB, ONESB, ONESF, BCAST, IDENT = (c.bufs[k] for k in
        ("SMALLF", "SMALLB", "ONESB", "ONESF", "BCAST", "IDENT"))
    ONESC = c.sbuf("ONESC", [P, 1], BF16)
    sMM, sEp, sAc, sV, sEv, sSq, sSt, sTb = (c.sem(k) for k in
        ("sMM", "sEp", "sAc", "sV", "sEv", "sSq", "sSt", "sTb"))
    sRs2 = c.sem("sRs2")
    sCst = c.sem("sCst")
    sC = c.sem("sC")
    sWA = [c.sem("sWA0"), c.sem("sWA1")]
    sXp = [c.sem("sX0"), c.sem("sX1"), c.sem("sX2"), c.sem("sX3")]
    sOutp = [c.sem("sOutA"), c.sem("sOutB")]
    sQW = c.sem("sKW")
    sQR = [c.sem("sKB0"), c.sem("sKB1")]
    TTB_PITCH = 16 * 8 * P
    PS = c.bufs["PSALL"]
    AB = ACC.bitcast(BF16)
    invD = 1.0 / float(np.sqrt(D))
    qbuf = nc.dram_tensor(c.ns + "l1ab_q", [D, L], BF16)

    c.op("vector", lambda e: e.memset(ONESC[:, :], 1.0))
    c.dma("gpsimd", SMALLB[:, 0:1024], ap3(W["bff"], 0, [[1024, 1], [1, 1024]]), sCst)

    VOFF, QSOFF, PTOFF = 0, 16640, 20736

    out_v = {}
    prev_b_done = None
    prod_ev = {4: None, 5: None}
    exp_last = av_last = norm_last = None
    bce_prev = None
    for b in range(NB):
        tiles = list(range(b * TPB, (b + 1) * TPB))
        xt_guards = None
        if b > 0:
            xt_guards = {s: (sV, out_v[(b - 1, 12 + s)]) for s in range(4)}
        ln = emit_ln(c, xsrc_fn, tiles, slot0=0, xt_guards=xt_guards)

        for qk in range(2):
            wsrc = W["cTk"] if qk == 0 else W["cTq"]
            slot = qk
            if prev_b_done is not None or qk == 1:
                c.wait("gpsimd", sMM, wa_qk if qk == 1 else prev_wa)
            c.dma("gpsimd", WA[:, slot * 4096:(slot + 1) * 4096]
                  .rearrange("p (dt f) -> p dt f", f=512),
                  wsrc[:, 0:512].rearrange("(dt p) f -> p dt f", p=P), sWA[slot])
            for ch in range(2):
                if ch == 1:
                    c.wait("gpsimd", sMM, mmv)
                    c.dma("gpsimd", WA[:, slot * 4096:(slot + 1) * 4096]
                          .rearrange("p (dt f) -> p dt f", f=512),
                          wsrc[:, 512:1024].rearrange("(dt p) f -> p dt f", p=P), sWA[slot])
                c.waitall("tensor", sWA[slot])
                for ft in range(4):
                    fglob = ch * 4 + ft
                    for th in range(4):
                        bk = 4 + (th % 2)
                        if prod_ev[bk] is not None:
                            c.wait("tensor", sAc, prod_ev[bk])
                        if qk == 0 and ch == 0 and ft == 0 and th == 0 and prev_b_done is not None:
                            c.wait("tensor", sV, prev_b_done)
                        for dt in range(8):
                            off = (th * 4) * 1024 + dt * P
                            rhs = ap3(TTB, off, [[TTB_PITCH, P], [1024, 4], [1, P]])
                            fn_ = lambda e, slot=slot, dt=dt, ft=ft, rhs=rhs, bk=bk: e.matmul(
                                out=c.bank(bk),
                                lhsT=WA[:, slot * 4096 + dt * 512 + 128 * ft:
                                        slot * 4096 + dt * 512 + 128 * (ft + 1)],
                                rhs=rhs, start=(dt == 0), stop=(dt == 7))
                            if dt == 7:
                                mmv = c.op("tensor", fn_, sMM)
                            else:
                                c.op("tensor", fn_)
                        c.wait("scalar", sMM, mmv)
                        if qk == 0:
                            prod_ev[bk] = c.op("scalar", lambda e, fglob=fglob, th=th, bk=bk: e.copy(
                                out=AB[:, fglob * 2048 + th * 512:fglob * 2048 + (th + 1) * 512],
                                in_=c.bank(bk)), sAc)
                        else:
                            pe = c.op("scalar", lambda e, th=th, bk=bk: e.copy(
                                out=ap3(H1T, QSOFF + th * 512, [[32768, P], [1, 512]]),
                                in_=c.bank(bk)), sAc)
                            prod_ev[bk] = pe
                            c.wait("sync", sAc, pe)
                            c.dma("sync", ap3(qbuf, fglob * P * L + th * 512,
                                              [[L, P], [1, 512]]),
                                  ap3(H1T, QSOFF + th * 512, [[32768, P], [1, 512]]), sQW)
            if qk == 0:
                wa_qk = mmv
            else:
                prev_wa = mmv
        qk_done = mmv
        seq = 0
        ev = {}
        c.wait("gpsimd", sMM, wa_qk)
        c.dma("gpsimd", WA[:, 0:4096].rearrange("p (dt f) -> p dt f", f=512),
              W["cTv"][:, 0:512].rearrange("(dt p) f -> p dt f", p=P), sWA[0])
        for ch in range(2):
            if ch == 1:
                c.wait("gpsimd", sMM, mmv)
                c.dma("gpsimd", WA[:, 0:4096].rearrange("p (dt f) -> p dt f", f=512),
                      W["cTv"][:, 512:1024].rearrange("(dt p) f -> p dt f", p=P), sWA[0])
            c.waitall("tensor", sWA[0])
            for j in range(16):
                bk = 4 + (seq % 2)
                if prod_ev[bk] is not None:
                    c.wait("tensor", sAc, prod_ev[bk])
                for dt in range(8):
                    fn_ = lambda e, j=j, dt=dt, bk=bk: e.matmul(
                        out=c.bank(bk),
                        lhsT=TTB[:, j * 1024 + dt * P:j * 1024 + (dt + 1) * P],
                        rhs=WA[:, dt * 512:(dt + 1) * 512],
                        start=(dt == 0), stop=(dt == 7))
                    if dt == 7:
                        mmv = c.op("tensor", fn_, sMM)
                    else:
                        c.op("tensor", fn_)
                c.wait("scalar", sMM, mmv)
                dst = ap3(H1T, VOFF + j * 1040 + (8 * ch) * 65, [[32768, P], [65, 8], [1, 64]])
                srcp = ap3(PS, bk * 512, [[4096, P], [64, 8], [1, 64]])
                pe = c.op("scalar", lambda e, dst=dst, srcp=srcp: e.copy(
                    out=dst, in_=srcp), sAc)
                prod_ev[bk] = pe
                ev[seq] = pe
                seq += 1
        ones_done = None
        for j in range(16):
            c.wait("vector", sAc, ev[16 + j])
            ones_done = c.op("vector", lambda e, j=j: e.memset(
                ap3(H1T, VOFF + j * 1040 + 64, [[32768, P], [65, 16], [1, 1]]), 1.0), sV)

        c.wait("scalar", sAc, ev[31])
        nm = {}
        for qk in range(2):
            srcbuf = (lambda ft: AB[:, ft * 2048:(ft + 1) * 2048]) if qk == 0 else None
            sq_ev = {}
            for ft in range(8):
                if qk == 0:
                    sq_ev[ft] = c.op("scalar", lambda e, ft=ft: e.activation(
                        out=ap3(H1T, PTOFF + (ft % 4) * 2048, [[32768, P], [1, 2048]]),
                        in_=AB[:, ft * 2048:(ft + 1) * 2048], func=AF.Square), sAc)
                else:
                    c.waitall("sync", sQW)
                    qv = c.dma("sync", ap3(H1T, QSOFF, [[32768, P], [1, 2048]]),
                               ap3(qbuf, ft * P * L, [[L, P], [1, 2048]]), sQR[0])
                    c.waitall("scalar", sQR[0])
                    sq_ev[ft] = c.op("scalar", lambda e, ft=ft: e.activation(
                        out=ap3(H1T, PTOFF + (ft % 4) * 2048, [[32768, P], [1, 2048]]),
                        in_=ap3(H1T, QSOFF, [[32768, P], [1, 2048]]), func=AF.Square), sAc)
                if ft % 4 == 3:
                    c.wait("tensor", sAc, sq_ev[ft])
                    for half in range(2):
                        if not (qk == 0 and ft == 3 and half == 0):
                            c.wait("tensor", sSt, nred)
                        for nh in range(2):
                            for sub in range(4):
                                fn_ = lambda e, nh=nh, sub=sub, half=half: e.matmul(
                                    out=ap3(PS, 3072 + nh * 512, [[4096, 1], [1, 512]]),
                                    lhsT=ONESC[:, 0:1],
                                    rhs=ap3(H1T, PTOFF + sub * 2048 + half * 1024 + nh * 512,
                                            [[32768, P], [1, 512]]),
                                    start=(sub == 0), stop=(sub == 3))
                                if nh == 1 and sub == 3:
                                    mv2 = c.op("tensor", fn_, sMM)
                                else:
                                    c.op("tensor", fn_)
                        c.wait("vector", sMM, mv2)
                        nred0 = c.opS("vector", lambda e: e.tensor_reduce(
                            out=STAT[0:1, 29:30], in_=ap3(PS, 3072, [[4096, 1], [1, 1024]]),
                            axis=AX.X, op=ALU.max))
                        first = (qk == 0 and ft == 3 and half == 0)
                        if first:
                            nred = c.op("vector", lambda e, qk=qk: e.tensor_copy(
                                out=STAT[0:1, 24 + qk:25 + qk], in_=STAT[0:1, 29:30]), sSt)
                        else:
                            if half == 0 and ft == 3:
                                nred = c.op("vector", lambda e, qk=qk: e.tensor_copy(
                                    out=STAT[0:1, 24 + qk:25 + qk], in_=STAT[0:1, 29:30]), sSt)
                            else:
                                nred = c.op("vector", lambda e, qk=qk: e.tensor_tensor(
                                    out=STAT[0:1, 24 + qk:25 + qk], in0=STAT[0:1, 24 + qk:25 + qk],
                                    in1=STAT[0:1, 29:30], op=ALU.max), sSt)
            nm[qk] = nred
        c.wait("vector", sSt, nm[1])
        pv = c.op("vector", lambda e: e.tensor_tensor(
            out=STAT[0:1, 26:27], in0=STAT[0:1, 24:25], in1=STAT[0:1, 25:26],
            op=ALU.mult), sSt)
        c.wait("scalar", sSt, pv)
        sqb = c.op("scalar", lambda e: e.activation(
            out=STAT[0:1, 27:28], in_=STAT[0:1, 26:27], func=AF.Sqrt), sRs2)
        c.wait("vector", sRs2, sqb)
        ngv = c.op("vector", lambda e: e.tensor_scalar(
            out=STAT[0:1, 26:27], in0=STAT[0:1, 27:28],
            scalar1=-invD, scalar2=None, op0=ALU.mult), sSt)
        c.wait("tensor", sSt, ngv)
        bmv = c.op("tensor", lambda e: e.matmul(
            out=c.bank(5)[:, 0:1], lhsT=ONESF[:, :], rhs=STAT[0:1, 26:27],
            start=True, stop=True), sMM)
        c.wait("scalar", sMM, bmv)
        ngb = c.op("scalar", lambda e: e.copy(out=BCAST[:, 37:38], in_=c.bank(5)[:, 0:1]), sAc)

        ttb_free = None
        qs_v = {}
        for i5 in range(4):
            if i5 >= 1:
                c.wait("sync", sMM, q_used[i5 - 1])
            else:
                c.waitall("sync", sQW)
            qs_v[i5] = c.dma("sync", ap3(H1T, QSOFF, [[32768, P], [256, 8], [1, 256]])
                             if False else
                             ap3(H1T, QSOFF, [[32768, P], [1, 4096]])
                             .rearrange("p (dt f) -> p dt f", f=512),
                             ap3(qbuf, 512 * i5, [[L, P], [P * L, 8], [1, 512]]), sQR[0])
            c.waitall("tensor", sQR[0])
            for h in range(16):
                ft = h // 2
                rb = 64 * (h % 2)
                for jt in range(16):
                    bk = jt % 2
                    if not (i5 == 0 and h == 0 and jt < 2):
                        c.wait("tensor", sSq, exp_prev[bk])
                    fn_ = lambda e, jt=jt, ft=ft, rb=rb, bk=bk: e.matmul(
                        out=c.bank(bk),
                        lhsT=ap3(AB, ft * 2048 + jt * P + rb * 16384, [[16384, 64], [1, P]]),
                        rhs=ap3(H1T, QSOFF + ft * 512 + rb * 32768, [[32768, 64], [1, 512]]),
                        start=True, stop=True)
                    smv = c.op("tensor", fn_, sMM)
                    c.wait("scalar", sMM, smv)
                    if jt == 0 and av_last is not None:
                        c.wait("scalar", sMM, av_last)
                    if i5 == 0 and h == 0 and jt < 2 and b == 0:
                        c.wait("scalar", sAc, ngb)
                    ee = c.op("scalar", lambda e, jt=jt, bk=bk: e.activation(
                        out=ap3(H1T, PTOFF + jt * 512, [[32768, P], [1, 512]]),
                        in_=c.bank(bk), func=AF.Exp, bias=BCAST[:, 37:38],
                        scale=invD), sSq)
                    if i5 == 0 and h == 0 and jt < 2:
                        if jt == 0:
                            exp_prev = {}
                    exp_prev[bk] = ee
                c.wait("tensor", sSq, exp_prev[0])
                c.wait("tensor", sSq, exp_prev[1])
                if norm_last is not None:
                    c.wait("tensor", sEp, norm_last)
                for jt in range(16):
                    fn_ = lambda e, jt=jt, h=h: e.matmul(
                        out=c.bank(2)[0:65, :],
                        lhsT=H1T[:, VOFF + jt * 1040 + 65 * h:VOFF + jt * 1040 + 65 * (h + 1)],
                        rhs=ap3(H1T, PTOFF + jt * 512, [[32768, P], [1, 512]]),
                        start=(jt == 0), stop=(jt == 15))
                    if jt == 15:
                        avv = c.op("tensor", fn_, sMM)
                    else:
                        c.op("tensor", fn_)
                av_last = avv
                c.wait("vector", sMM, avv)
                rcv = c.op("vector", lambda e: e.reciprocal(
                    out=STAT[0:1, 30:31] if False else MISC[0:1, 0:512],
                    in_=ap3(PS, 2 * 512 + 64 * 4096, [[4096, 1], [1, 512]])), sSt)
                c.wait("tensor", sSt, rcv)
                if bce_prev is not None:
                    c.wait("tensor", sAc, bce_prev)
                bcv = c.op("tensor", lambda e: e.matmul(
                    out=c.bank(3)[0:64, :], lhsT=ONESF[0:1, 0:64],
                    rhs=MISC[0:1, 0:512], start=True, stop=True), sMM)
                c.wait("scalar", sMM, bcv)
                bce = c.op("scalar", lambda e: e.copy(
                    out=MISC[0:64, 0:512], in_=c.bank(3)[0:64, :]), sAc)
                bce_prev = bce
                c.wait("vector", sAc, bce)
                norm_last = c.op("vector", lambda e, ft=ft, rb=rb, i5=i5: e.tensor_tensor(
                    out=ap3(TTB, ft * 2048 + rb * 16384 + i5 * 512, [[16384, 64], [1, 512]]),
                    in0=ap3(PS, 2 * 512, [[4096, 64], [1, 512]]),
                    in1=MISC[0:64, 0:512], op=ALU.mult), sEp)
            if i5 == 0:
                q_used = {}
            q_used[i5] = smv

        c.wait("gpsimd", sMM, q_used[3] if False else smv)
        c.dma("gpsimd", WA[:, 0:4096].rearrange("p (dt f) -> p dt f", f=512),
              W["cTff"][:, 0:512].rearrange("(dt p) f -> p dt f", p=P), sWA[0])
        c.dma("gpsimd", WA[:, 4096:8192].rearrange("p (dt f) -> p dt f", f=512),
              W["cTff"][:, 512:1024].rearrange("(dt p) f -> p dt f", p=P), sWA[1])
        c.waitall("tensor", sWA[0])
        c.waitall("tensor", sWA[1])
        c.waitall("tensor", sCst)
        pr_ev = {}
        for tau in range(16):
            bks = (tau % 2) * 2
            if tau == 0:
                c.wait("tensor", sEp, norm_last)
            if tau >= 2:
                c.wait("tensor", sV, pr_ev[tau - 2])
            for oh in range(2):
                c.op("tensor", lambda e, oh=oh, bks=bks: e.matmul(
                    out=c.bank(bks + oh), lhsT=ONESB[0:1, 0:P],
                    rhs=SMALLB[0:1, 512 * oh:512 * (oh + 1)], start=True, stop=False))
            for dt in range(8):
                for oh in range(2):
                    fn_ = lambda e, dt=dt, oh=oh, tau=tau, bks=bks: e.matmul(
                        out=c.bank(bks + oh),
                        lhsT=TTB[:, dt * 2048 + tau * P:dt * 2048 + tau * P + P],
                        rhs=WA[:, oh * 4096 + dt * 512:oh * 4096 + (dt + 1) * 512],
                        start=False, stop=(dt == 7))
                    if dt == 7 and oh == 1:
                        pmv = c.op("tensor", fn_, sMM)
                    else:
                        c.op("tensor", fn_)
            tt = b * TPB + tau
            sl = (tau % 4) * D
            c.wait("sync", sTb, ln["tbv"][12 + (tau % 4)])
            if tau >= 4:
                c.wait("sync", sV, out_v[(b, tau - 4)])
            sX = sXp[tau % 4]
            xv = c.dma("sync", XT[:, sl:sl + D], xsrc_fn(tt), sX)
            c.waitall("vector", sX)
            c.wait("vector", sMM, pmv)
            c.waitall("vector", sOutp[tau % 2])
            vv = c.op("vector", lambda e, sl=sl, tau=tau, bks=bks: e.tensor_tensor(
                out=OUTT[:, (tau % 2) * D:(tau % 2 + 1) * D], in0=XT[:, sl:sl + D],
                in1=ap3(c.bufs["PSALL"], bks * 512, [[4096, P], [1, 1024]]), op=ALU.add), sV)
            pr_ev[tau] = vv
            out_v[(b, tau)] = vv
            c.wait("sync", sV, vv)
            c.dma("sync", xdst_fn(tt), OUTT[:, (tau % 2) * D:(tau % 2 + 1) * D], sOutp[tau % 2])
        prev_b_done = out_v[(b, 15)]
        prev_wa = pmv
    c.waitall("sync", sOutp[0])
    c.waitall("sync", sOutp[1])
    c.waitall("sync", sQW)


def emit_l1aa(c, W, xsrc_fn, xdst_fn):
    nc = c.nc
    TTB, H1T, WA, XT, STAT, LNS, TBF, OUTT, MISC, ACC, GSC = (c.bufs[k] for k in
        ("TTB", "H1T", "WA", "XT", "STAT", "LNS", "TBF", "OUTT", "MISC", "ACC", "GSC"))
    SMALLF, SMALLB, ONESB, ONESF, BCAST, IDENT = (c.bufs[k] for k in
        ("SMALLF", "SMALLB", "ONESB", "ONESF", "BCAST", "IDENT"))
    sMM, sEp, sAc, sV, sEv, sSq, sSt, sTb = (c.sem(k) for k in
        ("sMM", "sEp", "sAc", "sV", "sEv", "sSq", "sSt", "sTb"))
    sRs2 = c.sem("sRs2")
    sCst = c.sem("sCst")
    sC = c.sem("sC")
    sWA = [c.sem("sWA0"), c.sem("sWA1")]
    sXp = [c.sem("sX0"), c.sem("sX1"), c.sem("sX2"), c.sem("sX3")]
    sOutp = [c.sem("sOutA"), c.sem("sOutB")]
    sSG = c.sem("sKW")
    sSR = [c.sem("sKB0"), c.sem("sKB1")]
    PS = c.bufs["PSALL"]
    TTB_PITCH = 16 * 8 * P
    H1T16 = H1T
    CLT = c.sbuf("CCS", [P, L], BF16)[:, 0:P]

    qpt_d = nc.dram_tensor(c.ns + "l1aa_qpt", [2048, L], BF16)
    kpt_d = nc.dram_tensor(c.ns + "l1aa_kpt", [2048, L], BF16)
    kpb_d = nc.dram_tensor(c.ns + "l1aa_kpb", [L, 2048], BF16)
    vb_d = nc.dram_tensor(c.ns + "l1aa_vb", [L, 1088], BF16)

    QPTO, KPTO, KPBO, VBO, AMTO, OPTO = 0, 2048, 4096, 6144, 7232, 8256

    c.dma("gpsimd", CLT[:, :], W["cLT"][:, :], sCst)
    c.dma("gpsimd", SMALLB[:, 0:1024], ap3(W["bq"], 0, [[1024, 1], [1, 1024]]), sCst)
    c.dma("gpsimd", SMALLB[:, 1024:2048], ap3(W["bk"], 0, [[1024, 1], [1, 1024]]), sCst)
    c.dma("gpsimd", SMALLB[:, 2048:3072], ap3(W["bv"], 0, [[1024, 1], [1, 1024]]), sCst)
    c.dma("gpsimd", SMALLB[:, 3072:4096], ap3(W["bo"], 0, [[1024, 1], [1, 1024]]), sCst)
    c.dma("sync", SMALLF[0:1, 0:1], ap3(W["beta"], 0, [[1, 1], [1, 1]]), sC)
    c.waitall("scalar", sC)
    bev = c.op("scalar", lambda e: e.activation(
        out=SMALLF[0:1, 1:2], in_=SMALLF[0:1, 0:1], func=AF.Exp), sRs2)
    c.wait("tensor", sRs2, bev)
    c.waitall("tensor", sC)
    c.waitall("tensor", sCst)
    v0 = c.op("tensor", lambda e: e.matmul(out=c.bank(6)[:, 0:1], lhsT=ONESF[:, :],
                                           rhs=SMALLF[0:1, 1:2], start=True, stop=True), sMM)
    c.wait("scalar", sMM, v0)
    be_v = c.op("scalar", lambda e: e.copy(out=BCAST[:, 36:37], in_=c.bank(6)[:, 0:1]), sAc)

    out_v = {}
    prev_b_done = None
    wa_last = None
    amt_done = {}
    amt_cons = {}
    sm_cons = {0: None, 1: None}
    sm_last = None
    fin_prev = None
    pr_prev = None
    st_add_prev = None
    for b in range(NB):
        tiles = list(range(b * TPB, (b + 1) * TPB))
        xt_guards = None
        if b > 0:
            xt_guards = {s: (sV, out_v[(b - 1, 12 + s)]) for s in range(4)}
        ln = emit_ln(c, xsrc_fn, tiles, slot0=0, xt_guards=xt_guards)

        zp_v = c.op("vector", lambda e: e.memset(ACC[:, 0:4096], 0.0), sV)

        tr_ev = {}
        tr_seq = 0
        for wsel in range(3):
            wsrc = [W["cTq"], W["cTk"], W["cTv"]][wsel]
            boff = [0, 1024, 2048][wsel]
            if wa_last is not None:
                c.wait("gpsimd", sMM, wa_last)
            elif prev_b_done is not None:
                c.wait("gpsimd", sV, prev_b_done)
            c.dma("gpsimd", WA[:, 0:8192].rearrange("p (dt f) -> p dt f", f=1024),
                  wsrc[:, :].rearrange("(dt p) f -> p dt f", p=P), sWA[0])
            c.waitall("tensor", sWA[0])
            for tau in range(16):
                bks = (tau % 2) * 2
                c.wait("tensor", sEv, ln["evv"][tau])
                if prev_b_done is not None and wsel == 0 and tau == 0:
                    c.wait("tensor", sV, prev_b_done)
                if sm_cons[tau % 2] is not None:
                    gs_, gv_ = sm_cons[tau % 2]
                    c.wait("tensor", gs_, gv_)
                for nh in range(2):
                    c.op("tensor", lambda e, nh=nh, bks=bks, boff=boff: e.matmul(
                        out=c.bank(bks + nh), lhsT=ONESB[0:1, 0:P],
                        rhs=SMALLB[0:1, boff + nh * 512:boff + (nh + 1) * 512],
                        start=True, stop=False))
                    for dt in range(8):
                        fn_ = lambda e, tau=tau, dt=dt, nh=nh, bks=bks: e.matmul(
                            out=c.bank(bks + nh),
                            lhsT=TTB[:, tau * 1024 + dt * P:tau * 1024 + (dt + 1) * P],
                            rhs=WA[:, dt * 1024 + nh * 512:dt * 1024 + nh * 512 + 512],
                            start=False, stop=(dt == 7))
                        if dt == 7 and nh == 1:
                            mmv = c.op("tensor", fn_, sMM)
                        else:
                            c.op("tensor", fn_)
                if wsel < 2:
                    qp = ap3(PS, bks * 512, [[4096, P], [1, 1024]])
                    qp3 = ap3(PS, bks * 512, [[4096, P], [16, 64], [1, 16]])
                    c.wait("vector", sMM, mmv)
                    if b == 0 and wsel == 0 and tau == 0:
                        c.wait("vector", sAc, be_v)
                        c.wait("vector", sV, zp_v)
                    c.opS("vector", lambda e, qp3=qp3: e.tensor_reduce(
                        out=GSC[:, 0:64], in_=qp3, axis=AX.X, op=ALU.max))
                    c.opS("vector", lambda e: e.tensor_scalar(
                        out=GSC[:, 64:128], in0=GSC[:, 0:64], scalar1=BCAST[:, 36:37],
                        scalar2=None, op0=ALU.mult))
                    sv = c.op("vector", lambda e, qp3=qp3: e.scalar_tensor_tensor(
                        out=MISC[:, 0:1024].rearrange("p (g f) -> p g f", f=16),
                        in0=qp3, scalar=BCAST[:, 36:37],
                        in1=ap3(GSC, 64, [[168, P], [1, 64], [0, 16]]),
                        op0=ALU.mult, op1=ALU.subtract), sSt)
                    c.wait("scalar", sSt, sv)
                    ev_ = c.op("scalar", lambda e: e.activation(
                        out=TBF[:, 0:1024], in_=MISC[:, 0:1024], func=AF.Exp), sSq)
                    c.wait("vector", sSq, ev_)
                    c.opS("vector", lambda e: e.tensor_reduce(
                        out=GSC[:, 0:64], in_=TBF[:, 0:1024].rearrange("p (g f) -> p g f", f=16),
                        axis=AX.X, op=ALU.add))
                    c.opS("vector", lambda e: e.reciprocal(
                        out=GSC[:, 64:128], in_=GSC[:, 0:64]))
                    poff = 0 if wsel == 0 else 2048
                    pdst = ap3(ACC, poff, [[8192, P], [32, 64], [1, 16]])
                    sm_l = c.op("vector", lambda e, pdst=pdst: e.tensor_tensor(
                        out=pdst, in0=TBF[:, 0:1024].rearrange("p (g f) -> p g f", f=16),
                        in1=ap3(GSC, 64, [[168, P], [1, 64], [0, 16]]), op=ALU.mult), sSt)
                    sm_cons[tau % 2] = (sSt, sm_l)
                    sm_last = sm_l
                    if wsel == 1:
                        c.wait("scalar", sSt, sm_l)
                        kb_ = c.op("scalar", lambda e: e.copy(
                            out=H1T[:, KPBO:KPBO + 2048], in_=ACC[:, 2048:4096]), sAc)
                        c.wait("sync", sAc, kb_)
                        c.dma("sync", ap3(kpb_d, tau * P * 2048, [[2048, P], [1, 2048]]),
                              H1T[:, KPBO:KPBO + 2048], sSG)
                    dstd = qpt_d if wsel == 0 else kpt_d
                    c.wait("tensor", sSt, sm_l)
                    for wv in range(4):
                        tb = 4 + (tr_seq % 2) * 2
                        if tr_seq >= 2:
                            c.wait("tensor", sAc, tr_ev[tr_seq - 2])
                        poff2 = (0 if wsel == 0 else 2048) + wv * 512
                        for q4 in range(4):
                            fn_ = lambda e, q4=q4, tb=tb, poff2=poff2: e.transpose(
                                out=ap3(PS, tb * 512 + q4 * P, [[4096, P], [1, P]]),
                                in_=ap3(ACC, poff2 + q4 * P, [[8192, P], [1, P]]),
                                identity=IDENT[:, :])
                            if q4 == 3:
                                tv_ = c.op("tensor", fn_, sMM)
                            else:
                                c.op("tensor", fn_)
                        c.wait("scalar", sMM, tv_)
                        c.waitall("scalar", sSG)
                        tr_ev[tr_seq] = c.op("scalar", lambda e, tb=tb: e.copy(
                            out=H1T[:, QPTO:QPTO + 512] if False else
                            ap3(H1T, QPTO, [[32768, P], [1, 512]]),
                            in_=ap3(PS, tb * 512, [[4096, P], [1, 512]])), sAc)
                        c.wait("sync", sAc, tr_ev[tr_seq])
                        c.dma("sync", ap3(dstd, (wv * 512) * L + tau * P,
                                          [[L, P], [P * L, 4], [1, P]]),
                              ap3(H1T, QPTO, [[32768, P], [128, 4], [1, P]]), sSG)
                        tr_seq += 1
                else:
                    c.wait("scalar", sMM, mmv)
                    c.waitall("scalar", sSG)
                    vb1 = c.op("scalar", lambda e, bks=bks: e.copy(
                        out=ap3(H1T, VBO, [[32768, P], [17, 64], [1, 16]]),
                        in_=ap3(PS, bks * 512, [[4096, P], [16, 64], [1, 16]])), sAc)
                    sm_cons[tau % 2] = (sAc, vb1)
                    c.wait("vector", sAc, vb1)
                    on_ = c.op("vector", lambda e: e.memset(
                        ap3(H1T, VBO + 16, [[32768, P], [17, 64], [1, 1]]), 1.0), sV)
                    c.wait("sync", sV, on_)
                    c.dma("sync", ap3(vb_d, tau * P * 1088, [[1088, P], [1, 1088]]),
                          ap3(H1T, VBO, [[32768, P], [1, 1088]]), sSG)
            wa_last = mmv

        MISCb = MISC.bitcast(BF16)
        kvz = c.op("vector", lambda e: e.memset(MISC[:, 0:272], 0.0), sV)
        kvz = c.op("vector", lambda e: e.memset(
            ap3(MISCb, 1024, [[2048, P], [1, 272]]), 0.0), sV)
        kvsb_v = None
        amt_prev = {0: None, 1: None}
        st_add_prev = kvz
        fin_prev = None
        pr_prev = None
        ld_prev = {}
        for tau in range(16):
            sl2 = tau % 2
            if tau >= 2:
                c.wait("sync", sMM, ld_prev[tau - 2])
            c.waitall("sync", sSG)
            c.dma("sync", ap3(H1T, QPTO, [[32768, P], [128, 16], [1, P]]),
                  ap3(qpt_d, tau * P, [[L, P], [P * L, 16], [1, P]]), sSR[0])
            c.dma("sync", ap3(H1T, KPTO, [[32768, P], [128, 16], [1, P]]),
                  ap3(kpt_d, tau * P, [[L, P], [P * L, 16], [1, P]]), sSR[0])
            c.dma("sync", ap3(H1T, KPBO, [[32768, P], [1, 2048]]),
                  ap3(kpb_d, tau * P * 2048, [[2048, P], [1, 2048]]), sSR[0])
            c.dma("sync", ap3(H1T, VBO, [[32768, P], [1, 1088]]),
                  ap3(vb_d, tau * P * 1088, [[1088, P], [1, 1088]]), sSR[0])
            c.waitall("tensor", sSR[0])
            for g in range(16):
                for r in range(4):
                    h = 4 * g + r
                    if amt_done.get((g % 2, r)) is not None:
                        c.wait("tensor", sEp, amt_done[(g % 2, r)])
                    if g <= 1 and r == 0 and st_add_prev is not None:
                        c.wait("tensor", sV, st_add_prev)
                    fn_ = lambda e, g=g, r=r: e.matmul(
                        out=ap3(PS, r * 512 + (g % 2) * P, [[4096, P], [1, P]]),
                        lhsT=bass.AP(H1T, KPTO + g * P + 32 * r * 32768, [[32768, 32], [1, P]]),
                        rhs=bass.AP(H1T, QPTO + g * P + 32 * r * 32768, [[32768, 32], [1, P]]),
                        start=True, stop=True, tile_position=(32 * r, 0))
                    amv = c.op("tensor", fn_, sMM)
                    c.wait("vector", sMM, amv)
                    if amt_cons.get((g % 2, r)) is not None:
                        c.wait("vector", sMM, amt_cons[(g % 2, r)])
                    amt_done[(g % 2, r)] = c.op("vector",
                        lambda e, g=g, r=r: e.tensor_tensor(
                            out=ap3(H1T, AMTO + (g % 2) * 512 + r * P, [[32768, P], [1, P]]),
                            in0=ap3(PS, r * 512 + (g % 2) * P, [[4096, P], [1, P]]),
                            in1=CLT[:, :], op=ALU.mult), sEp)
                for r in range(4):
                    h = 4 * g + r
                    c.wait("tensor", sEp, amt_done[(g % 2, r)])
                    if fin_prev is not None and g == 0 and r == 0:
                        c.wait("tensor", sSt, fin_prev)
                        c.wait("tensor", sV, pr_prev)
                    c.op("tensor", lambda e, g=g, r=r, h=h: e.matmul(
                        out=ap3(PS, (4 + r) * 512 + g * 17, [[4096, P], [1, 17]]),
                        lhsT=ap3(H1T, AMTO + (g % 2) * 512 + r * P, [[32768, P], [1, P]]),
                        rhs=ap3(H1T, VBO + 17 * h, [[32768, P], [1, 17]]),
                        start=True, stop=False))
                    if kvsb_v is not None and g == 0 and r == 0:
                        c.wait("tensor", sAc, kvsb_v)
                    elif kvsb_v is None and g == 0 and r == 0:
                        c.wait("tensor", sV, kvz)
                    iv = c.op("tensor", lambda e, g=g, r=r, h=h: e.matmul(
                        out=ap3(PS, (4 + r) * 512 + g * 17, [[4096, P], [1, 17]]),
                        lhsT=bass.AP(H1T, QPTO + g * P + 32 * r * 32768, [[32768, 32], [1, P]]),
                        rhs=bass.AP(MISCb, 1024 + 17 * g + 32 * r * 2048, [[2048, 32], [1, 17]]),
                        start=False, stop=True, tile_position=(32 * r, 0)), sMM)
                    amt_cons[(g % 2, r)] = iv
                av_done = iv
            for g in range(16):
                for r in range(4):
                    h = 4 * g + r
                    if g == 0 and r == 0:
                        for par in range(2):
                            for rr in range(4):
                                if amt_done.get((par, rr)) is not None:
                                    c.wait("tensor", sEp, amt_done[(par, rr)])
                        if st_add_prev is not None:
                            c.wait("tensor", sV, st_add_prev)
                    smv2 = c.op("tensor", lambda e, g=g, r=r, h=h: e.matmul(
                        out=bass.AP(PS, 17 * g + 32 * r * 4096, [[4096, 32], [1, 17]]),
                        lhsT=ap3(H1T, KPBO + 32 * h, [[32768, P], [1, 32]]),
                        rhs=ap3(H1T, VBO + 17 * h, [[32768, P], [1, 17]]),
                        start=True, stop=True, tile_position=(0, 32 * r)), sMM)
            st_last_mm = smv2
            c.wait("vector", sMM, av_done)
            c.opS("vector", lambda e: e.reciprocal(
                out=GSC[:, 0:64], in_=ap3(PS, 4 * 512 + 16, [[4096, P], [512, 4], [17, 16]])))
            fin = c.op("vector", lambda e: e.tensor_tensor(
                out=bass.AP(ACC, 4096, [[8192, P], [16, 4], [64, 16], [1, 16]]),
                in0=bass.AP(PS, 4 * 512, [[4096, P], [512, 4], [17, 16], [1, 16]]),
                in1=bass.AP(GSC, 0, [[168, P], [16, 4], [1, 16], [0, 16]]),
                op=ALU.mult), sSt)
            fin_prev = fin
            c.wait("vector", sMM, st_last_mm)
            st_add_prev = c.op("vector", lambda e: e.tensor_tensor(
                out=MISC[:, 0:272], in0=MISC[:, 0:272],
                in1=ap3(PS, 0, [[4096, P], [1, 272]]), op=ALU.add), sV)
            c.wait("scalar", sV, st_add_prev)
            kvsb_v = c.op("scalar", lambda e: e.copy(
                out=ap3(MISCb, 1024, [[2048, P], [1, 272]]), in_=MISC[:, 0:272]), sAc)
            c.wait("tensor", sV, st_add_prev)
            c.wait("tensor", sSt, fin)
            for wv in range(2):
                for q4 in range(4):
                    fn_ = lambda e, wv=wv, q4=q4: e.transpose(
                        out=ap3(PS, (2 + wv) * 512 + q4 * P, [[4096, P], [1, P]]),
                        in_=ap3(ACC, 4096 + wv * 512 + q4 * P, [[8192, P], [1, P]]),
                        identity=IDENT[:, :])
                    if q4 == 3:
                        tv_ = c.op("tensor", fn_, sMM)
                    else:
                        c.op("tensor", fn_)
                c.wait("scalar", sMM, tv_)
                ope = c.op("scalar", lambda e, wv=wv: e.copy(
                    out=ap3(H1T, OPTO + wv * 512, [[32768, P], [1, 512]]),
                    in_=ap3(PS, (2 + wv) * 512, [[4096, P], [1, 512]])), sAc)
            if tau == 0:
                c.wait("gpsimd", sMM, wa_last)
                c.dma("gpsimd", WA[:, 0:8192].rearrange("p (dt f) -> p dt f", f=1024),
                      W["cTo"][:, :].rearrange("(dt p) f -> p dt f", p=P), sWA[0])
                wa_last = None
            c.waitall("tensor", sWA[0])
            c.wait("tensor", sAc, ope)
            if pr_prev is not None:
                c.wait("tensor", sV, pr_prev)
            for nh in range(2):
                c.op("tensor", lambda e, nh=nh: e.matmul(
                    out=c.bank(6 + nh), lhsT=ONESB[0:1, 0:P],
                    rhs=SMALLB[0:1, 3072 + nh * 512:3072 + (nh + 1) * 512],
                    start=True, stop=False))
                for dt in range(8):
                    fn_ = lambda e, dt=dt, nh=nh: e.matmul(
                        out=c.bank(6 + nh),
                        lhsT=ap3(H1T, OPTO + dt * P, [[32768, P], [1, P]]),
                        rhs=WA[:, dt * 1024 + nh * 512:dt * 1024 + nh * 512 + 512],
                        start=False, stop=(dt == 7))
                    if dt == 7 and nh == 1:
                        pmv = c.op("tensor", fn_, sMM)
                    else:
                        c.op("tensor", fn_)
            ld_prev[tau] = pmv
            tt = b * TPB + tau
            sl = (tau % 4) * D
            c.wait("sync", sTb, ln["tbv"][12 + (tau % 4)])
            if tau >= 4:
                c.wait("sync", sV, out_v[(b, tau - 4)])
            sX = sXp[tau % 4]
            xv = c.dma("sync", XT[:, sl:sl + D], xsrc_fn(tt), sX)
            c.waitall("vector", sX)
            c.wait("vector", sMM, pmv)
            c.waitall("vector", sOutp[tau % 2])
            vv = c.op("vector", lambda e, sl=sl, tau=tau: e.tensor_tensor(
                out=OUTT[:, (tau % 2) * D:(tau % 2 + 1) * D], in0=XT[:, sl:sl + D],
                in1=ap3(PS, 6 * 512, [[4096, P], [1, 1024]]), op=ALU.add), sV)
            pr_prev = vv
            out_v[(b, tau)] = vv
            c.wait("sync", sV, vv)
            c.dma("sync", xdst_fn(tt), OUTT[:, (tau % 2) * D:(tau % 2 + 1) * D], sOutp[tau % 2])
        prev_b_done = out_v[(b, 15)]
        wa_last = pmv
    c.waitall("sync", sOutp[0])
    c.waitall("sync", sOutp[1])
    c.waitall("sync", sSG)


def emit_l0ffn_sp(c, W, xsrc_fn, xdst_fn, xfull):
    nc = c.nc
    TTB, H1T, WA, WC, XT, STAT, LNS, TBF, OUTT, MISC, ACC = (c.bufs[k] for k in
        ("TTB", "H1T", "WA", "WC", "XT", "STAT", "LNS", "TBF", "OUTT", "MISC", "ACC"))
    MASKS, SMALLF, SMALLB, ONESB, ONESF, BCAST, IDENT, WSM, GSC = (c.bufs[k] for k in
        ("MASKS", "SMALLF", "SMALLB", "ONESB", "ONESF", "BCAST", "IDENT", "WSM", "GSC"))
    _SSC = c.sbuf("SSC", [P, L], BF16)
    PM4 = _SSC.bitcast(F32)[:, 0:128]
    IVC = _SSC.bitcast(F32)[:, 128:140]
    IVU = _SSC.bitcast(U32)[:, 140:144]
    IVI = _SSC.bitcast(mybir.dt.int32)[:, 144:146]
    sMM, sEp, sAc, sV, sEv, sSq, sSt, sTb = (c.sem(k) for k in
        ("sMM", "sEp", "sAc", "sV", "sEv", "sSq", "sSt", "sTb"))
    sCst = c.sem("sCst")
    sWA = [c.sem("sWA0"), c.sem("sWA1")]
    sWC = [c.sem("sWC0"), c.sem("sWC1"), c.sem("sWC2"), c.sem("sWC3")]
    sXp = [c.sem("sX0"), c.sem("sX1"), c.sem("sX2"), c.sem("sX3")]
    sOutp = [c.sem("sOutA"), c.sem("sOutB")]
    sBE = c.sem("sBE")
    sIV = [c.sem("sKB0"), c.sem("sKB1")]
    sGX = [c.sem("sZQ0"), c.sem("sZQ1")]
    sSC = c.sem("sSCg")
    sYG = c.sem("sYGg")
    sPF = c.sem("sPF")
    sYW = c.sem("sYW")
    PS = c.bufs["PSALL"]
    TTB_PITCH = 16 * 8 * P
    SL = 512
    NSLOT = 12
    NPAD = NSLOT * SL
    H1F = H1T.bitcast(F32)

    invcw = nc.dram_tensor(c.ns + "l0f_ivc", [NPAD, 4], F32)
    ybuf = nc.dram_tensor(c.ns + "l0f_y", [NPAD, D], BF16)
    we1f = W["We1"]
    we2f = W["We2"]
    be1f = W["be1"]
    be2f = W["be2"]

    c.dma("gpsimd", WSM[:, 0:32].rearrange("p (dt e) -> p dt e", e=4),
          W["Wr"].rearrange("(dt p) e -> p dt e", p=P), sCst)
    c.dma("gpsimd", SMALLF[0:1, 0:4], ap3(W["br"], 0, [[4, 1], [1, 4]]), sCst)
    c.dma("sync", WSM[:, 32:36] if False else IVC[:, 8:9], W["cIOTAC"][:, :], c.sem("sC"))
    c.waitall("tensor", sCst)
    c.waitall("tensor", c.sem("sC"))
    v = c.op("tensor", lambda e: e.matmul(out=c.bank(6)[:, 0:4], lhsT=ONESF[:, :],
                                          rhs=SMALLF[0:1, 0:4], start=True, stop=True), sMM)
    c.wait("scalar", sMM, v)
    brB_v = c.op("scalar", lambda e: e.copy(out=BCAST[:, 0:4], in_=c.bank(6)[:, 0:4]), sAc)
    c.op("vector", lambda e: e.memset(STAT[:, 31:32], 0.0), sV)
    for e4 in range(4):
        c.op("vector", lambda e, e4=e4: e.memset(BCAST[:, 8 + e4:9 + e4], float(e4)), sV)
    c.op("vector", lambda e: e.memset(MISC[:, 0:128], 0.0), sV)
    pf_v = c.op("vector", lambda e: e.memset(
        ap3(MISC, 0, [[1024, P], [4, 32]]), 1.0e9), sV)
    c.waitall("sync", sV)
    c.dma("sync", ap3(invcw, 0, [[128, P], [1, 128]]), MISC[:, 0:128], sPF)
    c.dma("sync", ap3(invcw, P * 32 * 4, [[64, P], [1, 64]]), MISC[:, 0:64], sPF)
    c.waitall("scalar", sPF)

    ep_tt = {}
    mask_v = None
    tr_ev = {}
    ln_prev = None
    for half in range(2):
        tiles = list(range(half * 16, (half + 1) * 16))
        guards = None
        if ln_prev is not None:
            guards = {sl_: (sTb, ln_prev["tbv"][12 + sl_]) for sl_ in range(4)}
        ln = emit_ln(c, xsrc_fn, tiles, slot0=0, xt_guards=guards)
        ln_prev = ln
        for i in range(16):
            tid = tiles[i]
            bk = 6 + (i % 2)
            c.wait("tensor", sEv, ln["evv"][i])
            if ep_tt.get(tid - 2) is not None:
                c.wait("tensor", sEp, ep_tt[tid - 2])
            if half == 0 and i == 0:
                c.wait("tensor", sAc, brB_v)
            for dt in range(8):
                fn_ = lambda e, i=i, dt=dt, bk=bk: e.matmul(
                    out=c.bank(bk)[:, 0:4],
                    lhsT=TTB[:, i * 1024 + dt * P: i * 1024 + (dt + 1) * P],
                    rhs=WSM[:, 4 * dt:4 * dt + 4], start=(dt == 0), stop=(dt == 7))
                if dt == 7:
                    mmv = c.op("tensor", fn_, sMM)
                else:
                    c.op("tensor", fn_)
            gg = 16 + 8 * (i % 2)
            c.wait("vector", sMM, mmv)
            if half == 0 and i == 0:
                c.wait("vector", sAc, brB_v)
            ep_tt[tid] = c.op("vector", lambda e, gg=gg, bk=bk: e.tensor_tensor(
                out=STAT[:, gg:gg + 4], in0=c.bank(bk)[:, 0:4], in1=BCAST[:, 0:4],
                op=ALU.add), sEp)
            c.wait("vector", sEp, ep_tt[tid])
            c.opS("vector", lambda e, gg=gg: e.tensor_reduce(
                out=STAT[:, gg + 4:gg + 5], in_=STAT[:, gg:gg + 4], axis=AX.X, op=ALU.max))
            c.opS("vector", lambda e, gg=gg: e.tensor_scalar(
                out=STAT[:, gg:gg + 4], in0=STAT[:, gg:gg + 4],
                scalar1=STAT[:, gg + 4:gg + 5], scalar2=None, op0=ALU.is_ge))
            c.opS("vector", lambda e, t=tid, gg=gg: e.tensor_tensor_scan(
                out=MASKS[:, 4 * t:4 * t + 4], data0=STAT[:, gg:gg + 4],
                data1=STAT[:, gg:gg + 4], initial=STAT[:, 31:32], op0=ALU.add, op1=ALU.bypass))
            c.opS("vector", lambda e, t=tid, gg=gg: e.tensor_tensor(
                out=MASKS[:, 4 * t:4 * t + 4], in0=MASKS[:, 4 * t:4 * t + 4],
                in1=STAT[:, gg:gg + 4], op=ALU.is_equal))
            mask_v = c.op("vector", lambda e, t=tid, gg=gg: e.tensor_tensor(
                out=MASKS[:, 4 * t:4 * t + 4], in0=MASKS[:, 4 * t:4 * t + 4],
                in1=STAT[:, gg:gg + 4], op=ALU.mult), sEp)

    for t in range(NT):
        c.wait("tensor", sEp, mask_v if t == 0 else mask_v)
        if t >= 2:
            c.wait("tensor", sAc, tr_ev[t - 2])
        trv = c.op("tensor", lambda e, t=t: e.transpose(
            out=ap3(PS, (t % 2) * 512, [[4096, 4], [1, P]]),
            in_=MASKS[:, 4 * t:4 * t + 4], identity=IDENT[:, :]), sMM)
        c.wait("scalar", sMM, trv)
        tr_ev[t] = c.op("scalar", lambda e, t=t: e.copy(
            out=ap3(H1F, t * P, [[16384, 4], [1, P]]),
            in_=ap3(PS, (t % 2) * 512, [[4096, 4], [1, P]])), sAc)
    c.wait("vector", sAc, tr_ev[NT - 1])
    c.op("vector", lambda e: e.memset(GSC[:, 0:1], 0.0))
    sc_v = c.opS("vector", lambda e: e.tensor_tensor_scan(
        out=ap3(H1F, 0, [[16384, 4], [1, NTOK]]),
        data0=ap3(H1F, 0, [[16384, 4], [1, NTOK]]),
        data1=ap3(H1F, 0, [[16384, 4], [1, NTOK]]),
        initial=GSC[0:4, 0:1], op0=ALU.add, op1=ALU.bypass))
    cc_v = c.op("vector", lambda e: e.tensor_copy(
        out=GSC[0:4, 1:2], in_=ap3(H1F, NTOK - 1, [[16384, 4], [1, 1]])), sSt)
    c.wait("tensor", sSt, cc_v)
    trc = c.op("tensor", lambda e: e.transpose(
        out=ap3(PS, 0, [[4096, 1], [1, 4]]), in_=GSC[0:4, 1:2],
        identity=IDENT[0:4, 0:4]), sMM)
    c.wait("vector", sMM, trc)
    c.opS("vector", lambda e: e.tensor_scalar(
        out=GSC[0:1, 20:24], in0=ap3(PS, 0, [[4096, 1], [1, 4]]),
        scalar1=float(SL - 1), scalar2=1.0 / SL, op0=ALU.add, op1=ALU.mult))
    c.opS("vector", lambda e: e.tensor_copy(
        out=IVU[0:1, 0:4], in_=GSC[0:1, 20:24]))
    c.opS("vector", lambda e: e.tensor_copy(
        out=GSC[0:1, 8:12], in_=IVU[0:1, 0:4]))
    c.opS("vector", lambda e: e.tensor_scalar(
        out=GSC[0:1, 8:12], in0=GSC[0:1, 8:12], scalar1=float(SL), scalar2=None,
        op0=ALU.mult))
    c.op("vector", lambda e: e.memset(GSC[0:1, 12:13], 0.0))
    c.opS("vector", lambda e: e.tensor_tensor_scan(
        out=GSC[0:1, 16:20], data0=GSC[0:1, 8:12], data1=GSC[0:1, 8:12],
        initial=GSC[0:1, 12:13], op0=ALU.add, op1=ALU.bypass))
    of_v = c.op("vector", lambda e: e.tensor_tensor(
        out=GSC[0:1, 16:20], in0=GSC[0:1, 16:20], in1=GSC[0:1, 8:12], op=ALU.subtract), sSt)
    c.wait("vector", sSt, of_v)
    c.wait("tensor", sSt, of_v)
    obm = c.op("tensor", lambda e: e.matmul(
        out=c.bank(6)[:, 0:4], lhsT=ONESF[:, :], rhs=GSC[0:1, 16:20],
        start=True, stop=True), sMM)
    c.wait("scalar", sMM, obm)
    ofe = c.op("scalar", lambda e: e.copy(out=BCAST[:, 4:8], in_=c.bank(6)[:, 0:4]), sAc)

    sc_done = {}
    ps_rd = {}
    for t in range(NT):
        bkt = (t % 2)
        if t >= 2:
            c.wait("tensor", c.sem("sSelf_vector"), ps_rd[t - 2])
        if t == 0:
            c.wait("tensor", sAc, ofe)
        trv = c.op("tensor", lambda e, t=t, bkt=bkt: e.transpose(
            out=ap3(PS, bkt * 512, [[4096, P], [1, 4]]),
            in_=ap3(H1F, t * P, [[16384, 4], [1, P]]), identity=IDENT[0:4, 0:4]), sMM)
        c.wait("vector", sMM, trv)
        ps_rd[t] = c.opS("vector", lambda e, bkt=bkt: e.scalar_tensor_tensor(
            out=GSC[:, 32:36], in0=ap3(PS, bkt * 512, [[4096, P], [1, 4]]),
            scalar=-1.0, in1=BCAST[:, 4:8], op0=ALU.add, op1=ALU.add))
        c.opS("vector", lambda e, t=t: e.tensor_tensor(
            out=GSC[:, 32:36], in0=GSC[:, 32:36], in1=MASKS[:, 4 * t:4 * t + 4],
            op=ALU.mult))
        pv = c.op("vector", lambda e, t=t: e.tensor_reduce(
            out=PM4[:, t:t + 1], in_=GSC[:, 32:36], axis=AX.X, op=ALU.add), sEp)
        c.wait("vector", sEp, pv)
        c.opS("vector", lambda e, t=t: e.tensor_tensor(
            out=GSC[:, 36:40], in0=MASKS[:, 4 * t:4 * t + 4], in1=BCAST[:, 8:12],
            op=ALU.mult))
        c.opS("vector", lambda e: e.tensor_reduce(
            out=GSC[:, 40:41], in_=GSC[:, 36:40], axis=AX.X, op=ALU.add))
        if t >= 1:
            c.waitall("vector", sSC)
        c.opS("vector", lambda e, t=t: e.tensor_scalar(
            out=GSC[:, 44:45], in0=IVC[:, 8:9], scalar1=float(P * t), scalar2=None,
            op0=ALU.add))
        c.op("vector", lambda e: e.memset(GSC[:, 45:46], 1.0))
        c.opS("vector", lambda e: e.tensor_copy(out=GSC[:, 46:47], in_=GSC[:, 40:41]))
        c.op("vector", lambda e: e.memset(GSC[:, 47:48], 0.0))
        uv = c.op("vector", lambda e, t=t: e.tensor_copy(
            out=IVU[:, 2:3], in_=PM4[:, t:t + 1]), sV)
        sc_done[t] = uv
        c.wait("gpsimd", sV, uv)
        if t == 0:
            c.waitall("gpsimd", sPF)
        gsc_pay = bass.AP(GSC, 44, [[168, P], [1, 4]])
        c.op("gpsimd", c.dyn(lambda e, gsc_pay=gsc_pay: e.indirect_dma_start(
            out=invcw[:, :], out_offset=bass.IndirectOffsetOnAxis(ap=IVU[:, 2:3], axis=0),
            in_=gsc_pay, in_offset=None,
            bounds_check=NPAD - 1, oob_is_err=False)), sSC, 16)
    route_done = sSC.c
    route_ep_val = sEp.c

    prod_evac = {}
    prod_seq = 0
    wa_done = {}
    chunk_seq = 0
    wc_done = {}
    wc_seq = 0
    prev_ep_all = None
    prev_tbv = {sl_: ln_prev["tbv"][12 + sl_] for sl_ in range(4)}
    ivc_prev = {}
    if True:
        rE = c.es.enter_context(c.nc.gpsimd.register(c.ns + "rE"))
        rO1 = c.es.enter_context(c.nc.gpsimd.register(c.ns + "rO1"))
        rO2 = c.es.enter_context(c.nc.gpsimd.register(c.ns + "rO2"))
        rT = c.es.enter_context(c.nc.gpsimd.register(c.ns + "rT"))
        rV = c.es.enter_context(c.nc.gpsimd.register(c.ns + "rV"))
        for s in range(NSLOT):
            ivv = {}

            def loader(i, dst, s=s, ivv=ivv):
                st = i
                sem = sGX[st % 2]
                if s == 0 and st == 0:
                    c.waitall("sync", sPF)
                    c.waitall("sync", sSC)
                if s > 0 or st >= 2:
                    pass
                if ivc_prev.get(st % 2) is not None:
                    c.wait("sync", sV, ivc_prev[st % 2])
                c.dma("sync", IVC[:, 4 * (st % 2):4 * (st % 2) + 4],
                      ap3(invcw, (s * SL + st * P) * 4, [[4, P], [1, 4]]), sIV[st % 2])
                c.waitall("vector", sIV[st % 2])
                uv = c.op("vector", lambda e, st=st: e.tensor_copy(
                    out=IVU[:, st % 2:st % 2 + 1],
                    in_=IVC[:, 4 * (st % 2):4 * (st % 2) + 1]), sV)
                cwv = c.op("vector", lambda e, st=st: e.tensor_copy(
                    out=GSC[:, 48 + st:49 + st],
                    in_=IVC[:, 4 * (st % 2) + 1:4 * (st % 2) + 2]), sV)
                ivv[st] = cwv
                ivc_prev[st % 2] = cwv
                if st == 0:
                    eidv = c.op("vector", lambda e: e.tensor_copy(
                        out=IVC[:, 10:11], in_=IVC[:, 2:3]), sV)
                    tkv = c.op("vector", lambda e: e.tensor_copy(
                        out=IVC[:, 11:12], in_=IVC[:, 0:1]), sV)
                    ivv["eid"] = eidv
                    ivc_prev[0] = tkv
                c.wait("gpsimd", sV, uv)
                if prev_tbv is not None:
                    c.wait("gpsimd", sTb, prev_tbv[st])
                c.op("gpsimd", c.dyn(lambda e, st=st, dst=dst: e.indirect_dma_start(
                    out=dst, out_offset=None, in_=xfull,
                    in_offset=bass.IndirectOffsetOnAxis(ap=IVU[:, st % 2:st % 2 + 1], axis=0),
                    bounds_check=NTOK - 1, oob_is_err=False)), sem, 16)
                return sem

            gd = None
            if prev_tbv is not None:
                gd = {sl_: (sTb, prev_tbv[sl_]) for sl_ in range(4)}
            ln = emit_ln(c, None, [0, 1, 2, 3], slot0=0, xt_guards=gd, loader=loader)
            prev_tbv = ln["tbv"]
            c.wait("vector", sV, ivv["eid"])
            c.opS("vector", lambda e: e.tensor_scalar(
                out=IVC[:, 9:10], in0=IVC[:, 10:11], scalar1=0.0, scalar2=3.0,
                op0=ALU.max, op1=ALU.min))
            eci = c.op("vector", lambda e: e.tensor_copy(
                out=IVI[:, 0:1], in_=IVC[:, 9:10]), sV)
            c.opS("vector", lambda e: e.tensor_scalar(
                out=IVC[:, 11:12], in0=IVC[:, 11:12], scalar1=float(NTOK),
                scalar2=None, op0=ALU.is_lt))
            fci = c.op("vector", lambda e: e.tensor_copy(
                out=IVI[:, 1:2], in_=IVC[:, 11:12]), sV)
            c.wait("gpsimd", sV, eci)
            c.wait("gpsimd", sV, fci)
            c.op("gpsimd", lambda e: e.reg_load(rE, IVI[0:1, 0:1]))
            c.op("gpsimd", lambda e: e.reg_load(rV, IVI[0:1, 1:2]))
            c.op("gpsimd", lambda e: e.reg_mul(rO1, rE, 1024 * 4096))
            c.op("gpsimd", lambda e: e.reg_mul(rO2, rE, 4096 * 1024))
            DMY = (GSC[0:1, 58:59], GSC[0:1, 44:45])
            if prev_ep_all is not None:
                c.wait("gpsimd", sEp, prev_ep_all)
            c.op("gpsimd", lambda e: e.reg_mul(rT, rE, 4096))
            c.dmad_if("gpsimd", SMALLB[:, 0:4096],
                      bass.AP(be1f, rT, [[4096, 1], [1, 4096]]), sBE, rV, DMY)
            c.op("gpsimd", lambda e: e.reg_mul(rT, rE, 1024))
            c.dmad_if("gpsimd", SMALLB[:, 4096:5120],
                      bass.AP(be2f, rT, [[1024, 1], [1, 1024]]), sBE, rV, DMY)
            first_of_s = True
            for ch in range(8):
                slot = chunk_seq % 2
                if chunk_seq >= 2:
                    c.wait("gpsimd", sMM, wa_done[chunk_seq - 2])
                c.op("gpsimd", lambda e, ch=ch: e.reg_add(rT, rO1, 512 * ch))
                c.dmad_if("gpsimd",
                          WA[:, slot * 4096:(slot + 1) * 4096]
                          .rearrange("p (dt f) -> p dt f", f=512),
                          bass.AP(we1f, rT, [[4096, P], [P * 4096, 8], [1, 512]]),
                          sWA[slot], rV, DMY)
                c.waitall("tensor", sWA[slot])
                if ch == 0:
                    c.waitall("tensor", sBE)
                    if prev_ep_all is not None:
                        c.wait("tensor", sEp, prev_ep_all)
                for ft in range(4):
                    bk = 4 + (prod_seq % 2)
                    if prod_seq >= 2:
                        c.wait("tensor", sAc, prod_evac[prod_seq - 2])
                    c.op("tensor", lambda e, ch=ch, ft=ft, bk=bk: e.matmul(
                        out=c.bank(bk),
                        lhsT=SMALLB[0:1, 512 * ch + 128 * ft:512 * ch + 128 * (ft + 1)],
                        rhs=ONESB[0:1, 0:512], start=True, stop=False))
                    for dt in range(8):
                        rhs = ap3(TTB, dt * P, [[TTB_PITCH, P], [1024, 4], [1, P]])
                        if first_of_s and dt == 0:
                            c.wait("tensor", sEv, ln["evv"][3])
                            first_of_s = False
                        fn_ = lambda e, slot=slot, dt=dt, ft=ft, rhs=rhs, bk=bk: e.matmul(
                            out=c.bank(bk),
                            lhsT=WA[:, slot * 4096 + dt * 512 + 128 * ft:
                                    slot * 4096 + dt * 512 + 128 * (ft + 1)],
                            rhs=rhs, start=False, stop=(dt == 7))
                        if dt == 7:
                            mmv = c.op("tensor", fn_, sMM)
                        else:
                            c.op("tensor", fn_)
                    fi = ch * 4 + ft
                    c.wait("scalar", sMM, mmv)
                    prod_evac[prod_seq] = c.op("scalar",
                        lambda e, fi=fi, bk=bk: e.activation(
                            out=H1T[:, fi * 512:(fi + 1) * 512],
                            in_=c.bank(bk), func=AF.Relu), sAc)
                    prod_seq += 1
                wa_done[chunk_seq] = mmv
                chunk_seq += 1
            h1_done = prod_evac[prod_seq - 1]

            ep_bank = {}
            WCV = [H1T[:, 16384:18432], H1T[:, 18432:20480]]
            for oh in range(2):
                for fg in range(8):
                    wslot = wc_seq % 2
                    if wc_seq >= 2:
                        c.wait("gpsimd", sMM, wc_done[wc_seq - 2])
                    c.op("gpsimd", lambda e, fg=fg, oh=oh: e.reg_add(
                        rT, rO2, 512 * fg * 1024 + 512 * oh))
                    c.dmad_if("gpsimd", WCV[wslot].rearrange("p (ft f) -> p ft f", f=512),
                              bass.AP(we2f, rT, [[1024, P], [P * 1024, 4], [1, 512]]),
                              sWC[wslot], rV, DMY)
                    c.waitall("tensor", sWC[wslot])
                    if fg == 0:
                        c.wait("tensor", sAc, h1_done)
                        if s == 0 and oh == 0:
                            c.wait("tensor", sEp, route_ep_val)
                        if oh == 1:
                            for st in range(4):
                                c.wait("tensor", sEp, ep_bank[(st, 0)])
                        for st in range(4):
                            c.op("tensor", lambda e, st=st, oh=oh: e.matmul(
                                out=c.bank(2 * st + oh), lhsT=ONESB[0:1, 0:P],
                                rhs=SMALLB[0:1, 4096 + 512 * oh:4096 + 512 * (oh + 1)],
                                start=True, stop=False))
                    for ftl in range(4):
                        ft = 4 * fg + ftl
                        for st in range(4):
                            fn_ = lambda e, ft=ft, ftl=ftl, st=st, wslot=wslot, oh=oh: \
                                e.matmul(
                                    out=c.bank(2 * st + oh),
                                    lhsT=H1T[:, ft * 512 + st * P:ft * 512 + (st + 1) * P],
                                    rhs=WCV[wslot][:, ftl * 512:(ftl + 1) * 512],
                                    start=False, stop=(ft == 31))
                            if st == 3 and ftl == 3:
                                mmv = c.op("tensor", fn_, sMM)
                            else:
                                c.op("tensor", fn_)
                    wc_done[wc_seq] = mmv
                    wc_seq += 1
                c.wait("vector", sMM, mmv)
                OUTTb = OUTT.bitcast(BF16)
                for st in range(4):
                    if ivv.get(st) is not None:
                        c.wait("vector", sV, ivv[st])
                    if oh == 0 and st == 0:
                        c.waitall("vector", sYW)
                    ep_bank[(st, oh)] = c.op("vector",
                        lambda e, st=st, oh=oh, OUTTb=OUTTb: e.tensor_scalar(
                            out=ap3(OUTTb, 1024 * st + 512 * oh, [[4096, P], [1, 512]]),
                            in0=c.bank(2 * st + oh), scalar1=GSC[:, 48 + st:49 + st],
                            scalar2=None, op0=ALU.mult), sEp)
                if oh == 1:
                    for st in range(4):
                        c.wait("sync", sEp, ep_bank[(st, 1)])
                        c.wait("sync", sEp, ep_bank[(st, 0)])
                        c.dma("sync", ap3(ybuf, (s * SL + st * P) * D, [[D, P], [1, D]]),
                              ap3(OUTT.bitcast(BF16), 1024 * st, [[4096, P], [1, 1024]]), sYW)
            prev_ep_all = ep_bank[(3, 1)]

    c.waitall("gpsimd", sYW)
    out_v = {}
    for t in range(NT):
        sem = sYG
        uv = c.op("vector", lambda e, t=t: e.tensor_copy(
            out=IVU[:, 3:4], in_=PM4[:, t:t + 1]), sV)
        c.wait("gpsimd", sV, uv)
        if t >= 1:
            c.wait("gpsimd", sV, out_v[t - 1])
        c.op("gpsimd", c.dyn(lambda e, t=t: e.indirect_dma_start(
            out=ap3(OUTT.bitcast(BF16), 2048, [[4096, P], [1, 1024]]),
            out_offset=None, in_=ybuf[:, :],
            in_offset=bass.IndirectOffsetOnAxis(ap=IVU[:, 3:4], axis=0),
            bounds_check=NPAD - 1, oob_is_err=False)), sYG, 16)
        sl = (t % 4) * D
        c.wait("sync", sTb, sTb.c)
        sX = sXp[t % 4]
        xv = c.dma("sync", XT[:, sl:sl + D], xsrc_fn(t), sX)
        c.waitall("vector", sX)
        c.waitall("vector", sYG)
        c.waitall("vector", sOutp[0])
        vv = c.op("vector", lambda e, sl=sl: e.tensor_tensor(
            out=OUTT[:, 0:D], in0=XT[:, sl:sl + D],
            in1=ap3(OUTT.bitcast(BF16), 2048, [[4096, P], [1, 1024]]), op=ALU.add), sV)
        out_v[t] = vv
        c.wait("sync", sV, vv)
        c.dma("sync", xdst_fn(t), OUTT[:, 0:D], sOutp[0])
    c.waitall("sync", sOutp[0])
    c.waitall("sync", sSC)
    c.waitall("sync", sYW)
    c.waitall("sync", sYG)


def emit_l1ffn_sp(c, W, xsrc_fn, xdst_fn, xfull):
    nc = c.nc
    TTB, H1T, WA, WC, XT, STAT, LNS, TBF, OUTT, MISC, ACC = (c.bufs[k] for k in
        ("TTB", "H1T", "WA", "WC", "XT", "STAT", "LNS", "TBF", "OUTT", "MISC", "ACC"))
    MASKS, SMALLF, SMALLB, ONESB, ONESF, BCAST, IDENT, WSM2, GSC, CW = (c.bufs[k] for k in
        ("MASKS", "SMALLF", "SMALLB", "ONESB", "ONESF", "BCAST", "IDENT", "WSM2", "GSC", "CW"))
    _SSC = c.sbuf("SSC", [P, L], BF16)
    PM4 = _SSC.bitcast(F32)[:, 0:128]
    IVC = _SSC.bitcast(F32)[:, 128:140]
    IVU = _SSC.bitcast(U32)[:, 140:144]
    IVI = _SSC.bitcast(mybir.dt.int32)[:, 144:146]
    sMM, sEp, sAc, sV, sEv, sSq, sSt, sTb = (c.sem(k) for k in
        ("sMM", "sEp", "sAc", "sV", "sEv", "sSq", "sSt", "sTb"))
    sCst = c.sem("sCst")
    sWA = [c.sem("sWA0"), c.sem("sWA1")]
    sWC = [c.sem("sWC0"), c.sem("sWC1"), c.sem("sWC2"), c.sem("sWC3")]
    sXp = [c.sem("sX0"), c.sem("sX1"), c.sem("sX2"), c.sem("sX3")]
    sOutp = [c.sem("sOutA"), c.sem("sOutB")]
    sBE = c.sem("sBE")
    sIV = [c.sem("sKB0"), c.sem("sKB1")]
    sGX = [c.sem("sZQ0"), c.sem("sZQ1")]
    sSC = c.sem("sSCg")
    sYG = c.sem("sYGg")
    sPF = c.sem("sPF")
    sYW = c.sem("sYW")
    PS = c.bufs["PSALL"]
    TTB_PITCH = 16 * 8 * P
    SL = 512
    NE = 32
    NSLOT = 64
    NPAD = NSLOT * SL
    BIG = 60000.0
    H1F = H1T.bitcast(F32)
    AB = ACC.bitcast(BF16)

    invcw = nc.dram_tensor(c.ns + "l1f_ivc", [NPAD, 4], F32)
    ybuf = nc.dram_tensor(c.ns + "l1f_y", [NPAD, D], BF16)
    sybuf = nc.dram_tensor(c.ns + "l1f_sy", [NTOK, D], BF16)
    w1f = W["w1"]
    w3f = W["w3"]
    w2f = W["w2"]
    b1f = W["b1"]
    b3f = W["b3"]
    b2f = W["b2"]

    c.dma("gpsimd", WSM2[:, :].rearrange("p (dt e) -> p dt e", e=32),
          W["gw"].rearrange("(dt p) e -> p dt e", p=P), sCst)
    c.dma("gpsimd", SMALLF[0:1, 0:32], ap3(W["gb"], 0, [[32, 1], [1, 32]]), sCst)
    c.dma("sync", IVC[:, 8:9], W["cIOTAC"][:, :], c.sem("sC"))
    c.waitall("tensor", sCst)
    c.waitall("tensor", c.sem("sC"))
    v = c.op("tensor", lambda e: e.matmul(out=c.bank(5)[:, 0:32], lhsT=ONESF[:, :],
                                          rhs=SMALLF[0:1, 0:32], start=True, stop=True), sMM)
    c.wait("scalar", sMM, v)
    gbB_v = c.op("scalar", lambda e: e.copy(out=BCAST[:, 0:32], in_=c.bank(5)[:, 0:32]), sAc)
    c.op("vector", lambda e: e.memset(STAT[:, 31:32], 0.0), sV)
    for e32 in range(NE):
        c.op("vector", lambda e, e32=e32: e.memset(
            MASKS[:, 96 + e32:97 + e32], float(e32)), sV)
    c.op("vector", lambda e: e.memset(MISC[:, 0:128], 0.0), sV)
    pf_v = c.op("vector", lambda e: e.memset(
        ap3(MISC, 0, [[1024, P], [4, 32]]), 1.0e9), sV)
    c.wait("sync", sV, pf_v)
    for blk in range(NPAD // (P * 32)):
        c.dma("sync", ap3(invcw, blk * P * 32 * 4, [[128, P], [1, 128]]),
              MISC[:, 0:128], sPF)
    c.waitall("scalar", sPF)

    chunk_seq = 0
    wa_done = {}
    prod_seq = 0
    prod_evac = {}

    def sh_prod(half, gl, wsrc, bias_off, slot0, act, first_tw=None):
        nonlocal chunk_seq, prod_seq
        last_ev = None
        for ch in range(2):
            slot = chunk_seq % 2
            if chunk_seq >= 2:
                c.wait("gpsimd", sMM, wa_done[chunk_seq - 2])
            c.dma("gpsimd",
                  WA[:, slot * 4096:(slot + 1) * 4096]
                  .rearrange("p (dt f) -> p dt f", f=512),
                  wsrc[:, ch * 512:(ch + 1) * 512]
                  .rearrange("(dt p) f -> p dt f", p=P), sWA[slot])
            c.waitall("tensor", sWA[slot])
            if ch == 0:
                c.waitall("tensor", sBE)
                if first_tw is not None:
                    for w_ in first_tw:
                        c.wait("tensor", *w_)
            for ft in range(4):
                for th in range(2):
                    bk = 4 + (prod_seq % 2)
                    if prod_seq >= 2:
                        c.wait("tensor", sAc, prod_evac[prod_seq - 2])
                    c.op("tensor", lambda e, ch=ch, ft=ft, bk=bk, bias_off=bias_off: e.matmul(
                        out=c.bank(bk),
                        lhsT=SMALLB[0:1, bias_off + 512 * ch + 128 * ft:
                                    bias_off + 512 * ch + 128 * (ft + 1)],
                        rhs=ONESB[0:1, 0:512], start=True, stop=False))
                    for dt in range(8):
                        off = (gl * 8 + th * 4) * 1024 + dt * P
                        rhs = ap3(TTB, off, [[TTB_PITCH, P], [1024, 4], [1, P]])
                        fn_ = lambda e, slot=slot, dt=dt, ft=ft, rhs=rhs, bk=bk: e.matmul(
                            out=c.bank(bk),
                            lhsT=WA[:, slot * 4096 + dt * 512 + 128 * ft:
                                    slot * 4096 + dt * 512 + 128 * (ft + 1)],
                            rhs=rhs, start=False, stop=(dt == 7))
                        if dt == 7:
                            mmv = c.op("tensor", fn_, sMM)
                        else:
                            c.op("tensor", fn_)
                    fi = slot0 + ch * 4 + ft
                    c.wait("scalar", sMM, mmv)
                    prod_evac[prod_seq] = c.op("scalar",
                        lambda e, fi=fi, th=th, bk=bk, act=act: e.activation(
                            out=H1T[:, fi * 1024 + th * 512: fi * 1024 + th * 512 + 512],
                            in_=c.bank(bk), func=act), sAc)
                    last_ev = prod_evac[prod_seq]
                    prod_seq += 1
            wa_done[chunk_seq] = mmv
            chunk_seq += 1
        return last_ev

    sh_wc_seq = 0
    sh_wc_done = {}

    def sh_out(half, gl, hs_ready, ab_guard=None):
        nonlocal sh_wc_seq
        ep_bank = {}
        syv = {}
        for oh in range(2):
            for ft in range(8):
                slot = sh_wc_seq % 4
                if sh_wc_seq >= 4:
                    c.wait("gpsimd", sMM, sh_wc_done[sh_wc_seq - 4])
                c.dma("gpsimd", WC[:, slot * 512:(slot + 1) * 512],
                      W["sw2"][128 * ft:128 * (ft + 1), 512 * oh:512 * (oh + 1)],
                      sWC[slot])
                c.waitall("tensor", sWC[slot])
                if ft == 0:
                    c.wait("tensor", sAc, hs_ready)
                    if oh == 1:
                        for tau in range(8):
                            c.wait("tensor", sAc, ep_bank[(tau, 0)])
                    if ab_guard is not None and oh == 0:
                        for w_ in ab_guard:
                            c.wait("tensor", *w_)
                    for tau in range(8):
                        c.op("tensor", lambda e, tau=tau, oh=oh: e.matmul(
                            out=c.bank(tau), lhsT=ONESB[0:1, 0:P],
                            rhs=SMALLB[0:1, 4096 + 512 * oh:4096 + 512 * (oh + 1)],
                            start=True, stop=False))
                for tau in range(8):
                    fn_ = lambda e, ft=ft, tau=tau, slot=slot: e.matmul(
                        out=c.bank(tau),
                        lhsT=H1T[:, (24 + ft) * 1024 + tau * P:
                                 (24 + ft) * 1024 + (tau + 1) * P],
                        rhs=WC[:, slot * 512:(slot + 1) * 512],
                        start=False, stop=(ft == 7))
                    if tau == 7:
                        mmv = c.op("tensor", fn_, sMM)
                    else:
                        c.op("tensor", fn_)
                sh_wc_done[sh_wc_seq] = mmv
                sh_wc_seq += 1
            c.wait("scalar", sMM, mmv)
            for tau in range(8):
                tl = gl * 8 + tau
                ep_bank[(tau, oh)] = c.op("scalar",
                    lambda e, tau=tau, oh=oh, tl=tl: e.activation(
                        out=ap3(AB, tl * 1024 + oh * 512, [[16384, P], [1, 512]]),
                        in_=c.bank(tau), func=AF.Identity), sAc)
                if oh == 1:
                    tid = half * 16 + tl
                    c.wait("sync", sAc, ep_bank[(tau, 0)])
                    c.wait("sync", sAc, ep_bank[(tau, 1)])
                    syv[tau] = c.dma("sync", ap3(sybuf, tid * P * D, [[D, P], [1, D]]),
                                     ap3(AB, tl * 1024, [[16384, P], [1, 1024]]), sYW)
        return ep_bank[(7, 1)], syv

    ln_prev = None
    tr_ev = {}
    sh_ep_last = None
    for half in range(2):
        tiles = list(range(half * 16, (half + 1) * 16))
        guards = None
        if ln_prev is not None:
            guards = {sl_: (sTb, ln_prev["tbv"][12 + sl_]) for sl_ in range(4)}
        if sh_ep_last is not None:
            c.wait("tensor", sAc, sh_ep_last)
        ln = emit_ln(c, xsrc_fn, tiles, slot0=0, xt_guards=guards)
        ln_prev = ln
        um_v = {}
        for i in range(16):
            tid = tiles[i]
            bk = 6 + (i % 2)
            c.wait("tensor", sEv, ln["evv"][i])
            if i >= 2:
                c.wait("tensor", sEp, um_v[i - 2])
            elif sh_ep_last is not None:
                c.wait("tensor", sAc, sh_ep_last)
            if half == 0 and i == 0:
                c.wait("tensor", sAc, gbB_v)
            for dt in range(8):
                fn_ = lambda e, i=i, dt=dt, bk=bk: e.matmul(
                    out=c.bank(bk)[:, 0:32],
                    lhsT=TTB[:, i * 1024 + dt * P: i * 1024 + (dt + 1) * P],
                    rhs=WSM2[:, 32 * dt:32 * dt + 32], start=(dt == 0), stop=(dt == 7))
                if dt == 7:
                    mmv = c.op("tensor", fn_, sMM)
                else:
                    c.op("tensor", fn_)
            c.wait("vector", sMM, mmv)
            if half == 0 and i == 0:
                c.wait("vector", sAc, gbB_v)
            c.opS("vector", lambda e, bk=bk: e.tensor_tensor(
                out=GSC[:, 0:32], in0=c.bank(bk)[:, 0:32], in1=BCAST[:, 0:32], op=ALU.add))
            c.opS("vector", lambda e: e.tensor_reduce(
                out=GSC[:, 32:33], in_=GSC[:, 0:32], axis=AX.X, op=ALU.max))
            nm = c.op("vector", lambda e: e.tensor_scalar(
                out=GSC[:, 33:34], in0=GSC[:, 32:33], scalar1=-1.0, scalar2=None,
                op0=ALU.mult), sSt)
            c.wait("scalar", sSt, nm)
            ex = c.op("scalar", lambda e: e.activation(
                out=GSC[:, 64:96], in_=GSC[:, 0:32], func=AF.Exp,
                bias=GSC[:, 33:34], scale=1.0, accum_out=GSC[:, 34:35]), sSq)
            c.wait("vector", sSq, ex)
            c.opS("vector", lambda e: e.reciprocal(out=GSC[:, 35:36], in_=GSC[:, 34:35]))
            c.opS("vector", lambda e: e.tensor_scalar(
                out=GSC[:, 0:32], in0=GSC[:, 64:96], scalar1=GSC[:, 35:36],
                scalar2=None, op0=ALU.mult))
            pairs = [(0, 1), (0, 2), (0, 3), (1, 2), (1, 3), (2, 3)]
            for pi, (a, b) in enumerate(pairs):
                c.op("vector", lambda e, pi=pi, a=a, b=b: e.tensor_tensor(
                    out=GSC[:, 96 + 8 * pi:96 + 8 * (pi + 1)],
                    in0=ap3(GSC, a, [[168, P], [4, 8]]),
                    in1=ap3(GSC, b, [[168, P], [4, 8]]), op=ALU.add))
            c.opS("vector", lambda e: e.nop())
            c.opS("vector", lambda e: e.tensor_reduce(
                out=GSC[:, 144:152], in_=ap3(GSC, 96, [[168, P], [1, 8], [8, 6]]),
                axis=AX.X, op=ALU.max))
            c.op("vector", lambda e: e.memset(GSC[:, 152:160], 0.0))
            for gi in range(8):
                c.opS("vector", lambda e, gi=gi: e.tensor_scalar(
                    out=GSC[:, 160:168], in0=GSC[:, 144:152],
                    scalar1=GSC[:, 144 + gi:145 + gi], scalar2=None, op0=ALU.is_lt))
                c.opS("vector", lambda e: e.tensor_tensor(
                    out=GSC[:, 152:160], in0=GSC[:, 152:160], in1=GSC[:, 160:168],
                    op=ALU.add))
            c.opS("vector", lambda e: e.tensor_scalar(
                out=GSC[:, 152:160], in0=GSC[:, 152:160], scalar1=3.5, scalar2=None,
                op0=ALU.is_le))
            c.opS("vector", lambda e: e.tensor_scalar(
                out=GSC[:, 64:96], in0=GSC[:, 0:32], scalar1=BIG, scalar2=None, op0=ALU.add))
            c.opS("vector", lambda e: e.tensor_tensor(
                out=GSC[:, 64:96], in0=GSC[:, 64:96],
                in1=ap3(GSC, 152, [[168, P], [1, 8], [0, 4]]), op=ALU.mult))
            c.opS("vector", lambda e: e.tensor_scalar(
                out=GSC[:, 64:96], in0=GSC[:, 64:96], scalar1=-BIG, scalar2=None, op0=ALU.add))
            c.op("vector", lambda e: e.memset(GSC[:, 96:128], 0.0))
            for r in range(4):
                c.opS("vector", lambda e: e.tensor_reduce(
                    out=GSC[:, 36:37], in_=GSC[:, 64:96], axis=AX.X, op=ALU.max))
                c.opS("vector", lambda e: e.tensor_scalar(
                    out=MASKS[:, 0:32], in0=GSC[:, 64:96], scalar1=GSC[:, 36:37],
                    scalar2=None, op0=ALU.is_ge))
                c.opS("vector", lambda e: e.tensor_tensor_scan(
                    out=MASKS[:, 32:64], data0=MASKS[:, 0:32], data1=MASKS[:, 0:32],
                    initial=STAT[:, 31:32], op0=ALU.add, op1=ALU.bypass))
                c.opS("vector", lambda e: e.tensor_tensor(
                    out=MASKS[:, 32:64], in0=MASKS[:, 32:64], in1=MASKS[:, 0:32],
                    op=ALU.is_equal))
                c.opS("vector", lambda e: e.tensor_tensor(
                    out=MASKS[:, 32:64], in0=MASKS[:, 32:64], in1=MASKS[:, 0:32],
                    op=ALU.mult))
                c.opS("vector", lambda e: e.tensor_tensor(
                    out=GSC[:, 96:128], in0=GSC[:, 96:128], in1=MASKS[:, 32:64],
                    op=ALU.max))
                c.opS("vector", lambda e: e.tensor_tensor(
                    out=GSC[:, 128:160], in0=MASKS[:, 32:64], in1=MASKS[:, 96:128],
                    op=ALU.mult))
                c.op("vector", lambda e, tid=tid, r=r: e.tensor_reduce(
                    out=CW[:, 8 * tid + r:8 * tid + r + 1],
                    in_=GSC[:, 128:160], axis=AX.X, op=ALU.add), sEp)
                c.opS("vector", lambda e: e.tensor_tensor(
                    out=GSC[:, 128:160], in0=MASKS[:, 32:64], in1=GSC[:, 0:32],
                    op=ALU.mult))
                c.op("vector", lambda e, tid=tid, r=r: e.tensor_reduce(
                    out=CW[:, 8 * tid + 4 + r:8 * tid + 5 + r],
                    in_=GSC[:, 128:160], axis=AX.X, op=ALU.add), sEp)
                c.opS("vector", lambda e: e.scalar_tensor_tensor(
                    out=GSC[:, 64:96], in0=MASKS[:, 32:64], scalar=-2 * BIG,
                    in1=GSC[:, 64:96], op0=ALU.mult, op1=ALU.add))
            um_v[i] = c.op("vector", lambda e: e.nop(), sEp)
            bkt = 4 + (i % 2)
            c.wait("tensor", sEp, um_v[i])
            if tr_ev.get(tid - 2) is not None:
                c.wait("tensor", sAc, tr_ev[tid - 2])
            trv = c.op("tensor", lambda e, bkt=bkt: e.transpose(
                out=ap3(PS, bkt * 512, [[4096, 32], [1, P]]),
                in_=GSC[:, 96:128], identity=IDENT[:, :]), sMM)
            c.wait("scalar", sMM, trv)
            tr_ev[tid] = c.op("scalar", lambda e, tid=tid, bkt=bkt: e.copy(
                out=ap3(H1F, tid * P, [[16384, 32], [1, P]]),
                in_=ap3(PS, bkt * 512, [[4096, 32], [1, P]])), sAc)

        c.wait("gpsimd", sMM, mmv)
        c.dma("gpsimd", SMALLB[:, 0:1024], ap3(W["sb1"], 0, [[1024, 1], [1, 1024]]), sBE)
        c.dma("gpsimd", SMALLB[:, 1024:2048], ap3(W["sb3"], 0, [[1024, 1], [1, 1024]]), sBE)
        c.dma("gpsimd", SMALLB[:, 4096:5120], ap3(W["sb2"], 0, [[1024, 1], [1, 1024]]), sBE)
        for gl in range(2):
            ftw = [(sAc, tr_ev[half * 16 + 15])]
            if sh_ep_last is not None:
                ftw.append((sAc, sh_ep_last))
            sh_prod(half, gl, W["sw1"], 0, 8, AF.Silu, first_tw=ftw)
            ev3 = sh_prod(half, gl, W["sw3"], 1024, 16, AF.Identity)
            c.wait("vector", sAc, ev3)
            hsv = None
            for fi in range(8):
                hsv = c.op("vector", lambda e, fi=fi: e.tensor_tensor(
                    out=H1T[:, (24 + fi) * 1024:(25 + fi) * 1024],
                    in0=H1T[:, (8 + fi) * 1024:(9 + fi) * 1024],
                    in1=H1T[:, (16 + fi) * 1024:(17 + fi) * 1024], op=ALU.mult), sV)
            c.wait("scalar", sV, hsv)
            hs_ready = c.op("scalar", lambda e: e.copy(
                out=MISC[:, 128:129], in_=STAT[:, 31:32]), sAc)
            if half == 1 and gl == 0:
                c.waitall("scalar", sYW)
            sh_ep_last, _syv = sh_out(half, gl, hs_ready)

    c.wait("vector", sAc, tr_ev[NT - 1])
    c.op("vector", lambda e: e.memset(GSC[0:NE, 0:1], 0.0))
    sc_v = c.opS("vector", lambda e: e.tensor_tensor_scan(
        out=ap3(H1F, 0, [[16384, NE], [1, NTOK]]),
        data0=ap3(H1F, 0, [[16384, NE], [1, NTOK]]),
        data1=ap3(H1F, 0, [[16384, NE], [1, NTOK]]),
        initial=GSC[0:NE, 0:1], op0=ALU.add, op1=ALU.bypass))
    cc_v = c.op("vector", lambda e: e.tensor_copy(
        out=GSC[0:NE, 1:2], in_=ap3(H1F, NTOK - 1, [[16384, NE], [1, 1]])), sSt)
    c.wait("tensor", sSt, cc_v)
    c.wait("tensor", sAc, sh_ep_last)
    trc = c.op("tensor", lambda e: e.transpose(
        out=ap3(PS, 0, [[4096, 1], [1, NE]]), in_=GSC[0:NE, 1:2],
        identity=IDENT[0:NE, 0:NE]), sMM)
    c.wait("vector", sMM, trc)
    c.opS("vector", lambda e: e.tensor_scalar(
        out=GSC[0:1, 64:96], in0=ap3(PS, 0, [[4096, 1], [1, NE]]),
        scalar1=float(SL - 1), scalar2=1.0 / SL, op0=ALU.add, op1=ALU.mult))
    c.opS("vector", lambda e: e.tensor_copy(
        out=MASKS.bitcast(U32)[0:1, 0:32], in_=GSC[0:1, 64:96]))
    c.opS("vector", lambda e: e.tensor_copy(
        out=GSC[0:1, 64:96], in_=MASKS.bitcast(U32)[0:1, 0:32]))
    c.opS("vector", lambda e: e.tensor_scalar(
        out=GSC[0:1, 64:96], in0=GSC[0:1, 64:96], scalar1=float(SL), scalar2=None,
        op0=ALU.mult))
    c.op("vector", lambda e: e.memset(GSC[0:1, 63:64], 0.0))
    c.opS("vector", lambda e: e.tensor_tensor_scan(
        out=GSC[0:1, 96:128], data0=GSC[0:1, 64:96], data1=GSC[0:1, 64:96],
        initial=GSC[0:1, 63:64], op0=ALU.add, op1=ALU.bypass))
    of_v = c.op("vector", lambda e: e.tensor_tensor(
        out=GSC[0:1, 96:128], in0=GSC[0:1, 96:128], in1=GSC[0:1, 64:96],
        op=ALU.subtract), sSt)
    c.wait("tensor", sSt, of_v)
    obm = c.op("tensor", lambda e: e.matmul(
        out=c.bank(6)[:, 0:NE], lhsT=ONESF[:, :], rhs=GSC[0:1, 96:128],
        start=True, stop=True), sMM)
    c.wait("scalar", sMM, obm)
    ofe = c.op("scalar", lambda e: e.copy(out=BCAST[:, 0:NE], in_=c.bank(6)[:, 0:NE]), sAc)

    ps_rd = {}
    sc_vals = []
    for t in range(NT):
        bkt = (t % 2)
        if t >= 2:
            c.wait("tensor", c.sem("sSelf_vector"), ps_rd[t - 2])
        if t == 0:
            c.wait("tensor", sAc, ofe)
        trv = c.op("tensor", lambda e, t=t, bkt=bkt: e.transpose(
            out=ap3(PS, bkt * 512, [[4096, P], [1, NE]]),
            in_=ap3(H1F, t * P, [[16384, NE], [1, P]]),
            identity=IDENT[0:NE, 0:NE]), sMM)
        c.wait("vector", sMM, trv)
        ps_rd[t] = c.opS("vector", lambda e, bkt=bkt: e.scalar_tensor_tensor(
            out=GSC[:, 96:128], in0=ap3(PS, bkt * 512, [[4096, P], [1, NE]]),
            scalar=-1.0, in1=BCAST[:, 0:NE], op0=ALU.add, op1=ALU.add))
        c.op("vector", lambda e: e.memset(GSC[:, 47:48], 0.0))
        c.op("vector", lambda e: e.memset(GSC[:, 51:52], 0.0))
        for r in range(4):
            pcol = 44 if r % 2 == 0 else 48
            icol = 2 if r % 2 == 0 else 3
            c.opS("vector", lambda e, t=t, r=r: e.tensor_scalar(
                out=GSC[:, 64:96], in0=MASKS[:, 96:128],
                scalar1=CW[:, 8 * t + r:8 * t + r + 1],
                scalar2=None, op0=ALU.is_equal))
            c.opS("vector", lambda e: e.tensor_tensor(
                out=GSC[:, 64:96], in0=GSC[:, 64:96], in1=GSC[:, 96:128],
                op=ALU.mult))
            pv = c.op("vector", lambda e, t=t, r=r: e.tensor_reduce(
                out=PM4[:, 4 * t + r:4 * t + r + 1], in_=GSC[:, 64:96],
                axis=AX.X, op=ALU.add), sEp)
            c.wait("vector", sEp, pv)
            if len(sc_vals) >= 2:
                c.wait("vector", sSC, sc_vals[-2])
            c.opS("vector", lambda e, t=t, pcol=pcol: e.tensor_scalar(
                out=GSC[:, pcol:pcol + 1], in0=IVC[:, 8:9], scalar1=float(P * t),
                scalar2=None, op0=ALU.add))
            c.opS("vector", lambda e, t=t, r=r, pcol=pcol: e.tensor_copy(
                out=GSC[:, pcol + 1:pcol + 2],
                in_=CW[:, 8 * t + 4 + r:8 * t + 5 + r]))
            c.opS("vector", lambda e, t=t, r=r, pcol=pcol: e.tensor_copy(
                out=GSC[:, pcol + 2:pcol + 3],
                in_=CW[:, 8 * t + r:8 * t + r + 1]))
            uv = c.op("vector", lambda e, t=t, r=r, icol=icol: e.tensor_copy(
                out=IVU[:, icol:icol + 1], in_=PM4[:, 4 * t + r:4 * t + r + 1]), sV)
            c.wait("gpsimd", sV, uv)
            if t == 0 and r == 0:
                c.waitall("gpsimd", sPF)
            gsc_pay = bass.AP(GSC, pcol, [[168, P], [1, 4]])
            c.op("gpsimd", c.dyn(lambda e, gsc_pay=gsc_pay, icol=icol: e.indirect_dma_start(
                out=invcw[:, :],
                out_offset=bass.IndirectOffsetOnAxis(ap=IVU[:, icol:icol + 1], axis=0),
                in_=gsc_pay, in_offset=None,
                bounds_check=NPAD - 1, oob_is_err=False)), sSC, 16)
            sc_vals.append(sSC.c)
    route_ep_val = sEp.c
    route_mm_val = sMM.c

    prod_evac = {}
    prod_seq = 0
    wa_done = {}
    chunk_seq = 0
    wc_done = {}
    wc_seq = 0
    prev_ep_all = None
    prev_tbv = {sl_: ln_prev["tbv"][12 + sl_] for sl_ in range(4)}
    ivc_prev = {}
    if True:
        rE = c.es.enter_context(c.nc.gpsimd.register(c.ns + "rE1"))
        rO1 = c.es.enter_context(c.nc.gpsimd.register(c.ns + "rO1x"))
        rT = c.es.enter_context(c.nc.gpsimd.register(c.ns + "rT1"))
        rV = c.es.enter_context(c.nc.gpsimd.register(c.ns + "rV1"))
        for s in range(NSLOT):
            ivv = {}

            def loader(i, dst, s=s, ivv=ivv):
                st = i
                sem = sGX[st % 2]
                if s == 0 and st == 0:
                    c.waitall("sync", sPF)
                    c.waitall("sync", sSC)
                    c.waitall("vector", sSC)
                if ivc_prev.get(st % 2) is not None:
                    c.wait("sync", sV, ivc_prev[st % 2])
                c.dma("sync", IVC[:, 4 * (st % 2):4 * (st % 2) + 4],
                      ap3(invcw, (s * SL + st * P) * 4, [[4, P], [1, 4]]), sIV[st % 2])
                c.waitall("vector", sIV[st % 2])
                uv = c.op("vector", lambda e, st=st: e.tensor_copy(
                    out=IVU[:, st % 2:st % 2 + 1],
                    in_=IVC[:, 4 * (st % 2):4 * (st % 2) + 1]), sV)
                cwv = c.op("vector", lambda e, st=st: e.tensor_copy(
                    out=GSC[:, 48 + st:49 + st],
                    in_=IVC[:, 4 * (st % 2) + 1:4 * (st % 2) + 2]), sV)
                ivv[st] = cwv
                ivc_prev[st % 2] = cwv
                if st == 0:
                    eidv = c.op("vector", lambda e: e.tensor_copy(
                        out=IVC[:, 10:11], in_=IVC[:, 2:3]), sV)
                    tkv = c.op("vector", lambda e: e.tensor_copy(
                        out=IVC[:, 11:12], in_=IVC[:, 0:1]), sV)
                    ivv["eid"] = eidv
                    ivc_prev[0] = tkv
                c.wait("gpsimd", sV, uv)
                if prev_tbv is not None:
                    c.wait("gpsimd", sTb, prev_tbv[st])
                c.op("gpsimd", c.dyn(lambda e, st=st, dst=dst: e.indirect_dma_start(
                    out=dst, out_offset=None, in_=xfull,
                    in_offset=bass.IndirectOffsetOnAxis(ap=IVU[:, st % 2:st % 2 + 1], axis=0),
                    bounds_check=NTOK - 1, oob_is_err=False)), sem, 16)
                return sem

            gd = {sl_: (sTb, prev_tbv[sl_]) for sl_ in range(4)}
            ln = emit_ln(c, None, [0, 1, 2, 3], slot0=0, xt_guards=gd, loader=loader)
            prev_tbv = ln["tbv"]
            c.wait("vector", sV, ivv["eid"])
            c.opS("vector", lambda e: e.tensor_scalar(
                out=IVC[:, 9:10], in0=IVC[:, 10:11], scalar1=0.0, scalar2=float(NE - 1),
                op0=ALU.max, op1=ALU.min))
            eci = c.op("vector", lambda e: e.tensor_copy(
                out=IVI[:, 0:1], in_=IVC[:, 9:10]), sV)
            c.opS("vector", lambda e: e.tensor_scalar(
                out=IVC[:, 11:12], in0=IVC[:, 11:12], scalar1=float(NTOK),
                scalar2=None, op0=ALU.is_lt))
            if s > 0:
                c.opS("vector", lambda e: e.tensor_scalar(
                    out=GSC[:, 62:63], in0=IVC[:, 10:11], scalar1=GSC[:, 61:62],
                    scalar2=None, op0=ALU.not_equal))
                c.opS("vector", lambda e: e.tensor_tensor(
                    out=IVC[:, 11:12], in0=IVC[:, 11:12], in1=GSC[:, 62:63],
                    op=ALU.mult))
            fci = c.op("vector", lambda e: e.tensor_copy(
                out=IVI[:, 1:2], in_=IVC[:, 11:12]), sV)
            c.op("vector", lambda e: e.tensor_copy(
                out=GSC[:, 61:62], in_=IVC[:, 10:11]), sV)
            c.wait("gpsimd", sV, eci)
            c.wait("gpsimd", sV, fci)
            c.op("gpsimd", lambda e: e.reg_load(rE, IVI[0:1, 0:1]))
            c.op("gpsimd", lambda e: e.reg_load(rV, IVI[0:1, 1:2]))
            c.op("gpsimd", lambda e: e.reg_mul(rO1, rE, 1024 * 512))
            DMY = (GSC[0:1, 58:59], GSC[0:1, 44:45])
            if prev_ep_all is not None:
                c.wait("gpsimd", sEp, prev_ep_all)
            elif s == 0:
                c.wait("gpsimd", sEp, route_ep_val)
            c.op("gpsimd", lambda e: e.reg_mul(rT, rE, 512))
            c.dmad_if("gpsimd", SMALLB[:, 0:512],
                      bass.AP(b1f, rT, [[512, 1], [1, 512]]), sBE, rV, DMY)
            c.dmad_if("gpsimd", SMALLB[:, 512:1024],
                      bass.AP(b3f, rT, [[512, 1], [1, 512]]), sBE, rV, DMY)
            c.op("gpsimd", lambda e: e.reg_mul(rT, rE, 1024))
            c.dmad_if("gpsimd", SMALLB[:, 4096:5120],
                      bass.AP(b2f, rT, [[1024, 1], [1, 1024]]), sBE, rV, DMY)
            first_of_s = True
            h1ev = {}
            for wsel in range(2):
                wf = w1f if wsel == 0 else w3f
                slot = chunk_seq % 2
                if chunk_seq >= 2:
                    c.wait("gpsimd", sMM, wa_done[chunk_seq - 2])
                c.op("gpsimd", lambda e: e.reg_add(rT, rO1, 0))
                c.dmad_if("gpsimd",
                          WA[:, slot * 4096:(slot + 1) * 4096]
                          .rearrange("p (dt f) -> p dt f", f=512),
                          bass.AP(wf, rT, [[512, P], [P * 512, 8], [1, 512]]),
                          sWA[slot], rV, DMY)
                c.waitall("tensor", sWA[slot])
                if wsel == 0:
                    c.waitall("tensor", sBE)
                    if prev_ep_all is not None:
                        c.wait("tensor", sEp, prev_ep_all)
                for ft in range(4):
                    bk = 4 + (prod_seq % 2)
                    if prod_seq >= 2:
                        c.wait("tensor", sAc, prod_evac[prod_seq - 2])
                    c.op("tensor", lambda e, ft=ft, bk=bk, wsel=wsel: e.matmul(
                        out=c.bank(bk),
                        lhsT=SMALLB[0:1, wsel * 512 + 128 * ft:wsel * 512 + 128 * (ft + 1)],
                        rhs=ONESB[0:1, 0:512], start=True, stop=False))
                    for dt in range(8):
                        rhs = ap3(TTB, dt * P, [[TTB_PITCH, P], [1024, 4], [1, P]])
                        if first_of_s and dt == 0:
                            c.wait("tensor", sEv, ln["evv"][3])
                            first_of_s = False
                        fn_ = lambda e, slot=slot, dt=dt, ft=ft, rhs=rhs, bk=bk: e.matmul(
                            out=c.bank(bk),
                            lhsT=WA[:, slot * 4096 + dt * 512 + 128 * ft:
                                    slot * 4096 + dt * 512 + 128 * (ft + 1)],
                            rhs=rhs, start=False, stop=(dt == 7))
                        if dt == 7:
                            mmv = c.op("tensor", fn_, sMM)
                        else:
                            c.op("tensor", fn_)
                    if wsel == 0:
                        c.wait("scalar", sMM, mmv)
                        prod_evac[prod_seq] = c.op("scalar",
                            lambda e, ft=ft, bk=bk: e.activation(
                                out=H1T[:, ft * 512:(ft + 1) * 512],
                                in_=c.bank(bk), func=AF.Silu), sAc)
                        h1ev[ft] = prod_evac[prod_seq]
                    else:
                        c.wait("vector", sMM, mmv)
                        c.wait("vector", sAc, h1ev[ft])
                        prod_evac[prod_seq] = c.op("vector",
                            lambda e, ft=ft, bk=bk: e.tensor_tensor(
                                out=H1T[:, (8 + ft) * 512:(9 + ft) * 512],
                                in0=c.bank(bk),
                                in1=H1T[:, ft * 512:(ft + 1) * 512],
                                op=ALU.mult), sAc)
                    prod_seq += 1
                wa_done[chunk_seq] = mmv
                chunk_seq += 1
            hsv = prod_evac[prod_seq - 1]
            ep_bank = {}
            WCV = [WC[:, 0:2048], H1T[:, 6144:8192]]
            for oh in range(2):
                if wc_seq >= 2:
                    c.wait("gpsimd", sMM, wc_done[wc_seq - 2])
                if s == 0:
                    c.wait("gpsimd", sEp, route_ep_val)
                c.op("gpsimd", lambda e, oh=oh: e.reg_add(rT, rO1, 512 * oh))
                c.dmad_if("gpsimd",
                          WCV[oh].rearrange("p (ft f) -> p ft f", f=512),
                          bass.AP(w2f, rT, [[1024, P], [P * 1024, 4], [1, 512]]),
                          sWC[oh], rV, DMY)
                c.waitall("tensor", sWC[oh])
                c.wait("tensor", sAc, hsv)
                if s == 0 and oh == 0:
                    c.wait("tensor", sEp, route_ep_val)
                    c.wait("tensor", sMM, route_mm_val)
                if oh == 1:
                    for st in range(4):
                        c.wait("tensor", sEp, ep_bank[(st, 0)])
                for st in range(4):
                    c.op("tensor", lambda e, st=st, oh=oh: e.matmul(
                        out=c.bank(2 * st + oh), lhsT=ONESB[0:1, 0:P],
                        rhs=SMALLB[0:1, 4096 + 512 * oh:4096 + 512 * (oh + 1)],
                        start=True, stop=False))
                for ft in range(4):
                    for st in range(4):
                        fn_ = lambda e, ft=ft, st=st, oh=oh: e.matmul(
                            out=c.bank(2 * st + oh),
                            lhsT=H1T[:, (8 + ft) * 512 + st * P:(8 + ft) * 512 + (st + 1) * P],
                            rhs=WCV[oh][:, ft * 512:(ft + 1) * 512],
                            start=False, stop=(ft == 3))
                        if ft == 3 and st == 3:
                            mmv = c.op("tensor", fn_, sMM)
                        else:
                            c.op("tensor", fn_)
                wc_done[wc_seq] = mmv
                wc_seq += 1
                c.wait("vector", sMM, mmv)
                OUTTb = OUTT.bitcast(BF16)
                for st in range(4):
                    if ivv.get(st) is not None:
                        c.wait("vector", sV, ivv[st])
                    if oh == 0 and st == 0:
                        c.waitall("vector", sYW)
                    ep_bank[(st, oh)] = c.op("vector",
                        lambda e, st=st, oh=oh, OUTTb=OUTTb: e.tensor_scalar(
                            out=ap3(OUTTb, 1024 * st + 512 * oh, [[4096, P], [1, 512]]),
                            in0=c.bank(2 * st + oh), scalar1=GSC[:, 48 + st:49 + st],
                            scalar2=None, op0=ALU.mult), sEp)
                if oh == 1:
                    for st in range(4):
                        c.wait("sync", sEp, ep_bank[(st, 1)])
                        c.wait("sync", sEp, ep_bank[(st, 0)])
                        c.dma("sync", ap3(ybuf, (s * SL + st * P) * D, [[D, P], [1, D]]),
                              ap3(OUTT.bitcast(BF16), 1024 * st, [[4096, P], [1, 1024]]), sYW)
            prev_ep_all = ep_bank[(3, 1)]

    c.waitall("gpsimd", sYW)
    c.waitall("gpsimd", sMM)
    out_v = {}
    H1Tb_off = 28 * 1024
    for t in range(NT):
        uvl = {}
        for r in range(4):
            uvl[r] = c.op("vector", lambda e, t=t, r=r: e.tensor_copy(
                out=IVU[:, r:r + 1], in_=PM4[:, 4 * t + r:4 * t + r + 1]), sV)
        if t >= 1:
            c.wait("gpsimd", sV, out_v[t - 1])
        for r in range(4):
            c.wait("gpsimd", sV, uvl[r])
            c.op("gpsimd", c.dyn(lambda e, t=t, r=r: e.indirect_dma_start(
                out=ap3(H1T, H1Tb_off + r * 1024, [[32768, P], [1, 1024]]),
                out_offset=None, in_=ybuf[:, :],
                in_offset=bass.IndirectOffsetOnAxis(ap=IVU[:, r:r + 1], axis=0),
                bounds_check=NPAD - 1, oob_is_err=False)), sYG, 16)
        sl = (t % 4) * D
        c.wait("sync", sTb, sTb.c)
        sX = sXp[t % 4]
        xv = c.dma("sync", XT[:, sl:sl + D], xsrc_fn(t), sX)
        if t >= 1:
            c.wait("sync", sV, out_v[t - 1])
        c.dma("sync", ap3(OUTT.bitcast(BF16), 2048, [[4096, P], [1, 1024]]),
              ap3(sybuf, t * P * D, [[D, P], [1, D]]), sOutp[1])
        c.waitall("vector", sX)
        c.waitall("vector", sYG)
        c.waitall("vector", sOutp[1])
        c.waitall("vector", sOutp[0])
        c.opS("vector", lambda e, sl=sl: e.tensor_tensor(
            out=OUTT[:, 0:D], in0=XT[:, sl:sl + D],
            in1=ap3(OUTT.bitcast(BF16), 2048, [[4096, P], [1, 1024]]), op=ALU.add))
        for r in range(4):
            fn_ = lambda e, r=r: e.tensor_tensor(
                out=OUTT[:, 0:D], in0=OUTT[:, 0:D],
                in1=ap3(H1T, H1Tb_off + r * 1024, [[32768, P], [1, 1024]]), op=ALU.add)
            if r == 3:
                vv = c.op("vector", fn_, sV)
            else:
                c.opS("vector", fn_)
        out_v[t] = vv
        c.wait("sync", sV, vv)
        c.dma("sync", xdst_fn(t), OUTT[:, 0:D], sOutp[0])
    c.waitall("sync", sOutp[0])
    c.waitall("sync", sSC)
    c.waitall("sync", sYW)
    c.waitall("sync", sYG)
    c.waitall("sync", sPF)


def host_consts():
    return {"cIDENT": np.eye(P, dtype=np.float32)}


def rope_tables():
    import jax
    cpu = jax.devices("cpu")[0]
    with jax.default_device(cpu):
        import jax.numpy as jnp
        half = 64
        pos = jnp.arange(L, dtype=jnp.float32)[:, None]
        inv_freq = 10000.0 ** (jnp.arange(half, dtype=jnp.float32) / half)
        sinusoid = pos[..., None] * inv_freq[None, None, :]
        sin = np.asarray(jnp.sin(sinusoid))[:, 0, :]
        cos = np.asarray(jnp.cos(sinusoid))[:, 0, :]
    return np.ascontiguousarray(cos.T), np.ascontiguousarray(sin.T)


def derived_l0aa(inp):
    uv_w = inp["L0_attn_a_uv_w"]
    o_w = inp["L0_attn_a_o_w"]
    w_rel = inp["L0_attn_a_w_rel"]
    gamma = inp["L0_attn_a_gamma"]
    beta = inp["L0_attn_a_beta"]
    idx = np.clip(np.arange(L)[:, None] - np.arange(L)[None, :] + (L - 1), 0, 2 * L - 2)
    idxT = np.clip(np.arange(L)[:, None] - np.arange(L)[None, :] + (L - 1), 0, w_rel.shape[0] - 1)
    biasT = np.ascontiguousarray(w_rel[idxT]).astype(np.float32)
    cos, sin = rope_tables()
    cGB = np.ascontiguousarray(np.stack([gamma[0], beta[0], gamma[1], beta[1]], axis=1))
    return {
        "cTuv": np.ascontiguousarray(uv_w.T),
        "cTow": np.ascontiguousarray(o_w.T),
        "cBIAST": biasT,
        "cGB": cGB.astype(np.float32),
        "cCCS": np.ascontiguousarray(np.vstack([cos, sin])),
        "cSSC": np.ascontiguousarray(np.vstack([sin, cos])),
    }


def derived_l1aa(inp):
    lt = (np.arange(P)[:, None] <= np.arange(P)[None, :]).astype(np.float32)
    return {
        "cTq": np.ascontiguousarray(inp["L1_attn_a_Wq"].T),
        "cTk": np.ascontiguousarray(inp["L1_attn_a_Wk"].T),
        "cTv": np.ascontiguousarray(inp["L1_attn_a_Wv"].T),
        "cTo": np.ascontiguousarray(inp["L1_attn_a_Wo"].T),
        "cLT": lt,
    }


def derived_l1ab(inp):
    return {
        "cTq": np.ascontiguousarray(inp["L1_attn_b_Wq"].T),
        "cTk": np.ascontiguousarray(inp["L1_attn_b_Wk"].T),
        "cTv": np.ascontiguousarray(inp["L1_attn_b_Wv"].T),
        "cTff": np.ascontiguousarray(inp["L1_attn_b_Wff"].T),
    }


def derived_l0ab(inp):
    qkv_w = inp["L0_attn_b_qkv_w"]
    off_w = inp["L0_attn_b_off_w"]
    j = (np.arange(16)[:, None] * P + np.arange(P)[None, :]).reshape(-1)
    cst = np.zeros((16, P, 2048), np.float32)
    for i in range(16):
        itok = i * P + np.arange(P)
        jj = j.reshape(16, P)
        cm = jj.T[:, :, None] - itok[None, None, :]
        cm = np.transpose(jj.T[:, :, None] - itok[None, None, :], (0, 1, 2))
        cm = cm.reshape(P, 2048, order="C") if False else cm.reshape(P, 16 * P)
        cst[i] = np.where(cm < 0, 30000.0, cm)
    return {
        "cTqkv": np.ascontiguousarray(qkv_w.T),
        "cToff": np.ascontiguousarray(off_w.T),
        "cCST": cst.astype(np.float16),
    }


def derived_iotac(inp):
    return {"cIOTAC": np.arange(P, dtype=np.float32).reshape(P, 1)}


SUB_DERIVED = {"l0aa": derived_l0aa, "l0ab": derived_l0ab, "l1ab": derived_l1ab,
               "l1aa": derived_l1aa, "l0ffn_sp": derived_iotac,
               "l1ffn_sp": derived_iotac}


SUB_WEIGHTS = {
    "l0ffn": [("Wr", "L0_ffn_Wr"), ("br", "L0_ffn_br"), ("We1", "L0_ffn_We1"),
              ("be1", "L0_ffn_be1"), ("We2", "L0_ffn_We2"), ("be2", "L0_ffn_be2")],
    "l0ffn_sp": [("Wr", "L0_ffn_Wr"), ("br", "L0_ffn_br"), ("We1", "L0_ffn_We1"),
                 ("be1", "L0_ffn_be1"), ("We2", "L0_ffn_We2"), ("be2", "L0_ffn_be2")],
    "l0aa": [("g", "L0_attn_a_g"), ("res_scale", "L0_attn_a_res_scale")],
    "l0ab": [("qkv_b", "L0_attn_b_qkv_b"), ("lin2_w", "L0_attn_b_lin2_w"),
             ("lin2_b", "L0_attn_b_lin2_b")],
    "l1ab": [("bff", "L1_attn_b_bff")],
    "l1aa": [("bq", "L1_attn_a_bq"), ("bk", "L1_attn_a_bk"), ("bv", "L1_attn_a_bv"),
             ("bo", "L1_attn_a_bo"), ("beta", "L1_attn_a_beta")],
    "l1ffn": [("gw", "L1_ffn_gw"), ("gb", "L1_ffn_gb"), ("w1", "L1_ffn_w1"),
              ("b1", "L1_ffn_b1"), ("w3", "L1_ffn_w3"), ("b3", "L1_ffn_b3"),
              ("w2", "L1_ffn_w2"), ("b2", "L1_ffn_b2"), ("sw1", "L1_ffn_sw1"),
              ("sb1", "L1_ffn_sb1"), ("sw3", "L1_ffn_sw3"), ("sb3", "L1_ffn_sb3"),
              ("sw2", "L1_ffn_sw2"), ("sb2", "L1_ffn_sb2")],
}
SUB_WEIGHTS["l1ffn_sp"] = SUB_WEIGHTS["l1ffn"]

SUB_EMIT = {}


def _emit_sub_l0ffn(c, W, xsrc_fn, xdst_fn):
    emit_l0ffn(c, W, xsrc_fn, xdst_fn)


def _emit_sub_l0ffn_sp(c, W, xsrc_fn, xdst_fn):
    emit_l0ffn_sp(c, W, xsrc_fn, xdst_fn, W["_xfull"])


def _emit_sub_l1ffn_sp(c, W, xsrc_fn, xdst_fn):
    emit_l1ffn_sp(c, W, xsrc_fn, xdst_fn, W["_xfull"])


SUB_EMIT["l0ffn"] = _emit_sub_l0ffn
SUB_EMIT["l0ffn_sp"] = _emit_sub_l0ffn_sp
SUB_EMIT["l1ffn_sp"] = _emit_sub_l1ffn_sp
SUB_EMIT["l1ffn"] = emit_l1ffn
SUB_EMIT["l0aa"] = emit_l0aa
SUB_EMIT["l0ab"] = emit_l0ab
SUB_EMIT["l1ab"] = emit_l1ab
SUB_EMIT["l1aa"] = emit_l1aa


_WEIGHT_SHAPES = {
    "L0_ffn_Wr": (D, 4), "L0_ffn_br": (4,), "L0_ffn_We1": (4, D, 4096),
    "L0_ffn_be1": (4, 4096), "L0_ffn_We2": (4, 4096, D), "L0_ffn_be2": (4, D),
    "L1_ffn_gw": (D, 32), "L1_ffn_gb": (32,), "L1_ffn_w1": (32, D, 512),
    "L1_ffn_b1": (32, 512), "L1_ffn_w3": (32, D, 512), "L1_ffn_b3": (32, 512),
    "L1_ffn_w2": (32, 512, D), "L1_ffn_b2": (32, D), "L1_ffn_sw1": (D, 2 * 512),
    "L1_ffn_sb1": (2 * 512,), "L1_ffn_sw3": (D, 2 * 512), "L1_ffn_sb3": (2 * 512,),
    "L1_ffn_sw2": (2 * 512, D), "L1_ffn_sb2": (D,),
    "L0_attn_a_g": (1,), "L0_attn_a_res_scale": (D,),
    "L0_attn_b_qkv_b": (3 * D,), "L0_attn_b_lin2_w": (1, 256),
    "L0_attn_b_lin2_b": (1,), "L1_attn_b_bff": (D,),
    "L1_attn_a_bq": (D,), "L1_attn_a_bk": (D,), "L1_attn_a_bv": (D,),
    "L1_attn_a_bo": (D,), "L1_attn_a_beta": (1,),
}
NPDT = {np.dtype(np.float32): F32, np.dtype(np.float16): mybir.dt.float16,
        np.dtype(np.uint8): U8}


def build_single(sub, inp=None):
    nc = bass.Bass("TRN2", target_bir_lowering=False, detect_race_conditions=False)
    nc.gpsimd.get_value_cache().nocache = True
    es = contextlib.ExitStack()
    c = B(nc, es)
    xin = nc.dram_tensor("xin", [NB, L, D], F32, kind="ExternalInput")
    xout = nc.dram_tensor("xout", [NB, L, D], F32, kind="ExternalOutput")
    consts_np = host_consts()
    consts = {k: nc.dram_tensor(k, list(v.shape), F32, kind="ExternalInput")
              for k, v in consts_np.items()}
    W = {}
    for wname, iname in SUB_WEIGHTS[sub]:
        h = nc.dram_tensor(iname, list(_WEIGHT_SHAPES[iname]), F32, kind="ExternalInput")
        W[wname] = h
    extra_in = {}
    if sub in SUB_DERIVED:
        der = SUB_DERIVED[sub](inp)
        for k, v in der.items():
            W[k] = nc.dram_tensor(k, list(v.shape), NPDT[v.dtype], kind="ExternalInput")
            extra_in[k] = v

    W["_xfull"] = xin.rearrange("b l d -> (b l) d")

    def xsrc_fn(t):
        b, l0 = t // TPB, (t % TPB) * P
        return xin[b, l0:l0 + P, :]

    def xdst_fn(t):
        b, l0 = t // TPB, (t % TPB) * P
        return xout[b, l0:l0 + P, :]

    with es:
        alloc_shared(c)
        emit_init(c, consts)
        SUB_EMIT[sub](c, W, xsrc_fn, xdst_fn)
        c.emit_all()
    consts_np.update(extra_in)
    return nc, consts_np


_SUB_ORDER = ["l0aa", "l0ffn_sp", "l0ab", "l0ffn_sp", "l1aa", "l1ffn_sp", "l1ab",
              "l1ffn_sp"]

FUSED = True


def build_fused(inp):
    nc = bass.Bass("TRN2", target_bir_lowering=False, detect_race_conditions=False)
    nc.gpsimd.get_value_cache().nocache = True
    es = contextlib.ExitStack()
    c = B(nc, es)
    xin = nc.dram_tensor("xin", [NB, L, D], F32, kind="ExternalInput")
    xout = nc.dram_tensor("xout", [NB, L, D], F32, kind="ExternalOutput")
    nsub = len(_SUB_ORDER)
    mids = [nc.dram_tensor("xmid%d" % i, [NB, L, D], F32) for i in range(nsub - 1)]
    consts_np = host_consts()
    consts = {k: nc.dram_tensor(k, list(v.shape), F32, kind="ExternalInput")
              for k, v in consts_np.items()}
    whandles = {}
    extra_in = {}
    der_handles = {}
    for sub in set(_SUB_ORDER):
        for wname, iname in SUB_WEIGHTS[sub]:
            if iname not in whandles:
                whandles[iname] = nc.dram_tensor(
                    iname, list(_WEIGHT_SHAPES[iname]), F32, kind="ExternalInput")
        if sub in SUB_DERIVED and sub not in der_handles:
            der = SUB_DERIVED[sub](inp)
            dh = {}
            for k, v in der.items():
                name = sub + "_" + k
                dh[k] = nc.dram_tensor(name, list(v.shape), NPDT[v.dtype],
                                       kind="ExternalInput")
                extra_in[name] = v
            der_handles[sub] = dh

    with es:
        alloc_shared(c)
        emit_init(c, consts)
        snap = {}
        for i, sub in enumerate(_SUB_ORDER):
            c.ns = "i%d_" % i
            xs = xin if i == 0 else mids[i - 1]
            xd = xout if i == nsub - 1 else mids[i]
            W = {}
            for wname, iname in SUB_WEIGHTS[sub]:
                W[wname] = whandles[iname]
            if sub in der_handles:
                W.update(der_handles[sub])
            W["_xfull"] = xs.rearrange("b l d -> (b l) d")

            def xsrc_fn(t, xs=xs):
                b, l0 = t // TPB, (t % TPB) * P
                return xs[b, l0:l0 + P, :]

            def xdst_fn(t, xd=xd):
                b, l0 = t // TPB, (t % TPB) * P
                return xd[b, l0:l0 + P, :]

            SUB_EMIT[sub](c, W, xsrc_fn, xdst_fn)
            if i < nsub - 1:
                snap = c.barrier(snap)
        c.emit_all()
    consts_np.update(extra_in)
    return nc, consts_np


_FUSED_CACHE = {}


def kernel(**inputs):
    x = np.ascontiguousarray(inputs["L0_attn_a_x"], dtype=np.float32)
    if FUSED:
        if "prog" not in _FUSED_CACHE:
            _FUSED_CACHE["prog"] = build_fused(inputs)
        nc, consts_np = _FUSED_CACHE["prog"]
        wnames = set()
        for sub in set(_SUB_ORDER):
            for wname, iname in SUB_WEIGHTS[sub]:
                wnames.add(iname)
        in_maps = []
        for core in range(NCORES):
            m = {"xin": np.ascontiguousarray(x[NB * core:NB * (core + 1)])}
            m.update(consts_np)
            for iname in wnames:
                m[iname] = inputs[iname]
            in_maps.append(m)
        res = run_bass_kernel_spmd(nc, in_maps, core_ids=list(range(NCORES)))
        return np.concatenate([np.asarray(r["xout"]) for r in res.results], axis=0)
    progs = {}
    for sub in _SUB_ORDER:
        if sub not in progs:
            progs[sub] = build_single(sub, inputs)
    for sub in _SUB_ORDER:
        nc, consts_np = progs[sub]
        in_maps = []
        for core in range(NCORES):
            m = {"xin": np.ascontiguousarray(x[NB * core:NB * (core + 1)])}
            m.update(consts_np)
            for wname, iname in SUB_WEIGHTS[sub]:
                m[iname] = inputs[iname]
            in_maps.append(m)
        res = run_bass_kernel_spmd(nc, in_maps, core_ids=list(range(NCORES)))
        x = np.concatenate([np.asarray(r["xout"]) for r in res.results], axis=0)
    return x
